# version 03
# baseline (speedup 1.0000x reference)
import os

os.environ.setdefault("TRN_ALLOW_ICI_ALL_TO_ALL", "1")

from contextlib import ExitStack

import numpy as np
import ml_dtypes
import concourse.bass as bass
import concourse.mybir as mybir
from concourse.bass_utils import run_bass_kernel_spmd

F32 = mybir.dt.float32
BF16 = mybir.dt.bfloat16
AF = mybir.ActivationFunctionType
ALU = mybir.AluOpType
AX = mybir.AxisListType

NCORES = 8
TOK = 4096
DM = 1024
NCH = DM // 128
LN_TT = 256


def _sb(nc, name, shape, dtype):
    return nc.sbuf_tensor(name, shape, dtype, allow_name_mangling=True)


def _ps(nc, name, shape, dtype):
    return nc.psum_tensor(name, shape, dtype, allow_name_mangling=True)


class Eng:
    def __init__(self, name, stream, sem, step, serialize=False, dma=False):
        self.name = name
        self.stream = stream
        self.sem = sem
        self.cnt = 0
        self.step = step
        self.serialize = serialize
        self.dma = dma
        self.hw = {}


class B:
    STREAMS = ("sync", "gpsimd", "tensor", "scalar", "vector")

    def __init__(self, nc: bass.Bass):
        self.nc = nc
        self.engs = {}
        self.lists = {s: [] for s in self.STREAMS}

    def reg(self, name, stream, dma=False, serialize=False, step=None):
        sem = self.nc.alloc_semaphore(f"sem_{name}")
        e = Eng(name, stream, sem, step if step is not None else (16 if dma else 1),
                serialize, dma)
        self.engs[name] = e
        setattr(self, name, e)
        return e

    AUTO = ("act", "dve", "pool")

    def op(self, e: Eng, fn, dep=None):
        if e.serialize and e.cnt > e.hw.get(e.name, 0):
            sem, v = e.sem, e.cnt
            self.lists[e.stream].append(
                [lambda h, s=sem, vv=v: h.wait_ge(s, vv), []])
            e.hw[e.name] = v
        if e.name in self.AUTO:
            if dep is None or dep:
                if e.hw.get(e.name, 0) < e.cnt:
                    sem, v = e.sem, e.cnt
                    self.lists[e.stream].append(
                        [lambda h, s=sem, vv=v: h.wait_ge(s, vv), []])
                    e.hw[e.name] = v
            entry = [fn, [(e.sem, 1)]]
            e.cnt += 1
        else:
            entry = [fn, []]
        self.lists[e.stream].append(entry)
        self._last = {**getattr(self, "_last", {}), e.name: entry}
        return entry

    def inc(self, e: Eng, n=None):
        if e.name in self.AUTO:
            return e.cnt
        n = n if n is not None else e.step
        self._last[e.name][1].append((e.sem, n))
        e.cnt += n
        return e.cnt

    def wait(self, cons: Eng, prod: Eng, upto=None):
        v = upto if upto is not None else prod.cnt
        if v <= 0 or cons.hw.get(prod.name, 0) >= v:
            return
        sem = prod.sem
        self.lists[cons.stream].append([lambda h, s=sem, vv=v: h.wait_ge(s, vv), []])
        cons.hw[prod.name] = v

    def flush(self):
        lists = {s: l for s, l in self.lists.items() if l}
        self.lists = {s: [] for s in self.STREAMS}
        if not lists:
            return
        with self.nc.Block() as blk:
            def mk(lst):
                def run(h):
                    for fn, incs in lst:
                        ins = fn(h)
                        for sem, n in incs:
                            ins.then_inc(sem, n)
                return run
            for s, lst in lists.items():
                getattr(blk, s)(mk(lst))
        for c in self.engs.values():
            for p in self.engs.values():
                if not p.dma:
                    c.hw[p.name] = p.cnt


def make_b(nc):
    b = B(nc)
    b.reg("qs", "sync", dma=True, serialize=True)
    b.reg("qss", "sync", dma=True, serialize=True)
    b.reg("qw", "sync", dma=True, serialize=True)
    b.reg("qw2", "sync", dma=True, serialize=True)
    b.reg("qg", "gpsimd", dma=True, serialize=True)
    b.reg("qg2", "gpsimd", dma=True, serialize=True)
    b.reg("pe", "tensor")
    b.reg("act", "scalar")
    b.reg("dve", "vector")
    b.reg("pool", "gpsimd")
    return b


class LnScratch:
    def __init__(self, nc, stack, pst=None, pbc=None, tag=""):
        if pst is None:
            pst = stack.enter_context(_ps(nc, f"ln_pst{tag}", [1, 2, LN_TT], F32))[:, :, :]
        if pbc is None:
            pbc = stack.enter_context(_ps(nc, f"ln_pbc{tag}", [128, 2, LN_TT], F32))[:, :, :]
        self.sq = stack.enter_context(_sb(nc, f"ln_sq{tag}", [128, 2, LN_TT], F32))
        self.stats = stack.enter_context(_sb(nc, f"ln_stats{tag}", [1, 4, LN_TT], F32))
        self.ones_col = stack.enter_context(_sb(nc, f"ones_col{tag}", [128, 1], F32))
        self.ones_row = stack.enter_context(_sb(nc, f"ones_row{tag}", [1, 128], F32))
        self.pst = pst
        self.pbc = pbc

    def init(self, b: B, dve: Eng):
        b.op(dve, lambda h, s=self: h.memset(s.ones_col[:, :], 1.0))
        b.op(dve, lambda h, s=self: h.memset(s.ones_row[:, :], 1.0))
        b.inc(dve)


def emit_ln_tile(b: B, ls: LnScratch, xt, xln_out):
    T = xt.shape[2]
    pe, act, dve = b.pe, b.act, b.dve
    st = ls.stats
    b.wait(pe, dve)
    b.wait(pe, act)
    b.wait(act, dve)
    for c in range(NCH):
        b.op(pe, lambda h, c=c, T=T, xt=xt: h.matmul(
            ls.pst[0:1, 0, 0:T], ls.ones_col[:, 0:1], xt[:, c, :],
            start=(c == 0), stop=(c == NCH - 1)))
    b.inc(pe)
    for c in range(NCH):
        b.op(act, lambda h, c=c, T=T, xt=xt: h.activation(
            ls.sq[:, c % 2, 0:T], xt[:, c, :], AF.Square))
        a_mark = b.inc(act)
        b.wait(pe, act, a_mark)
        b.op(pe, lambda h, c=c, T=T: h.matmul(
            ls.pst[0:1, 1, 0:T], ls.ones_col[:, 0:1], ls.sq[:, c % 2, 0:T],
            start=(c == 0), stop=(c == NCH - 1)))
        pe_mark = b.inc(pe)
        b.wait(act, pe, pe_mark)
    b.wait(dve, pe)
    b.op(dve, lambda h, T=T: h.tensor_scalar_mul(st[:, 0, 0:T], ls.pst[0:1, 0, 0:T], 1.0 / DM))
    b.op(dve, lambda h, T=T: h.tensor_scalar_mul(st[:, 1, 0:T], ls.pst[0:1, 1, 0:T], 1.0 / DM))
    b.op(dve, lambda h, T=T: h.tensor_tensor(st[:, 2, 0:T], st[:, 0, 0:T], st[:, 0, 0:T], ALU.mult))
    b.op(dve, lambda h, T=T: h.tensor_tensor(st[:, 2, 0:T], st[:, 1, 0:T], st[:, 2, 0:T], ALU.subtract))
    b.op(dve, lambda h, T=T: h.tensor_scalar_add(st[:, 2, 0:T], st[:, 2, 0:T], 1e-5))
    d_mark = b.inc(dve)
    b.wait(act, dve, d_mark)
    b.op(act, lambda h, T=T: h.activation(st[:, 2, 0:T], st[:, 2, 0:T], AF.Ln))
    b.op(act, lambda h, T=T: h.activation(st[:, 3, 0:T], st[:, 2, 0:T], AF.Exp, scale=-0.5))
    a_mark = b.inc(act)
    b.wait(dve, act, a_mark)
    b.op(dve, lambda h, T=T: h.tensor_tensor(st[:, 2, 0:T], st[:, 0, 0:T], st[:, 3, 0:T], ALU.mult))
    d_mark = b.inc(dve)
    b.wait(pe, dve, d_mark)
    b.op(pe, lambda h, T=T: h.matmul(ls.pbc[:, 0, 0:T], ls.ones_row[:, :], st[:, 3, 0:T],
                                     start=True, stop=True))
    b.op(pe, lambda h, T=T: h.matmul(ls.pbc[:, 1, 0:T], ls.ones_row[:, :], st[:, 2, 0:T],
                                     start=True, stop=True))
    pe_mark = b.inc(pe)
    b.wait(dve, pe, pe_mark)
    for c in range(NCH):
        b.op(dve, lambda h, c=c, T=T, xt=xt: h.scalar_tensor_tensor(
            ls.sq[:, c % 2, 0:T], xt[:, c, :], 1.0, ls.pbc[:, 0, 0:T],
            ALU.mult, ALU.mult))
        b.op(dve, lambda h, c=c, T=T, xln_out=xln_out: h.tensor_tensor(
            xln_out[:, c, :], ls.sq[:, c % 2, 0:T], ls.pbc[:, 1, 0:T], ALU.subtract))
    b.inc(dve)


TSTEPS = 16
LLOC = 256


def emit_ffn0(nc, b, W, xin_f, xout_f, first=True):
    wts = W
    dbg = False
    with ExitStack() as stack:
        xln = stack.enter_context(_sb(nc, "xln", [128, NCH, TOK], BF16))
        wu = stack.enter_context(_sb(nc, "wu", [128, 6, NCH, DM], BF16))
        bias_sb = stack.enter_context(_sb(nc, "bias_sb", [128, 5, NCH], F32))
        c_f32 = stack.enter_context(_sb(nc, "c_f32", [128, NCH, LLOC], F32))
        c_bf = stack.enter_context(_sb(nc, "c_bf", [128, 2, NCH, LLOC], BF16))
        gates = stack.enter_context(_sb(nc, "gates", [128, 3, LLOC], F32))
        hbuf = stack.enter_context(_sb(nc, "hbuf", [128, 2, LLOC], F32))
        xt0 = stack.enter_context(_sb(nc, "xt0", [128, NCH, LLOC], F32))
        xt1 = stack.enter_context(_sb(nc, "xt1", [128, NCH, LLOC], F32))
        ls = LnScratch(nc, stack)
        pg0 = stack.enter_context(_ps(nc, "pg0", [128, 3, 512], F32))[:, :, 0:LLOC]
        pg1 = stack.enter_context(_ps(nc, "pg1", [128, 3, 512], F32))[:, :, 0:LLOC]

        qs, qss, qw, qw2 = b.qs, b.qss, b.qw, b.qw2
        qg, qg2, pe, act, dve, pool = b.qg, b.qg2, b.pe, b.act, b.dve, b.pool

        for vi, v in enumerate(["bf", "bi", "bc", "bo", "in_c"]):
            b.op(qs, lambda h, vi=vi, v=v: h.dma_start(
                out=bias_sb[:, vi, :], in_=wts[v]))
            b.inc(qs)
        for gi, w in enumerate(["wfT", "wiT", "wcT", "ufT", "uiT", "ucT"]):
            b.op(qg, lambda h, gi=gi, w=w: h.dma_start(
                out=wu[:, gi, :, :].rearrange("p k (o q) -> p k o q", q=128),
                in_=wts[w].rearrange("(k p) (o q) -> p k o q", p=128, q=128)))
            b.inc(qg)
        ls.init(b, dve)
        b.flush()

        NT = TOK // LN_TT
        for i in range(NT):
            buf = xt0
            b.wait(qs, dve)
            b.op(qs, lambda h, i=i, buf=buf: h.dma_start(
                out=buf[:, :, :], in_=xin_f(i * LN_TT, LN_TT)))
            ld = b.inc(qs)
            b.wait(pe, qs, ld)
            b.wait(act, qs, ld)
            emit_ln_tile(b, ls, buf[:, :, :], xln[:, :, i * LN_TT:(i + 1) * LN_TT])
        b.flush()

        b.wait(pe, qg)
        b.wait(act, qs)
        b.wait(dve, qs)
        for c in range(NCH):
            b.op(dve, lambda h, c=c: h.memset(c_f32[:, c, :], 1.0))
            b.op(dve, lambda h, c=c: h.tensor_scalar_mul(
                c_f32[:, c, :], c_f32[:, c, :], bias_sb[:, 4, c:c + 1]))
            b.op(dve, lambda h, c=c: h.tensor_copy(c_bf[:, 0, c, :], c_f32[:, c, :]))
        b.inc(dve)

        psum_free = {}
        dve_marks = {-1: b.dve.cnt}
        for t_ in range(TSTEPS):
            cur, nxt = t_ % 2, 1 - t_ % 2
            for o in range(NCH):
                pidx = (t_ * NCH + o) % 2
                ps = pg0 if pidx == 0 else pg1
                if pidx in psum_free:
                    b.wait(pe, dve, psum_free[pidx])
                b.wait(pe, dve, dve_marks[t_ - 1])
                for gi in range(3):
                    for k in range(NCH):
                        b.op(pe, lambda h, k=k, gi=gi, o=o, t_=t_, ps=ps: h.matmul(
                            ps[:, gi, :],
                            wu[:, gi, k, o * 128:(o + 1) * 128],
                            xln[:, k, t_ * LLOC:(t_ + 1) * LLOC],
                            start=(k == 0), stop=False))
                    for k in range(NCH):
                        b.op(pe, lambda h, k=k, gi=gi, o=o, cur=cur, ps=ps: h.matmul(
                            ps[:, gi, :],
                            wu[:, 3 + gi, k, o * 128:(o + 1) * 128],
                            c_bf[:, cur, k, :],
                            start=False, stop=(k == NCH - 1)))
                pe_mark = b.inc(pe)
                b.wait(act, pe, pe_mark)
                b.wait(act, dve)
                b.op(act, lambda h, o=o, ps=ps: h.activation(
                    gates[:, 0, :], ps[:, 0, :], AF.Sigmoid, bias=bias_sb[:, 0, o:o + 1]))
                b.op(act, lambda h, o=o, ps=ps: h.activation(
                    gates[:, 1, :], ps[:, 1, :], AF.Sigmoid, bias=bias_sb[:, 1, o:o + 1]))
                b.op(act, lambda h, o=o, ps=ps: h.activation(
                    gates[:, 2, :], ps[:, 2, :], AF.Tanh, bias=bias_sb[:, 2, o:o + 1]))
                act_mark = b.inc(act)
                b.wait(dve, act, act_mark)
                b.op(dve, lambda h, o=o: h.tensor_tensor(
                    c_f32[:, o, :], c_f32[:, o, :], gates[:, 0, :], ALU.mult))
                b.op(dve, lambda h: h.tensor_tensor(
                    gates[:, 1, :], gates[:, 1, :], gates[:, 2, :], ALU.mult))
                b.op(dve, lambda h, o=o: h.tensor_tensor(
                    c_f32[:, o, :], c_f32[:, o, :], gates[:, 1, :], ALU.add))
                b.op(dve, lambda h, o=o, nxt=nxt: h.tensor_copy(
                    c_bf[:, nxt, o, :], c_f32[:, o, :]))
                psum_free[pidx] = b.inc(dve)
            dve_marks[t_] = b.dve.cnt
        b.flush()

        if dbg:
            b.op(qs, lambda h: h.dma_start(out=xln_dbg, in_=xln[:, :, :]))
            b.inc(qs)
            b.op(qs, lambda h: h.dma_start(out=c_dbg, in_=c_f32[:, :, :]))
            b.inc(qs)
            b.flush()

        cfin = TSTEPS % 2
        for gi, w in enumerate(["woT", "uoT"]):
            b.op(qg, lambda h, gi=gi, w=w: h.dma_start(
                out=wu[:, gi, :, :].rearrange("p k (o q) -> p k o q", q=128),
                in_=wts[w].rearrange("(k p) (o q) -> p k o q", p=128, q=128)))
            b.inc(qg)
        b.wait(pe, qg)
        psum_free = {}
        store_mark = 0
        for t_ in range(TSTEPS):
            buf = xt0
            obuf = xt1
            if store_mark:
                b.wait(dve, qss, store_mark)
            b.wait(qs, dve)
            b.op(qs, lambda h, t_=t_, buf=buf: h.dma_start(
                out=buf[:, :, :], in_=xin_f(t_ * LLOC, LLOC)))
            ld = b.inc(qs)
            for o in range(NCH):
                pidx = (t_ * NCH + o) % 2
                ps = pg0 if pidx == 0 else pg1
                if pidx in psum_free:
                    b.wait(pe, dve, psum_free[pidx])
                for k in range(NCH):
                    b.op(pe, lambda h, k=k, o=o, t_=t_, ps=ps: h.matmul(
                        ps[:, 0, :], wu[:, 0, k, o * 128:(o + 1) * 128],
                        xln[:, k, t_ * LLOC:(t_ + 1) * LLOC],
                        start=(k == 0), stop=False))
                for k in range(NCH):
                    b.op(pe, lambda h, k=k, o=o, ps=ps: h.matmul(
                        ps[:, 0, :], wu[:, 1, k, o * 128:(o + 1) * 128],
                        c_bf[:, cfin, k, :],
                        start=False, stop=(k == NCH - 1)))
                pe_mark = b.inc(pe)
                b.wait(act, pe, pe_mark)
                b.wait(act, dve)
                b.op(act, lambda h, o=o, ps=ps: h.activation(
                    hbuf[:, o % 2, :], ps[:, 0, :], AF.Tanh, bias=bias_sb[:, 3, o:o + 1]))
                act_mark = b.inc(act)
                b.wait(dve, act, act_mark)
                b.wait(dve, qs, ld)
                b.op(dve, lambda h, o=o, buf=buf, obuf=obuf: h.scalar_tensor_tensor(
                    obuf[:, o, :], hbuf[:, o % 2, :], 0.5, buf[:, o, :],
                    ALU.mult, ALU.add))
                psum_free[pidx] = b.inc(dve)
            b.wait(qss, dve)
            b.op(qss, lambda h, t_=t_, obuf=obuf: h.dma_start(
                out=xout_f(t_ * LLOC, LLOC), in_=obuf[:, :, :]))
            store_mark = b.inc(qss)
        b.op(pool, lambda h, v=b.qss.cnt: h.wait_ge(b.qss.sem, v))
        b.inc(pool)
        b.flush()


NSEQ = 2
SEQ = 2048
NHEAD = 16
HD = 64


def emit_ffn1(nc, b, W, xin_f, xout_f, first=True):
    PH = 3
    wqkv, bqkv_row, bqkv_col = W["wqkv"], W["bqkv_row"], W["bqkv_col"]
    wgate, bgate, wexp, bexp = W["wgate"], W["bgate"], W["wexp"], W["bexp"]
    wexp_bf = W["wexp_bf"]
    with ExitStack() as stack:
        xln = stack.enter_context(_sb(nc, "xln", [128, NCH, TOK], BF16))
        attn_sb = stack.enter_context(_sb(nc, "attn_sb", [128, NCH, TOK], BF16))
        cfm = stack.enter_context(_sb(nc, "cfm", [8, TOK], BF16))
        ident = stack.enter_context(_sb(nc, "ident", [128, 128], F32))
        bq_col = stack.enter_context(_sb(nc, "bq_col", [128, 24], F32))
        bg_col = stack.enter_context(_sb(nc, "bg_col", [8, 1], F32))
        be_sb = stack.enter_context(_sb(nc, "be_sb", [8, DM], BF16))
        wg_sb = stack.enter_context(_sb(nc, "wg_sb", [128, NCH, 8], BF16))
        orow_bf = stack.enter_context(_sb(nc, "orow_bf", [1, 128], BF16))
        pa = stack.enter_context(_ps(nc, "pa", [128, 2, 512], F32))[:, :, :]
        pb = stack.enter_context(_ps(nc, "pb", [128, 4, 512], F32))[:, :, :]

        qs, qss, qw, qw2 = b.qs, b.qss, b.qw, b.qw2
        qg, qg2, pe, act, dve, pool = b.qg, b.qg2, b.pe, b.act, b.dve, b.pool

        b.op(qs, lambda h: h.dma_start(out=bq_col[:, :], in_=bqkv_col))
        b.inc(qs)
        b.op(qs, lambda h: h.dma_start(out=bg_col[:, :], in_=bgate))
        b.inc(qs)
        b.op(qg, lambda h: h.dma_start(out=be_sb[:, :], in_=bexp))
        b.inc(qg)
        b.op(qg, lambda h: h.dma_start(
            out=wg_sb[:, :, :],
            in_=wgate.rearrange("(c p) e -> p c e", p=128)))
        b.inc(qg)
        if first:
            for e in range(8):
                b.op(qg, lambda h, e=e: h.dma_start(out=wexp_bf[e], in_=wexp[e]))
                b.inc(qg)
        b.op(pool, lambda h: h.memset(ident[:, :], 0.0))
        b.op(pool, lambda h: h.affine_select(
            out=ident[:, :], in_=ident[:, :], compare_op=ALU.not_equal,
            fill=1.0, base=0, pattern=[[-1, 128]], channel_multiplier=1))
        b.op(dve, lambda h: h.memset(orow_bf[:, :], 1.0))
        b.inc(dve)

        with ExitStack() as _phase_guard:
            ln_stack = ExitStack()
            xt0 = ln_stack.enter_context(_sb(nc, "xt0", [128, NCH, LN_TT], F32))
            ls = LnScratch(nc, ln_stack)
            ls.init(b, dve)
            for i in range(TOK // LN_TT):
                b.wait(qs, dve)
                b.op(qs, lambda h, i=i: h.dma_start(
                    out=xt0[:, :, :], in_=xin_f(i * LN_TT, LN_TT)))
                ld = b.inc(qs)
                b.wait(pe, qs, ld)
                b.wait(act, qs, ld)
                emit_ln_tile(b, ls, xt0[:, :, :], xln[:, :, i * LN_TT:(i + 1) * LN_TT])
            b.flush()
            ln_stack.close()


            with (
                _sb(nc, "glog", [8, TOK], F32) as glog,
                _sb(nc, "gt", [128, 10, 8], F32) as gt,
                _sb(nc, "gcols", [128, 4], F32) as gcols,
                _sb(nc, "ct", [128, 8], F32) as ct,
            ):
                b.wait(pe, qg)
                b.wait(act, qs)
                for i in range(TOK // 512):
                    b.wait(pe, act)
                    for k in range(NCH):
                        b.op(pe, lambda h, i=i, k=k: h.matmul(
                            pa[0:8, 0, :], wg_sb[:, k, :],
                            xln[:, k, i * 512:(i + 1) * 512],
                            start=(k == 0), stop=(k == NCH - 1)))
                    m = b.inc(pe)
                    b.wait(act, pe, m)
                    b.op(act, lambda h, i=i: h.activation(
                        glog[:, i * 512:(i + 1) * 512], pa[0:8, 0, :],
                        AF.Identity, bias=bg_col[:, :]))
                b.flush()
                for i in range(TOK // 128):
                    sl = slice(i * 128, (i + 1) * 128)
                    b.wait(pe, dve)
                    b.op(pe, lambda h, sl=sl: h.transpose(
                        pa[:, 1, 0:8], glog[:, sl], ident[0:8, 0:8]))
                    m = b.inc(pe)
                    b.wait(dve, pe, m)
                    b.op(dve, lambda h: h.tensor_copy(gt[:, 0, :], pa[:, 1, 0:8]))
                    b.op(dve, lambda h: h.reduce_max(gcols[:, 0:1], gt[:, 0, :], axis=AX.X))
                    b.op(dve, lambda h: h.tensor_scalar_mul(gcols[:, 1:2], gcols[:, 0:1], -1.0))
                    d = b.inc(dve)
                    b.wait(act, dve, d)
                    b.op(act, lambda h: h.activation(gt[:, 1, :], gt[:, 0, :], AF.Exp,
                                                     bias=gcols[:, 1:2]))
                    a = b.inc(act)
                    b.wait(dve, act, a)
                    b.op(dve, lambda h: h.reduce_sum(gcols[:, 0:1], gt[:, 1, :], axis=AX.X))
                    b.op(dve, lambda h: h.reciprocal(gcols[:, 1:2], gcols[:, 0:1]))
                    b.op(dve, lambda h: h.tensor_scalar_mul(gt[:, 2, :], gt[:, 1, :], gcols[:, 1:2]))
                    b.op(dve, lambda h: h.reduce_max(gcols[:, 2:3], gt[:, 2, :], axis=AX.X))
                    b.op(dve, lambda h: h.tensor_single_scalar(gt[:, 3, :], gt[:, 2, :], gcols[:, 2:3], ALU.is_equal))
                    b.op(dve, lambda h: h.scalar_tensor_tensor(
                        gt[:, 4, :], gt[:, 3, :], -2.0, gt[:, 2, :], ALU.mult, ALU.add))
                    b.op(dve, lambda h: h.reduce_max(gcols[:, 3:4], gt[:, 4, :], axis=AX.X))
                    b.op(dve, lambda h: h.tensor_single_scalar(gt[:, 5, :], gt[:, 2, :], gcols[:, 3:4], ALU.is_ge))
                    b.op(dve, lambda h: h.tensor_tensor(ct[:, :], gt[:, 2, :], gt[:, 5, :], ALU.mult))
                    d = b.inc(dve)
                    b.wait(pe, dve, d)
                    b.op(pe, lambda h: h.transpose(pa[0:8, 1, 128:256], ct[:, :], ident[:, :]))
                    m = b.inc(pe)
                    b.wait(dve, pe, m)
                    b.op(dve, lambda h, sl=sl: h.tensor_copy(cfm[:, sl], pa[0:8, 1, 128:256]))
                    b.inc(dve)
                b.flush()


            with (
                _sb(nc, "qk_sb", [128, 4, SEQ], BF16) as qk_sb,
                _sb(nc, "v_sb", [128, 16, 4, 65], BF16) as v_sb,
                _sb(nc, "wbuf", [128, NCH, 1024], BF16) as wbuf,
                _sb(nc, "p_sb", [128, 2, 512], BF16) as p_sb,
                _sb(nc, "av_sb", [64, 512], BF16) as av_sb,
                _sb(nc, "r_sb", [128, 512], F32) as r_sb,
                _sb(nc, "ones128", [128, 128], F32) as ones128,
                _sb(nc, "bq_row", [1, 3 * DM], BF16) as bq_row,
            ):
                b.op(qg, lambda h: h.dma_start(out=bq_row[:, :], in_=bqkv_row))
                b.inc(qg)
                b.op(dve, lambda h: h.memset(ones128[:, :], 1.0))
                b.op(dve, lambda h: h.memset(
                    v_sb[:, :, :, 64:65].rearrange("p a c o -> p (a c o)"), 1.0))
                b.inc(dve)
                b.wait(act, qs)
                exp_marks = {}
                for hg in range(4):
                    b.wait(qg, pe)
                    for j, col in enumerate([hg * 256, DM + hg * 256, 2 * DM + hg * 256]):
                        b.op(qg, lambda h, j=j, col=col: h.dma_start(
                            out=wbuf[:, :, j * 256:(j + 1) * 256],
                            in_=wqkv.rearrange("(k p) f -> p k f", p=128)
                                [:, :, col:col + 256]))
                        b.inc(qg)
                    wld = b.qg.cnt
                    b.wait(pe, qg, wld)
                    for s_ in range(NSEQ):
                        seq0 = s_ * SEQ
                        b.wait(pe, dve)
                        for fc in range(4):
                            wslice = [0, 128, 256, 384][fc]
                            bcol = [2 * hg, 2 * hg + 1, 8 + 2 * hg, 8 + 2 * hg + 1][fc]
                            for i in range(SEQ // 512):
                                b.wait(pe, act)
                                for k in range(NCH):
                                    b.op(pe, lambda h, i=i, k=k, ws=wslice, seq0=seq0: h.matmul(
                                        pa[:, 0, :], wbuf[:, k, ws:ws + 128],
                                        xln[:, k, seq0 + i * 512:seq0 + (i + 1) * 512],
                                        start=(k == 0), stop=(k == NCH - 1)))
                                m = b.inc(pe)
                                b.wait(act, pe, m)
                                b.op(act, lambda h, fc=fc, i=i, bcol=bcol: h.activation(
                                    qk_sb[:, fc, i * 512:(i + 1) * 512], pa[:, 0, :],
                                    AF.Identity, bias=bq_col[:, bcol:bcol + 1]))
                        for tt in range(16):
                            tok0 = seq0 + tt * 128
                            b.wait(pe, act)
                            for k in range(NCH):
                                b.op(pe, lambda h, k=k, tok0=tok0: h.matmul(
                                    pa[:, 1, 0:256], xln[:, k, tok0:tok0 + 128],
                                    wbuf[:, k, 512:768],
                                    start=(k == 0), stop=False))
                            b.op(pe, lambda h, hg=hg: h.matmul(
                                pa[:, 1, 0:256], orow_bf[:, :],
                                bq_row[:, 2 * DM + hg * 256:2 * DM + (hg + 1) * 256],
                                start=False, stop=True))
                            m = b.inc(pe)
                            b.wait(act, pe, m)
                            b.op(act, lambda h, tt=tt: h.activation(
                                v_sb[:, tt, :, 0:64],
                                pa[:, 1, 0:256].rearrange("p (h f) -> p h f", h=4),
                                AF.Identity))
                        b.wait(pe, act)
                        for hi in range(4):
                            hglob = hg * 4 + hi
                            fc = hi // 2
                            rows = (hi % 2) * 64
                            for qt in range(4):
                                q0 = qt * 512
                                for kt in range(16):
                                    k0 = kt * 128
                                    pp = kt % 2
                                    if pp in exp_marks:
                                        b.wait(pe, act, exp_marks[pp])
                                    b.op(pe, lambda h, fc=fc, rows=rows, k0=k0, q0=q0, pp=pp: h.matmul(
                                        pa[:, pp, :], qk_sb[rows:rows + 64, 2 + fc, k0:k0 + 128],
                                        qk_sb[rows:rows + 64, fc, q0:q0 + 512],
                                        start=True, stop=True))
                                    m = b.inc(pe)
                                    b.wait(act, pe, m)
                                    b.op(act, lambda h, pp=pp: h.activation(
                                        p_sb[:, pp, :], pa[:, pp, :], AF.Exp, scale=0.125))
                                    exp_marks[pp] = b.inc(act)
                                    b.wait(pe, act, exp_marks[pp])
                                    b.op(pe, lambda h, kt=kt, hi=hi, pp=pp: h.matmul(
                                        pb[0:65, 0, :], v_sb[:, kt, hi, :],
                                        p_sb[:, pp, :],
                                        start=(kt == 0), stop=(kt == 15)))
                                    m = b.inc(pe)
                                b.wait(dve, pe, m)
                                b.op(dve, lambda h: h.reciprocal(r_sb[64:65, :], pb[64:65, 0, :]))
                                d = b.inc(dve)
                                b.wait(act, dve, d)
                                b.op(act, lambda h: h.activation(av_sb[:, :], pb[0:64, 0, :], AF.Identity))
                                a = b.inc(act)
                                b.wait(pe, dve, d)
                                b.op(pe, lambda h: h.matmul(pb[0:64, 1, :], ones128[64:65, 0:64], r_sb[64:65, :],
                                                            start=True, stop=True))
                                m2 = b.inc(pe)
                                b.wait(dve, pe, m2)
                                b.wait(dve, act, a)
                                b.op(dve, lambda h, rows=rows, hglob=hglob, q0=q0, seq0=seq0: h.scalar_tensor_tensor(
                                    attn_sb[rows:rows + 64, hglob // 2, seq0 + q0:seq0 + q0 + 512],
                                    pb[0:64, 1, :], 0.5, av_sb[:, :], ALU.mult, ALU.mult))
                                d = b.inc(dve)
                                b.wait(pe, dve, d)
                b.flush()


            with (
                _sb(nc, "cb_sb", [128, 2, 512], BF16) as cb_sb,
                _sb(nc, "xe", [128, 2, NCH, 512], BF16) as xe,
                _sb(nc, "we", [128, 2, NCH, 256], BF16) as we,
                _sb(nc, "xa", [128, NCH, 512], F32) as xa,
                _sb(nc, "ob", [128, NCH, 512], F32) as ob,
                _sb(nc, "cfm_row", [1, 8, 512], BF16) as cfm_row,
            ):
                store_mark = 0
                b.wait(pe, qg)
                for tt in range(TOK // 512):
                    t0 = tt * 512
                    b.wait(qs, pe)
                    b.op(qs, lambda h, t0=t0: h.dma_start(
                        out=cfm_row.ap().rearrange("o e t -> o (e t)"),
                        in_=cfm.ap()[:, t0:t0 + 512]))
                    crm = b.inc(qs)
                    b.wait(pe, qs, crm)
                    b.wait(qs, dve)
                    b.op(qs, lambda h, t0=t0: h.dma_start(
                        out=xa[:, :, :], in_=xin_f(t0, 512)))
                    xld = b.inc(qs)
                    b.wait(dve, qs, xld)
                    for c in range(NCH):
                        b.op(dve, lambda h, c=c, t0=t0: h.tensor_tensor(
                            xa[:, c, :], xa[:, c, :], attn_sb[:, c, t0:t0 + 512], ALU.add), dep=(c == 0))
                    for fh in range(2):
                        f0 = fh * 4
                        for e in range(8):
                            eb = e % 2
                            b.wait(pe, act)
                            b.op(pe, lambda h, e=e, eb=eb: h.matmul(
                                pa[:, eb, :], orow_bf[:, :], cfm_row[0:1, e, :],
                                start=True, stop=True))
                            m = b.inc(pe)
                            b.wait(act, pe, m)
                            b.wait(act, dve)
                            b.op(act, lambda h, eb=eb: h.activation(
                                cb_sb[:, eb, :], pa[:, eb, :], AF.Identity))
                            cbm = b.inc(act)
                            for half, q in ((0, qw), (1, qw2)):
                                b.wait(q, pe)
                                b.op(q, lambda h, e=e, f0=f0, half=half: h.dma_start(
                                    out=we[:, half, :, :],
                                    in_=wexp_bf[e].rearrange("(k p) f -> p k f", p=128)
                                        [:, :, (f0 + 2 * half) * 128:(f0 + 2 * half) * 128 + 256]))
                                b.inc(q)
                            b.wait(dve, act, cbm)
                            b.wait(dve, pe)
                            for k in range(NCH):
                                b.op(dve, lambda h, k=k, t0=t0, eb=eb: h.tensor_tensor(
                                    xe[:, eb, k, :], xln[:, k, t0:t0 + 512],
                                    cb_sb[:, eb, :], ALU.mult), dep=(k == 0))
                            xe_mark = b.inc(dve)
                            b.wait(pe, dve, xe_mark)
                            for f in range(4):
                                if f == 0:
                                    b.wait(pe, qw)
                                if f == 2:
                                    b.wait(pe, qw2)
                                for k in range(NCH):
                                    b.op(pe, lambda h, f=f, k=k, eb=eb, e=e: h.matmul(
                                        pb[:, f, :], we[:, f // 2, k, (f % 2) * 128:(f % 2) * 128 + 128],
                                        xe[:, eb, k, :],
                                        start=(e == 0 and k == 0), stop=False))
                        for f in range(4):
                            b.op(pe, lambda h, f=f, f0=f0, t0=t0: h.matmul(
                                pb[:, f, :], be_sb[:, (f0 + f) * 128:(f0 + f + 1) * 128],
                                cfm[:, t0:t0 + 512], start=False, stop=True))
                        m = b.inc(pe)
                        b.wait(dve, pe, m)
                        b.wait(dve, qss, store_mark)
                        for f in range(4):
                            b.op(dve, lambda h, f=f, f0=f0: h.scalar_tensor_tensor(
                                ob[:, f0 + f, :], pb[:, f, :], 0.5, xa[:, f0 + f, :],
                                ALU.mult, ALU.add), dep=(f == 0))
                        b.inc(dve)
                    b.wait(qss, dve)
                    b.op(qss, lambda h, t0=t0: h.dma_start(
                        out=xout_f(t0, 512), in_=ob[:, :, :]))
                    store_mark = b.inc(qss)
                b.op(pool, lambda h, v=b.qss.cnt: h.wait_ge(b.qss.sem, v))
                b.inc(pool)
                b.flush()


def ffn1_weight_map(I):
    bq = I["L1_ffn_b_qkv"]
    return dict(
        wqkv=_c(I["L1_ffn_W_qkv"]),
        bqkv_row=_c(bq.reshape(1, 3 * DM)),
        bqkv_col=_c(bq.reshape(24, 128).T),
        wgate=_c(I["L1_ffn_W_gate"]),
        bgate=_c(I["L1_ffn_b_gate"].reshape(8, 1)),
        wexp=_c(I["L1_ffn_W_experts"]),
        bexp=_c(I["L1_ffn_b_experts"]))


def run_ffn1(x_fm, I):
    nc = build_ffn1()
    wmap = ffn1_weight_map(I)
    in_maps = []
    for j in range(NCORES):
        xs = _c(x_fm[:, 2 * j:2 * j + 2, :].reshape(DM, TOK))
        in_maps.append(dict(x_in=xs, **wmap))
    res = run_bass_kernel_spmd(nc, in_maps, core_ids=list(range(NCORES)))
    out = np.empty((DM, 16, 2048), np.float32)
    for j in range(NCORES):
        out[:, 2 * j:2 * j + 2, :] = res.results[j]["x_out"].reshape(DM, 2, SEQ)
    return out


INTER = 4096
PADC = 1024
SCOL = PADC + SEQ


def emit_attn1(nc, b, W, xin_f, xout_f, first=True):
    A_in, BT, CT = W["A"], W["BT"], W["CT"]
    Dcol, ldt_col, ln1_col, ln2_col = W["Dcol"], W["ldt_col"], W["ln1_col"], W["ln2_col"]
    wgT, wuT, wdT = W["wgT"], W["wuT"], W["wdT"]
    wg_bf, wu_bf, wd_bf = W["wg_bf"], W["wu_bf"], W["wd_bf"]
    with ExitStack() as stack:
        xln = stack.enter_context(_sb(nc, "xln", [128, NCH, TOK], BF16))
        m1 = stack.enter_context(_sb(nc, "m1", [128, NCH, TOK], BF16))
        cols = stack.enter_context(_sb(nc, "cols", [128, 6, NCH], F32))
        i64bf = stack.enter_context(_sb(nc, "i64bf", [64, 64], BF16))
        ones_col = stack.enter_context(_sb(nc, "ones_c", [128, 1], F32))
        ones128 = stack.enter_context(_sb(nc, "ones128", [1, 128], F32))

        qs, qss, qw, qw2 = b.qs, b.qss, b.qw, b.qw2
        qg, qg2, pe, act, dve, pool = b.qg, b.qg2, b.pe, b.act, b.dve, b.pool

        for nm, dst in [("Dcol", 0), ("ldt_col", 1), ("ln1_col", 2), ("ln2_col", 3)]:
            src = {"Dcol": Dcol, "ldt_col": ldt_col, "ln1_col": ln1_col, "ln2_col": ln2_col}[nm]
            b.op(qs, lambda h, src=src, dst=dst: h.dma_start(out=cols[:, dst, :], in_=src))
            b.inc(qs)
        if first:
            for wsrc, wdst in [(wgT, wg_bf), (wuT, wu_bf), (wdT, wd_bf)]:
                b.op(qg, lambda h, a=wsrc, bb=wdst: h.dma_start(out=bb, in_=a))
                b.inc(qg)
        b.op(dve, lambda h: h.memset(ones_col[:, :], 1.0))
        b.op(dve, lambda h: h.memset(ones128[:, :], 1.0))
        b.op(pool, lambda h: h.memset(i64bf[:, :], 0.0))
        b.op(pool, lambda h: h.affine_select(
            out=i64bf[:, :], in_=i64bf[:, :], compare_op=ALU.not_equal,
            fill=1.0, base=0, pattern=[[-1, 64]], channel_multiplier=1))
        b.inc(dve)
        b.inc(pool)

        with ExitStack() as ph12:
            sbf = ph12.enter_context(_sb(nc, "sbuf_scan", [64, 2, 2, SCOL], BF16))
            bbt = ph12.enter_context(_sb(nc, "bbt", [128, NCH, 64], BF16))
            ct_sb = ph12.enter_context(_sb(nc, "ct_sb", [64, NCH, 128], BF16))
            qbf = ph12.enter_context(_sb(nc, "qbf", [64, 12, 64], BF16))
            stats = ph12.enter_context(_sb(nc, "stats", [1, 2, 512], F32))
            sq = ph12.enter_context(_sb(nc, "sqs", [128, 512], F32))
            b.op(qg, lambda h: h.dma_start(out=bbt[:, :, :], in_=BT.rearrange("(c p) n -> p c n", p=128)))
            b.inc(qg)
            b.op(qg, lambda h: h.dma_start(out=ct_sb[:, :, :], in_=CT.rearrange("n (c f) -> n c f", f=128)))
            b.inc(qg)
            b.op(dve, lambda h: h.memset(
                sbf.ap().rearrange("p a s c -> p (a s c)"), 0.0))
            b.inc(dve)
            ls = LnScratch(nc, ph12)
            ls.init(b, dve)
            xt0 = ph12.enter_context(_sb(nc, "xt0", [128, NCH, LN_TT], F32))
            for i in range(TOK // LN_TT):
                b.wait(qs, dve)
                b.op(qs, lambda h, i=i: h.dma_start(
                    out=xt0[:, :, :], in_=xin_f(i * LN_TT, LN_TT)))
                ld = b.inc(qs)
                b.wait(pe, qs, ld)
                b.wait(act, qs, ld)
                emit_ln_tile(b, ls, xt0[:, :, :], xln[:, :, i * LN_TT:(i + 1) * LN_TT])
            b.flush()

            pst, pbc = ls.pst, ls.pbc
            for i in range(TOK // 512):
                t0 = i * 512
                b.wait(pe, dve)
                b.wait(act, dve)
                for c in range(NCH):
                    b.op(act, lambda h, c=c, t0=t0: h.activation(
                        sq[:, :], xln[:, c, t0:t0 + 512], AF.Square))
                    a = b.inc(act)
                    b.wait(pe, act, a)
                    b.op(pe, lambda h, c=c: h.matmul(
                        pst[0:1, 0, 0:256].tensor.ap()[0:1, 0:512] if False else pst[0:1, 0:2, :].rearrange("p a t -> p (a t)")[0:1, 0:512],
                        ls.ones_col[:, 0:1], sq[:, :],
                        start=(c == 0), stop=(c == NCH - 1)))
                    m = b.inc(pe)
                    b.wait(act, pe, m)
                b.wait(dve, pe)
                b.op(dve, lambda h: h.tensor_scalar(
                    stats[:, 0, :], pst[0:1, 0:2, :].rearrange("p a t -> p (a t)")[0:1, 0:512],
                    1.0 / DM, 1e-6, ALU.mult, ALU.add))
                d = b.inc(dve)
                b.wait(act, dve, d)
                b.op(act, lambda h: h.activation(stats[:, 0, :], stats[:, 0, :], AF.Ln))
                b.op(act, lambda h: h.activation(stats[:, 0, :], stats[:, 0, :], AF.Exp, scale=-0.5))
                a = b.inc(act)
                b.wait(pe, act, a)
                b.op(pe, lambda h: h.matmul(
                    pbc[:, 0:2, :].rearrange("p a t -> p (a t)")[:, 0:512],
                    ls.ones_row[:, :], stats[:, 0, :], start=True, stop=True))
                m = b.inc(pe)
                b.wait(dve, pe, m)
                for c in range(NCH):
                    b.op(dve, lambda h, c=c, t0=t0: h.scalar_tensor_tensor(
                        m1[:, c, t0:t0 + 512], xln[:, c, t0:t0 + 512], 1.0,
                        pbc[:, 0:2, :].rearrange("p a t -> p (a t)")[:, 0:512],
                        ALU.mult, ALU.mult), dep=(c == 0))
                b.inc(dve)
            b.flush()

            with (
                _sb(nc, "A_sb", [64, 64], F32) as A_sb,
                _sb(nc, "E_sb", [64, 64], F32) as E_sb,
                _sb(nc, "pt_sb", [64, 2, 64], F32) as pt_sb,
                _sb(nc, "st_sb", [64, 64], F32) as st_sb,
                _sb(nc, "q_sb", [64, 2, 64], F32) as q_sb,
                _sb(nc, "i64f", [64, 64], F32) as i64f,
                _sb(nc, "hcol", [64, 1], F32) as hcol,
                _ps(nc, "pex", [128, 512], F32) as pex_t,
            ):
                pex = pex_t[:, :]
                b.op(qs, lambda h: h.dma_start(out=A_sb[:, :], in_=A_in))
                ald = b.inc(qs)
                b.op(pool, lambda h: h.memset(i64f[:, :], 0.0))
                b.op(pool, lambda h: h.affine_select(
                    out=i64f[:, :], in_=i64f[:, :], compare_op=ALU.not_equal,
                    fill=1.0, base=0, pattern=[[-1, 64]], channel_multiplier=1))
                p = b.inc(pool)
                b.wait(pe, pool, p)
                b.wait(dve, pool, p)
                b.wait(act, qs)
                b.op(act, lambda h: h.activation(cols[:, 1, :], cols[:, 1, :], AF.Exp))
                a = b.inc(act)
                b.wait(dve, act, a)
                b.op(dve, lambda h: h.tensor_tensor(cols[:, 4, :], cols[:, 1, :], cols[:, 2, :], ALU.mult))
                b.op(dve, lambda h: h.tensor_tensor(cols[:, 0, :], cols[:, 0, :], cols[:, 2, :], ALU.mult))
                d = b.inc(dve)
                b.wait(dve, qg)
                for c in range(NCH):
                    b.op(dve, lambda h, c=c: h.tensor_scalar_mul(
                        bbt[:, c, :], bbt[:, c, :], cols[:, 4, c:c + 1]))
                b.inc(dve)
                b.wait(pe, act, a)
                b.op(pe, lambda h: h.matmul(pex[0:1, 0:8], ones_col[:, 0:1], cols[:, 1, :],
                                            start=True, stop=True))
                m = b.inc(pe)
                b.wait(dve, pe, m)
                b.op(dve, lambda h: h.reduce_sum(stats[:, 0, 0:1], pex[0:1, 0:8], axis=AX.X))
                b.op(dve, lambda h: h.tensor_scalar_mul(stats[:, 0, 0:1], stats[:, 0, 0:1], 1.0 / DM))
                d = b.inc(dve)
                b.wait(pe, dve, d)
                b.op(pe, lambda h: h.matmul(pex[0:64, 8:9], ones128[0:1, 0:64], stats[:, 0, 0:1],
                                            start=True, stop=True))
                m = b.inc(pe)
                b.wait(dve, pe, m)
                b.op(dve, lambda h: h.tensor_copy(hcol[:, :], pex[0:64, 8:9]))
                b.wait(dve, qs, ald)
                b.op(dve, lambda h: h.tensor_scalar_mul(E_sb[:, :], A_sb[:, :], hcol[:, 0:1]))
                b.op(dve, lambda h: h.tensor_copy(pt_sb[:, 0, :], i64f[:, :]))
                b.op(dve, lambda h: h.tensor_copy(st_sb[:, :], i64f[:, :]))
                d = b.inc(dve)
                for k in range(1, 20):
                    cur, nxt = (k - 1) % 2, k % 2
                    b.wait(pe, dve)
                    b.wait(pe, act)
                    b.op(pe, lambda h, cur=cur: h.matmul(
                        pex[0:64, 64:128], E_sb[:, :], pt_sb[:, cur, :], start=True, stop=True))
                    m = b.inc(pe)
                    b.wait(act, pe, m)
                    b.op(act, lambda h, k=k, nxt=nxt: h.activation(
                        pt_sb[:, nxt, :], pex[0:64, 64:128], AF.Identity, scale=1.0 / k))
                    a = b.inc(act)
                    b.wait(dve, act, a)
                    b.op(dve, lambda h, nxt=nxt: h.tensor_tensor(
                        st_sb[:, :], st_sb[:, :], pt_sb[:, nxt, :], ALU.add))
                    b.inc(dve)
                b.op(dve, lambda h: h.tensor_copy(q_sb[:, 0, :], st_sb[:, :]))
                b.op(dve, lambda h: h.tensor_copy(qbf[:, 0, :], st_sb[:, :]))
                d = b.inc(dve)
                for j in range(1, 11):
                    cur, nxt = (j - 1) % 2, j % 2
                    b.wait(pe, dve)
                    b.op(pe, lambda h, cur=cur: h.transpose(
                        pex[0:64, 128:192], q_sb[:, cur, :], i64f[:, :]))
                    m = b.inc(pe)
                    b.wait(act, pe, m)
                    b.op(act, lambda h: h.activation(st_sb[:, :], pex[0:64, 128:192], AF.Identity))
                    a = b.inc(act)
                    b.wait(pe, act, a)
                    b.op(pe, lambda h, cur=cur: h.matmul(
                        pex[0:64, 64:128], st_sb[:, :], q_sb[:, cur, :], start=True, stop=True))
                    m = b.inc(pe)
                    b.wait(dve, pe, m)
                    b.op(dve, lambda h, nxt=nxt: h.tensor_copy(q_sb[:, nxt, :], pex[0:64, 64:128]))
                    b.op(dve, lambda h, j=j, nxt=nxt: h.tensor_copy(qbf[:, j, :], q_sb[:, nxt, :]))
                    d = b.inc(dve)
                b.wait(pe, dve, d)
                for s_ in range(NSEQ):
                    for i in range(SEQ // 512):
                        t0 = s_ * SEQ + i * 512
                        b.wait(pe, act)
                        for c in range(NCH):
                            b.op(pe, lambda h, c=c, t0=t0: h.matmul(
                                pex[0:64, 0:512], bbt[:, c, :], m1[:, c, t0:t0 + 512],
                                start=(c == 0), stop=(c == NCH - 1)))
                        m = b.inc(pe)
                        b.wait(act, pe, m)
                        b.op(act, lambda h, s_=s_, i=i: h.activation(
                            sbf[:, 0, s_, PADC + i * 512:PADC + (i + 1) * 512],
                            pex[0:64, 0:512], AF.Identity))
                        b.inc(act)
                b.wait(act, dve)
                for j in range(11):
                    src, dst = j % 2, (j + 1) % 2
                    shift = 1 << j
                    prev_mark = b.act.cnt
                    b.wait(pe, act, prev_mark)
                    for s_ in range(NSEQ):
                        for i in range(SEQ // 512):
                            c0 = PADC + i * 512
                            b.wait(pe, act)
                            b.op(pe, lambda h, j=j, src=src, s_=s_, c0=c0, shift=shift: h.matmul(
                                pex[0:64, 0:512], qbf[:, j, :],
                                sbf[:, src, s_, c0 - shift:c0 - shift + 512],
                                start=True, stop=False))
                            b.op(pe, lambda h, src=src, s_=s_, c0=c0: h.matmul(
                                pex[0:64, 0:512], i64bf[:, :],
                                sbf[:, src, s_, c0:c0 + 512],
                                start=False, stop=True))
                            m = b.inc(pe)
                            b.wait(act, pe, m)
                            b.op(act, lambda h, dst=dst, s_=s_, c0=c0: h.activation(
                                sbf[:, dst, s_, c0:c0 + 512], pex[0:64, 0:512], AF.Identity))
                            b.inc(act)
                fin = 11 % 2
                b.wait(pe, act)
                for s_ in range(NSEQ):
                    for i in range(SEQ // 512):
                        t0 = s_ * SEQ + i * 512
                        c0 = PADC + i * 512
                        for fc in range(NCH):
                            b.wait(pe, dve)
                            b.op(pe, lambda h, fc=fc, fin=fin, s_=s_, c0=c0: h.matmul(
                                pex[:, 0:512], ct_sb[:, fc, :],
                                sbf[:, fin, s_, c0:c0 + 512],
                                start=True, stop=True))
                            m = b.inc(pe)
                            b.wait(dve, pe, m)
                            b.op(dve, lambda h, fc=fc, t0=t0: h.scalar_tensor_tensor(
                                sq[:, :], m1[:, fc, t0:t0 + 512], cols[:, 0, fc:fc + 1],
                                xln[:, fc, t0:t0 + 512], ALU.mult, ALU.add))
                            b.op(dve, lambda h, fc=fc, t0=t0: h.tensor_tensor(
                                m1[:, fc, t0:t0 + 512], sq[:, :], pex[:, 0:512], ALU.add))
                            b.inc(dve)
                b.flush()

            for i in range(TOK // 512):
                t0 = i * 512
                b.wait(pe, dve)
                b.wait(act, dve)
                for c in range(NCH):
                    b.op(act, lambda h, c=c, t0=t0: h.activation(
                        sq[:, :], m1[:, c, t0:t0 + 512], AF.Square))
                    a = b.inc(act)
                    b.wait(pe, act, a)
                    b.op(pe, lambda h, c=c: h.matmul(
                        pst[0:1, 0:2, :].rearrange("p a t -> p (a t)")[0:1, 0:512],
                        ls.ones_col[:, 0:1], sq[:, :],
                        start=(c == 0), stop=(c == NCH - 1)))
                    m = b.inc(pe)
                    b.wait(act, pe, m)
                b.wait(dve, pe)
                b.op(dve, lambda h: h.tensor_scalar(
                    stats[:, 0, :], pst[0:1, 0:2, :].rearrange("p a t -> p (a t)")[0:1, 0:512],
                    1.0 / DM, 1e-6, ALU.mult, ALU.add))
                d = b.inc(dve)
                b.wait(act, dve, d)
                b.op(act, lambda h: h.activation(stats[:, 0, :], stats[:, 0, :], AF.Ln))
                b.op(act, lambda h: h.activation(stats[:, 0, :], stats[:, 0, :], AF.Exp, scale=-0.5))
                a = b.inc(act)
                b.wait(pe, act, a)
                b.op(pe, lambda h: h.matmul(
                    pbc[:, 0:2, :].rearrange("p a t -> p (a t)")[:, 0:512],
                    ls.ones_row[:, :], stats[:, 0, :], start=True, stop=True))
                m = b.inc(pe)
                b.wait(dve, pe, m)
                for c in range(NCH):
                    b.op(dve, lambda h, c=c, t0=t0: h.scalar_tensor_tensor(
                        sq[:, :], m1[:, c, t0:t0 + 512], 1.0,
                        pbc[:, 0:2, :].rearrange("p a t -> p (a t)")[:, 0:512],
                        ALU.mult, ALU.mult), dep=True)
                    b.op(dve, lambda h, c=c, t0=t0: h.tensor_scalar_mul(
                        xln[:, c, t0:t0 + 512], sq[:, :], cols[:, 3, c:c + 1]))
                b.inc(dve)
            b.flush()

        with (
            _sb(nc, "sgu", [128, 32, 512], BF16) as sgu,
            _sb(nc, "wb3", [128, 2, NCH, 256], BF16) as wb3,
            _sb(nc, "wd3", [128, 2, 512], BF16) as wd3,
            _sb(nc, "xm", [128, NCH, 512], F32) as xm,
            _sb(nc, "ob", [128, NCH, 512], F32) as ob,
            _sb(nc, "sg_tmp", [128, 512], F32) as sg_tmp,
            _ps(nc, "pgu", [128, 2, 512], F32) as pgu_t,
            _ps(nc, "pdn", [128, 4, 512], F32) as pdn_t,
        ):
            pgu, pdn = pgu_t[:, :, :], pdn_t[:, :, :]
            store_mark = 0
            for tt in range(TOK // 512):
                t0 = tt * 512
                b.wait(qs, dve)
                b.op(qs, lambda h, t0=t0: h.dma_start(
                    out=xm[:, :, :], in_=xin_f(t0, 512)))
                xld = b.inc(qs)
                b.wait(dve, qs, xld)
                for c in range(NCH):
                    b.op(dve, lambda h, c=c, t0=t0: h.tensor_tensor(
                        xm[:, c, :], xm[:, c, :], m1[:, c, t0:t0 + 512], ALU.add), dep=(c == 0))
                xm_mark = b.inc(dve)
                for ii in range(32):
                    ib = ii % 2
                    b.wait(qw if ib == 0 else qw2, pe)
                    q_ = qw if ib == 0 else qw2
                    b.op(q_, lambda h, ii=ii, ib=ib: h.dma_start(
                        out=wb3[:, ib, :, 0:128],
                        in_=wg_bf.rearrange("(k p) f -> p k f", p=128)[:, :, ii * 128:(ii + 1) * 128]))
                    b.inc(q_)
                    b.op(q_, lambda h, ii=ii, ib=ib: h.dma_start(
                        out=wb3[:, ib, :, 128:256],
                        in_=wu_bf.rearrange("(k p) f -> p k f", p=128)[:, :, ii * 128:(ii + 1) * 128]))
                    wld = b.inc(q_)
                    b.wait(pe, q_, wld)
                    b.wait(pe, act)
                    b.wait(pe, dve)
                    for k in range(NCH):
                        b.op(pe, lambda h, k=k, ib=ib, t0=t0: h.matmul(
                            pgu[:, 0, :], wb3[:, ib, k, 0:128], xln[:, k, t0:t0 + 512],
                            start=(k == 0), stop=(k == NCH - 1)))
                    for k in range(NCH):
                        b.op(pe, lambda h, k=k, ib=ib, t0=t0: h.matmul(
                            pgu[:, 1, :], wb3[:, ib, k, 128:256], xln[:, k, t0:t0 + 512],
                            start=(k == 0), stop=(k == NCH - 1)))
                    m = b.inc(pe)
                    b.wait(act, pe, m)
                    b.wait(act, dve)
                    b.op(act, lambda h: h.activation(sg_tmp[:, :], pgu[:, 0, :], AF.Silu))
                    a = b.inc(act)
                    b.wait(dve, act, a)
                    b.op(dve, lambda h, ii=ii: h.tensor_tensor(
                        sgu[:, ii, :], sg_tmp[:, :], pgu[:, 1, :], ALU.mult))
                    b.inc(dve)
                sgu_mark = b.dve.cnt
                for fh in range(2):
                    f0 = fh * 4
                    for ii in range(32):
                        ib = ii % 2
                        q_ = qw if ib == 0 else qw2
                        b.wait(q_, pe)
                        b.op(q_, lambda h, ii=ii, ib=ib, f0=f0: h.dma_start(
                            out=wd3[:, ib, :],
                            in_=wd_bf.rearrange("(i p) f -> p i f", p=128)[:, ii, f0 * 128:f0 * 128 + 512]))
                        wld = b.inc(q_)
                        b.wait(pe, q_, wld)
                        if ii == 0:
                            b.wait(pe, dve)
                        for f in range(4):
                            b.op(pe, lambda h, f=f, ii=ii, ib=ib: h.matmul(
                                pdn[:, f, :], wd3[:, ib, f * 128:(f + 1) * 128], sgu[:, ii, :],
                                start=(ii == 0), stop=(ii == 31)))
                        b.inc(pe)
                    m = b.pe.cnt
                    b.wait(dve, pe, m)
                    b.wait(dve, qss, store_mark)
                    for f in range(4):
                        b.op(dve, lambda h, f=f, f0=f0: h.scalar_tensor_tensor(
                            ob[:, f0 + f, :], pdn[:, f, :], 1.0, xm[:, f0 + f, :],
                            ALU.mult, ALU.add), dep=(f == 0))
                    b.inc(dve)
                b.wait(qss, dve)
                b.op(qss, lambda h, t0=t0: h.dma_start(
                    out=xout_f(t0, 512), in_=ob[:, :, :]))
                store_mark = b.inc(qss)
            b.op(pool, lambda h, v=b.qss.cnt: h.wait_ge(b.qss.sem, v))
            b.inc(pool)
            b.flush()


def attn1_weight_map(I):
    return dict(
        A=_c(I["L1_attn_A"]),
        BT=_c(I["L1_attn_B"].T),
        CT=_c(I["L1_attn_C"].T),
        Dcol=_c(I["L1_attn_D"].reshape(NCH, 128).T),
        ldt_col=_c(I["L1_attn_log_dt"].reshape(NCH, 128).T),
        ln1_col=_c(I["L1_attn_ln1_w"].reshape(NCH, 128).T),
        ln2_col=_c(I["L1_attn_ln2_w"].reshape(NCH, 128).T),
        wgT=_c(I["L1_attn_w_gate"].T),
        wuT=_c(I["L1_attn_w_up"].T),
        wdT=_c(I["L1_attn_w_down"].T))


def run_attn1(x_fm, I):
    nc = build_attn1()
    wmap = attn1_weight_map(I)
    in_maps = []
    for j in range(NCORES):
        xs = _c(x_fm[:, 2 * j:2 * j + 2, :].reshape(DM, TOK))
        in_maps.append(dict(x_in=xs, **wmap))
    res = run_bass_kernel_spmd(nc, in_maps, core_ids=list(range(NCORES)))
    out = np.empty((DM, 16, 2048), np.float32)
    for j in range(NCORES):
        out[:, 2 * j:2 * j + 2, :] = res.results[j]["x_out"].reshape(DM, 2, SEQ)
    return out


DIN = 2048
NH = 32
HDIM = 64
DST = 128
CK = 64
CONVD = 2304
NCONV = CONVD // 128


def emit_attn0(nc, b, W, xin_f, xout_f, first=True):
    DBG = False
    win, convd, convb = W["win"], W["convd"], W["convb"]
    dtb, alog, dcol, nwcol = W["dtb"], W["alog"], W["dcol"], W["nwcol"]
    wout, e32, wout_bf = W["wout"], W["e32"], W["wout_bf"]
    with ExitStack() as stack:
        e32_sb = stack.enter_context(_sb(nc, "e32_sb", [NH, DIN], BF16))
        tri = stack.enter_context(_sb(nc, "tri", [128, 64], F32))
        ones_col = stack.enter_context(_sb(nc, "ones_c", [128, 1], F32))
        ones_row = stack.enter_context(_sb(nc, "ones_r", [1, 128], F32))
        orow_bf = stack.enter_context(_sb(nc, "orow_bf", [1, 128], BF16))
        nrow_bf = stack.enter_context(_sb(nc, "nrow_bf", [1, 128], BF16))
        identF = stack.enter_context(_sb(nc, "identF", [128, 128], F32))
        identB = stack.enter_context(_sb(nc, "identB", [128, 128], BF16))
        cols = stack.enter_context(_sb(nc, "colsA", [128, 3, 18], F32))
        hcols = stack.enter_context(_sb(nc, "hcolsA", [NH, 2], F32))
        dt_bf = stack.enter_context(_sb(nc, "dt_bf", [NH, SEQ], BF16))
        cs_hi = stack.enter_context(_sb(nc, "cs_hi", [NH, SEQ], BF16))
        cs_lo = stack.enter_context(_sb(nc, "cs_lo", [NH, SEQ], BF16))
        dtdd = stack.enter_context(_sb(nc, "dtdd", [NH, SEQ], BF16))
        gam = stack.enter_context(_sb(nc, "gam", [NH, 32], BF16))
        grow = stack.enter_context(_sb(nc, "grow", [1, DIN], BF16))
        edrow = stack.enter_context(_sb(nc, "edrow", [1, DIN], F32))
        edrow_bf = stack.enter_context(_sb(nc, "edrow_bf", [1, DIN], BF16))
        rowbuf = stack.enter_context(_sb(nc, "rowbuf", [1, 2, 2, DIN], BF16))
        Ssb = stack.enter_context(_sb(nc, "Ssb", [128, 2, DIN], BF16))
        Stmp = stack.enter_context(_sb(nc, "Stmp", [128, DIN], BF16))
        Msb = stack.enter_context(_sb(nc, "Msb", [64, DIN], BF16))
        L2 = stack.enter_context(_sb(nc, "L2", [64, DIN], BF16))
        cbt = stack.enter_context(_sb(nc, "cbt", [64, 64], BF16))
        cdec = stack.enter_context(_sb(nc, "cdec", [128, DIN], BF16))
        xdt = stack.enter_context(_sb(nc, "xdt", [64, DIN], BF16))
        xdd = stack.enter_context(_sb(nc, "xdd", [64, DIN], BF16))
        xbc = stack.enter_context(_sb(nc, "xbc", [128, NCONV, 515], BF16))
        xbcs = stack.enter_context(_sb(nc, "xbcs", [128, 16, 512], BF16))
        bc_fm = stack.enter_context(_sb(nc, "bc_fm", [128, 2, 512], BF16))
        btok = stack.enter_context(_sb(nc, "btok", [64, 128], BF16))
        x_tok = stack.enter_context(_sb(nc, "x_tok", [64, DIN], BF16))
        zs = stack.enter_context(_sb(nc, "zs", [128, 2, 512], BF16))
        y_fm = stack.enter_context(_sb(nc, "y_fm", [128, 16, 512], BF16))
        sqa = stack.enter_context(_sb(nc, "sqa", [128, 512], F32))
        sqb = stack.enter_context(_sb(nc, "sqb", [128, 512], F32))
        st2 = stack.enter_context(_sb(nc, "st2", [1, 2, 512], F32))
        ob = stack.enter_context(_sb(nc, "obA", [128, 2, 512], F32))
        wpb = stack.enter_context(_sb(nc, "wpb", [128, 2, NCH, 512], BF16))
        cvd = stack.enter_context(_sb(nc, "cvd", [128, 2, 4, 128], BF16))
        xt0 = stack.enter_context(_sb(nc, "xt0A", [128, NCH, 512], F32))
        xlnt = stack.enter_context(_sb(nc, "xlnt", [128, NCH, 512], BF16))
        csf = stack.enter_context(_sb(nc, "csf", [NH, 64], F32))
        pA_t = stack.enter_context(_ps(nc, "pA", [128, 4, 512], F32))
        pS_t = stack.enter_context(_ps(nc, "pS", [128, 4, 512], F32))
        pA = pA_t[:, :, :]
        pS = pS_t[:, :, :]
        pAfull = pA_t.ap().rearrange("p a t -> p (a t)")
        pSfull = pS_t.ap().rearrange("p a t -> p (a t)")

        qs, qss, qw, qw2 = b.qs, b.qss, b.qw, b.qw2
        qg, qg2, pe, act, dve, pool = b.qg, b.qg2, b.pe, b.act, b.dve, b.pool
        wsrc = win.rearrange("(k p) f -> p k f", p=128)

        ls = LnScratch(nc, stack,
                       pS_t[0:1, 3, :].rearrange("p (a t) -> p a t", a=2),
                       pS_t[:, 2, :].rearrange("p (a t) -> p a t", a=2))

        b.op(qs, lambda h: h.dma_start(out=cols[:, 0, :], in_=convb))
        b.inc(qs)
        b.op(qs, lambda h: h.dma_start(out=cols[:, 1, 0:16], in_=dcol))
        b.inc(qs)
        b.op(qs, lambda h: h.dma_start(out=cols[:, 2, 0:16], in_=nwcol))
        b.inc(qs)
        b.op(qs, lambda h: h.dma_start(out=hcols[:, 0:1], in_=dtb))
        b.inc(qs)
        b.op(qs, lambda h: h.dma_start(out=hcols[:, 1:2], in_=alog))
        b.inc(qs)
        b.op(qg, lambda h: h.dma_start(out=e32_sb[:, :], in_=e32))
        b.inc(qg)
        if first:
            b.op(qg, lambda h: h.dma_start(out=wout_bf, in_=wout))
            b.inc(qg)
        b.op(dve, lambda h: h.memset(ones_col[:, :], 1.0))
        b.op(dve, lambda h: h.memset(ones_row[:, :], 1.0))
        b.op(dve, lambda h: h.memset(orow_bf[:, :], 1.0))
        b.op(dve, lambda h: h.memset(nrow_bf[:, :], -1.0))
        b.op(dve, lambda h: h.memset(Ssb[:, 0, :], 0.0))
        b.inc(dve)
        for idt in ("identF", "identB"):
            t_ = {"identF": identF, "identB": identB}[idt]
            b.op(pool, lambda h, t_=t_: h.memset(t_[:, :], 0.0))
            b.op(pool, lambda h, t_=t_: h.affine_select(
                out=t_[:, :], in_=t_[:, :], compare_op=ALU.not_equal,
                fill=1.0, base=0, pattern=[[-1, 128]], channel_multiplier=1))
        b.op(pool, lambda h: h.memset(tri[:, :], 0.0))
        b.op(pool, lambda h: h.affine_select(
            out=tri[0:64, :], in_=tri[0:64, :], compare_op=ALU.is_gt,
            fill=1.0, base=0, pattern=[[-1, 64]], channel_multiplier=1))
        b.op(pool, lambda h: h.affine_select(
            out=tri[64:128, :], in_=tri[64:128, :], compare_op=ALU.is_gt,
            fill=1.0, base=0, pattern=[[-1, 64]], channel_multiplier=1))
        pm = b.inc(pool)
        b.wait(pe, pool, pm)
        b.wait(dve, pool, pm)
        ls.init(b, dve)
        b.wait(act, qs)
        b.op(act, lambda h: h.activation(hcols[:, 1:2], hcols[:, 1:2], AF.Exp))
        a = b.inc(act)
        b.wait(dve, act, a)
        b.op(dve, lambda h: h.tensor_scalar_mul(hcols[:, 1:2], hcols[:, 1:2], -1.0))
        b.inc(dve)
        b.flush()

        wdt_piece = 8

        for s_ in range(NSEQ):
            b.wait(qg, pe)
            b.op(qg, lambda h: h.dma_start(
                out=wpb[:, 0, :, 0:NH], in_=wsrc[:, :, 4352:4384]))
            wld = b.inc(qg)
            for i in range(SEQ // 512):
                t0 = i * 512
                g0 = s_ * SEQ + t0
                b.wait(qs, dve)
                b.wait(qs, pe)
                b.op(qs, lambda h, g0=g0: h.dma_start(
                    out=xt0[:, :, :], in_=xin_f(g0, 512)))
                ld = b.inc(qs)
                b.wait(pe, qs, ld)
                b.wait(act, qs, ld)
                emit_ln_tile(b, ls, xt0[:, :, 0:256], xlnt[:, :, 0:256])
                emit_ln_tile(b, ls, xt0[:, :, 256:512], xlnt[:, :, 256:512])
                b.wait(pe, qg, wld)
                b.wait(pe, dve)
                for k in range(NCH):
                    b.op(pe, lambda h, k=k: h.matmul(
                        pA[0:NH, 0, :], wpb[:, 0, k, 0:NH], xlnt[:, k, :],
                        start=(k == 0), stop=(k == NCH - 1)))
                m = b.inc(pe)
                b.wait(act, pe, m)
                b.op(act, lambda h: h.activation(
                    sqa[0:NH, :], pA[0:NH, 0, :], AF.Identity, bias=hcols[:, 0:1]))
                a = b.inc(act)
                b.op(act, lambda h: h.activation(sqb[0:NH, :], sqa[0:NH, :], AF.Abs))
                a = b.inc(act)
                b.op(act, lambda h: h.activation(sqb[0:NH, :], sqb[0:NH, :], AF.Exp, scale=-1.0))
                a = b.inc(act)
                b.wait(dve, act, a)
                b.op(dve, lambda h: h.tensor_scalar_add(sqb[0:NH, :], sqb[0:NH, :], 1.0))
                d = b.inc(dve)
                b.wait(act, dve, d)
                b.op(act, lambda h: h.activation(sqb[0:NH, :], sqb[0:NH, :], AF.Ln))
                a = b.inc(act)
                b.wait(dve, act, a)
                b.op(dve, lambda h: h.tensor_single_scalar(
                    sqa[0:NH, :], sqa[0:NH, :], 0.0, ALU.max))
                b.op(dve, lambda h: h.tensor_tensor(
                    sqa[0:NH, :], sqa[0:NH, :], sqb[0:NH, :], ALU.add))
                b.op(dve, lambda h, t0=t0: h.tensor_copy(dt_bf[:, t0:t0 + 512], sqa[0:NH, :]))
                b.op(dve, lambda h: h.tensor_scalar_mul(sqb[0:NH, :], sqa[0:NH, :], hcols[:, 1:2]))
                d = b.inc(dve)
                for sub in range(4):
                    l0 = sub * 128
                    b.wait(pe, dve, d)
                    b.op(pe, lambda h, l0=l0: h.transpose(
                        pA[0:128, 2, 0:NH], sqb[0:NH, l0:l0 + 128], identF[0:NH, 0:NH]))
                    m = b.inc(pe)
                    b.wait(act, pe, m)
                    b.op(act, lambda h: h.activation(sqa[:, 0:NH], pA[0:NH, 2, 0:128], AF.Identity, apply_transpose=True) if False else h.activation(sqb[:, 448:480] if False else sqa[:, 0:32], pA[:, 2, 0:NH] if False else pA[0:128, 2, 0:NH], AF.Identity))
                    a = b.inc(act)
                    b.wait(pe, act, a)
                    b.op(pe, lambda h: h.matmul(
                        pA[0:64, 3, 0:NH], tri[0:64, :], sqa[0:64, 0:NH],
                        start=True, stop=True))
                    b.op(pe, lambda h: h.matmul(
                        pA[64:128, 3, 0:NH], tri[64:128, :], sqa[64:128, 0:NH],
                        start=True, stop=True))
                    m = b.inc(pe)
                    b.wait(act, pe, m)
                    b.op(act, lambda h: h.activation(sqb[:, 0:NH], pA[:, 3, 0:NH], AF.Identity))
                    a = b.inc(act)
                    b.wait(pe, act, a)
                    b.op(pe, lambda h: h.transpose(
                        pA[0:NH, 2, 128:256], sqb[:, 0:NH], identF[:, :]))
                    m = b.inc(pe)
                    b.wait(dve, pe, m)
                    b.op(dve, lambda h, t0=t0, l0=l0: h.tensor_copy(
                        cs_hi[:, t0 + l0:t0 + l0 + 128], pA[0:NH, 2, 128:256]))
                    b.op(dve, lambda h, t0=t0, l0=l0: h.tensor_tensor(
                        cs_lo[:, t0 + l0:t0 + l0 + 128], pA[0:NH, 2, 128:256],
                        cs_hi[:, t0 + l0:t0 + l0 + 128], ALU.subtract))
                    d = b.inc(dve)
            b.op(dve, lambda h: h.tensor_tensor(
                sqa[0:NH, 0:32], bass.AP(cs_hi, 63, [[SEQ, NH], [64, 32]]),
                bass.AP(cs_lo, 63, [[SEQ, NH], [64, 32]]), ALU.add))
            d = b.inc(dve)
            b.wait(act, dve, d)
            b.op(act, lambda h: h.activation(gam[:, :], sqa[0:NH, 0:32], AF.Exp))
            ga = b.inc(act)
            for c in range(32):
                cl0 = c * 64
                b.op(dve, lambda h, cl0=cl0: h.tensor_tensor(
                    csf[:, :], cs_hi[:, cl0:cl0 + 64], cs_lo[:, cl0:cl0 + 64], ALU.add))
                d = b.inc(dve)
                b.wait(act, dve, d)
                b.op(act, lambda h: h.activation(
                    csf[:, :], csf[:, :], AF.Exp, scale=-1.0, bias=csf[:, 63:64]))
                a = b.inc(act)
                b.wait(dve, act, a)
                b.op(dve, lambda h, cl0=cl0: h.tensor_tensor(
                    dtdd[:, cl0:cl0 + 64], dt_bf[:, cl0:cl0 + 64], csf[:, :], ALU.mult))
                b.inc(dve)
            if DBG and s_ == 0:
                b.wait(qg, dve)
                b.op(qg, lambda h: h.dma_start(out=dbg_dt, in_=dt_bf[:, :]))
                b.inc(qg)
                b.op(qg, lambda h: h.dma_start(out=dbg_cs[0], in_=cs_hi[:, :]))
                b.inc(qg)
                b.op(qg, lambda h: h.dma_start(out=dbg_cs[1], in_=cs_lo[:, :]))
                b.inc(qg)
            b.flush()

            for i in range(SEQ // 512):
                t0g = s_ * SEQ + i * 512
                b.wait(qs, dve)
                b.wait(qs, pe)
                b.op(qs, lambda h, t0g=t0g: h.dma_start(
                    out=xt0[:, :, :], in_=xin_f(t0g, 512)))
                ld = b.inc(qs)
                b.wait(pe, qs, ld)
                b.wait(act, qs, ld)
                emit_ln_tile(b, ls, xt0[:, :, 0:256], xlnt[:, :, 0:256])
                emit_ln_tile(b, ls, xt0[:, :, 256:512], xlnt[:, :, 256:512])
                b.wait(pe, dve)
                for fo in range(4, 9):
                    fp = fo % 2
                    q_ = qg if fp == 0 else qg2
                    width = 512 if fo < 8 else 256
                    b.wait(q_, pe)
                    b.op(q_, lambda h, fo=fo, fp=fp, width=width: h.dma_start(
                        out=wpb[:, fp, :, 0:width],
                        in_=wsrc[:, :, fo * 512:fo * 512 + width]))
                    wld = b.inc(q_)
                    b.wait(pe, q_, wld)
                    nfc = width // 128
                    for fc in range(nfc):
                        pb_ = (fo * 4 + fc) % 2
                        b.wait(pe, act)
                        for k in range(NCH):
                            b.op(pe, lambda h, k=k, fc=fc, fp=fp, pb_=pb_: h.matmul(
                                pA[:, pb_, :], wpb[:, fp, k, fc * 128:(fc + 1) * 128],
                                xlnt[:, k, :], start=(k == 0), stop=(k == NCH - 1)))
                        m = b.inc(pe)
                        b.wait(act, pe, m)
                        cglob = (fo - 4) * 4 + fc
                        b.op(act, lambda h, cglob=cglob, pb_=pb_: h.activation(
                            xbc[:, cglob, 3:515], pA[:, pb_, :], AF.Identity))
                        b.inc(act)
                if i == 0:
                    b.wait(dve, act)
                    b.op(dve, lambda h: h.memset(xbc[:, :, 0:3], 0.0))
                    b.inc(dve)
                b.wait(act, dve)
                b.wait(pe, dve)
                for c in range(NCONV):
                    cp_ = c % 2
                    q_ = qg if cp_ == 0 else qg2
                    b.wait(q_, pe)
                    b.op(q_, lambda h, c=c, cp_=cp_: h.dma_start(
                        out=cvd[:, cp_, :, :],
                        in_=convd[:, c].rearrange("j pp q -> pp j q")))
                    wld = b.inc(q_)
                    b.wait(pe, q_, wld)
                    b.wait(pe, act)
                    for j in range(4):
                        b.op(pe, lambda h, c=c, j=j, cp_=cp_: h.matmul(
                            pA[:, 2 + (c % 2) * 0, :] if False else pA[:, 2, :],
                            cvd[:, cp_, j, :], xbc[:, c, j:j + 512],
                            start=(j == 0), stop=(j == 3)))
                    m = b.inc(pe)
                    b.wait(act, pe, m)
                    if c < 16:
                        b.op(act, lambda h, c=c: h.activation(
                            xbcs[:, c, :], pA[:, 2, :], AF.Silu, bias=cols[:, 0, c:c + 1]))
                    else:
                        b.op(act, lambda h, c=c: h.activation(
                            bc_fm[:, c - 16, :], pA[:, 2, :], AF.Silu, bias=cols[:, 0, c:c + 1]))
                    b.inc(act)
                b.wait(dve, act)
                b.op(dve, lambda h: h.tensor_copy(
                    xbc[:, :, 0:3], xbc[:, :, 512:515]))
                b.inc(dve)
                if DBG and s_ == 0 and i == 0:
                    b.wait(qg, act)
                    b.op(qg, lambda h: h.dma_start(out=dbg_xbcs, in_=xbcs[:, :, :]))
                    b.inc(qg)
                    b.op(qg, lambda h: h.dma_start(out=dbg_bc, in_=bc_fm[:, :, :]))
                    b.inc(qg)
                for cp in range(4):
                    l0 = cp * 128
                    for j in range(2):
                        cseq = (i * 8 + cp * 2 + j)
                        q_ = qw if j == 0 else qw2
                        b.wait(q_, pe)
                        b.op(q_, lambda h, j=j, cseq=cseq: h.dma_start(
                            out=rowbuf[0:1, j, 0, :],
                            in_=cs_hi[:, cseq * 64:(cseq + 1) * 64]))
                        b.inc(q_)
                        b.op(q_, lambda h, j=j, cseq=cseq: h.dma_start(
                            out=rowbuf[0:1, j, 1, :],
                            in_=cs_lo[:, cseq * 64:(cseq + 1) * 64]))
                        rw = b.inc(q_)
                        b.wait(pe, q_, rw)
                    for j in range(2):
                        cseq = i * 8 + cp * 2 + j
                        jl0 = l0 + j * 64
                        base = 0
                        b.wait(pe, dve)
                        b.wait(pe, act)
                        for xc in range(16):
                            b.op(pe, lambda h, xc=xc, jl0=jl0: h.transpose(
                                pA_t.ap().bitcast(BF16)[0:64, 3, 0:128],
                                xbcs[:, xc, jl0:jl0 + 64], identB[:, :]))
                            m = b.inc(pe)
                            b.wait(act, pe, m)
                            b.op(act, lambda h, xc=xc: h.activation(
                                x_tok[:, xc * 128:(xc + 1) * 128],
                                pA_t.ap().bitcast(BF16)[0:64, 3, 0:128], AF.Identity))
                            a = b.inc(act)
                            b.wait(pe, act, a)
                        b.op(pe, lambda h, jl0=jl0: h.transpose(
                            pA_t.ap().bitcast(BF16)[0:64, 3, 0:128],
                            bc_fm[:, 0, jl0:jl0 + 64], identB[:, :]))
                        m = b.inc(pe)
                        b.wait(act, pe, m)
                        b.op(act, lambda h: h.activation(
                            btok[:, :], pA_t.ap().bitcast(BF16)[0:64, 3, 0:128], AF.Identity))
                        a = b.inc(act)
                        b.wait(pe, act, a)
                        b.wait(pe, dve)
                        b.op(pe, lambda h, jl0=jl0: h.matmul(
                            pS[0:64, 0, 0:64],
                            bc_fm[:, 0, jl0:jl0 + 64], bc_fm[:, 1, jl0:jl0 + 64],
                            start=True, stop=True))
                        m = b.inc(pe)
                        b.wait(dve, pe, m)
                        b.op(dve, lambda h: h.tensor_tensor(
                            cbt[:, :], pS[0:64, 0, 0:64], tri[0:64, :], ALU.mult))
                        dcb = b.inc(dve)
                        b.wait(pe, dve, dcb)
                        for q4 in range(4):
                            sl = slice(q4 * 512, (q4 + 1) * 512)
                            b.op(pe, lambda h, cseq=cseq, q4=q4, sl=sl: h.matmul(
                                pAfull[0:64, sl], cs_hi[:, cseq * 64:(cseq + 1) * 64],
                                e32_sb[:, sl], start=True, stop=False))
                            b.op(pe, lambda h, cseq=cseq, sl=sl: h.matmul(
                                pAfull[0:64, sl], cs_lo[:, cseq * 64:(cseq + 1) * 64],
                                e32_sb[:, sl], start=False, stop=False))
                            b.op(pe, lambda h, j=j, sl=sl: h.matmul(
                                pAfull[0:64, sl], nrow_bf[0:1, 0:64],
                                rowbuf[0:1, j, 0, sl], start=False, stop=False))
                            b.op(pe, lambda h, j=j, sl=sl: h.matmul(
                                pAfull[0:64, sl], nrow_bf[0:1, 0:64],
                                rowbuf[0:1, j, 1, sl], start=False, stop=True))
                        m = b.inc(pe)
                        b.wait(dve, pe, m)
                        b.op(dve, lambda h: h.tensor_single_scalar(
                            pAfull[0:64, :], pAfull[0:64, :], -30.0, ALU.max))
                        d = b.inc(dve)
                        b.wait(act, dve, d)
                        b.op(act, lambda h: h.activation(L2[:, :], pAfull[0:64, :], AF.Exp, scale=-1.0))
                        a = b.inc(act)
                        b.wait(dve, act, a)
                        b.op(dve, lambda h: h.tensor_tensor(
                            Msb[:, :], L2[:, :], bass.AP(cbt, 0, [[64, 64], [0, NH], [1, 64]]), ALU.mult))
                        dm = b.inc(dve)
                        b.wait(pe, dve, dm)
                        for q4 in range(4):
                            sl = slice(q4 * 512, (q4 + 1) * 512)
                            b.op(pe, lambda h, cseq=cseq, sl=sl: h.matmul(
                                pAfull[0:64, sl], dt_bf[:, cseq * 64:(cseq + 1) * 64],
                                e32_sb[:, sl], start=True, stop=True))
                        m = b.inc(pe)
                        b.wait(dve, pe, m)
                        b.op(dve, lambda h: h.scalar_tensor_tensor(
                            xdt[:, :], pAfull[0:64, :], 1.0,
                            x_tok[:, :], ALU.mult, ALU.mult))
                        dxdt = b.inc(dve)
                        b.wait(pe, dve, dxdt)
                        for q4 in range(4):
                            sl = slice(q4 * 512, (q4 + 1) * 512)
                            b.op(pe, lambda h, cseq=cseq, sl=sl: h.matmul(
                                pAfull[0:64, sl], dtdd[:, cseq * 64:(cseq + 1) * 64],
                                e32_sb[:, sl], start=True, stop=True))
                        m = b.inc(pe)
                        b.wait(dve, pe, m)
                        b.op(dve, lambda h: h.scalar_tensor_tensor(
                            xdd[:, :], pAfull[0:64, :], 1.0,
                            x_tok[:, :], ALU.mult, ALU.mult))
                        dxdd = b.inc(dve)
                        b.op(dve, lambda h, j=j: h.tensor_tensor(
                            edrow[0:1, :], rowbuf[0:1, j, 0, :], rowbuf[0:1, j, 1, :], ALU.add))
                        d = b.inc(dve)
                        b.wait(act, dve, d)
                        b.op(act, lambda h: h.activation(edrow_bf[0:1, :], edrow[0:1, :], AF.Exp))
                        a = b.inc(act)
                        b.wait(pe, act, a)
                        b.wait(pe, dve, dxdd)
                        for q4 in range(4):
                            sl = slice(q4 * 512, (q4 + 1) * 512)
                            b.op(pe, lambda h, sl=sl: h.matmul(
                                pAfull[:, sl], orow_bf[:, :], edrow_bf[0:1, sl],
                                start=True, stop=True))
                        m = b.inc(pe)
                        b.wait(dve, pe, m)
                        b.op(dve, lambda h, jl0=jl0: h.tensor_tensor(
                            cdec[:, :], pAfull[:, :],
                            bass.AP(bc_fm, 512 + jl0, [[1024, 128], [0, NH], [1, 64]]),
                            ALU.mult))
                        dcd = b.inc(dve)
                        cur = (s_ * 32 + cseq) % 2
                        nxt = 1 - cur
                        b.wait(pe, act, ga)
                        b.wait(pe, dve)
                        for q4 in range(4):
                            sl = slice(q4 * 512, (q4 + 1) * 512)
                            b.op(pe, lambda h, cseq=cseq, sl=sl: h.matmul(
                                pSfull[0:1, sl], gam[:, cseq:cseq + 1], e32_sb[:, sl],
                                start=True, stop=True))
                        m = b.inc(pe)
                        b.wait(dve, pe, m)
                        b.op(dve, lambda h: h.tensor_copy(grow[0:1, :], pSfull[0:1, 0:2048]))
                        d = b.inc(dve)
                        b.wait(pe, dve, d)
                        for q4 in range(4):
                            sl = slice(q4 * 512, (q4 + 1) * 512)
                            b.op(pe, lambda h, sl=sl: h.matmul(
                                pSfull[:, sl], orow_bf[:, :], grow[0:1, sl],
                                start=True, stop=True))
                        m = b.inc(pe)
                        b.wait(dve, pe, m)
                        b.op(dve, lambda h, cur=cur: h.tensor_tensor(
                            Stmp[:, :], pSfull[:, :], Ssb[:, cur, :], ALU.mult))
                        dstmp = b.inc(dve)
                        b.wait(pe, dve, dstmp)
                        for hh in range(NH):
                            b.op(pe, lambda h, hh=hh: h.matmul(
                                pSfull[:, hh * 64:(hh + 1) * 64],
                                btok[:, :],
                                xdd[:, hh * 64:(hh + 1) * 64],
                                start=True, stop=True))
                        m = b.inc(pe)
                        b.wait(dve, pe, m)
                        b.op(dve, lambda h, nxt=nxt: h.tensor_tensor(
                            Ssb[:, nxt, :], Stmp[:, :], pSfull[:, :], ALU.add))
                        dsn = b.inc(dve)
                        b.wait(pe, dve, dsn)
                        for hh in range(NH):
                            rb = (hh % 2) * 64
                            colr = (hh // 2) * 128 + j * 64
                            b.op(pe, lambda h, hh=hh, rb=rb, colr=colr: h.matmul(
                                pAfull[rb:rb + 64, colr:colr + 64],
                                xdt[:, hh * 64:(hh + 1) * 64],
                                Msb[:, hh * 64:(hh + 1) * 64],
                                start=True, stop=False))
                            b.op(pe, lambda h, hh=hh, cur=cur, rb=rb, colr=colr: h.matmul(
                                pAfull[rb:rb + 64, colr:colr + 64],
                                Ssb[:, cur, hh * 64:(hh + 1) * 64],
                                cdec[:, hh * 64:(hh + 1) * 64],
                                start=False, stop=True))
                        m = b.inc(pe)
                        b.wait(act, pe, m)
                        b.op(act, lambda h, l0=l0, j=j: h.activation(
                            y_fm[:, :, l0 + j * 64:l0 + j * 64 + 64],
                            pAfull[:, :].rearrange("p (c l) -> p c l", l=128)[:, :, j * 64:j * 64 + 64],
                            AF.Identity))
                        ya = b.inc(act)
                        b.wait(pe, act, ya)
                if DBG and s_ == 0 and i == 0:
                    b.wait(qg, act)
                    b.op(qg, lambda h: h.dma_start(out=dbg_y, in_=y_fm[:, :, :]))
                    b.inc(qg)
                for fo in range(4):
                    fp = fo % 2
                    q_ = qg if fp == 0 else qg2
                    b.wait(q_, pe)
                    b.op(q_, lambda h, fo=fo, fp=fp: h.dma_start(
                        out=wpb[:, fp, :, :], in_=wsrc[:, :, fo * 512:(fo + 1) * 512]))
                    wld = b.inc(q_)
                    b.wait(pe, q_, wld)
                    for fc in range(4):
                        zc = fo * 4 + fc
                        pb_ = zc % 2
                        b.wait(pe, act)
                        b.wait(pe, dve)
                        for k in range(NCH):
                            b.op(pe, lambda h, k=k, fc=fc, fp=fp, pb_=pb_: h.matmul(
                                pA[:, pb_, :], wpb[:, fp, k, fc * 128:(fc + 1) * 128],
                                xlnt[:, k, :], start=(k == 0), stop=(k == NCH - 1)))
                        m = b.inc(pe)
                        b.wait(act, pe, m)
                        b.wait(act, dve)
                        b.op(act, lambda h, pb_=pb_, zc=zc: h.activation(
                            zs[:, pb_, :], pA[:, pb_, :], AF.Silu))
                        a = b.inc(act)
                        b.wait(dve, act, a)
                        b.op(dve, lambda h, zc=zc: h.scalar_tensor_tensor(
                            sqa[:, :], xbcs[:, zc, :], cols[:, 1, zc:zc + 1],
                            y_fm[:, zc, :], ALU.mult, ALU.add))
                        b.op(dve, lambda h, zc=zc, pb_=pb_: h.tensor_tensor(
                            y_fm[:, zc, :], sqa[:, :], zs[:, pb_, :], ALU.mult))
                        b.inc(dve)
                b.wait(pe, dve)
                b.wait(act, dve)
                for zc in range(16):
                    b.op(act, lambda h, zc=zc: h.activation(
                        sqa[:, :], y_fm[:, zc, :], AF.Square))
                    a = b.inc(act)
                    b.wait(pe, act, a)
                    b.op(pe, lambda h, zc=zc: h.matmul(
                        pS[0:1, 0, :], ones_col[:, 0:1], sqa[:, :],
                        start=(zc == 0), stop=(zc == 15)))
                    m = b.inc(pe)
                    b.wait(act, pe, m)
                b.wait(dve, pe)
                b.op(dve, lambda h: h.tensor_scalar(
                    st2[:, 0, :], pS[0:1, 0, :], 1.0 / DIN, 1e-5, ALU.mult, ALU.add))
                d = b.inc(dve)
                b.wait(act, dve, d)
                b.op(act, lambda h: h.activation(st2[:, 0, :], st2[:, 0, :], AF.Ln))
                b.op(act, lambda h: h.activation(st2[:, 0, :], st2[:, 0, :], AF.Exp, scale=-0.5))
                a = b.inc(act)
                b.wait(pe, act, a)
                b.op(pe, lambda h: h.matmul(pS[:, 1, :], ones_row[:, :], st2[:, 0, :],
                                            start=True, stop=True))
                m = b.inc(pe)
                b.wait(dve, pe, m)
                for zc in range(16):
                    b.op(dve, lambda h, zc=zc: h.scalar_tensor_tensor(
                        sqa[:, :], y_fm[:, zc, :], 1.0, pS[:, 1, :], ALU.mult, ALU.mult), dep=True)
                    b.op(dve, lambda h, zc=zc: h.tensor_scalar_mul(
                        y_fm[:, zc, :], sqa[:, :], cols[:, 2, zc:zc + 1]))
                b.inc(dve)
                if DBG and s_ == 0 and i == 0:
                    b.wait(qg, dve)
                    b.op(qg, lambda h: h.dma_start(out=dbg_yg, in_=y_fm[:, :, :]))
                    b.inc(qg)
                b.wait(dve, qs)
                store_prev = b.qss.cnt
                for fc in range(NCH):
                    fp = fc % 2
                    q_ = qw if fp == 0 else qw2
                    b.wait(q_, pe)
                    b.op(q_, lambda h, fc=fc, fp=fp: h.dma_start(
                        out=wpb[:, fp, :, :].rearrange("p k f -> p (k f)")[:, 0:2048]
                            .rearrange("p (k f) -> p k f", f=128),
                        in_=wout_bf.rearrange("(k p) f -> p k f", p=128)
                            [:, :, fc * 128:(fc + 1) * 128]))
                    wld = b.inc(q_)
                    b.wait(pe, q_, wld)
                    b.wait(pe, dve)
                    pb_ = fc % 2
                    for k in range(16):
                        b.op(pe, lambda h, k=k, fc=fc, fp=fp, pb_=pb_: h.matmul(
                            pA[:, pb_, :],
                            wpb[:, fp, :, :].rearrange("p k f -> p (k f)")
                                [:, 0:2048].rearrange("p (k f) -> p k f", f=128)[:, k, :],
                            y_fm[:, k, :], start=(k == 0), stop=(k == 15)))
                    m = b.inc(pe)
                    b.wait(dve, pe, m)
                    b.wait(dve, qss)
                    b.op(dve, lambda h, fc=fc, pb_=pb_: h.scalar_tensor_tensor(
                        ob[:, fc % 2, :], pA[:, pb_, :], 1.0, xt0[:, fc, :],
                        ALU.mult, ALU.add), dep=(fc < 2))
                    b.inc(dve)
                    if fc % 2 == 1:
                        b.wait(qss, dve)
                        b.op(qss, lambda h, t0g=t0g, fc=fc: h.dma_start(
                            out=xout_f(t0g, 512)[:, fc - 1:fc + 1, :], in_=ob[:, :, :]))
                        b.inc(qss)
            b.wait(dve, pe)
            b.op(dve, lambda h, nxt=(s_ * 32 + 32) % 2: h.memset(Ssb[:, nxt, :], 0.0))
            b.inc(dve)
        b.op(pool, lambda h, v=b.qss.cnt: h.wait_ge(b.qss.sem, v))
        b.inc(pool)
        b.flush()


def attn0_weight_map(I):
    cw = I["L0_attn_conv_w"][:, 0, :]
    diag = np.zeros((4, NCONV, 128, 128), np.float32)
    for j in range(4):
        for c in range(NCONV):
            np.fill_diagonal(diag[j, c], cw[c * 128:(c + 1) * 128, j])
    e = np.kron(np.eye(NH, dtype=np.float32), np.ones((1, HDIM), np.float32))
    return dict(
        win=_c(I["L0_attn_in_proj_w"]),
        convd=diag,
        convb=_c(I["L0_attn_conv_b"].reshape(NCONV, 128).T),
        dtb=_c(I["L0_attn_dt_bias"].reshape(NH, 1)),
        alog=_c(I["L0_attn_A_log"].reshape(NH, 1)),
        dcol=_c(I["L0_attn_D"].repeat(HDIM).reshape(16, 128).T),
        nwcol=_c(I["L0_attn_norm_w"].reshape(16, 128).T),
        wout=_c(I["L0_attn_out_proj_w"]),
        e32=_c(e))


def run_attn0(x_fm, I):
    nc = build_attn0()
    wmap = attn0_weight_map(I)
    in_maps = []
    for j in range(NCORES):
        xs = _c(x_fm[:, 2 * j:2 * j + 2, :].reshape(DM, TOK))
        in_maps.append(dict(x_in=xs, **wmap))
    res = run_bass_kernel_spmd(nc, in_maps, core_ids=list(range(NCORES)))
    out = np.empty((DM, 16, 2048), np.float32)
    for j in range(NCORES):
        out[:, 2 * j:2 * j + 2, :] = res.results[j]["x_out"].reshape(DM, 2, SEQ)
    return out


def _c(a):
    return np.ascontiguousarray(a)


def ffn0_weight_map(I):
    w = dict(
        wfT=_c(I["L0_ffn_wf"].T), wiT=_c(I["L0_ffn_wi"].T), wcT=_c(I["L0_ffn_wc"].T),
        ufT=_c(I["L0_ffn_uf"].T), uiT=_c(I["L0_ffn_ui"].T), ucT=_c(I["L0_ffn_uc"].T),
        woT=_c(I["L0_ffn_wo"].T), uoT=_c(I["L0_ffn_uo"].T))
    for k, src in [("bf", "L0_ffn_bf"), ("bi", "L0_ffn_bi"), ("bc", "L0_ffn_bc"),
                   ("bo", "L0_ffn_bo"), ("in_c", "L0_ffn_in_c")]:
        w[k] = _c(I[src].reshape(NCH, 128).T)
    return w


def run_ffn0(x_fm, I):
    nc = build_ffn0()
    wmap = ffn0_weight_map(I)
    in_maps = []
    for j in range(NCORES):
        xs = _c(x_fm[:, :, j * LLOC:(j + 1) * LLOC].reshape(DM, TOK))
        in_maps.append(dict(x_in=xs, **wmap))
    res = run_bass_kernel_spmd(nc, in_maps, core_ids=list(range(NCORES)))
    out = np.empty((DM, 16, 2048), np.float32)
    for j in range(NCORES):
        out[:, :, j * LLOC:(j + 1) * LLOC] = \
            res.results[j]["x_out"].reshape(DM, 16, LLOC)
    return out


def decl_ffn0_w(nc, W):
    for w in ["wfT", "wiT", "wcT", "ufT", "uiT", "ucT", "woT", "uoT"]:
        W[w] = nc.dram_tensor(w, [DM, DM], F32, kind="ExternalInput").ap()
    for v in ["bf", "bi", "bc", "bo", "in_c"]:
        W[v] = nc.dram_tensor(v, [128, NCH], F32, kind="ExternalInput").ap()


def decl_ffn1_w(nc, W):
    W["wqkv"] = nc.dram_tensor("wqkv", [DM, 3 * DM], F32, kind="ExternalInput").ap()
    W["bqkv_row"] = nc.dram_tensor("bqkv_row", [1, 3 * DM], F32, kind="ExternalInput").ap()
    W["bqkv_col"] = nc.dram_tensor("bqkv_col", [128, 24], F32, kind="ExternalInput").ap()
    W["wgate"] = nc.dram_tensor("wgate", [DM, 8], F32, kind="ExternalInput").ap()
    W["bgate"] = nc.dram_tensor("bgate", [8, 1], F32, kind="ExternalInput").ap()
    W["wexp"] = nc.dram_tensor("wexp", [8, DM, DM], F32, kind="ExternalInput").ap()
    W["bexp"] = nc.dram_tensor("bexp", [8, DM], F32, kind="ExternalInput").ap()
    W["wexp_bf"] = nc.dram_tensor("wexp_bf", [8, DM, DM], BF16).ap()


def decl_attn1_w(nc, W):
    W["A"] = nc.dram_tensor("A", [64, 64], F32, kind="ExternalInput").ap()
    W["BT"] = nc.dram_tensor("BT", [DM, 64], F32, kind="ExternalInput").ap()
    W["CT"] = nc.dram_tensor("CT", [64, DM], F32, kind="ExternalInput").ap()
    for v in ["Dcol", "ldt_col", "ln1_col", "ln2_col"]:
        W[v] = nc.dram_tensor(v, [128, NCH], F32, kind="ExternalInput").ap()
    W["wgT"] = nc.dram_tensor("wgT", [DM, INTER], F32, kind="ExternalInput").ap()
    W["wuT"] = nc.dram_tensor("wuT", [DM, INTER], F32, kind="ExternalInput").ap()
    W["wdT"] = nc.dram_tensor("wdT", [INTER, DM], F32, kind="ExternalInput").ap()
    W["wg_bf"] = nc.dram_tensor("wg_bf", [DM, INTER], BF16).ap()
    W["wu_bf"] = nc.dram_tensor("wu_bf", [DM, INTER], BF16).ap()
    W["wd_bf"] = nc.dram_tensor("wd_bf", [INTER, DM], BF16).ap()


def decl_attn0_w(nc, W):
    W["win"] = nc.dram_tensor("win", [DM, 4384], F32, kind="ExternalInput").ap()
    W["convd"] = nc.dram_tensor("convd", [4, NCONV, 128, 128], F32, kind="ExternalInput").ap()
    W["convb"] = nc.dram_tensor("convb", [128, NCONV], F32, kind="ExternalInput").ap()
    W["dtb"] = nc.dram_tensor("dtb", [NH, 1], F32, kind="ExternalInput").ap()
    W["alog"] = nc.dram_tensor("alog", [NH, 1], F32, kind="ExternalInput").ap()
    W["dcol"] = nc.dram_tensor("dcol", [128, 16], F32, kind="ExternalInput").ap()
    W["nwcol"] = nc.dram_tensor("nwcol", [128, 16], F32, kind="ExternalInput").ap()
    W["wout"] = nc.dram_tensor("wout", [DIN, DM], F32, kind="ExternalInput").ap()
    W["e32"] = nc.dram_tensor("e32", [NH, DIN], F32, kind="ExternalInput").ap()
    W["wout_bf"] = nc.dram_tensor("wout_bf", [DIN, DM], BF16).ap()


def _plain_fns(xin, xout):
    xs = xin.rearrange("(c p) t -> p c t", p=128)
    xd = xout.rearrange("(c p) t -> p c t", p=128)
    return (lambda t0, w: xs[:, :, t0:t0 + w],
            lambda t0, w: xd[:, :, t0:t0 + w])


def _wrap(emit, decl):
    def build():
        nc = bass.Bass("TRN2", target_bir_lowering=False)
        xin = nc.dram_tensor("x_in", [DM, TOK], F32, kind="ExternalInput").ap()
        xout = nc.dram_tensor("x_out", [DM, TOK], F32, kind="ExternalOutput").ap()
        W = {}
        decl(nc, W)
        b = make_b(nc)
        fin, fout = _plain_fns(xin, xout)
        emit(nc, b, W, fin, fout)
        return nc
    return build


build_ffn0 = _wrap(emit_ffn0, decl_ffn0_w)
build_ffn1 = _wrap(emit_ffn1, decl_ffn1_w)
build_attn1 = _wrap(emit_attn1, decl_attn1_w)
build_attn0 = _wrap(emit_attn0, decl_attn0_w)


def build_fused():
    nc = bass.Bass("TRN2", target_bir_lowering=False)
    xin = nc.dram_tensor("x_in", [DM, TOK], F32, kind="ExternalInput").ap()
    xout = nc.dram_tensor("x_out", [DM, TOK], F32, kind="ExternalOutput").ap()
    W = {}
    decl_ffn0_w(nc, W)
    decl_attn0_w(nc, W)
    decl_ffn1_w(nc, W)
    decl_attn1_w(nc, W)
    ab = [nc.dram_tensor(f"a2a{i}{d}", [NCORES, DM * 512], F32)
          for i in range(3) for d in ("i", "o")]
    a1i, a1o, a2i, a2o, a3i, a3o = ab
    xbs = [nc.dram_tensor(f"xb{i}", [DM, TOK], F32).ap() for i in range(1, 6)]
    xb1, xb2, xb3, xb4, xb5 = xbs
    b = make_b(nc)
    cc = b.reg("cc", "gpsimd", dma=True, step=1)

    def lshard_in(t0, w):
        xs = xin.rearrange("(c p) t -> p c t", p=128)
        return xs[:, :, t0:t0 + w]

    def store_l(buf):
        def f(t0, w):
            assert w == 256 and t0 % 256 == 0
            t_ = t0 // 256
            return buf.ap().rearrange("j (c p two l) -> j p c two l", p=128,
                                      two=2, l=256)[t_ // 2][:, :, t_ % 2, :]
        return f

    load_l = store_l

    def plain(buf):
        xs = buf.rearrange("(c p) t -> p c t", p=128)
        return lambda t0, w: xs[:, :, t0:t0 + w]

    def a2a(bin_, bout):
        b.wait(cc, b.qss)
        b.wait(cc, b.qg)
        b.op(cc, lambda h: h.collective_compute(
            "AllToAll", ALU.bypass, replica_groups=[list(range(NCORES))],
            ins=[bin_.ap().opt()], outs=[bout.ap().opt()]))
        b.inc(cc)
        b.flush()

    def reorder_out(bufa2a, bufplain):
        b.wait(b.qg, cc)
        pl = bufplain.rearrange("(c p) t -> p c t", p=128)
        for j in range(NCORES):
            for two in range(2):
                dst = pl[:, :, two * SEQ + j * 256:two * SEQ + (j + 1) * 256]
                src = bufa2a.ap().rearrange("j (c p two l) -> j p c two l",
                                            p=128, two=2, l=256)[j][:, :, two, :]
                b.op(b.qg, lambda h, dst=dst, src=src: h.dma_start(out=dst, in_=src))
                b.inc(b.qg)
        b.flush()

    def reorder_in(bufplain, bufa2a):
        b.wait(b.qg, b.qss)
        pl = bufplain.rearrange("(c p) t -> p c t", p=128)
        for j in range(NCORES):
            for two in range(2):
                src = pl[:, :, two * SEQ + j * 256:two * SEQ + (j + 1) * 256]
                dst = bufa2a.ap().rearrange("j (c p two l) -> j p c two l",
                                            p=128, two=2, l=256)[j][:, :, two, :]
                b.op(b.qg, lambda h, dst=dst, src=src: h.dma_start(out=dst, in_=src))
                b.inc(b.qg)
        b.flush()

    FUSE_N = int(os.environ.get("FUSE_N", "6"))

    def finish(buf):
        b.wait(b.qg, b.qss)
        b.op(b.qg, lambda h: h.dma_start(out=xout, in_=buf))
        b.inc(b.qg)
        b.op(b.pool, lambda h, v=b.qg.cnt: h.wait_ge(b.qg.sem, v))
        b.inc(b.pool)
        b.flush()
        return nc

    emit_ffn0(nc, b, W, lshard_in, store_l(a1i), first=True)
    a2a(a1i, a1o)
    reorder_out(a1o, xb1)
    if FUSE_N <= 1:
        return finish(xb1)
    b.wait(b.qs, b.qg)
    emit_attn0(nc, b, W, plain(xb1), plain(xb2), first=True)
    if FUSE_N <= 2:
        return finish(xb2)
    reorder_in(xb2, a2i)
    a2a(a2i, a2o)
    b.wait(b.qs, cc)
    emit_ffn0(nc, b, W, load_l(a2o), store_l(a3i), first=False)
    a2a(a3i, a3o)
    reorder_out(a3o, xb3)
    if FUSE_N <= 3:
        return finish(xb3)
    b.wait(b.qs, b.qg)
    emit_ffn1(nc, b, W, plain(xb3), plain(xb4), first=True)
    if FUSE_N <= 4:
        return finish(xb4)
    b.wait(b.qs, b.qss)
    emit_attn1(nc, b, W, plain(xb4), plain(xb5), first=True)
    if FUSE_N <= 5:
        return finish(xb5)
    b.wait(b.qs, b.qss)
    emit_ffn1(nc, b, W, plain(xb5),
              _plain_fns(xin, xout)[1], first=False)
    return nc


def fused_in_maps(I):
    wmap = {}
    wmap.update(ffn0_weight_map(I))
    wmap.update(attn0_weight_map(I))
    wmap.update(ffn1_weight_map(I))
    wmap.update(attn1_weight_map(I))
    x_fm = _c(I["L0_attn_u"].astype(np.float32).transpose(2, 0, 1))
    in_maps = []
    for j in range(NCORES):
        xs = _c(x_fm[:, :, j * 256:(j + 1) * 256].reshape(DM, TOK))
        in_maps.append(dict(x_in=xs, **wmap))
    return in_maps


def kernel(**inputs):
    I = inputs
    nc = build_fused()
    res = run_bass_kernel_spmd(nc, fused_in_maps(I), core_ids=list(range(NCORES)))
    out = np.empty((16, 2048, DM), np.float32)
    for j in range(NCORES):
        xb = res.results[j]["x_out"].reshape(DM, 2, SEQ)
        out[2 * j:2 * j + 2] = xb.transpose(1, 2, 0)
    return _c(out)


def kernel_unfused(**inputs):
    I = inputs
    x_fm = _c(I["L0_attn_u"].astype(np.float32).transpose(2, 0, 1))
    x_fm = run_ffn0(x_fm, I)
    x_fm = run_attn0(x_fm, I)
    x_fm = run_ffn0(x_fm, I)
    x_fm = run_ffn1(x_fm, I)
    x_fm = run_attn1(x_fm, I)
    x_fm = run_ffn1(x_fm, I)
    return _c(x_fm.transpose(1, 2, 0))


# version 04
# speedup vs baseline: 1.1854x; 1.1854x over previous
import os

os.environ.setdefault("TRN_ALLOW_ICI_ALL_TO_ALL", "1")

from contextlib import ExitStack

import numpy as np
import ml_dtypes
import concourse.bass as bass
import concourse.mybir as mybir
from concourse.bass_utils import run_bass_kernel_spmd

F32 = mybir.dt.float32
BF16 = mybir.dt.bfloat16
AF = mybir.ActivationFunctionType
ALU = mybir.AluOpType
AX = mybir.AxisListType

NCORES = 8
TOK = 4096
DM = 1024
NCH = DM // 128
LN_TT = 256


def _sb(nc, name, shape, dtype):
    return nc.sbuf_tensor(name, shape, dtype, allow_name_mangling=True)


def _ps(nc, name, shape, dtype):
    return nc.psum_tensor(name, shape, dtype, allow_name_mangling=True)


class Eng:
    def __init__(self, name, stream, sem, step, serialize=False, dma=False):
        self.name = name
        self.stream = stream
        self.sem = sem
        self.cnt = 0
        self.step = step
        self.serialize = serialize
        self.dma = dma
        self.hw = {}


class B:
    STREAMS = ("sync", "gpsimd", "tensor", "scalar", "vector")

    def __init__(self, nc: bass.Bass):
        self.nc = nc
        self.engs = {}
        self.lists = {s: [] for s in self.STREAMS}

    def reg(self, name, stream, dma=False, serialize=False, step=None):
        sem = self.nc.alloc_semaphore(f"sem_{name}")
        e = Eng(name, stream, sem, step if step is not None else (16 if dma else 1),
                serialize, dma)
        self.engs[name] = e
        setattr(self, name, e)
        return e

    AUTO = ("act", "dve", "pool")

    def op(self, e: Eng, fn, dep=None):
        if e.serialize and e.cnt > e.hw.get(e.name, 0):
            sem, v = e.sem, e.cnt
            self.lists[e.stream].append(
                [lambda h, s=sem, vv=v: h.wait_ge(s, vv), []])
            e.hw[e.name] = v
        if e.name in self.AUTO:
            if dep is None or dep:
                if e.hw.get(e.name, 0) < e.cnt:
                    sem, v = e.sem, e.cnt
                    self.lists[e.stream].append(
                        [lambda h, s=sem, vv=v: h.wait_ge(s, vv), []])
                    e.hw[e.name] = v
            entry = [fn, [(e.sem, 1)]]
            e.cnt += 1
        else:
            entry = [fn, []]
        self.lists[e.stream].append(entry)
        self._last = {**getattr(self, "_last", {}), e.name: entry}
        return entry

    def inc(self, e: Eng, n=None):
        if e.name in self.AUTO:
            return e.cnt
        n = n if n is not None else e.step
        self._last[e.name][1].append((e.sem, n))
        e.cnt += n
        return e.cnt

    def wait(self, cons: Eng, prod: Eng, upto=None):
        v = upto if upto is not None else prod.cnt
        if v <= 0 or cons.hw.get(prod.name, 0) >= v:
            return
        sem = prod.sem
        self.lists[cons.stream].append([lambda h, s=sem, vv=v: h.wait_ge(s, vv), []])
        cons.hw[prod.name] = v

    def flush(self):
        lists = {s: l for s, l in self.lists.items() if l}
        self.lists = {s: [] for s in self.STREAMS}
        if not lists:
            return
        with self.nc.Block() as blk:
            def mk(lst):
                def run(h):
                    for fn, incs in lst:
                        ins = fn(h)
                        for sem, n in incs:
                            ins.then_inc(sem, n)
                return run
            for s, lst in lists.items():
                getattr(blk, s)(mk(lst))
        for c in self.engs.values():
            for p in self.engs.values():
                if not p.dma:
                    c.hw[p.name] = p.cnt


def make_b(nc):
    b = B(nc)
    ser = os.environ.get("Q_SERIALIZE", "0") == "1"
    b.reg("qs", "sync", dma=True, serialize=ser)
    b.reg("qss", "sync", dma=True, serialize=ser)
    b.reg("qw", "sync", dma=True, serialize=ser)
    b.reg("qw2", "sync", dma=True, serialize=ser)
    b.reg("qg", "gpsimd", dma=True, serialize=ser)
    b.reg("qg2", "gpsimd", dma=True, serialize=ser)
    b.reg("pe", "tensor")
    b.reg("act", "scalar")
    b.reg("dve", "vector")
    b.reg("pool", "gpsimd")
    return b


class LnScratch:
    def __init__(self, nc, stack, pst=None, pbc=None, tag=""):
        if pst is None:
            pst = stack.enter_context(_ps(nc, f"ln_pst{tag}", [1, 2, LN_TT], F32))[:, :, :]
        if pbc is None:
            pbc = stack.enter_context(_ps(nc, f"ln_pbc{tag}", [128, 2, LN_TT], F32))[:, :, :]
        self.sq = stack.enter_context(_sb(nc, f"ln_sq{tag}", [128, 2, LN_TT], F32))
        self.stats = stack.enter_context(_sb(nc, f"ln_stats{tag}", [1, 4, LN_TT], F32))
        self.ones_col = stack.enter_context(_sb(nc, f"ones_col{tag}", [128, 1], F32))
        self.ones_row = stack.enter_context(_sb(nc, f"ones_row{tag}", [1, 128], F32))
        self.pst = pst
        self.pbc = pbc

    def init(self, b: B, dve: Eng):
        b.op(dve, lambda h, s=self: h.memset(s.ones_col[:, :], 1.0))
        b.op(dve, lambda h, s=self: h.memset(s.ones_row[:, :], 1.0))
        b.inc(dve)


def emit_ln_tile(b: B, ls: LnScratch, xt, xln_out):
    T = xt.shape[2]
    pe, act, dve = b.pe, b.act, b.dve
    st = ls.stats
    b.wait(pe, dve)
    b.wait(pe, act)
    b.wait(act, dve)
    for c in range(NCH):
        b.op(pe, lambda h, c=c, T=T, xt=xt: h.matmul(
            ls.pst[0:1, 0, 0:T], ls.ones_col[:, 0:1], xt[:, c, :],
            start=(c == 0), stop=(c == NCH - 1)))
    b.inc(pe)
    for c in range(NCH):
        b.op(act, lambda h, c=c, T=T, xt=xt: h.activation(
            ls.sq[:, c % 2, 0:T], xt[:, c, :], AF.Square))
        a_mark = b.inc(act)
        b.wait(pe, act, a_mark)
        b.op(pe, lambda h, c=c, T=T: h.matmul(
            ls.pst[0:1, 1, 0:T], ls.ones_col[:, 0:1], ls.sq[:, c % 2, 0:T],
            start=(c == 0), stop=(c == NCH - 1)))
        pe_mark = b.inc(pe)
        b.wait(act, pe, pe_mark)
    b.wait(dve, pe)
    b.op(dve, lambda h, T=T: h.tensor_scalar_mul(st[:, 0, 0:T], ls.pst[0:1, 0, 0:T], 1.0 / DM))
    b.op(dve, lambda h, T=T: h.tensor_scalar_mul(st[:, 1, 0:T], ls.pst[0:1, 1, 0:T], 1.0 / DM))
    b.op(dve, lambda h, T=T: h.tensor_tensor(st[:, 2, 0:T], st[:, 0, 0:T], st[:, 0, 0:T], ALU.mult))
    b.op(dve, lambda h, T=T: h.tensor_tensor(st[:, 2, 0:T], st[:, 1, 0:T], st[:, 2, 0:T], ALU.subtract))
    b.op(dve, lambda h, T=T: h.tensor_scalar_add(st[:, 2, 0:T], st[:, 2, 0:T], 1e-5))
    d_mark = b.inc(dve)
    b.wait(act, dve, d_mark)
    b.op(act, lambda h, T=T: h.activation(st[:, 2, 0:T], st[:, 2, 0:T], AF.Ln))
    b.op(act, lambda h, T=T: h.activation(st[:, 3, 0:T], st[:, 2, 0:T], AF.Exp, scale=-0.5))
    a_mark = b.inc(act)
    b.wait(dve, act, a_mark)
    b.op(dve, lambda h, T=T: h.tensor_tensor(st[:, 2, 0:T], st[:, 0, 0:T], st[:, 3, 0:T], ALU.mult))
    d_mark = b.inc(dve)
    b.wait(pe, dve, d_mark)
    b.op(pe, lambda h, T=T: h.matmul(ls.pbc[:, 0, 0:T], ls.ones_row[:, :], st[:, 3, 0:T],
                                     start=True, stop=True))
    b.op(pe, lambda h, T=T: h.matmul(ls.pbc[:, 1, 0:T], ls.ones_row[:, :], st[:, 2, 0:T],
                                     start=True, stop=True))
    pe_mark = b.inc(pe)
    b.wait(dve, pe, pe_mark)
    for c in range(NCH):
        b.op(dve, lambda h, c=c, T=T, xt=xt: h.scalar_tensor_tensor(
            ls.sq[:, c % 2, 0:T], xt[:, c, :], 1.0, ls.pbc[:, 0, 0:T],
            ALU.mult, ALU.mult))
        b.op(dve, lambda h, c=c, T=T, xln_out=xln_out: h.tensor_tensor(
            xln_out[:, c, :], ls.sq[:, c % 2, 0:T], ls.pbc[:, 1, 0:T], ALU.subtract))
    b.inc(dve)


TSTEPS = 16
LLOC = 256


def emit_ffn0(nc, b, W, xin_f, xout_f, first=True):
    wts = W
    dbg = False
    with ExitStack() as stack:
        xln = stack.enter_context(_sb(nc, "xln", [128, NCH, TOK], BF16))
        wu = stack.enter_context(_sb(nc, "wu", [128, 6, NCH, DM], BF16))
        bias_sb = stack.enter_context(_sb(nc, "bias_sb", [128, 5, NCH], F32))
        c_f32 = stack.enter_context(_sb(nc, "c_f32", [128, NCH, LLOC], F32))
        c_bf = stack.enter_context(_sb(nc, "c_bf", [128, 2, NCH, LLOC], BF16))
        gates = stack.enter_context(_sb(nc, "gates", [128, 3, LLOC], F32))
        hbuf = stack.enter_context(_sb(nc, "hbuf", [128, 2, LLOC], F32))
        xt0 = stack.enter_context(_sb(nc, "xt0", [128, NCH, LLOC], F32))
        xt1 = stack.enter_context(_sb(nc, "xt1", [128, NCH, LLOC], F32))
        ls = LnScratch(nc, stack)
        pg0 = stack.enter_context(_ps(nc, "pg0", [128, 3, 512], F32))[:, :, 0:LLOC]
        pg1 = stack.enter_context(_ps(nc, "pg1", [128, 3, 512], F32))[:, :, 0:LLOC]

        qs, qss, qw, qw2 = b.qs, b.qss, b.qw, b.qw2
        qg, qg2, pe, act, dve, pool = b.qg, b.qg2, b.pe, b.act, b.dve, b.pool

        for vi, v in enumerate(["bf", "bi", "bc", "bo", "in_c"]):
            b.op(qs, lambda h, vi=vi, v=v: h.dma_start(
                out=bias_sb[:, vi, :], in_=wts[v]))
            b.inc(qs)
        for gi, w in enumerate(["wfT", "wiT", "wcT", "ufT", "uiT", "ucT"]):
            b.op(qg, lambda h, gi=gi, w=w: h.dma_start(
                out=wu[:, gi, :, :].rearrange("p k (o q) -> p k o q", q=128),
                in_=wts[w].rearrange("(k p) (o q) -> p k o q", p=128, q=128)))
            b.inc(qg)
        ls.init(b, dve)
        b.flush()

        NT = TOK // LN_TT
        for i in range(NT):
            buf = xt0
            b.wait(qs, dve)
            b.op(qs, lambda h, i=i, buf=buf: h.dma_start(
                out=buf[:, :, :], in_=xin_f(i * LN_TT, LN_TT)))
            ld = b.inc(qs)
            b.wait(pe, qs, ld)
            b.wait(act, qs, ld)
            emit_ln_tile(b, ls, buf[:, :, :], xln[:, :, i * LN_TT:(i + 1) * LN_TT])
        b.flush()

        b.wait(pe, qg)
        b.wait(act, qs)
        b.wait(dve, qs)
        for c in range(NCH):
            b.op(dve, lambda h, c=c: h.memset(c_f32[:, c, :], 1.0))
            b.op(dve, lambda h, c=c: h.tensor_scalar_mul(
                c_f32[:, c, :], c_f32[:, c, :], bias_sb[:, 4, c:c + 1]))
            b.op(dve, lambda h, c=c: h.tensor_copy(c_bf[:, 0, c, :], c_f32[:, c, :]))
        b.inc(dve)

        psum_free = {}
        dve_marks = {-1: b.dve.cnt}
        for t_ in range(TSTEPS):
            cur, nxt = t_ % 2, 1 - t_ % 2
            for o in range(NCH):
                pidx = (t_ * NCH + o) % 2
                ps = pg0 if pidx == 0 else pg1
                if pidx in psum_free:
                    b.wait(pe, dve, psum_free[pidx])
                b.wait(pe, dve, dve_marks[t_ - 1])
                for gi in range(3):
                    for k in range(NCH):
                        b.op(pe, lambda h, k=k, gi=gi, o=o, t_=t_, ps=ps: h.matmul(
                            ps[:, gi, :],
                            wu[:, gi, k, o * 128:(o + 1) * 128],
                            xln[:, k, t_ * LLOC:(t_ + 1) * LLOC],
                            start=(k == 0), stop=False))
                    for k in range(NCH):
                        b.op(pe, lambda h, k=k, gi=gi, o=o, cur=cur, ps=ps: h.matmul(
                            ps[:, gi, :],
                            wu[:, 3 + gi, k, o * 128:(o + 1) * 128],
                            c_bf[:, cur, k, :],
                            start=False, stop=(k == NCH - 1)))
                pe_mark = b.inc(pe)
                b.wait(act, pe, pe_mark)
                b.wait(act, dve)
                b.op(act, lambda h, o=o, ps=ps: h.activation(
                    gates[:, 0, :], ps[:, 0, :], AF.Sigmoid, bias=bias_sb[:, 0, o:o + 1]))
                b.op(act, lambda h, o=o, ps=ps: h.activation(
                    gates[:, 1, :], ps[:, 1, :], AF.Sigmoid, bias=bias_sb[:, 1, o:o + 1]))
                b.op(act, lambda h, o=o, ps=ps: h.activation(
                    gates[:, 2, :], ps[:, 2, :], AF.Tanh, bias=bias_sb[:, 2, o:o + 1]))
                act_mark = b.inc(act)
                b.wait(dve, act, act_mark)
                b.op(dve, lambda h, o=o: h.tensor_tensor(
                    c_f32[:, o, :], c_f32[:, o, :], gates[:, 0, :], ALU.mult))
                b.op(dve, lambda h: h.tensor_tensor(
                    gates[:, 1, :], gates[:, 1, :], gates[:, 2, :], ALU.mult))
                b.op(dve, lambda h, o=o: h.tensor_tensor(
                    c_f32[:, o, :], c_f32[:, o, :], gates[:, 1, :], ALU.add))
                b.op(dve, lambda h, o=o, nxt=nxt: h.tensor_copy(
                    c_bf[:, nxt, o, :], c_f32[:, o, :]))
                psum_free[pidx] = b.inc(dve)
            dve_marks[t_] = b.dve.cnt
        b.flush()

        if dbg:
            b.op(qs, lambda h: h.dma_start(out=xln_dbg, in_=xln[:, :, :]))
            b.inc(qs)
            b.op(qs, lambda h: h.dma_start(out=c_dbg, in_=c_f32[:, :, :]))
            b.inc(qs)
            b.flush()

        cfin = TSTEPS % 2
        for gi, w in enumerate(["woT", "uoT"]):
            b.op(qg, lambda h, gi=gi, w=w: h.dma_start(
                out=wu[:, gi, :, :].rearrange("p k (o q) -> p k o q", q=128),
                in_=wts[w].rearrange("(k p) (o q) -> p k o q", p=128, q=128)))
            b.inc(qg)
        b.wait(pe, qg)
        psum_free = {}
        store_mark = 0
        for t_ in range(TSTEPS):
            buf = xt0
            obuf = xt1
            if store_mark:
                b.wait(dve, qss, store_mark)
            b.wait(qs, dve)
            b.op(qs, lambda h, t_=t_, buf=buf: h.dma_start(
                out=buf[:, :, :], in_=xin_f(t_ * LLOC, LLOC)))
            ld = b.inc(qs)
            for o in range(NCH):
                pidx = (t_ * NCH + o) % 2
                ps = pg0 if pidx == 0 else pg1
                if pidx in psum_free:
                    b.wait(pe, dve, psum_free[pidx])
                for k in range(NCH):
                    b.op(pe, lambda h, k=k, o=o, t_=t_, ps=ps: h.matmul(
                        ps[:, 0, :], wu[:, 0, k, o * 128:(o + 1) * 128],
                        xln[:, k, t_ * LLOC:(t_ + 1) * LLOC],
                        start=(k == 0), stop=False))
                for k in range(NCH):
                    b.op(pe, lambda h, k=k, o=o, ps=ps: h.matmul(
                        ps[:, 0, :], wu[:, 1, k, o * 128:(o + 1) * 128],
                        c_bf[:, cfin, k, :],
                        start=False, stop=(k == NCH - 1)))
                pe_mark = b.inc(pe)
                b.wait(act, pe, pe_mark)
                b.wait(act, dve)
                b.op(act, lambda h, o=o, ps=ps: h.activation(
                    hbuf[:, o % 2, :], ps[:, 0, :], AF.Tanh, bias=bias_sb[:, 3, o:o + 1]))
                act_mark = b.inc(act)
                b.wait(dve, act, act_mark)
                b.wait(dve, qs, ld)
                b.op(dve, lambda h, o=o, buf=buf, obuf=obuf: h.scalar_tensor_tensor(
                    obuf[:, o, :], hbuf[:, o % 2, :], 0.5, buf[:, o, :],
                    ALU.mult, ALU.add))
                psum_free[pidx] = b.inc(dve)
            b.wait(qss, dve)
            b.op(qss, lambda h, t_=t_, obuf=obuf: h.dma_start(
                out=xout_f(t_ * LLOC, LLOC), in_=obuf[:, :, :]))
            store_mark = b.inc(qss)
        b.op(pool, lambda h, v=b.qss.cnt: h.wait_ge(b.qss.sem, v))
        b.inc(pool)
        b.flush()


NSEQ = 2
SEQ = 2048
NHEAD = 16
HD = 64


def emit_ffn1(nc, b, W, xin_f, xout_f, first=True):
    PH = 3
    wqkv, bqkv_row, bqkv_col = W["wqkv"], W["bqkv_row"], W["bqkv_col"]
    wgate, bgate, wexp, bexp = W["wgate"], W["bgate"], W["wexp"], W["bexp"]
    wexp_bf = W["wexp_bf"]
    with ExitStack() as stack:
        xln = stack.enter_context(_sb(nc, "xln", [128, NCH, TOK], BF16))
        attn_sb = stack.enter_context(_sb(nc, "attn_sb", [128, NCH, TOK], BF16))
        cfm = stack.enter_context(_sb(nc, "cfm", [8, TOK], BF16))
        ident = stack.enter_context(_sb(nc, "ident", [128, 128], F32))
        bq_col = stack.enter_context(_sb(nc, "bq_col", [128, 24], F32))
        bg_col = stack.enter_context(_sb(nc, "bg_col", [8, 1], F32))
        be_sb = stack.enter_context(_sb(nc, "be_sb", [8, DM], BF16))
        wg_sb = stack.enter_context(_sb(nc, "wg_sb", [128, NCH, 8], BF16))
        orow_bf = stack.enter_context(_sb(nc, "orow_bf", [1, 128], BF16))
        pa = stack.enter_context(_ps(nc, "pa", [128, 2, 512], F32))[:, :, :]
        pb = stack.enter_context(_ps(nc, "pb", [128, 4, 512], F32))[:, :, :]

        qs, qss, qw, qw2 = b.qs, b.qss, b.qw, b.qw2
        qg, qg2, pe, act, dve, pool = b.qg, b.qg2, b.pe, b.act, b.dve, b.pool

        b.op(qs, lambda h: h.dma_start(out=bq_col[:, :], in_=bqkv_col))
        b.inc(qs)
        b.op(qs, lambda h: h.dma_start(out=bg_col[:, :], in_=bgate))
        b.inc(qs)
        b.op(qg, lambda h: h.dma_start(out=be_sb[:, :], in_=bexp))
        b.inc(qg)
        b.op(qg, lambda h: h.dma_start(
            out=wg_sb[:, :, :],
            in_=wgate.rearrange("(c p) e -> p c e", p=128)))
        b.inc(qg)
        if first:
            for e in range(8):
                b.op(qg, lambda h, e=e: h.dma_start(out=wexp_bf[e], in_=wexp[e]))
                b.inc(qg)
        b.op(pool, lambda h: h.memset(ident[:, :], 0.0))
        b.op(pool, lambda h: h.affine_select(
            out=ident[:, :], in_=ident[:, :], compare_op=ALU.not_equal,
            fill=1.0, base=0, pattern=[[-1, 128]], channel_multiplier=1))
        b.op(dve, lambda h: h.memset(orow_bf[:, :], 1.0))
        b.inc(dve)

        with ExitStack() as _phase_guard:
            ln_stack = ExitStack()
            xt0 = ln_stack.enter_context(_sb(nc, "xt0", [128, NCH, LN_TT], F32))
            ls = LnScratch(nc, ln_stack)
            ls.init(b, dve)
            for i in range(TOK // LN_TT):
                b.wait(qs, dve)
                b.op(qs, lambda h, i=i: h.dma_start(
                    out=xt0[:, :, :], in_=xin_f(i * LN_TT, LN_TT)))
                ld = b.inc(qs)
                b.wait(pe, qs, ld)
                b.wait(act, qs, ld)
                emit_ln_tile(b, ls, xt0[:, :, :], xln[:, :, i * LN_TT:(i + 1) * LN_TT])
            b.flush()
            ln_stack.close()


            with (
                _sb(nc, "glog", [8, TOK], F32) as glog,
                _sb(nc, "gt", [128, 10, 8], F32) as gt,
                _sb(nc, "gcols", [128, 4], F32) as gcols,
                _sb(nc, "ct", [128, 8], F32) as ct,
            ):
                b.wait(pe, qg)
                b.wait(act, qs)
                for i in range(TOK // 512):
                    b.wait(pe, act)
                    for k in range(NCH):
                        b.op(pe, lambda h, i=i, k=k: h.matmul(
                            pa[0:8, 0, :], wg_sb[:, k, :],
                            xln[:, k, i * 512:(i + 1) * 512],
                            start=(k == 0), stop=(k == NCH - 1)))
                    m = b.inc(pe)
                    b.wait(act, pe, m)
                    b.op(act, lambda h, i=i: h.activation(
                        glog[:, i * 512:(i + 1) * 512], pa[0:8, 0, :],
                        AF.Identity, bias=bg_col[:, :]))
                b.flush()
                for i in range(TOK // 128):
                    sl = slice(i * 128, (i + 1) * 128)
                    b.wait(pe, dve)
                    b.op(pe, lambda h, sl=sl: h.transpose(
                        pa[:, 1, 0:8], glog[:, sl], ident[0:8, 0:8]))
                    m = b.inc(pe)
                    b.wait(dve, pe, m)
                    b.op(dve, lambda h: h.tensor_copy(gt[:, 0, :], pa[:, 1, 0:8]))
                    b.op(dve, lambda h: h.reduce_max(gcols[:, 0:1], gt[:, 0, :], axis=AX.X))
                    b.op(dve, lambda h: h.tensor_scalar_mul(gcols[:, 1:2], gcols[:, 0:1], -1.0))
                    d = b.inc(dve)
                    b.wait(act, dve, d)
                    b.op(act, lambda h: h.activation(gt[:, 1, :], gt[:, 0, :], AF.Exp,
                                                     bias=gcols[:, 1:2]))
                    a = b.inc(act)
                    b.wait(dve, act, a)
                    b.op(dve, lambda h: h.reduce_sum(gcols[:, 0:1], gt[:, 1, :], axis=AX.X))
                    b.op(dve, lambda h: h.reciprocal(gcols[:, 1:2], gcols[:, 0:1]))
                    b.op(dve, lambda h: h.tensor_scalar_mul(gt[:, 2, :], gt[:, 1, :], gcols[:, 1:2]))
                    b.op(dve, lambda h: h.reduce_max(gcols[:, 2:3], gt[:, 2, :], axis=AX.X))
                    b.op(dve, lambda h: h.tensor_single_scalar(gt[:, 3, :], gt[:, 2, :], gcols[:, 2:3], ALU.is_equal))
                    b.op(dve, lambda h: h.scalar_tensor_tensor(
                        gt[:, 4, :], gt[:, 3, :], -2.0, gt[:, 2, :], ALU.mult, ALU.add))
                    b.op(dve, lambda h: h.reduce_max(gcols[:, 3:4], gt[:, 4, :], axis=AX.X))
                    b.op(dve, lambda h: h.tensor_single_scalar(gt[:, 5, :], gt[:, 2, :], gcols[:, 3:4], ALU.is_ge))
                    b.op(dve, lambda h: h.tensor_tensor(ct[:, :], gt[:, 2, :], gt[:, 5, :], ALU.mult))
                    d = b.inc(dve)
                    b.wait(pe, dve, d)
                    b.op(pe, lambda h: h.transpose(pa[0:8, 1, 128:256], ct[:, :], ident[:, :]))
                    m = b.inc(pe)
                    b.wait(dve, pe, m)
                    b.op(dve, lambda h, sl=sl: h.tensor_copy(cfm[:, sl], pa[0:8, 1, 128:256]))
                    b.inc(dve)
                b.flush()


            with (
                _sb(nc, "qk_sb", [128, 4, SEQ], BF16) as qk_sb,
                _sb(nc, "v_sb", [128, 16, 4, 65], BF16) as v_sb,
                _sb(nc, "wbuf", [128, NCH, 1024], BF16) as wbuf,
                _sb(nc, "p_sb", [128, 2, 512], BF16) as p_sb,
                _sb(nc, "av_sb", [64, 512], BF16) as av_sb,
                _sb(nc, "r_sb", [128, 512], F32) as r_sb,
                _sb(nc, "ones128", [128, 128], F32) as ones128,
                _sb(nc, "bq_row", [1, 3 * DM], BF16) as bq_row,
            ):
                b.op(qg, lambda h: h.dma_start(out=bq_row[:, :], in_=bqkv_row))
                b.inc(qg)
                b.op(dve, lambda h: h.memset(ones128[:, :], 1.0))
                b.op(dve, lambda h: h.memset(
                    v_sb[:, :, :, 64:65].rearrange("p a c o -> p (a c o)"), 1.0))
                b.inc(dve)
                b.wait(act, qs)
                exp_marks = {}
                for hg in range(4):
                    b.wait(qg, pe)
                    for j, col in enumerate([hg * 256, DM + hg * 256, 2 * DM + hg * 256]):
                        b.op(qg, lambda h, j=j, col=col: h.dma_start(
                            out=wbuf[:, :, j * 256:(j + 1) * 256],
                            in_=wqkv.rearrange("(k p) f -> p k f", p=128)
                                [:, :, col:col + 256]))
                        b.inc(qg)
                    wld = b.qg.cnt
                    b.wait(pe, qg, wld)
                    for s_ in range(NSEQ):
                        seq0 = s_ * SEQ
                        b.wait(pe, dve)
                        for fc in range(4):
                            wslice = [0, 128, 256, 384][fc]
                            bcol = [2 * hg, 2 * hg + 1, 8 + 2 * hg, 8 + 2 * hg + 1][fc]
                            for i in range(SEQ // 512):
                                b.wait(pe, act)
                                for k in range(NCH):
                                    b.op(pe, lambda h, i=i, k=k, ws=wslice, seq0=seq0: h.matmul(
                                        pa[:, 0, :], wbuf[:, k, ws:ws + 128],
                                        xln[:, k, seq0 + i * 512:seq0 + (i + 1) * 512],
                                        start=(k == 0), stop=(k == NCH - 1)))
                                m = b.inc(pe)
                                b.wait(act, pe, m)
                                b.op(act, lambda h, fc=fc, i=i, bcol=bcol: h.activation(
                                    qk_sb[:, fc, i * 512:(i + 1) * 512], pa[:, 0, :],
                                    AF.Identity, bias=bq_col[:, bcol:bcol + 1]))
                        for tt in range(16):
                            tok0 = seq0 + tt * 128
                            b.wait(pe, act)
                            for k in range(NCH):
                                b.op(pe, lambda h, k=k, tok0=tok0: h.matmul(
                                    pa[:, 1, 0:256], xln[:, k, tok0:tok0 + 128],
                                    wbuf[:, k, 512:768],
                                    start=(k == 0), stop=False))
                            b.op(pe, lambda h, hg=hg: h.matmul(
                                pa[:, 1, 0:256], orow_bf[:, :],
                                bq_row[:, 2 * DM + hg * 256:2 * DM + (hg + 1) * 256],
                                start=False, stop=True))
                            m = b.inc(pe)
                            b.wait(act, pe, m)
                            b.op(act, lambda h, tt=tt: h.activation(
                                v_sb[:, tt, :, 0:64],
                                pa[:, 1, 0:256].rearrange("p (h f) -> p h f", h=4),
                                AF.Identity))
                        b.wait(pe, act)
                        for hi in range(4):
                            hglob = hg * 4 + hi
                            fc = hi // 2
                            rows = (hi % 2) * 64
                            for qt in range(4):
                                q0 = qt * 512
                                for kt in range(16):
                                    k0 = kt * 128
                                    pp = kt % 2
                                    if pp in exp_marks:
                                        b.wait(pe, act, exp_marks[pp])
                                    b.op(pe, lambda h, fc=fc, rows=rows, k0=k0, q0=q0, pp=pp: h.matmul(
                                        pa[:, pp, :], qk_sb[rows:rows + 64, 2 + fc, k0:k0 + 128],
                                        qk_sb[rows:rows + 64, fc, q0:q0 + 512],
                                        start=True, stop=True))
                                    m = b.inc(pe)
                                    b.wait(act, pe, m)
                                    b.op(act, lambda h, pp=pp: h.activation(
                                        p_sb[:, pp, :], pa[:, pp, :], AF.Exp, scale=0.125))
                                    exp_marks[pp] = b.inc(act)
                                    b.wait(pe, act, exp_marks[pp])
                                    b.op(pe, lambda h, kt=kt, hi=hi, pp=pp: h.matmul(
                                        pb[0:65, 0, :], v_sb[:, kt, hi, :],
                                        p_sb[:, pp, :],
                                        start=(kt == 0), stop=(kt == 15)))
                                    m = b.inc(pe)
                                b.wait(dve, pe, m)
                                b.op(dve, lambda h: h.reciprocal(r_sb[64:65, :], pb[64:65, 0, :]))
                                d = b.inc(dve)
                                b.wait(act, dve, d)
                                b.op(act, lambda h: h.activation(av_sb[:, :], pb[0:64, 0, :], AF.Identity))
                                a = b.inc(act)
                                b.wait(pe, dve, d)
                                b.op(pe, lambda h: h.matmul(pb[0:64, 1, :], ones128[64:65, 0:64], r_sb[64:65, :],
                                                            start=True, stop=True))
                                m2 = b.inc(pe)
                                b.wait(dve, pe, m2)
                                b.wait(dve, act, a)
                                b.op(dve, lambda h, rows=rows, hglob=hglob, q0=q0, seq0=seq0: h.scalar_tensor_tensor(
                                    attn_sb[rows:rows + 64, hglob // 2, seq0 + q0:seq0 + q0 + 512],
                                    pb[0:64, 1, :], 0.5, av_sb[:, :], ALU.mult, ALU.mult))
                                d = b.inc(dve)
                                b.wait(pe, dve, d)
                b.flush()


            with (
                _sb(nc, "cb_sb", [128, 2, 512], BF16) as cb_sb,
                _sb(nc, "xe", [128, 2, NCH, 512], BF16) as xe,
                _sb(nc, "we", [128, 2, NCH, 256], BF16) as we,
                _sb(nc, "xa", [128, NCH, 512], F32) as xa,
                _sb(nc, "ob", [128, NCH, 512], F32) as ob,
                _sb(nc, "cfm_row", [1, 8, 512], BF16) as cfm_row,
            ):
                store_mark = 0
                b.wait(pe, qg)
                for tt in range(TOK // 512):
                    t0 = tt * 512
                    b.wait(qs, pe)
                    b.op(qs, lambda h, t0=t0: h.dma_start(
                        out=cfm_row.ap().rearrange("o e t -> o (e t)"),
                        in_=cfm.ap()[:, t0:t0 + 512]))
                    crm = b.inc(qs)
                    b.wait(pe, qs, crm)
                    b.wait(qs, dve)
                    b.op(qs, lambda h, t0=t0: h.dma_start(
                        out=xa[:, :, :], in_=xin_f(t0, 512)))
                    xld = b.inc(qs)
                    b.wait(dve, qs, xld)
                    for c in range(NCH):
                        b.op(dve, lambda h, c=c, t0=t0: h.tensor_tensor(
                            xa[:, c, :], xa[:, c, :], attn_sb[:, c, t0:t0 + 512], ALU.add), dep=(c == 0))
                    for fh in range(2):
                        f0 = fh * 4
                        for e in range(8):
                            eb = e % 2
                            b.wait(pe, act)
                            b.op(pe, lambda h, e=e, eb=eb: h.matmul(
                                pa[:, eb, :], orow_bf[:, :], cfm_row[0:1, e, :],
                                start=True, stop=True))
                            m = b.inc(pe)
                            b.wait(act, pe, m)
                            b.wait(act, dve)
                            b.op(act, lambda h, eb=eb: h.activation(
                                cb_sb[:, eb, :], pa[:, eb, :], AF.Identity))
                            cbm = b.inc(act)
                            for half, q in ((0, qw), (1, qw2)):
                                b.wait(q, pe)
                                b.op(q, lambda h, e=e, f0=f0, half=half: h.dma_start(
                                    out=we[:, half, :, :],
                                    in_=wexp_bf[e].rearrange("(k p) f -> p k f", p=128)
                                        [:, :, (f0 + 2 * half) * 128:(f0 + 2 * half) * 128 + 256]))
                                b.inc(q)
                            b.wait(dve, act, cbm)
                            b.wait(dve, pe)
                            for k in range(NCH):
                                b.op(dve, lambda h, k=k, t0=t0, eb=eb: h.tensor_tensor(
                                    xe[:, eb, k, :], xln[:, k, t0:t0 + 512],
                                    cb_sb[:, eb, :], ALU.mult), dep=(k == 0))
                            xe_mark = b.inc(dve)
                            b.wait(pe, dve, xe_mark)
                            for f in range(4):
                                if f == 0:
                                    b.wait(pe, qw)
                                if f == 2:
                                    b.wait(pe, qw2)
                                for k in range(NCH):
                                    b.op(pe, lambda h, f=f, k=k, eb=eb, e=e: h.matmul(
                                        pb[:, f, :], we[:, f // 2, k, (f % 2) * 128:(f % 2) * 128 + 128],
                                        xe[:, eb, k, :],
                                        start=(e == 0 and k == 0), stop=False))
                        for f in range(4):
                            b.op(pe, lambda h, f=f, f0=f0, t0=t0: h.matmul(
                                pb[:, f, :], be_sb[:, (f0 + f) * 128:(f0 + f + 1) * 128],
                                cfm[:, t0:t0 + 512], start=False, stop=True))
                        m = b.inc(pe)
                        b.wait(dve, pe, m)
                        b.wait(dve, qss, store_mark)
                        for f in range(4):
                            b.op(dve, lambda h, f=f, f0=f0: h.scalar_tensor_tensor(
                                ob[:, f0 + f, :], pb[:, f, :], 0.5, xa[:, f0 + f, :],
                                ALU.mult, ALU.add), dep=(f == 0))
                        b.inc(dve)
                    b.wait(qss, dve)
                    b.op(qss, lambda h, t0=t0: h.dma_start(
                        out=xout_f(t0, 512), in_=ob[:, :, :]))
                    store_mark = b.inc(qss)
                b.op(pool, lambda h, v=b.qss.cnt: h.wait_ge(b.qss.sem, v))
                b.inc(pool)
                b.flush()


def ffn1_weight_map(I):
    bq = I["L1_ffn_b_qkv"]
    return dict(
        wqkv=_c(I["L1_ffn_W_qkv"]),
        bqkv_row=_c(bq.reshape(1, 3 * DM)),
        bqkv_col=_c(bq.reshape(24, 128).T),
        wgate=_c(I["L1_ffn_W_gate"]),
        bgate=_c(I["L1_ffn_b_gate"].reshape(8, 1)),
        wexp=_c(I["L1_ffn_W_experts"]),
        bexp=_c(I["L1_ffn_b_experts"]))


def run_ffn1(x_fm, I):
    nc = build_ffn1()
    wmap = ffn1_weight_map(I)
    in_maps = []
    for j in range(NCORES):
        xs = _c(x_fm[:, 2 * j:2 * j + 2, :].reshape(DM, TOK))
        in_maps.append(dict(x_in=xs, **wmap))
    res = run_bass_kernel_spmd(nc, in_maps, core_ids=list(range(NCORES)))
    out = np.empty((DM, 16, 2048), np.float32)
    for j in range(NCORES):
        out[:, 2 * j:2 * j + 2, :] = res.results[j]["x_out"].reshape(DM, 2, SEQ)
    return out


INTER = 4096
PADC = 1024
SCOL = PADC + SEQ


def emit_attn1(nc, b, W, xin_f, xout_f, first=True):
    A_in, BT, CT = W["A"], W["BT"], W["CT"]
    Dcol, ldt_col, ln1_col, ln2_col = W["Dcol"], W["ldt_col"], W["ln1_col"], W["ln2_col"]
    wgT, wuT, wdT = W["wgT"], W["wuT"], W["wdT"]
    wg_bf, wu_bf, wd_bf = W["wg_bf"], W["wu_bf"], W["wd_bf"]
    with ExitStack() as stack:
        xln = stack.enter_context(_sb(nc, "xln", [128, NCH, TOK], BF16))
        m1 = stack.enter_context(_sb(nc, "m1", [128, NCH, TOK], BF16))
        cols = stack.enter_context(_sb(nc, "cols", [128, 6, NCH], F32))
        i64bf = stack.enter_context(_sb(nc, "i64bf", [64, 64], BF16))
        ones_col = stack.enter_context(_sb(nc, "ones_c", [128, 1], F32))
        ones128 = stack.enter_context(_sb(nc, "ones128", [1, 128], F32))

        qs, qss, qw, qw2 = b.qs, b.qss, b.qw, b.qw2
        qg, qg2, pe, act, dve, pool = b.qg, b.qg2, b.pe, b.act, b.dve, b.pool

        for nm, dst in [("Dcol", 0), ("ldt_col", 1), ("ln1_col", 2), ("ln2_col", 3)]:
            src = {"Dcol": Dcol, "ldt_col": ldt_col, "ln1_col": ln1_col, "ln2_col": ln2_col}[nm]
            b.op(qs, lambda h, src=src, dst=dst: h.dma_start(out=cols[:, dst, :], in_=src))
            b.inc(qs)
        if first:
            for wsrc, wdst in [(wgT, wg_bf), (wuT, wu_bf), (wdT, wd_bf)]:
                b.op(qg, lambda h, a=wsrc, bb=wdst: h.dma_start(out=bb, in_=a))
                b.inc(qg)
        b.op(dve, lambda h: h.memset(ones_col[:, :], 1.0))
        b.op(dve, lambda h: h.memset(ones128[:, :], 1.0))
        b.op(pool, lambda h: h.memset(i64bf[:, :], 0.0))
        b.op(pool, lambda h: h.affine_select(
            out=i64bf[:, :], in_=i64bf[:, :], compare_op=ALU.not_equal,
            fill=1.0, base=0, pattern=[[-1, 64]], channel_multiplier=1))
        b.inc(dve)
        b.inc(pool)

        with ExitStack() as ph12:
            sbf = ph12.enter_context(_sb(nc, "sbuf_scan", [64, 2, 2, SCOL], BF16))
            bbt = ph12.enter_context(_sb(nc, "bbt", [128, NCH, 64], BF16))
            ct_sb = ph12.enter_context(_sb(nc, "ct_sb", [64, NCH, 128], BF16))
            qbf = ph12.enter_context(_sb(nc, "qbf", [64, 12, 64], BF16))
            stats = ph12.enter_context(_sb(nc, "stats", [1, 2, 512], F32))
            sq = ph12.enter_context(_sb(nc, "sqs", [128, 512], F32))
            b.op(qg, lambda h: h.dma_start(out=bbt[:, :, :], in_=BT.rearrange("(c p) n -> p c n", p=128)))
            b.inc(qg)
            b.op(qg, lambda h: h.dma_start(out=ct_sb[:, :, :], in_=CT.rearrange("n (c f) -> n c f", f=128)))
            b.inc(qg)
            b.op(dve, lambda h: h.memset(
                sbf.ap().rearrange("p a s c -> p (a s c)"), 0.0))
            b.inc(dve)
            ls = LnScratch(nc, ph12)
            ls.init(b, dve)
            xt0 = ph12.enter_context(_sb(nc, "xt0", [128, NCH, LN_TT], F32))
            for i in range(TOK // LN_TT):
                b.wait(qs, dve)
                b.op(qs, lambda h, i=i: h.dma_start(
                    out=xt0[:, :, :], in_=xin_f(i * LN_TT, LN_TT)))
                ld = b.inc(qs)
                b.wait(pe, qs, ld)
                b.wait(act, qs, ld)
                emit_ln_tile(b, ls, xt0[:, :, :], xln[:, :, i * LN_TT:(i + 1) * LN_TT])
            b.flush()

            pst, pbc = ls.pst, ls.pbc
            for i in range(TOK // 512):
                t0 = i * 512
                b.wait(pe, dve)
                b.wait(act, dve)
                for c in range(NCH):
                    b.op(act, lambda h, c=c, t0=t0: h.activation(
                        sq[:, :], xln[:, c, t0:t0 + 512], AF.Square))
                    a = b.inc(act)
                    b.wait(pe, act, a)
                    b.op(pe, lambda h, c=c: h.matmul(
                        pst[0:1, 0, 0:256].tensor.ap()[0:1, 0:512] if False else pst[0:1, 0:2, :].rearrange("p a t -> p (a t)")[0:1, 0:512],
                        ls.ones_col[:, 0:1], sq[:, :],
                        start=(c == 0), stop=(c == NCH - 1)))
                    m = b.inc(pe)
                    b.wait(act, pe, m)
                b.wait(dve, pe)
                b.op(dve, lambda h: h.tensor_scalar(
                    stats[:, 0, :], pst[0:1, 0:2, :].rearrange("p a t -> p (a t)")[0:1, 0:512],
                    1.0 / DM, 1e-6, ALU.mult, ALU.add))
                d = b.inc(dve)
                b.wait(act, dve, d)
                b.op(act, lambda h: h.activation(stats[:, 0, :], stats[:, 0, :], AF.Ln))
                b.op(act, lambda h: h.activation(stats[:, 0, :], stats[:, 0, :], AF.Exp, scale=-0.5))
                a = b.inc(act)
                b.wait(pe, act, a)
                b.op(pe, lambda h: h.matmul(
                    pbc[:, 0:2, :].rearrange("p a t -> p (a t)")[:, 0:512],
                    ls.ones_row[:, :], stats[:, 0, :], start=True, stop=True))
                m = b.inc(pe)
                b.wait(dve, pe, m)
                for c in range(NCH):
                    b.op(dve, lambda h, c=c, t0=t0: h.scalar_tensor_tensor(
                        m1[:, c, t0:t0 + 512], xln[:, c, t0:t0 + 512], 1.0,
                        pbc[:, 0:2, :].rearrange("p a t -> p (a t)")[:, 0:512],
                        ALU.mult, ALU.mult), dep=(c == 0))
                b.inc(dve)
            b.flush()

            with (
                _sb(nc, "A_sb", [64, 64], F32) as A_sb,
                _sb(nc, "E_sb", [64, 64], F32) as E_sb,
                _sb(nc, "pt_sb", [64, 2, 64], F32) as pt_sb,
                _sb(nc, "st_sb", [64, 64], F32) as st_sb,
                _sb(nc, "q_sb", [64, 2, 64], F32) as q_sb,
                _sb(nc, "i64f", [64, 64], F32) as i64f,
                _sb(nc, "hcol", [64, 1], F32) as hcol,
                _ps(nc, "pex", [128, 512], F32) as pex_t,
            ):
                pex = pex_t[:, :]
                b.op(qs, lambda h: h.dma_start(out=A_sb[:, :], in_=A_in))
                ald = b.inc(qs)
                b.op(pool, lambda h: h.memset(i64f[:, :], 0.0))
                b.op(pool, lambda h: h.affine_select(
                    out=i64f[:, :], in_=i64f[:, :], compare_op=ALU.not_equal,
                    fill=1.0, base=0, pattern=[[-1, 64]], channel_multiplier=1))
                p = b.inc(pool)
                b.wait(pe, pool, p)
                b.wait(dve, pool, p)
                b.wait(act, qs)
                b.op(act, lambda h: h.activation(cols[:, 1, :], cols[:, 1, :], AF.Exp))
                a = b.inc(act)
                b.wait(dve, act, a)
                b.op(dve, lambda h: h.tensor_tensor(cols[:, 4, :], cols[:, 1, :], cols[:, 2, :], ALU.mult))
                b.op(dve, lambda h: h.tensor_tensor(cols[:, 0, :], cols[:, 0, :], cols[:, 2, :], ALU.mult))
                d = b.inc(dve)
                b.wait(dve, qg)
                for c in range(NCH):
                    b.op(dve, lambda h, c=c: h.tensor_scalar_mul(
                        bbt[:, c, :], bbt[:, c, :], cols[:, 4, c:c + 1]))
                b.inc(dve)
                b.wait(pe, act, a)
                b.op(pe, lambda h: h.matmul(pex[0:1, 0:8], ones_col[:, 0:1], cols[:, 1, :],
                                            start=True, stop=True))
                m = b.inc(pe)
                b.wait(dve, pe, m)
                b.op(dve, lambda h: h.reduce_sum(stats[:, 0, 0:1], pex[0:1, 0:8], axis=AX.X))
                b.op(dve, lambda h: h.tensor_scalar_mul(stats[:, 0, 0:1], stats[:, 0, 0:1], 1.0 / DM))
                d = b.inc(dve)
                b.wait(pe, dve, d)
                b.op(pe, lambda h: h.matmul(pex[0:64, 8:9], ones128[0:1, 0:64], stats[:, 0, 0:1],
                                            start=True, stop=True))
                m = b.inc(pe)
                b.wait(dve, pe, m)
                b.op(dve, lambda h: h.tensor_copy(hcol[:, :], pex[0:64, 8:9]))
                b.wait(dve, qs, ald)
                b.op(dve, lambda h: h.tensor_scalar_mul(E_sb[:, :], A_sb[:, :], hcol[:, 0:1]))
                b.op(dve, lambda h: h.tensor_copy(pt_sb[:, 0, :], i64f[:, :]))
                b.op(dve, lambda h: h.tensor_copy(st_sb[:, :], i64f[:, :]))
                d = b.inc(dve)
                for k in range(1, 20):
                    cur, nxt = (k - 1) % 2, k % 2
                    b.wait(pe, dve)
                    b.wait(pe, act)
                    b.op(pe, lambda h, cur=cur: h.matmul(
                        pex[0:64, 64:128], E_sb[:, :], pt_sb[:, cur, :], start=True, stop=True))
                    m = b.inc(pe)
                    b.wait(act, pe, m)
                    b.op(act, lambda h, k=k, nxt=nxt: h.activation(
                        pt_sb[:, nxt, :], pex[0:64, 64:128], AF.Identity, scale=1.0 / k))
                    a = b.inc(act)
                    b.wait(dve, act, a)
                    b.op(dve, lambda h, nxt=nxt: h.tensor_tensor(
                        st_sb[:, :], st_sb[:, :], pt_sb[:, nxt, :], ALU.add))
                    b.inc(dve)
                b.op(dve, lambda h: h.tensor_copy(q_sb[:, 0, :], st_sb[:, :]))
                b.op(dve, lambda h: h.tensor_copy(qbf[:, 0, :], st_sb[:, :]))
                d = b.inc(dve)
                for j in range(1, 11):
                    cur, nxt = (j - 1) % 2, j % 2
                    b.wait(pe, dve)
                    b.op(pe, lambda h, cur=cur: h.transpose(
                        pex[0:64, 128:192], q_sb[:, cur, :], i64f[:, :]))
                    m = b.inc(pe)
                    b.wait(act, pe, m)
                    b.op(act, lambda h: h.activation(st_sb[:, :], pex[0:64, 128:192], AF.Identity))
                    a = b.inc(act)
                    b.wait(pe, act, a)
                    b.op(pe, lambda h, cur=cur: h.matmul(
                        pex[0:64, 64:128], st_sb[:, :], q_sb[:, cur, :], start=True, stop=True))
                    m = b.inc(pe)
                    b.wait(dve, pe, m)
                    b.op(dve, lambda h, nxt=nxt: h.tensor_copy(q_sb[:, nxt, :], pex[0:64, 64:128]))
                    b.op(dve, lambda h, j=j, nxt=nxt: h.tensor_copy(qbf[:, j, :], q_sb[:, nxt, :]))
                    d = b.inc(dve)
                b.wait(pe, dve, d)
                for s_ in range(NSEQ):
                    for i in range(SEQ // 512):
                        t0 = s_ * SEQ + i * 512
                        b.wait(pe, act)
                        for c in range(NCH):
                            b.op(pe, lambda h, c=c, t0=t0: h.matmul(
                                pex[0:64, 0:512], bbt[:, c, :], m1[:, c, t0:t0 + 512],
                                start=(c == 0), stop=(c == NCH - 1)))
                        m = b.inc(pe)
                        b.wait(act, pe, m)
                        b.op(act, lambda h, s_=s_, i=i: h.activation(
                            sbf[:, 0, s_, PADC + i * 512:PADC + (i + 1) * 512],
                            pex[0:64, 0:512], AF.Identity))
                        b.inc(act)
                b.wait(act, dve)
                for j in range(11):
                    src, dst = j % 2, (j + 1) % 2
                    shift = 1 << j
                    prev_mark = b.act.cnt
                    b.wait(pe, act, prev_mark)
                    for s_ in range(NSEQ):
                        for i in range(SEQ // 512):
                            c0 = PADC + i * 512
                            b.wait(pe, act)
                            b.op(pe, lambda h, j=j, src=src, s_=s_, c0=c0, shift=shift: h.matmul(
                                pex[0:64, 0:512], qbf[:, j, :],
                                sbf[:, src, s_, c0 - shift:c0 - shift + 512],
                                start=True, stop=False))
                            b.op(pe, lambda h, src=src, s_=s_, c0=c0: h.matmul(
                                pex[0:64, 0:512], i64bf[:, :],
                                sbf[:, src, s_, c0:c0 + 512],
                                start=False, stop=True))
                            m = b.inc(pe)
                            b.wait(act, pe, m)
                            b.op(act, lambda h, dst=dst, s_=s_, c0=c0: h.activation(
                                sbf[:, dst, s_, c0:c0 + 512], pex[0:64, 0:512], AF.Identity))
                            b.inc(act)
                fin = 11 % 2
                b.wait(pe, act)
                for s_ in range(NSEQ):
                    for i in range(SEQ // 512):
                        t0 = s_ * SEQ + i * 512
                        c0 = PADC + i * 512
                        for fc in range(NCH):
                            b.wait(pe, dve)
                            b.op(pe, lambda h, fc=fc, fin=fin, s_=s_, c0=c0: h.matmul(
                                pex[:, 0:512], ct_sb[:, fc, :],
                                sbf[:, fin, s_, c0:c0 + 512],
                                start=True, stop=True))
                            m = b.inc(pe)
                            b.wait(dve, pe, m)
                            b.op(dve, lambda h, fc=fc, t0=t0: h.scalar_tensor_tensor(
                                sq[:, :], m1[:, fc, t0:t0 + 512], cols[:, 0, fc:fc + 1],
                                xln[:, fc, t0:t0 + 512], ALU.mult, ALU.add))
                            b.op(dve, lambda h, fc=fc, t0=t0: h.tensor_tensor(
                                m1[:, fc, t0:t0 + 512], sq[:, :], pex[:, 0:512], ALU.add))
                            b.inc(dve)
                b.flush()

            for i in range(TOK // 512):
                t0 = i * 512
                b.wait(pe, dve)
                b.wait(act, dve)
                for c in range(NCH):
                    b.op(act, lambda h, c=c, t0=t0: h.activation(
                        sq[:, :], m1[:, c, t0:t0 + 512], AF.Square))
                    a = b.inc(act)
                    b.wait(pe, act, a)
                    b.op(pe, lambda h, c=c: h.matmul(
                        pst[0:1, 0:2, :].rearrange("p a t -> p (a t)")[0:1, 0:512],
                        ls.ones_col[:, 0:1], sq[:, :],
                        start=(c == 0), stop=(c == NCH - 1)))
                    m = b.inc(pe)
                    b.wait(act, pe, m)
                b.wait(dve, pe)
                b.op(dve, lambda h: h.tensor_scalar(
                    stats[:, 0, :], pst[0:1, 0:2, :].rearrange("p a t -> p (a t)")[0:1, 0:512],
                    1.0 / DM, 1e-6, ALU.mult, ALU.add))
                d = b.inc(dve)
                b.wait(act, dve, d)
                b.op(act, lambda h: h.activation(stats[:, 0, :], stats[:, 0, :], AF.Ln))
                b.op(act, lambda h: h.activation(stats[:, 0, :], stats[:, 0, :], AF.Exp, scale=-0.5))
                a = b.inc(act)
                b.wait(pe, act, a)
                b.op(pe, lambda h: h.matmul(
                    pbc[:, 0:2, :].rearrange("p a t -> p (a t)")[:, 0:512],
                    ls.ones_row[:, :], stats[:, 0, :], start=True, stop=True))
                m = b.inc(pe)
                b.wait(dve, pe, m)
                for c in range(NCH):
                    b.op(dve, lambda h, c=c, t0=t0: h.scalar_tensor_tensor(
                        sq[:, :], m1[:, c, t0:t0 + 512], 1.0,
                        pbc[:, 0:2, :].rearrange("p a t -> p (a t)")[:, 0:512],
                        ALU.mult, ALU.mult), dep=True)
                    b.op(dve, lambda h, c=c, t0=t0: h.tensor_scalar_mul(
                        xln[:, c, t0:t0 + 512], sq[:, :], cols[:, 3, c:c + 1]))
                b.inc(dve)
            b.flush()

        with (
            _sb(nc, "sgu", [128, 32, 512], BF16) as sgu,
            _sb(nc, "wb3", [128, 2, NCH, 256], BF16) as wb3,
            _sb(nc, "wd3", [128, 2, 512], BF16) as wd3,
            _sb(nc, "xm", [128, NCH, 512], F32) as xm,
            _sb(nc, "ob", [128, NCH, 512], F32) as ob,
            _sb(nc, "sg_tmp", [128, 512], F32) as sg_tmp,
            _ps(nc, "pgu", [128, 2, 512], F32) as pgu_t,
            _ps(nc, "pdn", [128, 4, 512], F32) as pdn_t,
        ):
            pgu, pdn = pgu_t[:, :, :], pdn_t[:, :, :]
            store_mark = 0
            for tt in range(TOK // 512):
                t0 = tt * 512
                b.wait(qs, dve)
                b.op(qs, lambda h, t0=t0: h.dma_start(
                    out=xm[:, :, :], in_=xin_f(t0, 512)))
                xld = b.inc(qs)
                b.wait(dve, qs, xld)
                for c in range(NCH):
                    b.op(dve, lambda h, c=c, t0=t0: h.tensor_tensor(
                        xm[:, c, :], xm[:, c, :], m1[:, c, t0:t0 + 512], ALU.add), dep=(c == 0))
                xm_mark = b.inc(dve)
                for ii in range(32):
                    ib = ii % 2
                    b.wait(qw if ib == 0 else qw2, pe)
                    q_ = qw if ib == 0 else qw2
                    b.op(q_, lambda h, ii=ii, ib=ib: h.dma_start(
                        out=wb3[:, ib, :, 0:128],
                        in_=wg_bf.rearrange("(k p) f -> p k f", p=128)[:, :, ii * 128:(ii + 1) * 128]))
                    b.inc(q_)
                    b.op(q_, lambda h, ii=ii, ib=ib: h.dma_start(
                        out=wb3[:, ib, :, 128:256],
                        in_=wu_bf.rearrange("(k p) f -> p k f", p=128)[:, :, ii * 128:(ii + 1) * 128]))
                    wld = b.inc(q_)
                    b.wait(pe, q_, wld)
                    b.wait(pe, act)
                    b.wait(pe, dve)
                    for k in range(NCH):
                        b.op(pe, lambda h, k=k, ib=ib, t0=t0: h.matmul(
                            pgu[:, 0, :], wb3[:, ib, k, 0:128], xln[:, k, t0:t0 + 512],
                            start=(k == 0), stop=(k == NCH - 1)))
                    for k in range(NCH):
                        b.op(pe, lambda h, k=k, ib=ib, t0=t0: h.matmul(
                            pgu[:, 1, :], wb3[:, ib, k, 128:256], xln[:, k, t0:t0 + 512],
                            start=(k == 0), stop=(k == NCH - 1)))
                    m = b.inc(pe)
                    b.wait(act, pe, m)
                    b.wait(act, dve)
                    b.op(act, lambda h: h.activation(sg_tmp[:, :], pgu[:, 0, :], AF.Silu))
                    a = b.inc(act)
                    b.wait(dve, act, a)
                    b.op(dve, lambda h, ii=ii: h.tensor_tensor(
                        sgu[:, ii, :], sg_tmp[:, :], pgu[:, 1, :], ALU.mult))
                    b.inc(dve)
                sgu_mark = b.dve.cnt
                for fh in range(2):
                    f0 = fh * 4
                    for ii in range(32):
                        ib = ii % 2
                        q_ = qw if ib == 0 else qw2
                        b.wait(q_, pe)
                        b.op(q_, lambda h, ii=ii, ib=ib, f0=f0: h.dma_start(
                            out=wd3[:, ib, :],
                            in_=wd_bf.rearrange("(i p) f -> p i f", p=128)[:, ii, f0 * 128:f0 * 128 + 512]))
                        wld = b.inc(q_)
                        b.wait(pe, q_, wld)
                        if ii == 0:
                            b.wait(pe, dve)
                        for f in range(4):
                            b.op(pe, lambda h, f=f, ii=ii, ib=ib: h.matmul(
                                pdn[:, f, :], wd3[:, ib, f * 128:(f + 1) * 128], sgu[:, ii, :],
                                start=(ii == 0), stop=(ii == 31)))
                        b.inc(pe)
                    m = b.pe.cnt
                    b.wait(dve, pe, m)
                    b.wait(dve, qss, store_mark)
                    for f in range(4):
                        b.op(dve, lambda h, f=f, f0=f0: h.scalar_tensor_tensor(
                            ob[:, f0 + f, :], pdn[:, f, :], 1.0, xm[:, f0 + f, :],
                            ALU.mult, ALU.add), dep=(f == 0))
                    b.inc(dve)
                b.wait(qss, dve)
                b.op(qss, lambda h, t0=t0: h.dma_start(
                    out=xout_f(t0, 512), in_=ob[:, :, :]))
                store_mark = b.inc(qss)
            b.op(pool, lambda h, v=b.qss.cnt: h.wait_ge(b.qss.sem, v))
            b.inc(pool)
            b.flush()


def attn1_weight_map(I):
    return dict(
        A=_c(I["L1_attn_A"]),
        BT=_c(I["L1_attn_B"].T),
        CT=_c(I["L1_attn_C"].T),
        Dcol=_c(I["L1_attn_D"].reshape(NCH, 128).T),
        ldt_col=_c(I["L1_attn_log_dt"].reshape(NCH, 128).T),
        ln1_col=_c(I["L1_attn_ln1_w"].reshape(NCH, 128).T),
        ln2_col=_c(I["L1_attn_ln2_w"].reshape(NCH, 128).T),
        wgT=_c(I["L1_attn_w_gate"].T),
        wuT=_c(I["L1_attn_w_up"].T),
        wdT=_c(I["L1_attn_w_down"].T))


def run_attn1(x_fm, I):
    nc = build_attn1()
    wmap = attn1_weight_map(I)
    in_maps = []
    for j in range(NCORES):
        xs = _c(x_fm[:, 2 * j:2 * j + 2, :].reshape(DM, TOK))
        in_maps.append(dict(x_in=xs, **wmap))
    res = run_bass_kernel_spmd(nc, in_maps, core_ids=list(range(NCORES)))
    out = np.empty((DM, 16, 2048), np.float32)
    for j in range(NCORES):
        out[:, 2 * j:2 * j + 2, :] = res.results[j]["x_out"].reshape(DM, 2, SEQ)
    return out


DIN = 2048
NH = 32
HDIM = 64
DST = 128
CK = 64
CONVD = 2304
NCONV = CONVD // 128


def emit_attn0(nc, b, W, xin_f, xout_f, first=True):
    DBG = False
    win, convd, convb = W["win"], W["convd"], W["convb"]
    dtb, alog, dcol, nwcol = W["dtb"], W["alog"], W["dcol"], W["nwcol"]
    wout, e32, wout_bf = W["wout"], W["e32"], W["wout_bf"]
    with ExitStack() as stack:
        e32_sb = stack.enter_context(_sb(nc, "e32_sb", [NH, DIN], BF16))
        tri = stack.enter_context(_sb(nc, "tri", [128, 64], F32))
        ones_col = stack.enter_context(_sb(nc, "ones_c", [128, 1], F32))
        ones_row = stack.enter_context(_sb(nc, "ones_r", [1, 128], F32))
        orow_bf = stack.enter_context(_sb(nc, "orow_bf", [1, 128], BF16))
        nrow_bf = stack.enter_context(_sb(nc, "nrow_bf", [1, 128], BF16))
        identF = stack.enter_context(_sb(nc, "identF", [128, 128], F32))
        identB = stack.enter_context(_sb(nc, "identB", [128, 128], BF16))
        cols = stack.enter_context(_sb(nc, "colsA", [128, 3, 18], F32))
        hcols = stack.enter_context(_sb(nc, "hcolsA", [NH, 2], F32))
        dt_bf = stack.enter_context(_sb(nc, "dt_bf", [NH, SEQ], BF16))
        cs_hi = stack.enter_context(_sb(nc, "cs_hi", [NH, SEQ], BF16))
        cs_lo = stack.enter_context(_sb(nc, "cs_lo", [NH, SEQ], BF16))
        dtdd = stack.enter_context(_sb(nc, "dtdd", [NH, SEQ], BF16))
        gam = stack.enter_context(_sb(nc, "gam", [NH, 32], BF16))
        grow = stack.enter_context(_sb(nc, "grow", [1, DIN], BF16))
        edrow = stack.enter_context(_sb(nc, "edrow", [1, DIN], F32))
        edrow_bf = stack.enter_context(_sb(nc, "edrow_bf", [1, DIN], BF16))
        rowbuf = stack.enter_context(_sb(nc, "rowbuf", [1, 2, 2, DIN], BF16))
        Ssb = stack.enter_context(_sb(nc, "Ssb", [128, 2, DIN], BF16))
        Stmp = stack.enter_context(_sb(nc, "Stmp", [128, DIN], BF16))
        Msb = stack.enter_context(_sb(nc, "Msb", [64, DIN], BF16))
        L2 = stack.enter_context(_sb(nc, "L2", [64, DIN], BF16))
        cbt = stack.enter_context(_sb(nc, "cbt", [64, 64], BF16))
        cdec = stack.enter_context(_sb(nc, "cdec", [128, DIN], BF16))
        xdt = stack.enter_context(_sb(nc, "xdt", [64, DIN], BF16))
        xdd = stack.enter_context(_sb(nc, "xdd", [64, DIN], BF16))
        xbc = stack.enter_context(_sb(nc, "xbc", [128, NCONV, 515], BF16))
        xbcs = stack.enter_context(_sb(nc, "xbcs", [128, 16, 512], BF16))
        bc_fm = stack.enter_context(_sb(nc, "bc_fm", [128, 2, 512], BF16))
        btok = stack.enter_context(_sb(nc, "btok", [64, 128], BF16))
        x_tok = stack.enter_context(_sb(nc, "x_tok", [64, DIN], BF16))
        zs = stack.enter_context(_sb(nc, "zs", [128, 2, 512], BF16))
        y_fm = stack.enter_context(_sb(nc, "y_fm", [128, 16, 512], BF16))
        sqa = stack.enter_context(_sb(nc, "sqa", [128, 512], F32))
        sqb = stack.enter_context(_sb(nc, "sqb", [128, 512], F32))
        st2 = stack.enter_context(_sb(nc, "st2", [1, 2, 512], F32))
        ob = stack.enter_context(_sb(nc, "obA", [128, 2, 512], F32))
        wpb = stack.enter_context(_sb(nc, "wpb", [128, 2, NCH, 512], BF16))
        cvd = stack.enter_context(_sb(nc, "cvd", [128, 2, 4, 128], BF16))
        xt0 = stack.enter_context(_sb(nc, "xt0A", [128, NCH, 512], F32))
        xlnt = stack.enter_context(_sb(nc, "xlnt", [128, NCH, 512], BF16))
        csf = stack.enter_context(_sb(nc, "csf", [NH, 64], F32))
        pA_t = stack.enter_context(_ps(nc, "pA", [128, 4, 512], F32))
        pS_t = stack.enter_context(_ps(nc, "pS", [128, 4, 512], F32))
        pA = pA_t[:, :, :]
        pS = pS_t[:, :, :]
        pAfull = pA_t.ap().rearrange("p a t -> p (a t)")
        pSfull = pS_t.ap().rearrange("p a t -> p (a t)")

        qs, qss, qw, qw2 = b.qs, b.qss, b.qw, b.qw2
        qg, qg2, pe, act, dve, pool = b.qg, b.qg2, b.pe, b.act, b.dve, b.pool
        wsrc = win.rearrange("(k p) f -> p k f", p=128)

        ls = LnScratch(nc, stack,
                       pS_t[0:1, 3, :].rearrange("p (a t) -> p a t", a=2),
                       pS_t[:, 2, :].rearrange("p (a t) -> p a t", a=2))

        b.op(qs, lambda h: h.dma_start(out=cols[:, 0, :], in_=convb))
        b.inc(qs)
        b.op(qs, lambda h: h.dma_start(out=cols[:, 1, 0:16], in_=dcol))
        b.inc(qs)
        b.op(qs, lambda h: h.dma_start(out=cols[:, 2, 0:16], in_=nwcol))
        b.inc(qs)
        b.op(qs, lambda h: h.dma_start(out=hcols[:, 0:1], in_=dtb))
        b.inc(qs)
        b.op(qs, lambda h: h.dma_start(out=hcols[:, 1:2], in_=alog))
        b.inc(qs)
        b.op(qg, lambda h: h.dma_start(out=e32_sb[:, :], in_=e32))
        b.inc(qg)
        if first:
            b.op(qg, lambda h: h.dma_start(out=wout_bf, in_=wout))
            b.inc(qg)
        b.op(dve, lambda h: h.memset(ones_col[:, :], 1.0))
        b.op(dve, lambda h: h.memset(ones_row[:, :], 1.0))
        b.op(dve, lambda h: h.memset(orow_bf[:, :], 1.0))
        b.op(dve, lambda h: h.memset(nrow_bf[:, :], -1.0))
        b.op(dve, lambda h: h.memset(Ssb[:, 0, :], 0.0))
        b.inc(dve)
        for idt in ("identF", "identB"):
            t_ = {"identF": identF, "identB": identB}[idt]
            b.op(pool, lambda h, t_=t_: h.memset(t_[:, :], 0.0))
            b.op(pool, lambda h, t_=t_: h.affine_select(
                out=t_[:, :], in_=t_[:, :], compare_op=ALU.not_equal,
                fill=1.0, base=0, pattern=[[-1, 128]], channel_multiplier=1))
        b.op(pool, lambda h: h.memset(tri[:, :], 0.0))
        b.op(pool, lambda h: h.affine_select(
            out=tri[0:64, :], in_=tri[0:64, :], compare_op=ALU.is_gt,
            fill=1.0, base=0, pattern=[[-1, 64]], channel_multiplier=1))
        b.op(pool, lambda h: h.affine_select(
            out=tri[64:128, :], in_=tri[64:128, :], compare_op=ALU.is_gt,
            fill=1.0, base=0, pattern=[[-1, 64]], channel_multiplier=1))
        pm = b.inc(pool)
        b.wait(pe, pool, pm)
        b.wait(dve, pool, pm)
        ls.init(b, dve)
        b.wait(act, qs)
        b.op(act, lambda h: h.activation(hcols[:, 1:2], hcols[:, 1:2], AF.Exp))
        a = b.inc(act)
        b.wait(dve, act, a)
        b.op(dve, lambda h: h.tensor_scalar_mul(hcols[:, 1:2], hcols[:, 1:2], -1.0))
        b.inc(dve)
        b.flush()

        wdt_piece = 8

        for s_ in range(NSEQ):
            b.wait(qg, pe)
            b.op(qg, lambda h: h.dma_start(
                out=wpb[:, 0, :, 0:NH], in_=wsrc[:, :, 4352:4384]))
            wld = b.inc(qg)
            for i in range(SEQ // 512):
                t0 = i * 512
                g0 = s_ * SEQ + t0
                b.wait(qs, dve)
                b.wait(qs, pe)
                b.op(qs, lambda h, g0=g0: h.dma_start(
                    out=xt0[:, :, :], in_=xin_f(g0, 512)))
                ld = b.inc(qs)
                b.wait(pe, qs, ld)
                b.wait(act, qs, ld)
                emit_ln_tile(b, ls, xt0[:, :, 0:256], xlnt[:, :, 0:256])
                emit_ln_tile(b, ls, xt0[:, :, 256:512], xlnt[:, :, 256:512])
                b.wait(pe, qg, wld)
                b.wait(pe, dve)
                for k in range(NCH):
                    b.op(pe, lambda h, k=k: h.matmul(
                        pA[0:NH, 0, :], wpb[:, 0, k, 0:NH], xlnt[:, k, :],
                        start=(k == 0), stop=(k == NCH - 1)))
                m = b.inc(pe)
                b.wait(act, pe, m)
                b.op(act, lambda h: h.activation(
                    sqa[0:NH, :], pA[0:NH, 0, :], AF.Identity, bias=hcols[:, 0:1]))
                a = b.inc(act)
                b.op(act, lambda h: h.activation(sqb[0:NH, :], sqa[0:NH, :], AF.Abs))
                a = b.inc(act)
                b.op(act, lambda h: h.activation(sqb[0:NH, :], sqb[0:NH, :], AF.Exp, scale=-1.0))
                a = b.inc(act)
                b.wait(dve, act, a)
                b.op(dve, lambda h: h.tensor_scalar_add(sqb[0:NH, :], sqb[0:NH, :], 1.0))
                d = b.inc(dve)
                b.wait(act, dve, d)
                b.op(act, lambda h: h.activation(sqb[0:NH, :], sqb[0:NH, :], AF.Ln))
                a = b.inc(act)
                b.wait(dve, act, a)
                b.op(dve, lambda h: h.tensor_single_scalar(
                    sqa[0:NH, :], sqa[0:NH, :], 0.0, ALU.max))
                b.op(dve, lambda h: h.tensor_tensor(
                    sqa[0:NH, :], sqa[0:NH, :], sqb[0:NH, :], ALU.add))
                b.op(dve, lambda h, t0=t0: h.tensor_copy(dt_bf[:, t0:t0 + 512], sqa[0:NH, :]))
                b.op(dve, lambda h: h.tensor_scalar_mul(sqb[0:NH, :], sqa[0:NH, :], hcols[:, 1:2]))
                d = b.inc(dve)
                for sub in range(4):
                    l0 = sub * 128
                    b.wait(pe, dve, d)
                    b.op(pe, lambda h, l0=l0: h.transpose(
                        pA[0:128, 2, 0:NH], sqb[0:NH, l0:l0 + 128], identF[0:NH, 0:NH]))
                    m = b.inc(pe)
                    b.wait(act, pe, m)
                    b.op(act, lambda h: h.activation(sqa[:, 0:NH], pA[0:NH, 2, 0:128], AF.Identity, apply_transpose=True) if False else h.activation(sqb[:, 448:480] if False else sqa[:, 0:32], pA[:, 2, 0:NH] if False else pA[0:128, 2, 0:NH], AF.Identity))
                    a = b.inc(act)
                    b.wait(pe, act, a)
                    b.op(pe, lambda h: h.matmul(
                        pA[0:64, 3, 0:NH], tri[0:64, :], sqa[0:64, 0:NH],
                        start=True, stop=True))
                    b.op(pe, lambda h: h.matmul(
                        pA[64:128, 3, 0:NH], tri[64:128, :], sqa[64:128, 0:NH],
                        start=True, stop=True))
                    m = b.inc(pe)
                    b.wait(act, pe, m)
                    b.op(act, lambda h: h.activation(sqb[:, 0:NH], pA[:, 3, 0:NH], AF.Identity))
                    a = b.inc(act)
                    b.wait(pe, act, a)
                    b.op(pe, lambda h: h.transpose(
                        pA[0:NH, 2, 128:256], sqb[:, 0:NH], identF[:, :]))
                    m = b.inc(pe)
                    b.wait(dve, pe, m)
                    b.op(dve, lambda h, t0=t0, l0=l0: h.tensor_copy(
                        cs_hi[:, t0 + l0:t0 + l0 + 128], pA[0:NH, 2, 128:256]))
                    b.op(dve, lambda h, t0=t0, l0=l0: h.tensor_tensor(
                        cs_lo[:, t0 + l0:t0 + l0 + 128], pA[0:NH, 2, 128:256],
                        cs_hi[:, t0 + l0:t0 + l0 + 128], ALU.subtract))
                    d = b.inc(dve)
            b.op(dve, lambda h: h.tensor_tensor(
                sqa[0:NH, 0:32], bass.AP(cs_hi, 63, [[SEQ, NH], [64, 32]]),
                bass.AP(cs_lo, 63, [[SEQ, NH], [64, 32]]), ALU.add))
            d = b.inc(dve)
            b.wait(act, dve, d)
            b.op(act, lambda h: h.activation(gam[:, :], sqa[0:NH, 0:32], AF.Exp))
            ga = b.inc(act)
            for c in range(32):
                cl0 = c * 64
                b.op(dve, lambda h, cl0=cl0: h.tensor_tensor(
                    csf[:, :], cs_hi[:, cl0:cl0 + 64], cs_lo[:, cl0:cl0 + 64], ALU.add))
                d = b.inc(dve)
                b.wait(act, dve, d)
                b.op(act, lambda h: h.activation(
                    csf[:, :], csf[:, :], AF.Exp, scale=-1.0, bias=csf[:, 63:64]))
                a = b.inc(act)
                b.wait(dve, act, a)
                b.op(dve, lambda h, cl0=cl0: h.tensor_tensor(
                    dtdd[:, cl0:cl0 + 64], dt_bf[:, cl0:cl0 + 64], csf[:, :], ALU.mult))
                b.inc(dve)
            if DBG and s_ == 0:
                b.wait(qg, dve)
                b.op(qg, lambda h: h.dma_start(out=dbg_dt, in_=dt_bf[:, :]))
                b.inc(qg)
                b.op(qg, lambda h: h.dma_start(out=dbg_cs[0], in_=cs_hi[:, :]))
                b.inc(qg)
                b.op(qg, lambda h: h.dma_start(out=dbg_cs[1], in_=cs_lo[:, :]))
                b.inc(qg)
            b.flush()

            for i in range(SEQ // 512):
                t0g = s_ * SEQ + i * 512
                b.wait(qs, dve)
                b.wait(qs, pe)
                b.op(qs, lambda h, t0g=t0g: h.dma_start(
                    out=xt0[:, :, :], in_=xin_f(t0g, 512)))
                ld = b.inc(qs)
                b.wait(pe, qs, ld)
                b.wait(act, qs, ld)
                emit_ln_tile(b, ls, xt0[:, :, 0:256], xlnt[:, :, 0:256])
                emit_ln_tile(b, ls, xt0[:, :, 256:512], xlnt[:, :, 256:512])
                b.wait(pe, dve)
                for fo in range(4, 9):
                    fp = fo % 2
                    q_ = qg if fp == 0 else qg2
                    width = 512 if fo < 8 else 256
                    b.wait(q_, pe)
                    b.op(q_, lambda h, fo=fo, fp=fp, width=width: h.dma_start(
                        out=wpb[:, fp, :, 0:width],
                        in_=wsrc[:, :, fo * 512:fo * 512 + width]))
                    wld = b.inc(q_)
                    b.wait(pe, q_, wld)
                    nfc = width // 128
                    for fc in range(nfc):
                        pb_ = (fo * 4 + fc) % 2
                        b.wait(pe, act)
                        for k in range(NCH):
                            b.op(pe, lambda h, k=k, fc=fc, fp=fp, pb_=pb_: h.matmul(
                                pA[:, pb_, :], wpb[:, fp, k, fc * 128:(fc + 1) * 128],
                                xlnt[:, k, :], start=(k == 0), stop=(k == NCH - 1)))
                        m = b.inc(pe)
                        b.wait(act, pe, m)
                        cglob = (fo - 4) * 4 + fc
                        b.op(act, lambda h, cglob=cglob, pb_=pb_: h.activation(
                            xbc[:, cglob, 3:515], pA[:, pb_, :], AF.Identity))
                        b.inc(act)
                if i == 0:
                    b.wait(dve, act)
                    b.op(dve, lambda h: h.memset(xbc[:, :, 0:3], 0.0))
                    b.inc(dve)
                b.wait(act, dve)
                b.wait(pe, dve)
                for c in range(NCONV):
                    cp_ = c % 2
                    q_ = qg if cp_ == 0 else qg2
                    b.wait(q_, pe)
                    b.op(q_, lambda h, c=c, cp_=cp_: h.dma_start(
                        out=cvd[:, cp_, :, :],
                        in_=convd[:, c].rearrange("j pp q -> pp j q")))
                    wld = b.inc(q_)
                    b.wait(pe, q_, wld)
                    b.wait(pe, act)
                    for j in range(4):
                        b.op(pe, lambda h, c=c, j=j, cp_=cp_: h.matmul(
                            pA[:, 2 + (c % 2) * 0, :] if False else pA[:, 2, :],
                            cvd[:, cp_, j, :], xbc[:, c, j:j + 512],
                            start=(j == 0), stop=(j == 3)))
                    m = b.inc(pe)
                    b.wait(act, pe, m)
                    if c < 16:
                        b.op(act, lambda h, c=c: h.activation(
                            xbcs[:, c, :], pA[:, 2, :], AF.Silu, bias=cols[:, 0, c:c + 1]))
                    else:
                        b.op(act, lambda h, c=c: h.activation(
                            bc_fm[:, c - 16, :], pA[:, 2, :], AF.Silu, bias=cols[:, 0, c:c + 1]))
                    b.inc(act)
                b.wait(dve, act)
                b.op(dve, lambda h: h.tensor_copy(
                    xbc[:, :, 0:3], xbc[:, :, 512:515]))
                b.inc(dve)
                if DBG and s_ == 0 and i == 0:
                    b.wait(qg, act)
                    b.op(qg, lambda h: h.dma_start(out=dbg_xbcs, in_=xbcs[:, :, :]))
                    b.inc(qg)
                    b.op(qg, lambda h: h.dma_start(out=dbg_bc, in_=bc_fm[:, :, :]))
                    b.inc(qg)
                for cp in range(4):
                    l0 = cp * 128
                    for j in range(2):
                        cseq = (i * 8 + cp * 2 + j)
                        q_ = qw if j == 0 else qw2
                        b.wait(q_, pe)
                        b.op(q_, lambda h, j=j, cseq=cseq: h.dma_start(
                            out=rowbuf[0:1, j, 0, :],
                            in_=cs_hi[:, cseq * 64:(cseq + 1) * 64]))
                        b.inc(q_)
                        b.op(q_, lambda h, j=j, cseq=cseq: h.dma_start(
                            out=rowbuf[0:1, j, 1, :],
                            in_=cs_lo[:, cseq * 64:(cseq + 1) * 64]))
                        rw = b.inc(q_)
                        b.wait(pe, q_, rw)
                    for j in range(2):
                        cseq = i * 8 + cp * 2 + j
                        jl0 = l0 + j * 64
                        base = 0
                        b.wait(pe, dve)
                        b.wait(pe, act)
                        for xc in range(16):
                            b.op(pe, lambda h, xc=xc, jl0=jl0: h.transpose(
                                pA_t.ap().bitcast(BF16)[0:64, 3, 0:128],
                                xbcs[:, xc, jl0:jl0 + 64], identB[:, :]))
                            m = b.inc(pe)
                            b.wait(act, pe, m)
                            b.op(act, lambda h, xc=xc: h.activation(
                                x_tok[:, xc * 128:(xc + 1) * 128],
                                pA_t.ap().bitcast(BF16)[0:64, 3, 0:128], AF.Identity))
                            a = b.inc(act)
                            b.wait(pe, act, a)
                        b.op(pe, lambda h, jl0=jl0: h.transpose(
                            pA_t.ap().bitcast(BF16)[0:64, 3, 0:128],
                            bc_fm[:, 0, jl0:jl0 + 64], identB[:, :]))
                        m = b.inc(pe)
                        b.wait(act, pe, m)
                        b.op(act, lambda h: h.activation(
                            btok[:, :], pA_t.ap().bitcast(BF16)[0:64, 3, 0:128], AF.Identity))
                        a = b.inc(act)
                        b.wait(pe, act, a)
                        b.wait(pe, dve)
                        b.op(pe, lambda h, jl0=jl0: h.matmul(
                            pS[0:64, 0, 0:64],
                            bc_fm[:, 0, jl0:jl0 + 64], bc_fm[:, 1, jl0:jl0 + 64],
                            start=True, stop=True))
                        m = b.inc(pe)
                        b.wait(dve, pe, m)
                        b.op(dve, lambda h: h.tensor_tensor(
                            cbt[:, :], pS[0:64, 0, 0:64], tri[0:64, :], ALU.mult))
                        dcb = b.inc(dve)
                        b.wait(pe, dve, dcb)
                        for q4 in range(4):
                            sl = slice(q4 * 512, (q4 + 1) * 512)
                            b.op(pe, lambda h, cseq=cseq, q4=q4, sl=sl: h.matmul(
                                pAfull[0:64, sl], cs_hi[:, cseq * 64:(cseq + 1) * 64],
                                e32_sb[:, sl], start=True, stop=False))
                            b.op(pe, lambda h, cseq=cseq, sl=sl: h.matmul(
                                pAfull[0:64, sl], cs_lo[:, cseq * 64:(cseq + 1) * 64],
                                e32_sb[:, sl], start=False, stop=False))
                            b.op(pe, lambda h, j=j, sl=sl: h.matmul(
                                pAfull[0:64, sl], nrow_bf[0:1, 0:64],
                                rowbuf[0:1, j, 0, sl], start=False, stop=False))
                            b.op(pe, lambda h, j=j, sl=sl: h.matmul(
                                pAfull[0:64, sl], nrow_bf[0:1, 0:64],
                                rowbuf[0:1, j, 1, sl], start=False, stop=True))
                        m = b.inc(pe)
                        b.wait(dve, pe, m)
                        b.op(dve, lambda h: h.tensor_single_scalar(
                            pAfull[0:64, :], pAfull[0:64, :], -30.0, ALU.max))
                        d = b.inc(dve)
                        b.wait(act, dve, d)
                        b.op(act, lambda h: h.activation(L2[:, :], pAfull[0:64, :], AF.Exp, scale=-1.0))
                        a = b.inc(act)
                        b.wait(dve, act, a)
                        b.op(dve, lambda h: h.tensor_tensor(
                            Msb[:, :], L2[:, :], bass.AP(cbt, 0, [[64, 64], [0, NH], [1, 64]]), ALU.mult))
                        dm = b.inc(dve)
                        b.wait(pe, dve, dm)
                        for q4 in range(4):
                            sl = slice(q4 * 512, (q4 + 1) * 512)
                            b.op(pe, lambda h, cseq=cseq, sl=sl: h.matmul(
                                pAfull[0:64, sl], dt_bf[:, cseq * 64:(cseq + 1) * 64],
                                e32_sb[:, sl], start=True, stop=True))
                        m = b.inc(pe)
                        b.wait(dve, pe, m)
                        b.op(dve, lambda h: h.scalar_tensor_tensor(
                            xdt[:, :], pAfull[0:64, :], 1.0,
                            x_tok[:, :], ALU.mult, ALU.mult))
                        dxdt = b.inc(dve)
                        b.wait(pe, dve, dxdt)
                        for q4 in range(4):
                            sl = slice(q4 * 512, (q4 + 1) * 512)
                            b.op(pe, lambda h, cseq=cseq, sl=sl: h.matmul(
                                pAfull[0:64, sl], dtdd[:, cseq * 64:(cseq + 1) * 64],
                                e32_sb[:, sl], start=True, stop=True))
                        m = b.inc(pe)
                        b.wait(dve, pe, m)
                        b.op(dve, lambda h: h.scalar_tensor_tensor(
                            xdd[:, :], pAfull[0:64, :], 1.0,
                            x_tok[:, :], ALU.mult, ALU.mult))
                        dxdd = b.inc(dve)
                        b.op(dve, lambda h, j=j: h.tensor_tensor(
                            edrow[0:1, :], rowbuf[0:1, j, 0, :], rowbuf[0:1, j, 1, :], ALU.add))
                        d = b.inc(dve)
                        b.wait(act, dve, d)
                        b.op(act, lambda h: h.activation(edrow_bf[0:1, :], edrow[0:1, :], AF.Exp))
                        a = b.inc(act)
                        b.wait(pe, act, a)
                        b.wait(pe, dve, dxdd)
                        for q4 in range(4):
                            sl = slice(q4 * 512, (q4 + 1) * 512)
                            b.op(pe, lambda h, sl=sl: h.matmul(
                                pAfull[:, sl], orow_bf[:, :], edrow_bf[0:1, sl],
                                start=True, stop=True))
                        m = b.inc(pe)
                        b.wait(dve, pe, m)
                        b.op(dve, lambda h, jl0=jl0: h.tensor_tensor(
                            cdec[:, :], pAfull[:, :],
                            bass.AP(bc_fm, 512 + jl0, [[1024, 128], [0, NH], [1, 64]]),
                            ALU.mult))
                        dcd = b.inc(dve)
                        cur = (s_ * 32 + cseq) % 2
                        nxt = 1 - cur
                        b.wait(pe, act, ga)
                        b.wait(pe, dve)
                        for q4 in range(4):
                            sl = slice(q4 * 512, (q4 + 1) * 512)
                            b.op(pe, lambda h, cseq=cseq, sl=sl: h.matmul(
                                pSfull[0:1, sl], gam[:, cseq:cseq + 1], e32_sb[:, sl],
                                start=True, stop=True))
                        m = b.inc(pe)
                        b.wait(dve, pe, m)
                        b.op(dve, lambda h: h.tensor_copy(grow[0:1, :], pSfull[0:1, 0:2048]))
                        d = b.inc(dve)
                        b.wait(pe, dve, d)
                        for q4 in range(4):
                            sl = slice(q4 * 512, (q4 + 1) * 512)
                            b.op(pe, lambda h, sl=sl: h.matmul(
                                pSfull[:, sl], orow_bf[:, :], grow[0:1, sl],
                                start=True, stop=True))
                        m = b.inc(pe)
                        b.wait(dve, pe, m)
                        b.op(dve, lambda h, cur=cur: h.tensor_tensor(
                            Stmp[:, :], pSfull[:, :], Ssb[:, cur, :], ALU.mult))
                        dstmp = b.inc(dve)
                        b.wait(pe, dve, dstmp)
                        for hh in range(NH):
                            b.op(pe, lambda h, hh=hh: h.matmul(
                                pSfull[:, hh * 64:(hh + 1) * 64],
                                btok[:, :],
                                xdd[:, hh * 64:(hh + 1) * 64],
                                start=True, stop=True))
                        m = b.inc(pe)
                        b.wait(dve, pe, m)
                        b.op(dve, lambda h, nxt=nxt: h.tensor_tensor(
                            Ssb[:, nxt, :], Stmp[:, :], pSfull[:, :], ALU.add))
                        dsn = b.inc(dve)
                        b.wait(pe, dve, dsn)
                        for hh in range(NH):
                            rb = (hh % 2) * 64
                            colr = (hh // 2) * 128 + j * 64
                            b.op(pe, lambda h, hh=hh, rb=rb, colr=colr: h.matmul(
                                pAfull[rb:rb + 64, colr:colr + 64],
                                xdt[:, hh * 64:(hh + 1) * 64],
                                Msb[:, hh * 64:(hh + 1) * 64],
                                start=True, stop=False))
                            b.op(pe, lambda h, hh=hh, cur=cur, rb=rb, colr=colr: h.matmul(
                                pAfull[rb:rb + 64, colr:colr + 64],
                                Ssb[:, cur, hh * 64:(hh + 1) * 64],
                                cdec[:, hh * 64:(hh + 1) * 64],
                                start=False, stop=True))
                        m = b.inc(pe)
                        b.wait(act, pe, m)
                        b.op(act, lambda h, l0=l0, j=j: h.activation(
                            y_fm[:, :, l0 + j * 64:l0 + j * 64 + 64],
                            pAfull[:, :].rearrange("p (c l) -> p c l", l=128)[:, :, j * 64:j * 64 + 64],
                            AF.Identity))
                        ya = b.inc(act)
                        b.wait(pe, act, ya)
                if DBG and s_ == 0 and i == 0:
                    b.wait(qg, act)
                    b.op(qg, lambda h: h.dma_start(out=dbg_y, in_=y_fm[:, :, :]))
                    b.inc(qg)
                for fo in range(4):
                    fp = fo % 2
                    q_ = qg if fp == 0 else qg2
                    b.wait(q_, pe)
                    b.op(q_, lambda h, fo=fo, fp=fp: h.dma_start(
                        out=wpb[:, fp, :, :], in_=wsrc[:, :, fo * 512:(fo + 1) * 512]))
                    wld = b.inc(q_)
                    b.wait(pe, q_, wld)
                    for fc in range(4):
                        zc = fo * 4 + fc
                        pb_ = zc % 2
                        b.wait(pe, act)
                        b.wait(pe, dve)
                        for k in range(NCH):
                            b.op(pe, lambda h, k=k, fc=fc, fp=fp, pb_=pb_: h.matmul(
                                pA[:, pb_, :], wpb[:, fp, k, fc * 128:(fc + 1) * 128],
                                xlnt[:, k, :], start=(k == 0), stop=(k == NCH - 1)))
                        m = b.inc(pe)
                        b.wait(act, pe, m)
                        b.wait(act, dve)
                        b.op(act, lambda h, pb_=pb_, zc=zc: h.activation(
                            zs[:, pb_, :], pA[:, pb_, :], AF.Silu))
                        a = b.inc(act)
                        b.wait(dve, act, a)
                        b.op(dve, lambda h, zc=zc: h.scalar_tensor_tensor(
                            sqa[:, :], xbcs[:, zc, :], cols[:, 1, zc:zc + 1],
                            y_fm[:, zc, :], ALU.mult, ALU.add))
                        b.op(dve, lambda h, zc=zc, pb_=pb_: h.tensor_tensor(
                            y_fm[:, zc, :], sqa[:, :], zs[:, pb_, :], ALU.mult))
                        b.inc(dve)
                b.wait(pe, dve)
                b.wait(act, dve)
                for zc in range(16):
                    b.op(act, lambda h, zc=zc: h.activation(
                        sqa[:, :], y_fm[:, zc, :], AF.Square))
                    a = b.inc(act)
                    b.wait(pe, act, a)
                    b.op(pe, lambda h, zc=zc: h.matmul(
                        pS[0:1, 0, :], ones_col[:, 0:1], sqa[:, :],
                        start=(zc == 0), stop=(zc == 15)))
                    m = b.inc(pe)
                    b.wait(act, pe, m)
                b.wait(dve, pe)
                b.op(dve, lambda h: h.tensor_scalar(
                    st2[:, 0, :], pS[0:1, 0, :], 1.0 / DIN, 1e-5, ALU.mult, ALU.add))
                d = b.inc(dve)
                b.wait(act, dve, d)
                b.op(act, lambda h: h.activation(st2[:, 0, :], st2[:, 0, :], AF.Ln))
                b.op(act, lambda h: h.activation(st2[:, 0, :], st2[:, 0, :], AF.Exp, scale=-0.5))
                a = b.inc(act)
                b.wait(pe, act, a)
                b.op(pe, lambda h: h.matmul(pS[:, 1, :], ones_row[:, :], st2[:, 0, :],
                                            start=True, stop=True))
                m = b.inc(pe)
                b.wait(dve, pe, m)
                for zc in range(16):
                    b.op(dve, lambda h, zc=zc: h.scalar_tensor_tensor(
                        sqa[:, :], y_fm[:, zc, :], 1.0, pS[:, 1, :], ALU.mult, ALU.mult), dep=True)
                    b.op(dve, lambda h, zc=zc: h.tensor_scalar_mul(
                        y_fm[:, zc, :], sqa[:, :], cols[:, 2, zc:zc + 1]))
                b.inc(dve)
                if DBG and s_ == 0 and i == 0:
                    b.wait(qg, dve)
                    b.op(qg, lambda h: h.dma_start(out=dbg_yg, in_=y_fm[:, :, :]))
                    b.inc(qg)
                b.wait(dve, qs)
                store_prev = b.qss.cnt
                for fc in range(NCH):
                    fp = fc % 2
                    q_ = qw if fp == 0 else qw2
                    b.wait(q_, pe)
                    b.op(q_, lambda h, fc=fc, fp=fp: h.dma_start(
                        out=wpb[:, fp, :, :].rearrange("p k f -> p (k f)")[:, 0:2048]
                            .rearrange("p (k f) -> p k f", f=128),
                        in_=wout_bf.rearrange("(k p) f -> p k f", p=128)
                            [:, :, fc * 128:(fc + 1) * 128]))
                    wld = b.inc(q_)
                    b.wait(pe, q_, wld)
                    b.wait(pe, dve)
                    pb_ = fc % 2
                    for k in range(16):
                        b.op(pe, lambda h, k=k, fc=fc, fp=fp, pb_=pb_: h.matmul(
                            pA[:, pb_, :],
                            wpb[:, fp, :, :].rearrange("p k f -> p (k f)")
                                [:, 0:2048].rearrange("p (k f) -> p k f", f=128)[:, k, :],
                            y_fm[:, k, :], start=(k == 0), stop=(k == 15)))
                    m = b.inc(pe)
                    b.wait(dve, pe, m)
                    b.wait(dve, qss)
                    b.op(dve, lambda h, fc=fc, pb_=pb_: h.scalar_tensor_tensor(
                        ob[:, fc % 2, :], pA[:, pb_, :], 1.0, xt0[:, fc, :],
                        ALU.mult, ALU.add), dep=(fc < 2))
                    b.inc(dve)
                    if fc % 2 == 1:
                        b.wait(qss, dve)
                        b.op(qss, lambda h, t0g=t0g, fc=fc: h.dma_start(
                            out=xout_f(t0g, 512)[:, fc - 1:fc + 1, :], in_=ob[:, :, :]))
                        b.inc(qss)
            b.wait(dve, pe)
            b.op(dve, lambda h, nxt=(s_ * 32 + 32) % 2: h.memset(Ssb[:, nxt, :], 0.0))
            b.inc(dve)
        b.op(pool, lambda h, v=b.qss.cnt: h.wait_ge(b.qss.sem, v))
        b.inc(pool)
        b.flush()


def attn0_weight_map(I):
    cw = I["L0_attn_conv_w"][:, 0, :]
    diag = np.zeros((4, NCONV, 128, 128), np.float32)
    for j in range(4):
        for c in range(NCONV):
            np.fill_diagonal(diag[j, c], cw[c * 128:(c + 1) * 128, j])
    e = np.kron(np.eye(NH, dtype=np.float32), np.ones((1, HDIM), np.float32))
    return dict(
        win=_c(I["L0_attn_in_proj_w"]),
        convd=diag,
        convb=_c(I["L0_attn_conv_b"].reshape(NCONV, 128).T),
        dtb=_c(I["L0_attn_dt_bias"].reshape(NH, 1)),
        alog=_c(I["L0_attn_A_log"].reshape(NH, 1)),
        dcol=_c(I["L0_attn_D"].repeat(HDIM).reshape(16, 128).T),
        nwcol=_c(I["L0_attn_norm_w"].reshape(16, 128).T),
        wout=_c(I["L0_attn_out_proj_w"]),
        e32=_c(e))


def run_attn0(x_fm, I):
    nc = build_attn0()
    wmap = attn0_weight_map(I)
    in_maps = []
    for j in range(NCORES):
        xs = _c(x_fm[:, 2 * j:2 * j + 2, :].reshape(DM, TOK))
        in_maps.append(dict(x_in=xs, **wmap))
    res = run_bass_kernel_spmd(nc, in_maps, core_ids=list(range(NCORES)))
    out = np.empty((DM, 16, 2048), np.float32)
    for j in range(NCORES):
        out[:, 2 * j:2 * j + 2, :] = res.results[j]["x_out"].reshape(DM, 2, SEQ)
    return out


def _c(a):
    return np.ascontiguousarray(a)


def ffn0_weight_map(I):
    w = dict(
        wfT=_c(I["L0_ffn_wf"].T), wiT=_c(I["L0_ffn_wi"].T), wcT=_c(I["L0_ffn_wc"].T),
        ufT=_c(I["L0_ffn_uf"].T), uiT=_c(I["L0_ffn_ui"].T), ucT=_c(I["L0_ffn_uc"].T),
        woT=_c(I["L0_ffn_wo"].T), uoT=_c(I["L0_ffn_uo"].T))
    for k, src in [("bf", "L0_ffn_bf"), ("bi", "L0_ffn_bi"), ("bc", "L0_ffn_bc"),
                   ("bo", "L0_ffn_bo"), ("in_c", "L0_ffn_in_c")]:
        w[k] = _c(I[src].reshape(NCH, 128).T)
    return w


def run_ffn0(x_fm, I):
    nc = build_ffn0()
    wmap = ffn0_weight_map(I)
    in_maps = []
    for j in range(NCORES):
        xs = _c(x_fm[:, :, j * LLOC:(j + 1) * LLOC].reshape(DM, TOK))
        in_maps.append(dict(x_in=xs, **wmap))
    res = run_bass_kernel_spmd(nc, in_maps, core_ids=list(range(NCORES)))
    out = np.empty((DM, 16, 2048), np.float32)
    for j in range(NCORES):
        out[:, :, j * LLOC:(j + 1) * LLOC] = \
            res.results[j]["x_out"].reshape(DM, 16, LLOC)
    return out


def decl_ffn0_w(nc, W):
    for w in ["wfT", "wiT", "wcT", "ufT", "uiT", "ucT", "woT", "uoT"]:
        W[w] = nc.dram_tensor(w, [DM, DM], F32, kind="ExternalInput").ap()
    for v in ["bf", "bi", "bc", "bo", "in_c"]:
        W[v] = nc.dram_tensor(v, [128, NCH], F32, kind="ExternalInput").ap()


def decl_ffn1_w(nc, W):
    W["wqkv"] = nc.dram_tensor("wqkv", [DM, 3 * DM], F32, kind="ExternalInput").ap()
    W["bqkv_row"] = nc.dram_tensor("bqkv_row", [1, 3 * DM], F32, kind="ExternalInput").ap()
    W["bqkv_col"] = nc.dram_tensor("bqkv_col", [128, 24], F32, kind="ExternalInput").ap()
    W["wgate"] = nc.dram_tensor("wgate", [DM, 8], F32, kind="ExternalInput").ap()
    W["bgate"] = nc.dram_tensor("bgate", [8, 1], F32, kind="ExternalInput").ap()
    W["wexp"] = nc.dram_tensor("wexp", [8, DM, DM], F32, kind="ExternalInput").ap()
    W["bexp"] = nc.dram_tensor("bexp", [8, DM], F32, kind="ExternalInput").ap()
    W["wexp_bf"] = nc.dram_tensor("wexp_bf", [8, DM, DM], BF16).ap()


def decl_attn1_w(nc, W):
    W["A"] = nc.dram_tensor("A", [64, 64], F32, kind="ExternalInput").ap()
    W["BT"] = nc.dram_tensor("BT", [DM, 64], F32, kind="ExternalInput").ap()
    W["CT"] = nc.dram_tensor("CT", [64, DM], F32, kind="ExternalInput").ap()
    for v in ["Dcol", "ldt_col", "ln1_col", "ln2_col"]:
        W[v] = nc.dram_tensor(v, [128, NCH], F32, kind="ExternalInput").ap()
    W["wgT"] = nc.dram_tensor("wgT", [DM, INTER], F32, kind="ExternalInput").ap()
    W["wuT"] = nc.dram_tensor("wuT", [DM, INTER], F32, kind="ExternalInput").ap()
    W["wdT"] = nc.dram_tensor("wdT", [INTER, DM], F32, kind="ExternalInput").ap()
    W["wg_bf"] = nc.dram_tensor("wg_bf", [DM, INTER], BF16).ap()
    W["wu_bf"] = nc.dram_tensor("wu_bf", [DM, INTER], BF16).ap()
    W["wd_bf"] = nc.dram_tensor("wd_bf", [INTER, DM], BF16).ap()


def decl_attn0_w(nc, W):
    W["win"] = nc.dram_tensor("win", [DM, 4384], F32, kind="ExternalInput").ap()
    W["convd"] = nc.dram_tensor("convd", [4, NCONV, 128, 128], F32, kind="ExternalInput").ap()
    W["convb"] = nc.dram_tensor("convb", [128, NCONV], F32, kind="ExternalInput").ap()
    W["dtb"] = nc.dram_tensor("dtb", [NH, 1], F32, kind="ExternalInput").ap()
    W["alog"] = nc.dram_tensor("alog", [NH, 1], F32, kind="ExternalInput").ap()
    W["dcol"] = nc.dram_tensor("dcol", [128, 16], F32, kind="ExternalInput").ap()
    W["nwcol"] = nc.dram_tensor("nwcol", [128, 16], F32, kind="ExternalInput").ap()
    W["wout"] = nc.dram_tensor("wout", [DIN, DM], F32, kind="ExternalInput").ap()
    W["e32"] = nc.dram_tensor("e32", [NH, DIN], F32, kind="ExternalInput").ap()
    W["wout_bf"] = nc.dram_tensor("wout_bf", [DIN, DM], BF16).ap()


def _plain_fns(xin, xout):
    xs = xin.rearrange("(c p) t -> p c t", p=128)
    xd = xout.rearrange("(c p) t -> p c t", p=128)
    return (lambda t0, w: xs[:, :, t0:t0 + w],
            lambda t0, w: xd[:, :, t0:t0 + w])


def _wrap(emit, decl):
    def build():
        nc = bass.Bass("TRN2", target_bir_lowering=False)
        xin = nc.dram_tensor("x_in", [DM, TOK], F32, kind="ExternalInput").ap()
        xout = nc.dram_tensor("x_out", [DM, TOK], F32, kind="ExternalOutput").ap()
        W = {}
        decl(nc, W)
        b = make_b(nc)
        fin, fout = _plain_fns(xin, xout)
        emit(nc, b, W, fin, fout)
        return nc
    return build


build_ffn0 = _wrap(emit_ffn0, decl_ffn0_w)
build_ffn1 = _wrap(emit_ffn1, decl_ffn1_w)
build_attn1 = _wrap(emit_attn1, decl_attn1_w)
build_attn0 = _wrap(emit_attn0, decl_attn0_w)


def build_fused():
    nc = bass.Bass("TRN2", target_bir_lowering=False)
    xin = nc.dram_tensor("x_in", [DM, TOK], F32, kind="ExternalInput").ap()
    xout = nc.dram_tensor("x_out", [DM, TOK], F32, kind="ExternalOutput").ap()
    W = {}
    decl_ffn0_w(nc, W)
    decl_attn0_w(nc, W)
    decl_ffn1_w(nc, W)
    decl_attn1_w(nc, W)
    ab = [nc.dram_tensor(f"a2a{i}{d}", [NCORES, DM * 512], F32)
          for i in range(3) for d in ("i", "o")]
    a1i, a1o, a2i, a2o, a3i, a3o = ab
    xbs = [nc.dram_tensor(f"xb{i}", [DM, TOK], F32).ap() for i in range(1, 6)]
    xb1, xb2, xb3, xb4, xb5 = xbs
    b = make_b(nc)
    cc = b.reg("cc", "gpsimd", dma=True, step=1)

    def lshard_in(t0, w):
        xs = xin.rearrange("(c p) t -> p c t", p=128)
        return xs[:, :, t0:t0 + w]

    def store_l(buf):
        def f(t0, w):
            assert w == 256 and t0 % 256 == 0
            t_ = t0 // 256
            return buf.ap().rearrange("j (c p two l) -> j p c two l", p=128,
                                      two=2, l=256)[t_ // 2][:, :, t_ % 2, :]
        return f

    load_l = store_l

    def plain(buf):
        xs = buf.rearrange("(c p) t -> p c t", p=128)
        return lambda t0, w: xs[:, :, t0:t0 + w]

    def a2a(bin_, bout):
        b.wait(cc, b.qss)
        b.wait(cc, b.qg)
        b.op(cc, lambda h: h.collective_compute(
            "AllToAll", ALU.bypass, replica_groups=[list(range(NCORES))],
            ins=[bin_.ap().opt()], outs=[bout.ap().opt()]))
        b.inc(cc)
        b.flush()

    def reorder_out(bufa2a, bufplain):
        b.wait(b.qg, cc)
        pl = bufplain.rearrange("(c p) t -> p c t", p=128)
        for j in range(NCORES):
            for two in range(2):
                dst = pl[:, :, two * SEQ + j * 256:two * SEQ + (j + 1) * 256]
                src = bufa2a.ap().rearrange("j (c p two l) -> j p c two l",
                                            p=128, two=2, l=256)[j][:, :, two, :]
                b.op(b.qg, lambda h, dst=dst, src=src: h.dma_start(out=dst, in_=src))
                b.inc(b.qg)
        b.flush()

    def reorder_in(bufplain, bufa2a):
        b.wait(b.qg, b.qss)
        pl = bufplain.rearrange("(c p) t -> p c t", p=128)
        for j in range(NCORES):
            for two in range(2):
                src = pl[:, :, two * SEQ + j * 256:two * SEQ + (j + 1) * 256]
                dst = bufa2a.ap().rearrange("j (c p two l) -> j p c two l",
                                            p=128, two=2, l=256)[j][:, :, two, :]
                b.op(b.qg, lambda h, dst=dst, src=src: h.dma_start(out=dst, in_=src))
                b.inc(b.qg)
        b.flush()

    FUSE_N = int(os.environ.get("FUSE_N", "6"))

    def finish(buf):
        b.wait(b.qg, b.qss)
        b.op(b.qg, lambda h: h.dma_start(out=xout, in_=buf))
        b.inc(b.qg)
        b.op(b.pool, lambda h, v=b.qg.cnt: h.wait_ge(b.qg.sem, v))
        b.inc(b.pool)
        b.flush()
        return nc

    emit_ffn0(nc, b, W, lshard_in, store_l(a1i), first=True)
    a2a(a1i, a1o)
    reorder_out(a1o, xb1)
    if FUSE_N <= 1:
        return finish(xb1)
    b.wait(b.qs, b.qg)
    emit_attn0(nc, b, W, plain(xb1), plain(xb2), first=True)
    if FUSE_N <= 2:
        return finish(xb2)
    reorder_in(xb2, a2i)
    a2a(a2i, a2o)
    b.wait(b.qs, cc)
    emit_ffn0(nc, b, W, load_l(a2o), store_l(a3i), first=False)
    a2a(a3i, a3o)
    reorder_out(a3o, xb3)
    if FUSE_N <= 3:
        return finish(xb3)
    b.wait(b.qs, b.qg)
    emit_ffn1(nc, b, W, plain(xb3), plain(xb4), first=True)
    if FUSE_N <= 4:
        return finish(xb4)
    b.wait(b.qs, b.qss)
    emit_attn1(nc, b, W, plain(xb4), plain(xb5), first=True)
    if FUSE_N <= 5:
        return finish(xb5)
    b.wait(b.qs, b.qss)
    emit_ffn1(nc, b, W, plain(xb5),
              _plain_fns(xin, xout)[1], first=False)
    return nc


def fused_in_maps(I):
    wmap = {}
    wmap.update(ffn0_weight_map(I))
    wmap.update(attn0_weight_map(I))
    wmap.update(ffn1_weight_map(I))
    wmap.update(attn1_weight_map(I))
    x_fm = _c(I["L0_attn_u"].astype(np.float32).transpose(2, 0, 1))
    in_maps = []
    for j in range(NCORES):
        xs = _c(x_fm[:, :, j * 256:(j + 1) * 256].reshape(DM, TOK))
        in_maps.append(dict(x_in=xs, **wmap))
    return in_maps


def kernel(**inputs):
    I = inputs
    nc = build_fused()
    res = run_bass_kernel_spmd(nc, fused_in_maps(I), core_ids=list(range(NCORES)))
    out = np.empty((16, 2048, DM), np.float32)
    for j in range(NCORES):
        xb = res.results[j]["x_out"].reshape(DM, 2, SEQ)
        out[2 * j:2 * j + 2] = xb.transpose(1, 2, 0)
    return _c(out)


def kernel_unfused(**inputs):
    I = inputs
    x_fm = _c(I["L0_attn_u"].astype(np.float32).transpose(2, 0, 1))
    x_fm = run_ffn0(x_fm, I)
    x_fm = run_attn0(x_fm, I)
    x_fm = run_ffn0(x_fm, I)
    x_fm = run_ffn1(x_fm, I)
    x_fm = run_attn1(x_fm, I)
    x_fm = run_ffn1(x_fm, I)
    return _c(x_fm.transpose(1, 2, 0))


# version 05
# speedup vs baseline: 1.3277x; 1.1201x over previous
import os

os.environ.setdefault("TRN_ALLOW_ICI_ALL_TO_ALL", "1")

from contextlib import ExitStack

import numpy as np
import ml_dtypes
import concourse.bass as bass
import concourse.mybir as mybir
from concourse.bass_utils import run_bass_kernel_spmd

F32 = mybir.dt.float32
BF16 = mybir.dt.bfloat16
AF = mybir.ActivationFunctionType
ALU = mybir.AluOpType
AX = mybir.AxisListType

NCORES = 8
TOK = 4096
DM = 1024
NCH = DM // 128
LN_TT = 256


def _sb(nc, name, shape, dtype):
    return nc.sbuf_tensor(name, shape, dtype, allow_name_mangling=True)


def _ps(nc, name, shape, dtype):
    return nc.psum_tensor(name, shape, dtype, allow_name_mangling=True)


class Eng:
    def __init__(self, name, stream, sem, step, serialize=False, dma=False):
        self.name = name
        self.stream = stream
        self.sem = sem
        self.cnt = 0
        self.step = step
        self.serialize = serialize
        self.dma = dma
        self.hw = {}


class B:
    STREAMS = ("sync", "gpsimd", "tensor", "scalar", "vector")

    def __init__(self, nc: bass.Bass):
        self.nc = nc
        self.engs = {}
        self.lists = {s: [] for s in self.STREAMS}

    def reg(self, name, stream, dma=False, serialize=False, step=None):
        sem = self.nc.alloc_semaphore(f"sem_{name}")
        e = Eng(name, stream, sem, step if step is not None else (16 if dma else 1),
                serialize, dma)
        self.engs[name] = e
        setattr(self, name, e)
        return e

    AUTO = ("act", "dve", "pool")

    def op(self, e: Eng, fn, dep=None):
        if e.serialize and e.cnt > e.hw.get(e.name, 0):
            sem, v = e.sem, e.cnt
            self.lists[e.stream].append(
                [lambda h, s=sem, vv=v: h.wait_ge(s, vv), []])
            e.hw[e.name] = v
        if e.name in self.AUTO:
            if dep is None or dep:
                if e.hw.get(e.name, 0) < e.cnt:
                    sem, v = e.sem, e.cnt
                    self.lists[e.stream].append(
                        [lambda h, s=sem, vv=v: h.wait_ge(s, vv), []])
                    e.hw[e.name] = v
            entry = [fn, [(e.sem, 1)]]
            e.cnt += 1
        else:
            entry = [fn, []]
        self.lists[e.stream].append(entry)
        self._last = {**getattr(self, "_last", {}), e.name: entry}
        return entry

    def inc(self, e: Eng, n=None):
        if e.name in self.AUTO:
            return e.cnt
        n = n if n is not None else e.step
        self._last[e.name][1].append((e.sem, n))
        e.cnt += n
        return e.cnt

    def wait(self, cons: Eng, prod: Eng, upto=None):
        v = upto if upto is not None else prod.cnt
        if v <= 0 or cons.hw.get(prod.name, 0) >= v:
            return
        sem = prod.sem
        self.lists[cons.stream].append([lambda h, s=sem, vv=v: h.wait_ge(s, vv), []])
        cons.hw[prod.name] = v

    def flush(self):
        lists = {s: l for s, l in self.lists.items() if l}
        self.lists = {s: [] for s in self.STREAMS}
        if not lists:
            return
        with self.nc.Block() as blk:
            def mk(lst):
                def run(h):
                    for fn, incs in lst:
                        ins = fn(h)
                        for sem, n in incs:
                            ins.then_inc(sem, n)
                return run
            for s, lst in lists.items():
                getattr(blk, s)(mk(lst))
        for c in self.engs.values():
            for p in self.engs.values():
                if not p.dma:
                    c.hw[p.name] = p.cnt


def make_b(nc):
    b = B(nc)
    ser = os.environ.get("Q_SERIALIZE", "0") == "1"
    b.reg("qs", "sync", dma=True, serialize=ser)
    b.reg("qss", "sync", dma=True, serialize=ser)
    b.reg("qw", "sync", dma=True, serialize=ser)
    b.reg("qw2", "sync", dma=True, serialize=ser)
    b.reg("qg", "gpsimd", dma=True, serialize=ser)
    b.reg("qg2", "gpsimd", dma=True, serialize=ser)
    b.reg("pe", "tensor")
    b.reg("act", "scalar")
    b.reg("dve", "vector")
    b.reg("pool", "gpsimd")
    return b


class LnScratch:
    def __init__(self, nc, stack, pst=None, pbc=None, tag=""):
        if pst is None:
            pst = stack.enter_context(_ps(nc, f"ln_pst{tag}", [1, 2, LN_TT], F32))[:, :, :]
        if pbc is None:
            pbc = stack.enter_context(_ps(nc, f"ln_pbc{tag}", [128, 2, LN_TT], F32))[:, :, :]
        self.sq = stack.enter_context(_sb(nc, f"ln_sq{tag}", [128, 2, LN_TT], F32))
        self.stats = stack.enter_context(_sb(nc, f"ln_stats{tag}", [1, 4, LN_TT], F32))
        self.ones_col = stack.enter_context(_sb(nc, f"ones_col{tag}", [128, 1], F32))
        self.ones_row = stack.enter_context(_sb(nc, f"ones_row{tag}", [1, 128], F32))
        self.pst = pst
        self.pbc = pbc

    def init(self, b: B, dve: Eng):
        b.op(dve, lambda h, s=self: h.memset(s.ones_col[:, :], 1.0))
        b.op(dve, lambda h, s=self: h.memset(s.ones_row[:, :], 1.0))
        b.inc(dve)


def emit_ln_tile(b: B, ls: LnScratch, xt, xln_out):
    T = xt.shape[2]
    pe, act, dve = b.pe, b.act, b.dve
    st = ls.stats
    b.wait(pe, dve)
    b.wait(pe, act)
    b.wait(act, dve)
    for c in range(NCH):
        b.op(pe, lambda h, c=c, T=T, xt=xt: h.matmul(
            ls.pst[0:1, 0, 0:T], ls.ones_col[:, 0:1], xt[:, c, :],
            start=(c == 0), stop=(c == NCH - 1)))
    b.inc(pe)
    for c in range(NCH):
        b.op(act, lambda h, c=c, T=T, xt=xt: h.activation(
            ls.sq[:, c % 2, 0:T], xt[:, c, :], AF.Square))
        a_mark = b.inc(act)
        b.wait(pe, act, a_mark)
        b.op(pe, lambda h, c=c, T=T: h.matmul(
            ls.pst[0:1, 1, 0:T], ls.ones_col[:, 0:1], ls.sq[:, c % 2, 0:T],
            start=(c == 0), stop=(c == NCH - 1)))
        pe_mark = b.inc(pe)
        b.wait(act, pe, pe_mark)
    b.wait(dve, pe)
    b.op(dve, lambda h, T=T: h.tensor_scalar_mul(st[:, 0, 0:T], ls.pst[0:1, 0, 0:T], 1.0 / DM))
    b.op(dve, lambda h, T=T: h.tensor_scalar_mul(st[:, 1, 0:T], ls.pst[0:1, 1, 0:T], 1.0 / DM))
    b.op(dve, lambda h, T=T: h.tensor_tensor(st[:, 2, 0:T], st[:, 0, 0:T], st[:, 0, 0:T], ALU.mult))
    b.op(dve, lambda h, T=T: h.tensor_tensor(st[:, 2, 0:T], st[:, 1, 0:T], st[:, 2, 0:T], ALU.subtract))
    b.op(dve, lambda h, T=T: h.tensor_scalar_add(st[:, 2, 0:T], st[:, 2, 0:T], 1e-5))
    d_mark = b.inc(dve)
    b.wait(act, dve, d_mark)
    b.op(act, lambda h, T=T: h.activation(st[:, 2, 0:T], st[:, 2, 0:T], AF.Ln))
    b.op(act, lambda h, T=T: h.activation(st[:, 3, 0:T], st[:, 2, 0:T], AF.Exp, scale=-0.5))
    a_mark = b.inc(act)
    b.wait(dve, act, a_mark)
    b.op(dve, lambda h, T=T: h.tensor_tensor(st[:, 2, 0:T], st[:, 0, 0:T], st[:, 3, 0:T], ALU.mult))
    d_mark = b.inc(dve)
    b.wait(pe, dve, d_mark)
    b.op(pe, lambda h, T=T: h.matmul(ls.pbc[:, 0, 0:T], ls.ones_row[:, :], st[:, 3, 0:T],
                                     start=True, stop=True))
    b.op(pe, lambda h, T=T: h.matmul(ls.pbc[:, 1, 0:T], ls.ones_row[:, :], st[:, 2, 0:T],
                                     start=True, stop=True))
    pe_mark = b.inc(pe)
    b.wait(dve, pe, pe_mark)
    for c in range(NCH):
        b.op(dve, lambda h, c=c, T=T, xt=xt: h.scalar_tensor_tensor(
            ls.sq[:, c % 2, 0:T], xt[:, c, :], 1.0, ls.pbc[:, 0, 0:T],
            ALU.mult, ALU.mult))
        b.op(dve, lambda h, c=c, T=T, xln_out=xln_out: h.tensor_tensor(
            xln_out[:, c, :], ls.sq[:, c % 2, 0:T], ls.pbc[:, 1, 0:T], ALU.subtract))
    b.inc(dve)


TSTEPS = 16
LLOC = 256


def emit_ffn0(nc, b, W, xin_f, xout_f, first=True):
    wts = W
    dbg = False
    with ExitStack() as stack:
        xln = stack.enter_context(_sb(nc, "xln", [128, NCH, TOK], BF16))
        wu = stack.enter_context(_sb(nc, "wu", [128, 6, NCH, DM], BF16))
        bias_sb = stack.enter_context(_sb(nc, "bias_sb", [128, 5, NCH], F32))
        c_f32 = stack.enter_context(_sb(nc, "c_f32", [128, NCH, LLOC], F32))
        c_bf = stack.enter_context(_sb(nc, "c_bf", [128, 2, NCH, LLOC], BF16))
        gates = stack.enter_context(_sb(nc, "gates", [128, 3, LLOC], F32))
        hbuf = stack.enter_context(_sb(nc, "hbuf", [128, 2, LLOC], F32))
        xt0 = stack.enter_context(_sb(nc, "xt0", [128, NCH, LLOC], F32))
        xt1 = stack.enter_context(_sb(nc, "xt1", [128, NCH, LLOC], F32))
        ls = LnScratch(nc, stack)
        pg0 = stack.enter_context(_ps(nc, "pg0", [128, 3, 512], F32))[:, :, 0:LLOC]
        pg1 = stack.enter_context(_ps(nc, "pg1", [128, 3, 512], F32))[:, :, 0:LLOC]

        qs, qss, qw, qw2 = b.qs, b.qss, b.qw, b.qw2
        qg, qg2, pe, act, dve, pool = b.qg, b.qg2, b.pe, b.act, b.dve, b.pool

        for vi, v in enumerate(["bf", "bi", "bc", "bo", "in_c"]):
            b.op(qs, lambda h, vi=vi, v=v: h.dma_start(
                out=bias_sb[:, vi, :], in_=wts[v]))
            b.inc(qs)
        for gi, w in enumerate(["wfT", "wiT", "wcT", "ufT", "uiT", "ucT"]):
            b.op(qg, lambda h, gi=gi, w=w: h.dma_start(
                out=wu[:, gi, :, :].rearrange("p k (o q) -> p k o q", q=128),
                in_=wts[w].rearrange("(k p) (o q) -> p k o q", p=128, q=128)))
            b.inc(qg)
        ls.init(b, dve)
        b.flush()

        NT = TOK // LN_TT
        for i in range(NT):
            buf = xt0
            b.wait(qs, dve)
            b.op(qs, lambda h, i=i, buf=buf: h.dma_start(
                out=buf[:, :, :], in_=xin_f(i * LN_TT, LN_TT)))
            ld = b.inc(qs)
            b.wait(pe, qs, ld)
            b.wait(act, qs, ld)
            emit_ln_tile(b, ls, buf[:, :, :], xln[:, :, i * LN_TT:(i + 1) * LN_TT])
        b.flush()

        b.wait(pe, qg)
        b.wait(act, qs)
        b.wait(dve, qs)
        for c in range(NCH):
            b.op(dve, lambda h, c=c: h.memset(c_f32[:, c, :], 1.0))
            b.op(dve, lambda h, c=c: h.tensor_scalar_mul(
                c_f32[:, c, :], c_f32[:, c, :], bias_sb[:, 4, c:c + 1]))
            b.op(dve, lambda h, c=c: h.tensor_copy(c_bf[:, 0, c, :], c_f32[:, c, :]))
        b.inc(dve)

        psum_free = {}
        dve_marks = {-1: b.dve.cnt}
        for t_ in range(TSTEPS):
            cur, nxt = t_ % 2, 1 - t_ % 2
            for o in range(NCH):
                pidx = (t_ * NCH + o) % 2
                ps = pg0 if pidx == 0 else pg1
                if pidx in psum_free:
                    b.wait(pe, dve, psum_free[pidx])
                b.wait(pe, dve, dve_marks[t_ - 1])
                for gi in range(3):
                    for k in range(NCH):
                        b.op(pe, lambda h, k=k, gi=gi, o=o, t_=t_, ps=ps: h.matmul(
                            ps[:, gi, :],
                            wu[:, gi, k, o * 128:(o + 1) * 128],
                            xln[:, k, t_ * LLOC:(t_ + 1) * LLOC],
                            start=(k == 0), stop=False))
                    for k in range(NCH):
                        b.op(pe, lambda h, k=k, gi=gi, o=o, cur=cur, ps=ps: h.matmul(
                            ps[:, gi, :],
                            wu[:, 3 + gi, k, o * 128:(o + 1) * 128],
                            c_bf[:, cur, k, :],
                            start=False, stop=(k == NCH - 1)))
                pe_mark = b.inc(pe)
                b.wait(act, pe, pe_mark)
                b.wait(act, dve)
                b.op(act, lambda h, o=o, ps=ps: h.activation(
                    gates[:, 0, :], ps[:, 0, :], AF.Sigmoid, bias=bias_sb[:, 0, o:o + 1]))
                b.op(act, lambda h, o=o, ps=ps: h.activation(
                    gates[:, 1, :], ps[:, 1, :], AF.Sigmoid, bias=bias_sb[:, 1, o:o + 1]))
                b.op(act, lambda h, o=o, ps=ps: h.activation(
                    gates[:, 2, :], ps[:, 2, :], AF.Tanh, bias=bias_sb[:, 2, o:o + 1]))
                act_mark = b.inc(act)
                b.wait(dve, act, act_mark)
                b.op(dve, lambda h, o=o: h.tensor_tensor(
                    c_f32[:, o, :], c_f32[:, o, :], gates[:, 0, :], ALU.mult))
                b.op(dve, lambda h: h.tensor_tensor(
                    gates[:, 1, :], gates[:, 1, :], gates[:, 2, :], ALU.mult))
                b.op(dve, lambda h, o=o: h.tensor_tensor(
                    c_f32[:, o, :], c_f32[:, o, :], gates[:, 1, :], ALU.add))
                b.op(dve, lambda h, o=o, nxt=nxt: h.tensor_copy(
                    c_bf[:, nxt, o, :], c_f32[:, o, :]))
                psum_free[pidx] = b.inc(dve)
            dve_marks[t_] = b.dve.cnt
        b.flush()

        if dbg:
            b.op(qs, lambda h: h.dma_start(out=xln_dbg, in_=xln[:, :, :]))
            b.inc(qs)
            b.op(qs, lambda h: h.dma_start(out=c_dbg, in_=c_f32[:, :, :]))
            b.inc(qs)
            b.flush()

        cfin = TSTEPS % 2
        for gi, w in enumerate(["woT", "uoT"]):
            b.op(qg, lambda h, gi=gi, w=w: h.dma_start(
                out=wu[:, gi, :, :].rearrange("p k (o q) -> p k o q", q=128),
                in_=wts[w].rearrange("(k p) (o q) -> p k o q", p=128, q=128)))
            b.inc(qg)
        b.wait(pe, qg)
        psum_free = {}
        store_mark = 0
        for t_ in range(TSTEPS):
            buf = xt0
            obuf = xt1
            if store_mark:
                b.wait(dve, qss, store_mark)
            b.wait(qs, dve)
            b.op(qs, lambda h, t_=t_, buf=buf: h.dma_start(
                out=buf[:, :, :], in_=xin_f(t_ * LLOC, LLOC)))
            ld = b.inc(qs)
            for o in range(NCH):
                pidx = (t_ * NCH + o) % 2
                ps = pg0 if pidx == 0 else pg1
                if pidx in psum_free:
                    b.wait(pe, dve, psum_free[pidx])
                for k in range(NCH):
                    b.op(pe, lambda h, k=k, o=o, t_=t_, ps=ps: h.matmul(
                        ps[:, 0, :], wu[:, 0, k, o * 128:(o + 1) * 128],
                        xln[:, k, t_ * LLOC:(t_ + 1) * LLOC],
                        start=(k == 0), stop=False))
                for k in range(NCH):
                    b.op(pe, lambda h, k=k, o=o, ps=ps: h.matmul(
                        ps[:, 0, :], wu[:, 1, k, o * 128:(o + 1) * 128],
                        c_bf[:, cfin, k, :],
                        start=False, stop=(k == NCH - 1)))
                pe_mark = b.inc(pe)
                b.wait(act, pe, pe_mark)
                b.wait(act, dve)
                b.op(act, lambda h, o=o, ps=ps: h.activation(
                    hbuf[:, o % 2, :], ps[:, 0, :], AF.Tanh, bias=bias_sb[:, 3, o:o + 1]))
                act_mark = b.inc(act)
                b.wait(dve, act, act_mark)
                b.wait(dve, qs, ld)
                b.op(dve, lambda h, o=o, buf=buf, obuf=obuf: h.scalar_tensor_tensor(
                    obuf[:, o, :], hbuf[:, o % 2, :], 0.5, buf[:, o, :],
                    ALU.mult, ALU.add))
                psum_free[pidx] = b.inc(dve)
            b.wait(qss, dve)
            b.op(qss, lambda h, t_=t_, obuf=obuf: h.dma_start(
                out=xout_f(t_ * LLOC, LLOC), in_=obuf[:, :, :]))
            store_mark = b.inc(qss)
        b.op(pool, lambda h, v=b.qss.cnt: h.wait_ge(b.qss.sem, v))
        b.inc(pool)
        b.flush()


NSEQ = 2
SEQ = 2048
NHEAD = 16
HD = 64


def emit_ffn1(nc, b, W, xin_f, xout_f, first=True):
    PH = 3
    wqkv, bqkv_row, bqkv_col = W["wqkv"], W["bqkv_row"], W["bqkv_col"]
    wgate, bgate, wexp, bexp = W["wgate"], W["bgate"], W["wexp"], W["bexp"]
    wexp_bf = W["wexp_bf"]
    with ExitStack() as stack:
        xln = stack.enter_context(_sb(nc, "xln", [128, NCH, TOK], BF16))
        attn_sb = stack.enter_context(_sb(nc, "attn_sb", [128, NCH, TOK], BF16))
        cfm = stack.enter_context(_sb(nc, "cfm", [8, TOK], BF16))
        ident = stack.enter_context(_sb(nc, "ident", [128, 128], F32))
        bq_col = stack.enter_context(_sb(nc, "bq_col", [128, 24], F32))
        bg_col = stack.enter_context(_sb(nc, "bg_col", [8, 1], F32))
        be_sb = stack.enter_context(_sb(nc, "be_sb", [8, DM], BF16))
        wg_sb = stack.enter_context(_sb(nc, "wg_sb", [128, NCH, 8], BF16))
        orow_bf = stack.enter_context(_sb(nc, "orow_bf", [1, 128], BF16))
        pa = stack.enter_context(_ps(nc, "pa", [128, 2, 512], F32))[:, :, :]
        pb = stack.enter_context(_ps(nc, "pb", [128, 4, 512], F32))[:, :, :]

        qs, qss, qw, qw2 = b.qs, b.qss, b.qw, b.qw2
        qg, qg2, pe, act, dve, pool = b.qg, b.qg2, b.pe, b.act, b.dve, b.pool

        b.op(qs, lambda h: h.dma_start(out=bq_col[:, :], in_=bqkv_col))
        b.inc(qs)
        b.op(qs, lambda h: h.dma_start(out=bg_col[:, :], in_=bgate))
        b.inc(qs)
        b.op(qg, lambda h: h.dma_start(out=be_sb[:, :], in_=bexp))
        b.inc(qg)
        b.op(qg, lambda h: h.dma_start(
            out=wg_sb[:, :, :],
            in_=wgate.rearrange("(c p) e -> p c e", p=128)))
        b.inc(qg)
        if first:
            for e in range(8):
                b.op(qg, lambda h, e=e: h.dma_start(out=wexp_bf[e], in_=wexp[e]))
                b.inc(qg)
        b.op(pool, lambda h: h.memset(ident[:, :], 0.0))
        b.op(pool, lambda h: h.affine_select(
            out=ident[:, :], in_=ident[:, :], compare_op=ALU.not_equal,
            fill=1.0, base=0, pattern=[[-1, 128]], channel_multiplier=1))
        b.op(dve, lambda h: h.memset(orow_bf[:, :], 1.0))
        b.inc(dve)

        with ExitStack() as _phase_guard:
            ln_stack = ExitStack()
            xt0 = ln_stack.enter_context(_sb(nc, "xt0", [128, NCH, LN_TT], F32))
            ls = LnScratch(nc, ln_stack)
            ls.init(b, dve)
            for i in range(TOK // LN_TT):
                b.wait(qs, dve)
                b.op(qs, lambda h, i=i: h.dma_start(
                    out=xt0[:, :, :], in_=xin_f(i * LN_TT, LN_TT)))
                ld = b.inc(qs)
                b.wait(pe, qs, ld)
                b.wait(act, qs, ld)
                emit_ln_tile(b, ls, xt0[:, :, :], xln[:, :, i * LN_TT:(i + 1) * LN_TT])
            b.flush()
            ln_stack.close()


            with (
                _sb(nc, "glog", [8, TOK], F32) as glog,
                _sb(nc, "gt", [128, 10, 8], F32) as gt,
                _sb(nc, "gcols", [128, 4], F32) as gcols,
                _sb(nc, "ct", [128, 8], F32) as ct,
            ):
                b.wait(pe, qg)
                b.wait(act, qs)
                for i in range(TOK // 512):
                    b.wait(pe, act)
                    for k in range(NCH):
                        b.op(pe, lambda h, i=i, k=k: h.matmul(
                            pa[0:8, 0, :], wg_sb[:, k, :],
                            xln[:, k, i * 512:(i + 1) * 512],
                            start=(k == 0), stop=(k == NCH - 1)))
                    m = b.inc(pe)
                    b.wait(act, pe, m)
                    b.op(act, lambda h, i=i: h.activation(
                        glog[:, i * 512:(i + 1) * 512], pa[0:8, 0, :],
                        AF.Identity, bias=bg_col[:, :]))
                b.flush()
                for i in range(TOK // 128):
                    sl = slice(i * 128, (i + 1) * 128)
                    b.wait(pe, dve)
                    b.op(pe, lambda h, sl=sl: h.transpose(
                        pa[:, 1, 0:8], glog[:, sl], ident[0:8, 0:8]))
                    m = b.inc(pe)
                    b.wait(dve, pe, m)
                    b.op(dve, lambda h: h.tensor_copy(gt[:, 0, :], pa[:, 1, 0:8]))
                    b.op(dve, lambda h: h.reduce_max(gcols[:, 0:1], gt[:, 0, :], axis=AX.X))
                    b.op(dve, lambda h: h.tensor_scalar_mul(gcols[:, 1:2], gcols[:, 0:1], -1.0))
                    d = b.inc(dve)
                    b.wait(act, dve, d)
                    b.op(act, lambda h: h.activation(gt[:, 1, :], gt[:, 0, :], AF.Exp,
                                                     bias=gcols[:, 1:2]))
                    a = b.inc(act)
                    b.wait(dve, act, a)
                    b.op(dve, lambda h: h.reduce_sum(gcols[:, 0:1], gt[:, 1, :], axis=AX.X))
                    b.op(dve, lambda h: h.reciprocal(gcols[:, 1:2], gcols[:, 0:1]))
                    b.op(dve, lambda h: h.tensor_scalar_mul(gt[:, 2, :], gt[:, 1, :], gcols[:, 1:2]))
                    b.op(dve, lambda h: h.reduce_max(gcols[:, 2:3], gt[:, 2, :], axis=AX.X))
                    b.op(dve, lambda h: h.tensor_single_scalar(gt[:, 3, :], gt[:, 2, :], gcols[:, 2:3], ALU.is_equal))
                    b.op(dve, lambda h: h.scalar_tensor_tensor(
                        gt[:, 4, :], gt[:, 3, :], -2.0, gt[:, 2, :], ALU.mult, ALU.add))
                    b.op(dve, lambda h: h.reduce_max(gcols[:, 3:4], gt[:, 4, :], axis=AX.X))
                    b.op(dve, lambda h: h.tensor_single_scalar(gt[:, 5, :], gt[:, 2, :], gcols[:, 3:4], ALU.is_ge))
                    b.op(dve, lambda h: h.tensor_tensor(ct[:, :], gt[:, 2, :], gt[:, 5, :], ALU.mult))
                    d = b.inc(dve)
                    b.wait(pe, dve, d)
                    b.op(pe, lambda h: h.transpose(pa[0:8, 1, 128:256], ct[:, :], ident[:, :]))
                    m = b.inc(pe)
                    b.wait(dve, pe, m)
                    b.op(dve, lambda h, sl=sl: h.tensor_copy(cfm[:, sl], pa[0:8, 1, 128:256]))
                    b.inc(dve)
                b.flush()


            with (
                _sb(nc, "qk_sb", [128, 4, SEQ], BF16) as qk_sb,
                _sb(nc, "v_sb", [128, 16, 4, 65], BF16) as v_sb,
                _sb(nc, "wbuf", [128, NCH, 1024], BF16) as wbuf,
                _sb(nc, "p_sb", [128, 2, 512], BF16) as p_sb,
                _sb(nc, "av_sb", [64, 512], BF16) as av_sb,
                _sb(nc, "r_sb", [128, 512], F32) as r_sb,
                _sb(nc, "ones128", [128, 128], F32) as ones128,
                _sb(nc, "bq_row", [1, 3 * DM], BF16) as bq_row,
            ):
                b.op(qg, lambda h: h.dma_start(out=bq_row[:, :], in_=bqkv_row))
                b.inc(qg)
                b.op(dve, lambda h: h.memset(ones128[:, :], 1.0))
                b.op(dve, lambda h: h.memset(
                    v_sb[:, :, :, 64:65].rearrange("p a c o -> p (a c o)"), 1.0))
                b.inc(dve)
                b.wait(act, qs)
                exp_marks = {}
                for hg in range(4):
                    b.wait(qg, pe)
                    for j, col in enumerate([hg * 256, DM + hg * 256, 2 * DM + hg * 256]):
                        b.op(qg, lambda h, j=j, col=col: h.dma_start(
                            out=wbuf[:, :, j * 256:(j + 1) * 256],
                            in_=wqkv.rearrange("(k p) f -> p k f", p=128)
                                [:, :, col:col + 256]))
                        b.inc(qg)
                    wld = b.qg.cnt
                    b.wait(pe, qg, wld)
                    for s_ in range(NSEQ):
                        seq0 = s_ * SEQ
                        b.wait(pe, dve)
                        for fc in range(4):
                            wslice = [0, 128, 256, 384][fc]
                            bcol = [2 * hg, 2 * hg + 1, 8 + 2 * hg, 8 + 2 * hg + 1][fc]
                            for i in range(SEQ // 512):
                                b.wait(pe, act)
                                for k in range(NCH):
                                    b.op(pe, lambda h, i=i, k=k, ws=wslice, seq0=seq0: h.matmul(
                                        pa[:, 0, :], wbuf[:, k, ws:ws + 128],
                                        xln[:, k, seq0 + i * 512:seq0 + (i + 1) * 512],
                                        start=(k == 0), stop=(k == NCH - 1)))
                                m = b.inc(pe)
                                b.wait(act, pe, m)
                                b.op(act, lambda h, fc=fc, i=i, bcol=bcol: h.activation(
                                    qk_sb[:, fc, i * 512:(i + 1) * 512], pa[:, 0, :],
                                    AF.Identity, bias=bq_col[:, bcol:bcol + 1]))
                        for tt in range(16):
                            tok0 = seq0 + tt * 128
                            b.wait(pe, act)
                            for k in range(NCH):
                                b.op(pe, lambda h, k=k, tok0=tok0: h.matmul(
                                    pa[:, 1, 0:256], xln[:, k, tok0:tok0 + 128],
                                    wbuf[:, k, 512:768],
                                    start=(k == 0), stop=False))
                            b.op(pe, lambda h, hg=hg: h.matmul(
                                pa[:, 1, 0:256], orow_bf[:, :],
                                bq_row[:, 2 * DM + hg * 256:2 * DM + (hg + 1) * 256],
                                start=False, stop=True))
                            m = b.inc(pe)
                            b.wait(act, pe, m)
                            b.op(act, lambda h, tt=tt: h.activation(
                                v_sb[:, tt, :, 0:64],
                                pa[:, 1, 0:256].rearrange("p (h f) -> p h f", h=4),
                                AF.Identity))
                        b.wait(pe, act)
                        tail = getattr(b, "_ffn1_tail",
                                       {"a": 0, "dr": 0, "ds": 0, "pending": None})
                        for hi in range(4):
                            hglob = hg * 4 + hi
                            fc = hi // 2
                            rows = (hi % 2) * 64
                            for qt in range(4):
                                q0 = qt * 512
                                for kt in range(16):
                                    k0 = kt * 128
                                    pp = kt % 2
                                    if pp in exp_marks:
                                        b.wait(pe, act, exp_marks[pp])
                                    b.op(pe, lambda h, fc=fc, rows=rows, k0=k0, q0=q0, pp=pp: h.matmul(
                                        pa[:, pp, :], qk_sb[rows:rows + 64, 2 + fc, k0:k0 + 128],
                                        qk_sb[rows:rows + 64, fc, q0:q0 + 512],
                                        start=True, stop=True))
                                    ms = b.inc(pe)
                                    if kt > 0:
                                        ppv = 1 - pp
                                        b.wait(pe, act, exp_marks[ppv])
                                        if kt == 1:
                                            b.wait(pe, act, tail["a"])
                                            b.wait(pe, dve, tail["dr"])
                                        b.op(pe, lambda h, kt=kt, hi=hi, ppv=ppv: h.matmul(
                                            pb[0:65, 0, :], v_sb[:, kt - 1, hi, :],
                                            p_sb[:, ppv, :],
                                            start=(kt == 1), stop=False))
                                        b.inc(pe)
                                    if kt == 6 and tail["pending"] is not None:
                                        tail["pending"]()
                                        tail["pending"] = None
                                    b.wait(act, pe, ms)
                                    b.op(act, lambda h, pp=pp: h.activation(
                                        p_sb[:, pp, :], pa[:, pp, :], AF.Exp, scale=0.125))
                                    exp_marks[pp] = b.inc(act)
                                b.wait(pe, act, exp_marks[1])
                                b.op(pe, lambda h, hi=hi: h.matmul(
                                    pb[0:65, 0, :], v_sb[:, 15, hi, :],
                                    p_sb[:, 1, :],
                                    start=False, stop=True))
                                m = b.inc(pe)
                                b.wait(dve, pe, m)
                                b.op(dve, lambda h: h.reciprocal(r_sb[64:65, :], pb[64:65, 0, :]))
                                tail["dr"] = d = b.inc(dve)
                                b.wait(act, dve, d)
                                b.op(act, lambda h: h.activation(av_sb[:, :], pb[0:64, 0, :], AF.Identity))
                                tail["a"] = a = b.inc(act)

                                def _emit_tail(rows=rows, hglob=hglob, q0=q0,
                                               seq0=seq0, dr=d, a=a, ds_prev=tail["ds"]):
                                    b.wait(pe, dve, dr)
                                    b.wait(pe, dve, ds_prev)
                                    b.op(pe, lambda h: h.matmul(
                                        pb[0:64, 1, :], ones128[64:65, 0:64],
                                        r_sb[64:65, :], start=True, stop=True))
                                    m2 = b.inc(pe)
                                    b.wait(dve, pe, m2)
                                    b.wait(dve, act, a)
                                    b.op(dve, lambda h: h.scalar_tensor_tensor(
                                        attn_sb[rows:rows + 64, hglob // 2, seq0 + q0:seq0 + q0 + 512],
                                        pb[0:64, 1, :], 0.5, av_sb[:, :], ALU.mult, ALU.mult))
                                    tail["ds"] = b.inc(dve)
                                tail["pending"] = _emit_tail
                        b._ffn1_tail = tail
                if b._ffn1_tail["pending"] is not None:
                    b._ffn1_tail["pending"]()
                    b._ffn1_tail["pending"] = None
                b.flush()


            with (
                _sb(nc, "cb_sb", [128, 2, 512], BF16) as cb_sb,
                _sb(nc, "xe", [128, 2, NCH, 512], BF16) as xe,
                _sb(nc, "we", [128, 2, NCH, 256], BF16) as we,
                _sb(nc, "xa", [128, NCH, 512], F32) as xa,
                _sb(nc, "ob", [128, NCH, 512], F32) as ob,
                _sb(nc, "cfm_row", [1, 8, 512], BF16) as cfm_row,
            ):
                store_mark = 0
                b.wait(pe, qg)
                for tt in range(TOK // 512):
                    t0 = tt * 512
                    b.wait(qs, pe)
                    b.op(qs, lambda h, t0=t0: h.dma_start(
                        out=cfm_row.ap().rearrange("o e t -> o (e t)"),
                        in_=cfm.ap()[:, t0:t0 + 512]))
                    crm = b.inc(qs)
                    b.wait(pe, qs, crm)
                    b.wait(qs, dve)
                    b.op(qs, lambda h, t0=t0: h.dma_start(
                        out=xa[:, :, :], in_=xin_f(t0, 512)))
                    xld = b.inc(qs)
                    b.wait(dve, qs, xld)
                    for c in range(NCH):
                        b.op(dve, lambda h, c=c, t0=t0: h.tensor_tensor(
                            xa[:, c, :], xa[:, c, :], attn_sb[:, c, t0:t0 + 512], ALU.add), dep=(c == 0))
                    for fh in range(2):
                        f0 = fh * 4
                        for e in range(8):
                            eb = e % 2
                            b.wait(pe, act)
                            b.op(pe, lambda h, e=e, eb=eb: h.matmul(
                                pa[:, eb, :], orow_bf[:, :], cfm_row[0:1, e, :],
                                start=True, stop=True))
                            m = b.inc(pe)
                            b.wait(act, pe, m)
                            b.wait(act, dve)
                            b.op(act, lambda h, eb=eb: h.activation(
                                cb_sb[:, eb, :], pa[:, eb, :], AF.Identity))
                            cbm = b.inc(act)
                            for half, q in ((0, qw), (1, qw2)):
                                b.wait(q, pe)
                                b.op(q, lambda h, e=e, f0=f0, half=half: h.dma_start(
                                    out=we[:, half, :, :],
                                    in_=wexp_bf[e].rearrange("(k p) f -> p k f", p=128)
                                        [:, :, (f0 + 2 * half) * 128:(f0 + 2 * half) * 128 + 256]))
                                b.inc(q)
                            b.wait(dve, act, cbm)
                            b.wait(dve, pe)
                            for k in range(NCH):
                                b.op(dve, lambda h, k=k, t0=t0, eb=eb: h.tensor_tensor(
                                    xe[:, eb, k, :], xln[:, k, t0:t0 + 512],
                                    cb_sb[:, eb, :], ALU.mult), dep=(k == 0))
                            xe_mark = b.inc(dve)
                            b.wait(pe, dve, xe_mark)
                            for f in range(4):
                                if f == 0:
                                    b.wait(pe, qw)
                                if f == 2:
                                    b.wait(pe, qw2)
                                for k in range(NCH):
                                    b.op(pe, lambda h, f=f, k=k, eb=eb, e=e: h.matmul(
                                        pb[:, f, :], we[:, f // 2, k, (f % 2) * 128:(f % 2) * 128 + 128],
                                        xe[:, eb, k, :],
                                        start=(e == 0 and k == 0), stop=False))
                        for f in range(4):
                            b.op(pe, lambda h, f=f, f0=f0, t0=t0: h.matmul(
                                pb[:, f, :], be_sb[:, (f0 + f) * 128:(f0 + f + 1) * 128],
                                cfm[:, t0:t0 + 512], start=False, stop=True))
                        m = b.inc(pe)
                        b.wait(dve, pe, m)
                        b.wait(dve, qss, store_mark)
                        for f in range(4):
                            b.op(dve, lambda h, f=f, f0=f0: h.scalar_tensor_tensor(
                                ob[:, f0 + f, :], pb[:, f, :], 0.5, xa[:, f0 + f, :],
                                ALU.mult, ALU.add), dep=(f == 0))
                        b.inc(dve)
                    b.wait(qss, dve)
                    b.op(qss, lambda h, t0=t0: h.dma_start(
                        out=xout_f(t0, 512), in_=ob[:, :, :]))
                    store_mark = b.inc(qss)
                b.op(pool, lambda h, v=b.qss.cnt: h.wait_ge(b.qss.sem, v))
                b.inc(pool)
                b.flush()


def ffn1_weight_map(I):
    bq = I["L1_ffn_b_qkv"]
    return dict(
        wqkv=_c(I["L1_ffn_W_qkv"]),
        bqkv_row=_c(bq.reshape(1, 3 * DM)),
        bqkv_col=_c(bq.reshape(24, 128).T),
        wgate=_c(I["L1_ffn_W_gate"]),
        bgate=_c(I["L1_ffn_b_gate"].reshape(8, 1)),
        wexp=_c(I["L1_ffn_W_experts"]),
        bexp=_c(I["L1_ffn_b_experts"]))


def run_ffn1(x_fm, I):
    nc = build_ffn1()
    wmap = ffn1_weight_map(I)
    in_maps = []
    for j in range(NCORES):
        xs = _c(x_fm[:, 2 * j:2 * j + 2, :].reshape(DM, TOK))
        in_maps.append(dict(x_in=xs, **wmap))
    res = run_bass_kernel_spmd(nc, in_maps, core_ids=list(range(NCORES)))
    out = np.empty((DM, 16, 2048), np.float32)
    for j in range(NCORES):
        out[:, 2 * j:2 * j + 2, :] = res.results[j]["x_out"].reshape(DM, 2, SEQ)
    return out


INTER = 4096
PADC = 1024
SCOL = PADC + SEQ


def emit_attn1(nc, b, W, xin_f, xout_f, first=True):
    A_in, BT, CT = W["A"], W["BT"], W["CT"]
    Dcol, ldt_col, ln1_col, ln2_col = W["Dcol"], W["ldt_col"], W["ln1_col"], W["ln2_col"]
    wgT, wuT, wdT = W["wgT"], W["wuT"], W["wdT"]
    wg_bf, wu_bf, wd_bf = W["wg_bf"], W["wu_bf"], W["wd_bf"]
    with ExitStack() as stack:
        xln = stack.enter_context(_sb(nc, "xln", [128, NCH, TOK], BF16))
        m1 = stack.enter_context(_sb(nc, "m1", [128, NCH, TOK], BF16))
        cols = stack.enter_context(_sb(nc, "cols", [128, 6, NCH], F32))
        i64bf = stack.enter_context(_sb(nc, "i64bf", [64, 64], BF16))
        ones_col = stack.enter_context(_sb(nc, "ones_c", [128, 1], F32))
        ones128 = stack.enter_context(_sb(nc, "ones128", [1, 128], F32))

        qs, qss, qw, qw2 = b.qs, b.qss, b.qw, b.qw2
        qg, qg2, pe, act, dve, pool = b.qg, b.qg2, b.pe, b.act, b.dve, b.pool

        for nm, dst in [("Dcol", 0), ("ldt_col", 1), ("ln1_col", 2), ("ln2_col", 3)]:
            src = {"Dcol": Dcol, "ldt_col": ldt_col, "ln1_col": ln1_col, "ln2_col": ln2_col}[nm]
            b.op(qs, lambda h, src=src, dst=dst: h.dma_start(out=cols[:, dst, :], in_=src))
            b.inc(qs)
        if first:
            for wsrc, wdst in [(wgT, wg_bf), (wuT, wu_bf), (wdT, wd_bf)]:
                b.op(qg, lambda h, a=wsrc, bb=wdst: h.dma_start(out=bb, in_=a))
                b.inc(qg)
        b.op(dve, lambda h: h.memset(ones_col[:, :], 1.0))
        b.op(dve, lambda h: h.memset(ones128[:, :], 1.0))
        b.op(pool, lambda h: h.memset(i64bf[:, :], 0.0))
        b.op(pool, lambda h: h.affine_select(
            out=i64bf[:, :], in_=i64bf[:, :], compare_op=ALU.not_equal,
            fill=1.0, base=0, pattern=[[-1, 64]], channel_multiplier=1))
        b.inc(dve)
        b.inc(pool)

        with ExitStack() as ph12:
            sbf = ph12.enter_context(_sb(nc, "sbuf_scan", [64, 2, 2, SCOL], BF16))
            bbt = ph12.enter_context(_sb(nc, "bbt", [128, NCH, 64], BF16))
            ct_sb = ph12.enter_context(_sb(nc, "ct_sb", [64, NCH, 128], BF16))
            qbf = ph12.enter_context(_sb(nc, "qbf", [64, 12, 64], BF16))
            stats = ph12.enter_context(_sb(nc, "stats", [1, 2, 512], F32))
            sq = ph12.enter_context(_sb(nc, "sqs", [128, 512], F32))
            b.op(qg, lambda h: h.dma_start(out=bbt[:, :, :], in_=BT.rearrange("(c p) n -> p c n", p=128)))
            b.inc(qg)
            b.op(qg, lambda h: h.dma_start(out=ct_sb[:, :, :], in_=CT.rearrange("n (c f) -> n c f", f=128)))
            b.inc(qg)
            b.op(dve, lambda h: h.memset(
                sbf.ap().rearrange("p a s c -> p (a s c)"), 0.0))
            b.inc(dve)
            ls = LnScratch(nc, ph12)
            ls.init(b, dve)
            xt0 = ph12.enter_context(_sb(nc, "xt0", [128, NCH, LN_TT], F32))
            for i in range(TOK // LN_TT):
                b.wait(qs, dve)
                b.op(qs, lambda h, i=i: h.dma_start(
                    out=xt0[:, :, :], in_=xin_f(i * LN_TT, LN_TT)))
                ld = b.inc(qs)
                b.wait(pe, qs, ld)
                b.wait(act, qs, ld)
                emit_ln_tile(b, ls, xt0[:, :, :], xln[:, :, i * LN_TT:(i + 1) * LN_TT])
            b.flush()

            pst, pbc = ls.pst, ls.pbc
            for i in range(TOK // 512):
                t0 = i * 512
                b.wait(pe, dve)
                b.wait(act, dve)
                for c in range(NCH):
                    b.op(act, lambda h, c=c, t0=t0: h.activation(
                        sq[:, :], xln[:, c, t0:t0 + 512], AF.Square))
                    a = b.inc(act)
                    b.wait(pe, act, a)
                    b.op(pe, lambda h, c=c: h.matmul(
                        pst[0:1, 0, 0:256].tensor.ap()[0:1, 0:512] if False else pst[0:1, 0:2, :].rearrange("p a t -> p (a t)")[0:1, 0:512],
                        ls.ones_col[:, 0:1], sq[:, :],
                        start=(c == 0), stop=(c == NCH - 1)))
                    m = b.inc(pe)
                    b.wait(act, pe, m)
                b.wait(dve, pe)
                b.op(dve, lambda h: h.tensor_scalar(
                    stats[:, 0, :], pst[0:1, 0:2, :].rearrange("p a t -> p (a t)")[0:1, 0:512],
                    1.0 / DM, 1e-6, ALU.mult, ALU.add))
                d = b.inc(dve)
                b.wait(act, dve, d)
                b.op(act, lambda h: h.activation(stats[:, 0, :], stats[:, 0, :], AF.Ln))
                b.op(act, lambda h: h.activation(stats[:, 0, :], stats[:, 0, :], AF.Exp, scale=-0.5))
                a = b.inc(act)
                b.wait(pe, act, a)
                b.op(pe, lambda h: h.matmul(
                    pbc[:, 0:2, :].rearrange("p a t -> p (a t)")[:, 0:512],
                    ls.ones_row[:, :], stats[:, 0, :], start=True, stop=True))
                m = b.inc(pe)
                b.wait(dve, pe, m)
                for c in range(NCH):
                    b.op(dve, lambda h, c=c, t0=t0: h.scalar_tensor_tensor(
                        m1[:, c, t0:t0 + 512], xln[:, c, t0:t0 + 512], 1.0,
                        pbc[:, 0:2, :].rearrange("p a t -> p (a t)")[:, 0:512],
                        ALU.mult, ALU.mult), dep=(c == 0))
                b.inc(dve)
            b.flush()

            with (
                _sb(nc, "A_sb", [64, 64], F32) as A_sb,
                _sb(nc, "E_sb", [64, 64], F32) as E_sb,
                _sb(nc, "pt_sb", [64, 2, 64], F32) as pt_sb,
                _sb(nc, "st_sb", [64, 64], F32) as st_sb,
                _sb(nc, "q_sb", [64, 2, 64], F32) as q_sb,
                _sb(nc, "i64f", [64, 64], F32) as i64f,
                _sb(nc, "hcol", [64, 1], F32) as hcol,
                _ps(nc, "pex", [128, 512], F32) as pex_t,
            ):
                pex = pex_t[:, :]
                b.op(qs, lambda h: h.dma_start(out=A_sb[:, :], in_=A_in))
                ald = b.inc(qs)
                b.op(pool, lambda h: h.memset(i64f[:, :], 0.0))
                b.op(pool, lambda h: h.affine_select(
                    out=i64f[:, :], in_=i64f[:, :], compare_op=ALU.not_equal,
                    fill=1.0, base=0, pattern=[[-1, 64]], channel_multiplier=1))
                p = b.inc(pool)
                b.wait(pe, pool, p)
                b.wait(dve, pool, p)
                b.wait(act, qs)
                b.op(act, lambda h: h.activation(cols[:, 1, :], cols[:, 1, :], AF.Exp))
                a = b.inc(act)
                b.wait(dve, act, a)
                b.op(dve, lambda h: h.tensor_tensor(cols[:, 4, :], cols[:, 1, :], cols[:, 2, :], ALU.mult))
                b.op(dve, lambda h: h.tensor_tensor(cols[:, 0, :], cols[:, 0, :], cols[:, 2, :], ALU.mult))
                d = b.inc(dve)
                b.wait(dve, qg)
                for c in range(NCH):
                    b.op(dve, lambda h, c=c: h.tensor_scalar_mul(
                        bbt[:, c, :], bbt[:, c, :], cols[:, 4, c:c + 1]))
                b.inc(dve)
                b.wait(pe, act, a)
                b.op(pe, lambda h: h.matmul(pex[0:1, 0:8], ones_col[:, 0:1], cols[:, 1, :],
                                            start=True, stop=True))
                m = b.inc(pe)
                b.wait(dve, pe, m)
                b.op(dve, lambda h: h.reduce_sum(stats[:, 0, 0:1], pex[0:1, 0:8], axis=AX.X))
                b.op(dve, lambda h: h.tensor_scalar_mul(stats[:, 0, 0:1], stats[:, 0, 0:1], 1.0 / DM))
                d = b.inc(dve)
                b.wait(pe, dve, d)
                b.op(pe, lambda h: h.matmul(pex[0:64, 8:9], ones128[0:1, 0:64], stats[:, 0, 0:1],
                                            start=True, stop=True))
                m = b.inc(pe)
                b.wait(dve, pe, m)
                b.op(dve, lambda h: h.tensor_copy(hcol[:, :], pex[0:64, 8:9]))
                b.wait(dve, qs, ald)
                b.op(dve, lambda h: h.tensor_scalar_mul(E_sb[:, :], A_sb[:, :], hcol[:, 0:1]))
                b.op(dve, lambda h: h.tensor_copy(pt_sb[:, 0, :], i64f[:, :]))
                b.op(dve, lambda h: h.tensor_copy(st_sb[:, :], i64f[:, :]))
                d = b.inc(dve)
                for k in range(1, 20):
                    cur, nxt = (k - 1) % 2, k % 2
                    b.wait(pe, dve)
                    b.wait(pe, act)
                    b.op(pe, lambda h, cur=cur: h.matmul(
                        pex[0:64, 64:128], E_sb[:, :], pt_sb[:, cur, :], start=True, stop=True))
                    m = b.inc(pe)
                    b.wait(act, pe, m)
                    b.op(act, lambda h, k=k, nxt=nxt: h.activation(
                        pt_sb[:, nxt, :], pex[0:64, 64:128], AF.Identity, scale=1.0 / k))
                    a = b.inc(act)
                    b.wait(dve, act, a)
                    b.op(dve, lambda h, nxt=nxt: h.tensor_tensor(
                        st_sb[:, :], st_sb[:, :], pt_sb[:, nxt, :], ALU.add))
                    b.inc(dve)
                b.op(dve, lambda h: h.tensor_copy(q_sb[:, 0, :], st_sb[:, :]))
                b.op(dve, lambda h: h.tensor_copy(qbf[:, 0, :], st_sb[:, :]))
                d = b.inc(dve)
                for j in range(1, 11):
                    cur, nxt = (j - 1) % 2, j % 2
                    b.wait(pe, dve)
                    b.op(pe, lambda h, cur=cur: h.transpose(
                        pex[0:64, 128:192], q_sb[:, cur, :], i64f[:, :]))
                    m = b.inc(pe)
                    b.wait(act, pe, m)
                    b.op(act, lambda h: h.activation(st_sb[:, :], pex[0:64, 128:192], AF.Identity))
                    a = b.inc(act)
                    b.wait(pe, act, a)
                    b.op(pe, lambda h, cur=cur: h.matmul(
                        pex[0:64, 64:128], st_sb[:, :], q_sb[:, cur, :], start=True, stop=True))
                    m = b.inc(pe)
                    b.wait(dve, pe, m)
                    b.op(dve, lambda h, nxt=nxt: h.tensor_copy(q_sb[:, nxt, :], pex[0:64, 64:128]))
                    b.op(dve, lambda h, j=j, nxt=nxt: h.tensor_copy(qbf[:, j, :], q_sb[:, nxt, :]))
                    d = b.inc(dve)
                b.wait(pe, dve, d)
                for s_ in range(NSEQ):
                    for i in range(SEQ // 512):
                        t0 = s_ * SEQ + i * 512
                        b.wait(pe, act)
                        for c in range(NCH):
                            b.op(pe, lambda h, c=c, t0=t0: h.matmul(
                                pex[0:64, 0:512], bbt[:, c, :], m1[:, c, t0:t0 + 512],
                                start=(c == 0), stop=(c == NCH - 1)))
                        m = b.inc(pe)
                        b.wait(act, pe, m)
                        b.op(act, lambda h, s_=s_, i=i: h.activation(
                            sbf[:, 0, s_, PADC + i * 512:PADC + (i + 1) * 512],
                            pex[0:64, 0:512], AF.Identity))
                        b.inc(act)
                b.wait(act, dve)
                for j in range(11):
                    src, dst = j % 2, (j + 1) % 2
                    shift = 1 << j
                    prev_mark = b.act.cnt
                    b.wait(pe, act, prev_mark)
                    for s_ in range(NSEQ):
                        for i in range(SEQ // 512):
                            c0 = PADC + i * 512
                            b.wait(pe, act)
                            b.op(pe, lambda h, j=j, src=src, s_=s_, c0=c0, shift=shift: h.matmul(
                                pex[0:64, 0:512], qbf[:, j, :],
                                sbf[:, src, s_, c0 - shift:c0 - shift + 512],
                                start=True, stop=False))
                            b.op(pe, lambda h, src=src, s_=s_, c0=c0: h.matmul(
                                pex[0:64, 0:512], i64bf[:, :],
                                sbf[:, src, s_, c0:c0 + 512],
                                start=False, stop=True))
                            m = b.inc(pe)
                            b.wait(act, pe, m)
                            b.op(act, lambda h, dst=dst, s_=s_, c0=c0: h.activation(
                                sbf[:, dst, s_, c0:c0 + 512], pex[0:64, 0:512], AF.Identity))
                            b.inc(act)
                fin = 11 % 2
                b.wait(pe, act)
                for s_ in range(NSEQ):
                    for i in range(SEQ // 512):
                        t0 = s_ * SEQ + i * 512
                        c0 = PADC + i * 512
                        for fc in range(NCH):
                            b.wait(pe, dve)
                            b.op(pe, lambda h, fc=fc, fin=fin, s_=s_, c0=c0: h.matmul(
                                pex[:, 0:512], ct_sb[:, fc, :],
                                sbf[:, fin, s_, c0:c0 + 512],
                                start=True, stop=True))
                            m = b.inc(pe)
                            b.wait(dve, pe, m)
                            b.op(dve, lambda h, fc=fc, t0=t0: h.scalar_tensor_tensor(
                                sq[:, :], m1[:, fc, t0:t0 + 512], cols[:, 0, fc:fc + 1],
                                xln[:, fc, t0:t0 + 512], ALU.mult, ALU.add))
                            b.op(dve, lambda h, fc=fc, t0=t0: h.tensor_tensor(
                                m1[:, fc, t0:t0 + 512], sq[:, :], pex[:, 0:512], ALU.add))
                            b.inc(dve)
                b.flush()

            for i in range(TOK // 512):
                t0 = i * 512
                b.wait(pe, dve)
                b.wait(act, dve)
                for c in range(NCH):
                    b.op(act, lambda h, c=c, t0=t0: h.activation(
                        sq[:, :], m1[:, c, t0:t0 + 512], AF.Square))
                    a = b.inc(act)
                    b.wait(pe, act, a)
                    b.op(pe, lambda h, c=c: h.matmul(
                        pst[0:1, 0:2, :].rearrange("p a t -> p (a t)")[0:1, 0:512],
                        ls.ones_col[:, 0:1], sq[:, :],
                        start=(c == 0), stop=(c == NCH - 1)))
                    m = b.inc(pe)
                    b.wait(act, pe, m)
                b.wait(dve, pe)
                b.op(dve, lambda h: h.tensor_scalar(
                    stats[:, 0, :], pst[0:1, 0:2, :].rearrange("p a t -> p (a t)")[0:1, 0:512],
                    1.0 / DM, 1e-6, ALU.mult, ALU.add))
                d = b.inc(dve)
                b.wait(act, dve, d)
                b.op(act, lambda h: h.activation(stats[:, 0, :], stats[:, 0, :], AF.Ln))
                b.op(act, lambda h: h.activation(stats[:, 0, :], stats[:, 0, :], AF.Exp, scale=-0.5))
                a = b.inc(act)
                b.wait(pe, act, a)
                b.op(pe, lambda h: h.matmul(
                    pbc[:, 0:2, :].rearrange("p a t -> p (a t)")[:, 0:512],
                    ls.ones_row[:, :], stats[:, 0, :], start=True, stop=True))
                m = b.inc(pe)
                b.wait(dve, pe, m)
                for c in range(NCH):
                    b.op(dve, lambda h, c=c, t0=t0: h.scalar_tensor_tensor(
                        sq[:, :], m1[:, c, t0:t0 + 512], 1.0,
                        pbc[:, 0:2, :].rearrange("p a t -> p (a t)")[:, 0:512],
                        ALU.mult, ALU.mult), dep=True)
                    b.op(dve, lambda h, c=c, t0=t0: h.tensor_scalar_mul(
                        xln[:, c, t0:t0 + 512], sq[:, :], cols[:, 3, c:c + 1]))
                b.inc(dve)
            b.flush()

        with (
            _sb(nc, "sgu", [128, 32, 512], BF16) as sgu,
            _sb(nc, "wb3", [128, 2, NCH, 256], BF16) as wb3,
            _sb(nc, "wd3", [128, 2, 512], BF16) as wd3,
            _sb(nc, "xm", [128, NCH, 512], F32) as xm,
            _sb(nc, "ob", [128, NCH, 512], F32) as ob,
            _sb(nc, "sg_tmp", [128, 512], F32) as sg_tmp,
            _ps(nc, "pgu", [128, 2, 512], F32) as pgu_t,
            _ps(nc, "pdn", [128, 4, 512], F32) as pdn_t,
        ):
            pgu, pdn = pgu_t[:, :, :], pdn_t[:, :, :]
            store_mark = 0
            for tt in range(TOK // 512):
                t0 = tt * 512
                b.wait(qs, dve)
                b.op(qs, lambda h, t0=t0: h.dma_start(
                    out=xm[:, :, :], in_=xin_f(t0, 512)))
                xld = b.inc(qs)
                b.wait(dve, qs, xld)
                for c in range(NCH):
                    b.op(dve, lambda h, c=c, t0=t0: h.tensor_tensor(
                        xm[:, c, :], xm[:, c, :], m1[:, c, t0:t0 + 512], ALU.add), dep=(c == 0))
                xm_mark = b.inc(dve)
                for ii in range(32):
                    ib = ii % 2
                    b.wait(qw if ib == 0 else qw2, pe)
                    q_ = qw if ib == 0 else qw2
                    b.op(q_, lambda h, ii=ii, ib=ib: h.dma_start(
                        out=wb3[:, ib, :, 0:128],
                        in_=wg_bf.rearrange("(k p) f -> p k f", p=128)[:, :, ii * 128:(ii + 1) * 128]))
                    b.inc(q_)
                    b.op(q_, lambda h, ii=ii, ib=ib: h.dma_start(
                        out=wb3[:, ib, :, 128:256],
                        in_=wu_bf.rearrange("(k p) f -> p k f", p=128)[:, :, ii * 128:(ii + 1) * 128]))
                    wld = b.inc(q_)
                    b.wait(pe, q_, wld)
                    b.wait(pe, act)
                    b.wait(pe, dve)
                    for k in range(NCH):
                        b.op(pe, lambda h, k=k, ib=ib, t0=t0: h.matmul(
                            pgu[:, 0, :], wb3[:, ib, k, 0:128], xln[:, k, t0:t0 + 512],
                            start=(k == 0), stop=(k == NCH - 1)))
                    for k in range(NCH):
                        b.op(pe, lambda h, k=k, ib=ib, t0=t0: h.matmul(
                            pgu[:, 1, :], wb3[:, ib, k, 128:256], xln[:, k, t0:t0 + 512],
                            start=(k == 0), stop=(k == NCH - 1)))
                    m = b.inc(pe)
                    b.wait(act, pe, m)
                    b.wait(act, dve)
                    b.op(act, lambda h: h.activation(sg_tmp[:, :], pgu[:, 0, :], AF.Silu))
                    a = b.inc(act)
                    b.wait(dve, act, a)
                    b.op(dve, lambda h, ii=ii: h.tensor_tensor(
                        sgu[:, ii, :], sg_tmp[:, :], pgu[:, 1, :], ALU.mult))
                    b.inc(dve)
                sgu_mark = b.dve.cnt
                for fh in range(2):
                    f0 = fh * 4
                    for ii in range(32):
                        ib = ii % 2
                        q_ = qw if ib == 0 else qw2
                        b.wait(q_, pe)
                        b.op(q_, lambda h, ii=ii, ib=ib, f0=f0: h.dma_start(
                            out=wd3[:, ib, :],
                            in_=wd_bf.rearrange("(i p) f -> p i f", p=128)[:, ii, f0 * 128:f0 * 128 + 512]))
                        wld = b.inc(q_)
                        b.wait(pe, q_, wld)
                        if ii == 0:
                            b.wait(pe, dve)
                        for f in range(4):
                            b.op(pe, lambda h, f=f, ii=ii, ib=ib: h.matmul(
                                pdn[:, f, :], wd3[:, ib, f * 128:(f + 1) * 128], sgu[:, ii, :],
                                start=(ii == 0), stop=(ii == 31)))
                        b.inc(pe)
                    m = b.pe.cnt
                    b.wait(dve, pe, m)
                    b.wait(dve, qss, store_mark)
                    for f in range(4):
                        b.op(dve, lambda h, f=f, f0=f0: h.scalar_tensor_tensor(
                            ob[:, f0 + f, :], pdn[:, f, :], 1.0, xm[:, f0 + f, :],
                            ALU.mult, ALU.add), dep=(f == 0))
                    b.inc(dve)
                b.wait(qss, dve)
                b.op(qss, lambda h, t0=t0: h.dma_start(
                    out=xout_f(t0, 512), in_=ob[:, :, :]))
                store_mark = b.inc(qss)
            b.op(pool, lambda h, v=b.qss.cnt: h.wait_ge(b.qss.sem, v))
            b.inc(pool)
            b.flush()


def attn1_weight_map(I):
    return dict(
        A=_c(I["L1_attn_A"]),
        BT=_c(I["L1_attn_B"].T),
        CT=_c(I["L1_attn_C"].T),
        Dcol=_c(I["L1_attn_D"].reshape(NCH, 128).T),
        ldt_col=_c(I["L1_attn_log_dt"].reshape(NCH, 128).T),
        ln1_col=_c(I["L1_attn_ln1_w"].reshape(NCH, 128).T),
        ln2_col=_c(I["L1_attn_ln2_w"].reshape(NCH, 128).T),
        wgT=_c(I["L1_attn_w_gate"].T),
        wuT=_c(I["L1_attn_w_up"].T),
        wdT=_c(I["L1_attn_w_down"].T))


def run_attn1(x_fm, I):
    nc = build_attn1()
    wmap = attn1_weight_map(I)
    in_maps = []
    for j in range(NCORES):
        xs = _c(x_fm[:, 2 * j:2 * j + 2, :].reshape(DM, TOK))
        in_maps.append(dict(x_in=xs, **wmap))
    res = run_bass_kernel_spmd(nc, in_maps, core_ids=list(range(NCORES)))
    out = np.empty((DM, 16, 2048), np.float32)
    for j in range(NCORES):
        out[:, 2 * j:2 * j + 2, :] = res.results[j]["x_out"].reshape(DM, 2, SEQ)
    return out


DIN = 2048
NH = 32
HDIM = 64
DST = 128
CK = 64
CONVD = 2304
NCONV = CONVD // 128


def emit_attn0(nc, b, W, xin_f, xout_f, first=True):
    DBG = False
    win, convd, convb = W["win"], W["convd"], W["convb"]
    dtb, alog, dcol, nwcol = W["dtb"], W["alog"], W["dcol"], W["nwcol"]
    wout, e32, wout_bf = W["wout"], W["e32"], W["wout_bf"]
    with ExitStack() as stack:
        e32_sb = stack.enter_context(_sb(nc, "e32_sb", [NH, DIN], BF16))
        tri = stack.enter_context(_sb(nc, "tri", [128, 64], F32))
        ones_col = stack.enter_context(_sb(nc, "ones_c", [128, 1], F32))
        ones_row = stack.enter_context(_sb(nc, "ones_r", [1, 128], F32))
        orow_bf = stack.enter_context(_sb(nc, "orow_bf", [1, 128], BF16))
        nrow_bf = stack.enter_context(_sb(nc, "nrow_bf", [1, 128], BF16))
        identF = stack.enter_context(_sb(nc, "identF", [128, 128], F32))
        identB = stack.enter_context(_sb(nc, "identB", [128, 128], BF16))
        cols = stack.enter_context(_sb(nc, "colsA", [128, 3, 18], F32))
        hcols = stack.enter_context(_sb(nc, "hcolsA", [NH, 2], F32))
        dt_bf = stack.enter_context(_sb(nc, "dt_bf", [NH, SEQ], BF16))
        cs_hi = stack.enter_context(_sb(nc, "cs_hi", [NH, SEQ], BF16))
        cs_lo = stack.enter_context(_sb(nc, "cs_lo", [NH, SEQ], BF16))
        dtdd = stack.enter_context(_sb(nc, "dtdd", [NH, SEQ], BF16))
        gam = stack.enter_context(_sb(nc, "gam", [NH, 32], BF16))
        grow = stack.enter_context(_sb(nc, "grow", [1, DIN], BF16))
        edrow = stack.enter_context(_sb(nc, "edrow", [1, DIN], F32))
        edrow_bf = stack.enter_context(_sb(nc, "edrow_bf", [1, DIN], BF16))
        rowbuf = stack.enter_context(_sb(nc, "rowbuf", [1, 2, 2, DIN], BF16))
        Ssb = stack.enter_context(_sb(nc, "Ssb", [128, 2, DIN], BF16))
        Stmp = stack.enter_context(_sb(nc, "Stmp", [128, DIN], BF16))
        Msb = stack.enter_context(_sb(nc, "Msb", [64, DIN], BF16))
        L2 = stack.enter_context(_sb(nc, "L2", [64, DIN], BF16))
        cbt = stack.enter_context(_sb(nc, "cbt", [64, 64], BF16))
        cdec = stack.enter_context(_sb(nc, "cdec", [128, DIN], BF16))
        xdt = stack.enter_context(_sb(nc, "xdt", [64, DIN], BF16))
        xdd = stack.enter_context(_sb(nc, "xdd", [64, DIN], BF16))
        xbc = stack.enter_context(_sb(nc, "xbc", [128, NCONV, 515], BF16))
        xbcs = stack.enter_context(_sb(nc, "xbcs", [128, 16, 512], BF16))
        bc_fm = stack.enter_context(_sb(nc, "bc_fm", [128, 2, 512], BF16))
        btok = stack.enter_context(_sb(nc, "btok", [64, 128], BF16))
        x_tok = stack.enter_context(_sb(nc, "x_tok", [64, DIN], BF16))
        zs = stack.enter_context(_sb(nc, "zs", [128, 2, 512], BF16))
        y_fm = stack.enter_context(_sb(nc, "y_fm", [128, 16, 512], BF16))
        sqa = stack.enter_context(_sb(nc, "sqa", [128, 512], F32))
        sqb = stack.enter_context(_sb(nc, "sqb", [128, 512], F32))
        st2 = stack.enter_context(_sb(nc, "st2", [1, 2, 512], F32))
        ob = stack.enter_context(_sb(nc, "obA", [128, 2, 512], F32))
        wpb = stack.enter_context(_sb(nc, "wpb", [128, 2, NCH, 512], BF16))
        cvd = stack.enter_context(_sb(nc, "cvd", [128, 2, 4, 128], BF16))
        xt0 = stack.enter_context(_sb(nc, "xt0A", [128, NCH, 512], F32))
        xlnt = stack.enter_context(_sb(nc, "xlnt", [128, NCH, 512], BF16))
        csf = stack.enter_context(_sb(nc, "csf", [NH, 64], F32))
        pA_t = stack.enter_context(_ps(nc, "pA", [128, 4, 512], F32))
        pS_t = stack.enter_context(_ps(nc, "pS", [128, 4, 512], F32))
        pA = pA_t[:, :, :]
        pS = pS_t[:, :, :]
        pAfull = pA_t.ap().rearrange("p a t -> p (a t)")
        pSfull = pS_t.ap().rearrange("p a t -> p (a t)")

        qs, qss, qw, qw2 = b.qs, b.qss, b.qw, b.qw2
        qg, qg2, pe, act, dve, pool = b.qg, b.qg2, b.pe, b.act, b.dve, b.pool
        wsrc = win.rearrange("(k p) f -> p k f", p=128)

        ls = LnScratch(nc, stack,
                       pS_t[0:1, 3, :].rearrange("p (a t) -> p a t", a=2),
                       pS_t[:, 2, :].rearrange("p (a t) -> p a t", a=2))

        b.op(qs, lambda h: h.dma_start(out=cols[:, 0, :], in_=convb))
        b.inc(qs)
        b.op(qs, lambda h: h.dma_start(out=cols[:, 1, 0:16], in_=dcol))
        b.inc(qs)
        b.op(qs, lambda h: h.dma_start(out=cols[:, 2, 0:16], in_=nwcol))
        b.inc(qs)
        b.op(qs, lambda h: h.dma_start(out=hcols[:, 0:1], in_=dtb))
        b.inc(qs)
        b.op(qs, lambda h: h.dma_start(out=hcols[:, 1:2], in_=alog))
        b.inc(qs)
        b.op(qg, lambda h: h.dma_start(out=e32_sb[:, :], in_=e32))
        b.inc(qg)
        if first:
            b.op(qg, lambda h: h.dma_start(out=wout_bf, in_=wout))
            b.inc(qg)
        b.op(dve, lambda h: h.memset(ones_col[:, :], 1.0))
        b.op(dve, lambda h: h.memset(ones_row[:, :], 1.0))
        b.op(dve, lambda h: h.memset(orow_bf[:, :], 1.0))
        b.op(dve, lambda h: h.memset(nrow_bf[:, :], -1.0))
        b.op(dve, lambda h: h.memset(Ssb[:, 0, :], 0.0))
        b.inc(dve)
        for idt in ("identF", "identB"):
            t_ = {"identF": identF, "identB": identB}[idt]
            b.op(pool, lambda h, t_=t_: h.memset(t_[:, :], 0.0))
            b.op(pool, lambda h, t_=t_: h.affine_select(
                out=t_[:, :], in_=t_[:, :], compare_op=ALU.not_equal,
                fill=1.0, base=0, pattern=[[-1, 128]], channel_multiplier=1))
        b.op(pool, lambda h: h.memset(tri[:, :], 0.0))
        b.op(pool, lambda h: h.affine_select(
            out=tri[0:64, :], in_=tri[0:64, :], compare_op=ALU.is_gt,
            fill=1.0, base=0, pattern=[[-1, 64]], channel_multiplier=1))
        b.op(pool, lambda h: h.affine_select(
            out=tri[64:128, :], in_=tri[64:128, :], compare_op=ALU.is_gt,
            fill=1.0, base=0, pattern=[[-1, 64]], channel_multiplier=1))
        pm = b.inc(pool)
        b.wait(pe, pool, pm)
        b.wait(dve, pool, pm)
        ls.init(b, dve)
        b.wait(act, qs)
        b.op(act, lambda h: h.activation(hcols[:, 1:2], hcols[:, 1:2], AF.Exp))
        a = b.inc(act)
        b.wait(dve, act, a)
        b.op(dve, lambda h: h.tensor_scalar_mul(hcols[:, 1:2], hcols[:, 1:2], -1.0))
        b.inc(dve)
        b.flush()

        wdt_piece = 8

        for s_ in range(NSEQ):
            b.wait(qg, pe)
            b.op(qg, lambda h: h.dma_start(
                out=wpb[:, 0, :, 0:NH], in_=wsrc[:, :, 4352:4384]))
            wld = b.inc(qg)
            for i in range(SEQ // 512):
                t0 = i * 512
                g0 = s_ * SEQ + t0
                b.wait(qs, dve)
                b.wait(qs, pe)
                b.op(qs, lambda h, g0=g0: h.dma_start(
                    out=xt0[:, :, :], in_=xin_f(g0, 512)))
                ld = b.inc(qs)
                b.wait(pe, qs, ld)
                b.wait(act, qs, ld)
                emit_ln_tile(b, ls, xt0[:, :, 0:256], xlnt[:, :, 0:256])
                emit_ln_tile(b, ls, xt0[:, :, 256:512], xlnt[:, :, 256:512])
                b.wait(pe, qg, wld)
                b.wait(pe, dve)
                for k in range(NCH):
                    b.op(pe, lambda h, k=k: h.matmul(
                        pA[0:NH, 0, :], wpb[:, 0, k, 0:NH], xlnt[:, k, :],
                        start=(k == 0), stop=(k == NCH - 1)))
                m = b.inc(pe)
                b.wait(act, pe, m)
                b.op(act, lambda h: h.activation(
                    sqa[0:NH, :], pA[0:NH, 0, :], AF.Identity, bias=hcols[:, 0:1]))
                a = b.inc(act)
                b.op(act, lambda h: h.activation(sqb[0:NH, :], sqa[0:NH, :], AF.Abs))
                a = b.inc(act)
                b.op(act, lambda h: h.activation(sqb[0:NH, :], sqb[0:NH, :], AF.Exp, scale=-1.0))
                a = b.inc(act)
                b.wait(dve, act, a)
                b.op(dve, lambda h: h.tensor_scalar_add(sqb[0:NH, :], sqb[0:NH, :], 1.0))
                d = b.inc(dve)
                b.wait(act, dve, d)
                b.op(act, lambda h: h.activation(sqb[0:NH, :], sqb[0:NH, :], AF.Ln))
                a = b.inc(act)
                b.wait(dve, act, a)
                b.op(dve, lambda h: h.tensor_single_scalar(
                    sqa[0:NH, :], sqa[0:NH, :], 0.0, ALU.max))
                b.op(dve, lambda h: h.tensor_tensor(
                    sqa[0:NH, :], sqa[0:NH, :], sqb[0:NH, :], ALU.add))
                b.op(dve, lambda h, t0=t0: h.tensor_copy(dt_bf[:, t0:t0 + 512], sqa[0:NH, :]))
                b.op(dve, lambda h: h.tensor_scalar_mul(sqb[0:NH, :], sqa[0:NH, :], hcols[:, 1:2]))
                d = b.inc(dve)
                for sub in range(4):
                    l0 = sub * 128
                    b.wait(pe, dve, d)
                    b.op(pe, lambda h, l0=l0: h.transpose(
                        pA[0:128, 2, 0:NH], sqb[0:NH, l0:l0 + 128], identF[0:NH, 0:NH]))
                    m = b.inc(pe)
                    b.wait(act, pe, m)
                    b.op(act, lambda h: h.activation(sqa[:, 0:NH], pA[0:NH, 2, 0:128], AF.Identity, apply_transpose=True) if False else h.activation(sqb[:, 448:480] if False else sqa[:, 0:32], pA[:, 2, 0:NH] if False else pA[0:128, 2, 0:NH], AF.Identity))
                    a = b.inc(act)
                    b.wait(pe, act, a)
                    b.op(pe, lambda h: h.matmul(
                        pA[0:64, 3, 0:NH], tri[0:64, :], sqa[0:64, 0:NH],
                        start=True, stop=True))
                    b.op(pe, lambda h: h.matmul(
                        pA[64:128, 3, 0:NH], tri[64:128, :], sqa[64:128, 0:NH],
                        start=True, stop=True))
                    m = b.inc(pe)
                    b.wait(act, pe, m)
                    b.op(act, lambda h: h.activation(sqb[:, 0:NH], pA[:, 3, 0:NH], AF.Identity))
                    a = b.inc(act)
                    b.wait(pe, act, a)
                    b.op(pe, lambda h: h.transpose(
                        pA[0:NH, 2, 128:256], sqb[:, 0:NH], identF[:, :]))
                    m = b.inc(pe)
                    b.wait(dve, pe, m)
                    b.op(dve, lambda h, t0=t0, l0=l0: h.tensor_copy(
                        cs_hi[:, t0 + l0:t0 + l0 + 128], pA[0:NH, 2, 128:256]))
                    b.op(dve, lambda h, t0=t0, l0=l0: h.tensor_tensor(
                        cs_lo[:, t0 + l0:t0 + l0 + 128], pA[0:NH, 2, 128:256],
                        cs_hi[:, t0 + l0:t0 + l0 + 128], ALU.subtract))
                    d = b.inc(dve)
            b.op(dve, lambda h: h.tensor_tensor(
                sqa[0:NH, 0:32], bass.AP(cs_hi, 63, [[SEQ, NH], [64, 32]]),
                bass.AP(cs_lo, 63, [[SEQ, NH], [64, 32]]), ALU.add))
            d = b.inc(dve)
            b.wait(act, dve, d)
            b.op(act, lambda h: h.activation(gam[:, :], sqa[0:NH, 0:32], AF.Exp))
            ga = b.inc(act)
            for c in range(32):
                cl0 = c * 64
                b.op(dve, lambda h, cl0=cl0: h.tensor_tensor(
                    csf[:, :], cs_hi[:, cl0:cl0 + 64], cs_lo[:, cl0:cl0 + 64], ALU.add))
                d = b.inc(dve)
                b.wait(act, dve, d)
                b.op(act, lambda h: h.activation(
                    csf[:, :], csf[:, :], AF.Exp, scale=-1.0, bias=csf[:, 63:64]))
                a = b.inc(act)
                b.wait(dve, act, a)
                b.op(dve, lambda h, cl0=cl0: h.tensor_tensor(
                    dtdd[:, cl0:cl0 + 64], dt_bf[:, cl0:cl0 + 64], csf[:, :], ALU.mult))
                b.inc(dve)
            if DBG and s_ == 0:
                b.wait(qg, dve)
                b.op(qg, lambda h: h.dma_start(out=dbg_dt, in_=dt_bf[:, :]))
                b.inc(qg)
                b.op(qg, lambda h: h.dma_start(out=dbg_cs[0], in_=cs_hi[:, :]))
                b.inc(qg)
                b.op(qg, lambda h: h.dma_start(out=dbg_cs[1], in_=cs_lo[:, :]))
                b.inc(qg)
            b.flush()

            for i in range(SEQ // 512):
                t0g = s_ * SEQ + i * 512
                b.wait(qs, dve)
                b.wait(qs, pe)
                b.op(qs, lambda h, t0g=t0g: h.dma_start(
                    out=xt0[:, :, :], in_=xin_f(t0g, 512)))
                ld = b.inc(qs)
                b.wait(pe, qs, ld)
                b.wait(act, qs, ld)
                emit_ln_tile(b, ls, xt0[:, :, 0:256], xlnt[:, :, 0:256])
                emit_ln_tile(b, ls, xt0[:, :, 256:512], xlnt[:, :, 256:512])
                b.wait(pe, dve)
                for fo in range(4, 9):
                    fp = fo % 2
                    q_ = qg if fp == 0 else qg2
                    width = 512 if fo < 8 else 256
                    b.wait(q_, pe)
                    b.op(q_, lambda h, fo=fo, fp=fp, width=width: h.dma_start(
                        out=wpb[:, fp, :, 0:width],
                        in_=wsrc[:, :, fo * 512:fo * 512 + width]))
                    wld = b.inc(q_)
                    b.wait(pe, q_, wld)
                    nfc = width // 128
                    for fc in range(nfc):
                        pb_ = (fo * 4 + fc) % 2
                        b.wait(pe, act)
                        for k in range(NCH):
                            b.op(pe, lambda h, k=k, fc=fc, fp=fp, pb_=pb_: h.matmul(
                                pA[:, pb_, :], wpb[:, fp, k, fc * 128:(fc + 1) * 128],
                                xlnt[:, k, :], start=(k == 0), stop=(k == NCH - 1)))
                        m = b.inc(pe)
                        b.wait(act, pe, m)
                        cglob = (fo - 4) * 4 + fc
                        b.op(act, lambda h, cglob=cglob, pb_=pb_: h.activation(
                            xbc[:, cglob, 3:515], pA[:, pb_, :], AF.Identity))
                        b.inc(act)
                if i == 0:
                    b.wait(dve, act)
                    b.op(dve, lambda h: h.memset(xbc[:, :, 0:3], 0.0))
                    b.inc(dve)
                b.wait(act, dve)
                b.wait(pe, dve)
                for c in range(NCONV):
                    cp_ = c % 2
                    q_ = qg if cp_ == 0 else qg2
                    b.wait(q_, pe)
                    b.op(q_, lambda h, c=c, cp_=cp_: h.dma_start(
                        out=cvd[:, cp_, :, :],
                        in_=convd[:, c].rearrange("j pp q -> pp j q")))
                    wld = b.inc(q_)
                    b.wait(pe, q_, wld)
                    b.wait(pe, act)
                    for j in range(4):
                        b.op(pe, lambda h, c=c, j=j, cp_=cp_: h.matmul(
                            pA[:, 2 + (c % 2) * 0, :] if False else pA[:, 2, :],
                            cvd[:, cp_, j, :], xbc[:, c, j:j + 512],
                            start=(j == 0), stop=(j == 3)))
                    m = b.inc(pe)
                    b.wait(act, pe, m)
                    if c < 16:
                        b.op(act, lambda h, c=c: h.activation(
                            xbcs[:, c, :], pA[:, 2, :], AF.Silu, bias=cols[:, 0, c:c + 1]))
                    else:
                        b.op(act, lambda h, c=c: h.activation(
                            bc_fm[:, c - 16, :], pA[:, 2, :], AF.Silu, bias=cols[:, 0, c:c + 1]))
                    b.inc(act)
                b.wait(dve, act)
                b.op(dve, lambda h: h.tensor_copy(
                    xbc[:, :, 0:3], xbc[:, :, 512:515]))
                b.inc(dve)
                if DBG and s_ == 0 and i == 0:
                    b.wait(qg, act)
                    b.op(qg, lambda h: h.dma_start(out=dbg_xbcs, in_=xbcs[:, :, :]))
                    b.inc(qg)
                    b.op(qg, lambda h: h.dma_start(out=dbg_bc, in_=bc_fm[:, :, :]))
                    b.inc(qg)
                for cp in range(4):
                    l0 = cp * 128
                    for j in range(2):
                        cseq = (i * 8 + cp * 2 + j)
                        q_ = qw if j == 0 else qw2
                        b.wait(q_, pe)
                        b.op(q_, lambda h, j=j, cseq=cseq: h.dma_start(
                            out=rowbuf[0:1, j, 0, :],
                            in_=cs_hi[:, cseq * 64:(cseq + 1) * 64]))
                        b.inc(q_)
                        b.op(q_, lambda h, j=j, cseq=cseq: h.dma_start(
                            out=rowbuf[0:1, j, 1, :],
                            in_=cs_lo[:, cseq * 64:(cseq + 1) * 64]))
                        rw = b.inc(q_)
                        b.wait(pe, q_, rw)
                    for j in range(2):
                        cseq = i * 8 + cp * 2 + j
                        jl0 = l0 + j * 64
                        base = 0
                        b.wait(pe, dve)
                        b.wait(pe, act)
                        for xc in range(16):
                            b.op(pe, lambda h, xc=xc, jl0=jl0: h.transpose(
                                pA_t.ap().bitcast(BF16)[0:64, 3, 0:128],
                                xbcs[:, xc, jl0:jl0 + 64], identB[:, :]))
                            m = b.inc(pe)
                            b.wait(act, pe, m)
                            b.op(act, lambda h, xc=xc: h.activation(
                                x_tok[:, xc * 128:(xc + 1) * 128],
                                pA_t.ap().bitcast(BF16)[0:64, 3, 0:128], AF.Identity))
                            a = b.inc(act)
                            b.wait(pe, act, a)
                        b.op(pe, lambda h, jl0=jl0: h.transpose(
                            pA_t.ap().bitcast(BF16)[0:64, 3, 0:128],
                            bc_fm[:, 0, jl0:jl0 + 64], identB[:, :]))
                        m = b.inc(pe)
                        b.wait(act, pe, m)
                        b.op(act, lambda h: h.activation(
                            btok[:, :], pA_t.ap().bitcast(BF16)[0:64, 3, 0:128], AF.Identity))
                        a = b.inc(act)
                        b.wait(pe, act, a)
                        b.wait(pe, dve)
                        b.op(pe, lambda h, jl0=jl0: h.matmul(
                            pS[0:64, 0, 0:64],
                            bc_fm[:, 0, jl0:jl0 + 64], bc_fm[:, 1, jl0:jl0 + 64],
                            start=True, stop=True))
                        m = b.inc(pe)
                        b.wait(dve, pe, m)
                        b.op(dve, lambda h: h.tensor_tensor(
                            cbt[:, :], pS[0:64, 0, 0:64], tri[0:64, :], ALU.mult))
                        dcb = b.inc(dve)
                        b.wait(pe, dve, dcb)
                        for q4 in range(4):
                            sl = slice(q4 * 512, (q4 + 1) * 512)
                            b.op(pe, lambda h, cseq=cseq, q4=q4, sl=sl: h.matmul(
                                pAfull[0:64, sl], cs_hi[:, cseq * 64:(cseq + 1) * 64],
                                e32_sb[:, sl], start=True, stop=False))
                            b.op(pe, lambda h, cseq=cseq, sl=sl: h.matmul(
                                pAfull[0:64, sl], cs_lo[:, cseq * 64:(cseq + 1) * 64],
                                e32_sb[:, sl], start=False, stop=False))
                            b.op(pe, lambda h, j=j, sl=sl: h.matmul(
                                pAfull[0:64, sl], nrow_bf[0:1, 0:64],
                                rowbuf[0:1, j, 0, sl], start=False, stop=False))
                            b.op(pe, lambda h, j=j, sl=sl: h.matmul(
                                pAfull[0:64, sl], nrow_bf[0:1, 0:64],
                                rowbuf[0:1, j, 1, sl], start=False, stop=True))
                        m = b.inc(pe)
                        b.wait(dve, pe, m)
                        b.op(dve, lambda h: h.tensor_single_scalar(
                            pAfull[0:64, :], pAfull[0:64, :], -30.0, ALU.max))
                        d = b.inc(dve)
                        b.wait(act, dve, d)
                        b.op(act, lambda h: h.activation(L2[:, :], pAfull[0:64, :], AF.Exp, scale=-1.0))
                        a = b.inc(act)
                        b.wait(dve, act, a)
                        b.op(dve, lambda h: h.tensor_tensor(
                            Msb[:, :], L2[:, :], bass.AP(cbt, 0, [[64, 64], [0, NH], [1, 64]]), ALU.mult))
                        dm = b.inc(dve)
                        b.wait(pe, dve, dm)
                        for q4 in range(4):
                            sl = slice(q4 * 512, (q4 + 1) * 512)
                            b.op(pe, lambda h, cseq=cseq, sl=sl: h.matmul(
                                pAfull[0:64, sl], dt_bf[:, cseq * 64:(cseq + 1) * 64],
                                e32_sb[:, sl], start=True, stop=True))
                        m = b.inc(pe)
                        b.wait(dve, pe, m)
                        b.op(dve, lambda h: h.scalar_tensor_tensor(
                            xdt[:, :], pAfull[0:64, :], 1.0,
                            x_tok[:, :], ALU.mult, ALU.mult))
                        dxdt = b.inc(dve)
                        b.wait(pe, dve, dxdt)
                        for q4 in range(4):
                            sl = slice(q4 * 512, (q4 + 1) * 512)
                            b.op(pe, lambda h, cseq=cseq, sl=sl: h.matmul(
                                pAfull[0:64, sl], dtdd[:, cseq * 64:(cseq + 1) * 64],
                                e32_sb[:, sl], start=True, stop=True))
                        m = b.inc(pe)
                        b.wait(dve, pe, m)
                        b.op(dve, lambda h: h.scalar_tensor_tensor(
                            xdd[:, :], pAfull[0:64, :], 1.0,
                            x_tok[:, :], ALU.mult, ALU.mult))
                        dxdd = b.inc(dve)
                        b.op(dve, lambda h, j=j: h.tensor_tensor(
                            edrow[0:1, :], rowbuf[0:1, j, 0, :], rowbuf[0:1, j, 1, :], ALU.add))
                        d = b.inc(dve)
                        b.wait(act, dve, d)
                        b.op(act, lambda h: h.activation(edrow_bf[0:1, :], edrow[0:1, :], AF.Exp))
                        a = b.inc(act)
                        b.wait(pe, act, a)
                        b.wait(pe, dve, dxdd)
                        for q4 in range(4):
                            sl = slice(q4 * 512, (q4 + 1) * 512)
                            b.op(pe, lambda h, sl=sl: h.matmul(
                                pAfull[:, sl], orow_bf[:, :], edrow_bf[0:1, sl],
                                start=True, stop=True))
                        m = b.inc(pe)
                        b.wait(dve, pe, m)
                        b.op(dve, lambda h, jl0=jl0: h.tensor_tensor(
                            cdec[:, :], pAfull[:, :],
                            bass.AP(bc_fm, 512 + jl0, [[1024, 128], [0, NH], [1, 64]]),
                            ALU.mult))
                        dcd = b.inc(dve)
                        cur = (s_ * 32 + cseq) % 2
                        nxt = 1 - cur
                        b.wait(pe, act, ga)
                        b.wait(pe, dve)
                        for q4 in range(4):
                            sl = slice(q4 * 512, (q4 + 1) * 512)
                            b.op(pe, lambda h, cseq=cseq, sl=sl: h.matmul(
                                pSfull[0:1, sl], gam[:, cseq:cseq + 1], e32_sb[:, sl],
                                start=True, stop=True))
                        m = b.inc(pe)
                        b.wait(dve, pe, m)
                        b.op(dve, lambda h: h.tensor_copy(grow[0:1, :], pSfull[0:1, 0:2048]))
                        d = b.inc(dve)
                        b.wait(pe, dve, d)
                        for q4 in range(4):
                            sl = slice(q4 * 512, (q4 + 1) * 512)
                            b.op(pe, lambda h, sl=sl: h.matmul(
                                pSfull[:, sl], orow_bf[:, :], grow[0:1, sl],
                                start=True, stop=True))
                        m = b.inc(pe)
                        b.wait(dve, pe, m)
                        b.op(dve, lambda h, cur=cur: h.tensor_tensor(
                            Stmp[:, :], pSfull[:, :], Ssb[:, cur, :], ALU.mult))
                        dstmp = b.inc(dve)
                        b.wait(pe, dve, dstmp)
                        for hh in range(NH):
                            b.op(pe, lambda h, hh=hh: h.matmul(
                                pSfull[:, hh * 64:(hh + 1) * 64],
                                btok[:, :],
                                xdd[:, hh * 64:(hh + 1) * 64],
                                start=True, stop=True))
                        m = b.inc(pe)
                        b.wait(dve, pe, m)
                        b.op(dve, lambda h, nxt=nxt: h.tensor_tensor(
                            Ssb[:, nxt, :], Stmp[:, :], pSfull[:, :], ALU.add))
                        dsn = b.inc(dve)
                        b.wait(pe, dve, dsn)
                        for hh in range(NH):
                            rb = (hh % 2) * 64
                            colr = (hh // 2) * 128 + j * 64
                            b.op(pe, lambda h, hh=hh, rb=rb, colr=colr: h.matmul(
                                pAfull[rb:rb + 64, colr:colr + 64],
                                xdt[:, hh * 64:(hh + 1) * 64],
                                Msb[:, hh * 64:(hh + 1) * 64],
                                start=True, stop=False))
                            b.op(pe, lambda h, hh=hh, cur=cur, rb=rb, colr=colr: h.matmul(
                                pAfull[rb:rb + 64, colr:colr + 64],
                                Ssb[:, cur, hh * 64:(hh + 1) * 64],
                                cdec[:, hh * 64:(hh + 1) * 64],
                                start=False, stop=True))
                        m = b.inc(pe)
                        b.wait(act, pe, m)
                        b.op(act, lambda h, l0=l0, j=j: h.activation(
                            y_fm[:, :, l0 + j * 64:l0 + j * 64 + 64],
                            pAfull[:, :].rearrange("p (c l) -> p c l", l=128)[:, :, j * 64:j * 64 + 64],
                            AF.Identity))
                        ya = b.inc(act)
                        b.wait(pe, act, ya)
                if DBG and s_ == 0 and i == 0:
                    b.wait(qg, act)
                    b.op(qg, lambda h: h.dma_start(out=dbg_y, in_=y_fm[:, :, :]))
                    b.inc(qg)
                for fo in range(4):
                    fp = fo % 2
                    q_ = qg if fp == 0 else qg2
                    b.wait(q_, pe)
                    b.op(q_, lambda h, fo=fo, fp=fp: h.dma_start(
                        out=wpb[:, fp, :, :], in_=wsrc[:, :, fo * 512:(fo + 1) * 512]))
                    wld = b.inc(q_)
                    b.wait(pe, q_, wld)
                    for fc in range(4):
                        zc = fo * 4 + fc
                        pb_ = zc % 2
                        b.wait(pe, act)
                        b.wait(pe, dve)
                        for k in range(NCH):
                            b.op(pe, lambda h, k=k, fc=fc, fp=fp, pb_=pb_: h.matmul(
                                pA[:, pb_, :], wpb[:, fp, k, fc * 128:(fc + 1) * 128],
                                xlnt[:, k, :], start=(k == 0), stop=(k == NCH - 1)))
                        m = b.inc(pe)
                        b.wait(act, pe, m)
                        b.wait(act, dve)
                        b.op(act, lambda h, pb_=pb_, zc=zc: h.activation(
                            zs[:, pb_, :], pA[:, pb_, :], AF.Silu))
                        a = b.inc(act)
                        b.wait(dve, act, a)
                        b.op(dve, lambda h, zc=zc: h.scalar_tensor_tensor(
                            sqa[:, :], xbcs[:, zc, :], cols[:, 1, zc:zc + 1],
                            y_fm[:, zc, :], ALU.mult, ALU.add))
                        b.op(dve, lambda h, zc=zc, pb_=pb_: h.tensor_tensor(
                            y_fm[:, zc, :], sqa[:, :], zs[:, pb_, :], ALU.mult))
                        b.inc(dve)
                b.wait(pe, dve)
                b.wait(act, dve)
                for zc in range(16):
                    b.op(act, lambda h, zc=zc: h.activation(
                        sqa[:, :], y_fm[:, zc, :], AF.Square))
                    a = b.inc(act)
                    b.wait(pe, act, a)
                    b.op(pe, lambda h, zc=zc: h.matmul(
                        pS[0:1, 0, :], ones_col[:, 0:1], sqa[:, :],
                        start=(zc == 0), stop=(zc == 15)))
                    m = b.inc(pe)
                    b.wait(act, pe, m)
                b.wait(dve, pe)
                b.op(dve, lambda h: h.tensor_scalar(
                    st2[:, 0, :], pS[0:1, 0, :], 1.0 / DIN, 1e-5, ALU.mult, ALU.add))
                d = b.inc(dve)
                b.wait(act, dve, d)
                b.op(act, lambda h: h.activation(st2[:, 0, :], st2[:, 0, :], AF.Ln))
                b.op(act, lambda h: h.activation(st2[:, 0, :], st2[:, 0, :], AF.Exp, scale=-0.5))
                a = b.inc(act)
                b.wait(pe, act, a)
                b.op(pe, lambda h: h.matmul(pS[:, 1, :], ones_row[:, :], st2[:, 0, :],
                                            start=True, stop=True))
                m = b.inc(pe)
                b.wait(dve, pe, m)
                for zc in range(16):
                    b.op(dve, lambda h, zc=zc: h.scalar_tensor_tensor(
                        sqa[:, :], y_fm[:, zc, :], 1.0, pS[:, 1, :], ALU.mult, ALU.mult), dep=True)
                    b.op(dve, lambda h, zc=zc: h.tensor_scalar_mul(
                        y_fm[:, zc, :], sqa[:, :], cols[:, 2, zc:zc + 1]))
                b.inc(dve)
                if DBG and s_ == 0 and i == 0:
                    b.wait(qg, dve)
                    b.op(qg, lambda h: h.dma_start(out=dbg_yg, in_=y_fm[:, :, :]))
                    b.inc(qg)
                b.wait(dve, qs)
                store_prev = b.qss.cnt
                for fc in range(NCH):
                    fp = fc % 2
                    q_ = qw if fp == 0 else qw2
                    b.wait(q_, pe)
                    b.op(q_, lambda h, fc=fc, fp=fp: h.dma_start(
                        out=wpb[:, fp, :, :].rearrange("p k f -> p (k f)")[:, 0:2048]
                            .rearrange("p (k f) -> p k f", f=128),
                        in_=wout_bf.rearrange("(k p) f -> p k f", p=128)
                            [:, :, fc * 128:(fc + 1) * 128]))
                    wld = b.inc(q_)
                    b.wait(pe, q_, wld)
                    b.wait(pe, dve)
                    pb_ = fc % 2
                    for k in range(16):
                        b.op(pe, lambda h, k=k, fc=fc, fp=fp, pb_=pb_: h.matmul(
                            pA[:, pb_, :],
                            wpb[:, fp, :, :].rearrange("p k f -> p (k f)")
                                [:, 0:2048].rearrange("p (k f) -> p k f", f=128)[:, k, :],
                            y_fm[:, k, :], start=(k == 0), stop=(k == 15)))
                    m = b.inc(pe)
                    b.wait(dve, pe, m)
                    b.wait(dve, qss)
                    b.op(dve, lambda h, fc=fc, pb_=pb_: h.scalar_tensor_tensor(
                        ob[:, fc % 2, :], pA[:, pb_, :], 1.0, xt0[:, fc, :],
                        ALU.mult, ALU.add), dep=(fc < 2))
                    b.inc(dve)
                    if fc % 2 == 1:
                        b.wait(qss, dve)
                        b.op(qss, lambda h, t0g=t0g, fc=fc: h.dma_start(
                            out=xout_f(t0g, 512)[:, fc - 1:fc + 1, :], in_=ob[:, :, :]))
                        b.inc(qss)
            b.wait(dve, pe)
            b.op(dve, lambda h, nxt=(s_ * 32 + 32) % 2: h.memset(Ssb[:, nxt, :], 0.0))
            b.inc(dve)
        b.op(pool, lambda h, v=b.qss.cnt: h.wait_ge(b.qss.sem, v))
        b.inc(pool)
        b.flush()


def attn0_weight_map(I):
    cw = I["L0_attn_conv_w"][:, 0, :]
    diag = np.zeros((4, NCONV, 128, 128), np.float32)
    for j in range(4):
        for c in range(NCONV):
            np.fill_diagonal(diag[j, c], cw[c * 128:(c + 1) * 128, j])
    e = np.kron(np.eye(NH, dtype=np.float32), np.ones((1, HDIM), np.float32))
    return dict(
        win=_c(I["L0_attn_in_proj_w"]),
        convd=diag,
        convb=_c(I["L0_attn_conv_b"].reshape(NCONV, 128).T),
        dtb=_c(I["L0_attn_dt_bias"].reshape(NH, 1)),
        alog=_c(I["L0_attn_A_log"].reshape(NH, 1)),
        dcol=_c(I["L0_attn_D"].repeat(HDIM).reshape(16, 128).T),
        nwcol=_c(I["L0_attn_norm_w"].reshape(16, 128).T),
        wout=_c(I["L0_attn_out_proj_w"]),
        e32=_c(e))


def run_attn0(x_fm, I):
    nc = build_attn0()
    wmap = attn0_weight_map(I)
    in_maps = []
    for j in range(NCORES):
        xs = _c(x_fm[:, 2 * j:2 * j + 2, :].reshape(DM, TOK))
        in_maps.append(dict(x_in=xs, **wmap))
    res = run_bass_kernel_spmd(nc, in_maps, core_ids=list(range(NCORES)))
    out = np.empty((DM, 16, 2048), np.float32)
    for j in range(NCORES):
        out[:, 2 * j:2 * j + 2, :] = res.results[j]["x_out"].reshape(DM, 2, SEQ)
    return out


def _c(a):
    return np.ascontiguousarray(a)


def ffn0_weight_map(I):
    w = dict(
        wfT=_c(I["L0_ffn_wf"].T), wiT=_c(I["L0_ffn_wi"].T), wcT=_c(I["L0_ffn_wc"].T),
        ufT=_c(I["L0_ffn_uf"].T), uiT=_c(I["L0_ffn_ui"].T), ucT=_c(I["L0_ffn_uc"].T),
        woT=_c(I["L0_ffn_wo"].T), uoT=_c(I["L0_ffn_uo"].T))
    for k, src in [("bf", "L0_ffn_bf"), ("bi", "L0_ffn_bi"), ("bc", "L0_ffn_bc"),
                   ("bo", "L0_ffn_bo"), ("in_c", "L0_ffn_in_c")]:
        w[k] = _c(I[src].reshape(NCH, 128).T)
    return w


def run_ffn0(x_fm, I):
    nc = build_ffn0()
    wmap = ffn0_weight_map(I)
    in_maps = []
    for j in range(NCORES):
        xs = _c(x_fm[:, :, j * LLOC:(j + 1) * LLOC].reshape(DM, TOK))
        in_maps.append(dict(x_in=xs, **wmap))
    res = run_bass_kernel_spmd(nc, in_maps, core_ids=list(range(NCORES)))
    out = np.empty((DM, 16, 2048), np.float32)
    for j in range(NCORES):
        out[:, :, j * LLOC:(j + 1) * LLOC] = \
            res.results[j]["x_out"].reshape(DM, 16, LLOC)
    return out


def decl_ffn0_w(nc, W):
    for w in ["wfT", "wiT", "wcT", "ufT", "uiT", "ucT", "woT", "uoT"]:
        W[w] = nc.dram_tensor(w, [DM, DM], F32, kind="ExternalInput").ap()
    for v in ["bf", "bi", "bc", "bo", "in_c"]:
        W[v] = nc.dram_tensor(v, [128, NCH], F32, kind="ExternalInput").ap()


def decl_ffn1_w(nc, W):
    W["wqkv"] = nc.dram_tensor("wqkv", [DM, 3 * DM], F32, kind="ExternalInput").ap()
    W["bqkv_row"] = nc.dram_tensor("bqkv_row", [1, 3 * DM], F32, kind="ExternalInput").ap()
    W["bqkv_col"] = nc.dram_tensor("bqkv_col", [128, 24], F32, kind="ExternalInput").ap()
    W["wgate"] = nc.dram_tensor("wgate", [DM, 8], F32, kind="ExternalInput").ap()
    W["bgate"] = nc.dram_tensor("bgate", [8, 1], F32, kind="ExternalInput").ap()
    W["wexp"] = nc.dram_tensor("wexp", [8, DM, DM], F32, kind="ExternalInput").ap()
    W["bexp"] = nc.dram_tensor("bexp", [8, DM], F32, kind="ExternalInput").ap()
    W["wexp_bf"] = nc.dram_tensor("wexp_bf", [8, DM, DM], BF16).ap()


def decl_attn1_w(nc, W):
    W["A"] = nc.dram_tensor("A", [64, 64], F32, kind="ExternalInput").ap()
    W["BT"] = nc.dram_tensor("BT", [DM, 64], F32, kind="ExternalInput").ap()
    W["CT"] = nc.dram_tensor("CT", [64, DM], F32, kind="ExternalInput").ap()
    for v in ["Dcol", "ldt_col", "ln1_col", "ln2_col"]:
        W[v] = nc.dram_tensor(v, [128, NCH], F32, kind="ExternalInput").ap()
    W["wgT"] = nc.dram_tensor("wgT", [DM, INTER], F32, kind="ExternalInput").ap()
    W["wuT"] = nc.dram_tensor("wuT", [DM, INTER], F32, kind="ExternalInput").ap()
    W["wdT"] = nc.dram_tensor("wdT", [INTER, DM], F32, kind="ExternalInput").ap()
    W["wg_bf"] = nc.dram_tensor("wg_bf", [DM, INTER], BF16).ap()
    W["wu_bf"] = nc.dram_tensor("wu_bf", [DM, INTER], BF16).ap()
    W["wd_bf"] = nc.dram_tensor("wd_bf", [INTER, DM], BF16).ap()


def decl_attn0_w(nc, W):
    W["win"] = nc.dram_tensor("win", [DM, 4384], F32, kind="ExternalInput").ap()
    W["convd"] = nc.dram_tensor("convd", [4, NCONV, 128, 128], F32, kind="ExternalInput").ap()
    W["convb"] = nc.dram_tensor("convb", [128, NCONV], F32, kind="ExternalInput").ap()
    W["dtb"] = nc.dram_tensor("dtb", [NH, 1], F32, kind="ExternalInput").ap()
    W["alog"] = nc.dram_tensor("alog", [NH, 1], F32, kind="ExternalInput").ap()
    W["dcol"] = nc.dram_tensor("dcol", [128, 16], F32, kind="ExternalInput").ap()
    W["nwcol"] = nc.dram_tensor("nwcol", [128, 16], F32, kind="ExternalInput").ap()
    W["wout"] = nc.dram_tensor("wout", [DIN, DM], F32, kind="ExternalInput").ap()
    W["e32"] = nc.dram_tensor("e32", [NH, DIN], F32, kind="ExternalInput").ap()
    W["wout_bf"] = nc.dram_tensor("wout_bf", [DIN, DM], BF16).ap()


def _plain_fns(xin, xout):
    xs = xin.rearrange("(c p) t -> p c t", p=128)
    xd = xout.rearrange("(c p) t -> p c t", p=128)
    return (lambda t0, w: xs[:, :, t0:t0 + w],
            lambda t0, w: xd[:, :, t0:t0 + w])


def _wrap(emit, decl):
    def build():
        nc = bass.Bass("TRN2", target_bir_lowering=False)
        xin = nc.dram_tensor("x_in", [DM, TOK], F32, kind="ExternalInput").ap()
        xout = nc.dram_tensor("x_out", [DM, TOK], F32, kind="ExternalOutput").ap()
        W = {}
        decl(nc, W)
        b = make_b(nc)
        fin, fout = _plain_fns(xin, xout)
        emit(nc, b, W, fin, fout)
        return nc
    return build


build_ffn0 = _wrap(emit_ffn0, decl_ffn0_w)
build_ffn1 = _wrap(emit_ffn1, decl_ffn1_w)
build_attn1 = _wrap(emit_attn1, decl_attn1_w)
build_attn0 = _wrap(emit_attn0, decl_attn0_w)


def build_fused():
    nc = bass.Bass("TRN2", target_bir_lowering=False)
    xin = nc.dram_tensor("x_in", [DM, TOK], F32, kind="ExternalInput").ap()
    xout = nc.dram_tensor("x_out", [DM, TOK], F32, kind="ExternalOutput").ap()
    W = {}
    decl_ffn0_w(nc, W)
    decl_attn0_w(nc, W)
    decl_ffn1_w(nc, W)
    decl_attn1_w(nc, W)
    ab = [nc.dram_tensor(f"a2a{i}{d}", [NCORES, DM * 512], F32)
          for i in range(3) for d in ("i", "o")]
    a1i, a1o, a2i, a2o, a3i, a3o = ab
    xbs = [nc.dram_tensor(f"xb{i}", [DM, TOK], F32).ap() for i in range(1, 6)]
    xb1, xb2, xb3, xb4, xb5 = xbs
    b = make_b(nc)
    cc = b.reg("cc", "gpsimd", dma=True, step=1)

    def lshard_in(t0, w):
        xs = xin.rearrange("(c p) t -> p c t", p=128)
        return xs[:, :, t0:t0 + w]

    def store_l(buf):
        def f(t0, w):
            assert w == 256 and t0 % 256 == 0
            t_ = t0 // 256
            return buf.ap().rearrange("j (c p two l) -> j p c two l", p=128,
                                      two=2, l=256)[t_ // 2][:, :, t_ % 2, :]
        return f

    load_l = store_l

    def plain(buf):
        xs = buf.rearrange("(c p) t -> p c t", p=128)
        return lambda t0, w: xs[:, :, t0:t0 + w]

    def a2a(bin_, bout):
        b.wait(cc, b.qss)
        b.wait(cc, b.qg)
        b.op(cc, lambda h: h.collective_compute(
            "AllToAll", ALU.bypass, replica_groups=[list(range(NCORES))],
            ins=[bin_.ap().opt()], outs=[bout.ap().opt()]))
        b.inc(cc)
        b.flush()

    def reorder_out(bufa2a, bufplain):
        b.wait(b.qg, cc)
        pl = bufplain.rearrange("(c p) t -> p c t", p=128)
        for j in range(NCORES):
            for two in range(2):
                dst = pl[:, :, two * SEQ + j * 256:two * SEQ + (j + 1) * 256]
                src = bufa2a.ap().rearrange("j (c p two l) -> j p c two l",
                                            p=128, two=2, l=256)[j][:, :, two, :]
                b.op(b.qg, lambda h, dst=dst, src=src: h.dma_start(out=dst, in_=src))
                b.inc(b.qg)
        b.flush()

    def reorder_in(bufplain, bufa2a):
        b.wait(b.qg, b.qss)
        pl = bufplain.rearrange("(c p) t -> p c t", p=128)
        for j in range(NCORES):
            for two in range(2):
                src = pl[:, :, two * SEQ + j * 256:two * SEQ + (j + 1) * 256]
                dst = bufa2a.ap().rearrange("j (c p two l) -> j p c two l",
                                            p=128, two=2, l=256)[j][:, :, two, :]
                b.op(b.qg, lambda h, dst=dst, src=src: h.dma_start(out=dst, in_=src))
                b.inc(b.qg)
        b.flush()

    FUSE_N = int(os.environ.get("FUSE_N", "6"))

    def finish(buf):
        b.wait(b.qg, b.qss)
        b.op(b.qg, lambda h: h.dma_start(out=xout, in_=buf))
        b.inc(b.qg)
        b.op(b.pool, lambda h, v=b.qg.cnt: h.wait_ge(b.qg.sem, v))
        b.inc(b.pool)
        b.flush()
        return nc

    emit_ffn0(nc, b, W, lshard_in, store_l(a1i), first=True)
    a2a(a1i, a1o)
    reorder_out(a1o, xb1)
    if FUSE_N <= 1:
        return finish(xb1)
    b.wait(b.qs, b.qg)
    emit_attn0(nc, b, W, plain(xb1), plain(xb2), first=True)
    if FUSE_N <= 2:
        return finish(xb2)
    reorder_in(xb2, a2i)
    a2a(a2i, a2o)
    b.wait(b.qs, cc)
    emit_ffn0(nc, b, W, load_l(a2o), store_l(a3i), first=False)
    a2a(a3i, a3o)
    reorder_out(a3o, xb3)
    if FUSE_N <= 3:
        return finish(xb3)
    b.wait(b.qs, b.qg)
    emit_ffn1(nc, b, W, plain(xb3), plain(xb4), first=True)
    if FUSE_N <= 4:
        return finish(xb4)
    b.wait(b.qs, b.qss)
    emit_attn1(nc, b, W, plain(xb4), plain(xb5), first=True)
    if FUSE_N <= 5:
        return finish(xb5)
    b.wait(b.qs, b.qss)
    emit_ffn1(nc, b, W, plain(xb5),
              _plain_fns(xin, xout)[1], first=False)
    return nc


def fused_in_maps(I):
    wmap = {}
    wmap.update(ffn0_weight_map(I))
    wmap.update(attn0_weight_map(I))
    wmap.update(ffn1_weight_map(I))
    wmap.update(attn1_weight_map(I))
    x_fm = _c(I["L0_attn_u"].astype(np.float32).transpose(2, 0, 1))
    in_maps = []
    for j in range(NCORES):
        xs = _c(x_fm[:, :, j * 256:(j + 1) * 256].reshape(DM, TOK))
        in_maps.append(dict(x_in=xs, **wmap))
    return in_maps


def kernel(**inputs):
    I = inputs
    nc = build_fused()
    res = run_bass_kernel_spmd(nc, fused_in_maps(I), core_ids=list(range(NCORES)))
    out = np.empty((16, 2048, DM), np.float32)
    for j in range(NCORES):
        xb = res.results[j]["x_out"].reshape(DM, 2, SEQ)
        out[2 * j:2 * j + 2] = xb.transpose(1, 2, 0)
    return _c(out)


def kernel_unfused(**inputs):
    I = inputs
    x_fm = _c(I["L0_attn_u"].astype(np.float32).transpose(2, 0, 1))
    x_fm = run_ffn0(x_fm, I)
    x_fm = run_attn0(x_fm, I)
    x_fm = run_ffn0(x_fm, I)
    x_fm = run_ffn1(x_fm, I)
    x_fm = run_attn1(x_fm, I)
    x_fm = run_ffn1(x_fm, I)
    return _c(x_fm.transpose(1, 2, 0))


# version 06
# speedup vs baseline: 1.3943x; 1.0502x over previous
import os

os.environ.setdefault("TRN_ALLOW_ICI_ALL_TO_ALL", "1")

from contextlib import ExitStack

import numpy as np
import ml_dtypes
import concourse.bass as bass
import concourse.mybir as mybir
from concourse.bass_utils import run_bass_kernel_spmd

F32 = mybir.dt.float32
BF16 = mybir.dt.bfloat16
AF = mybir.ActivationFunctionType
ALU = mybir.AluOpType
AX = mybir.AxisListType

NCORES = 8
TOK = 4096
DM = 1024
NCH = DM // 128
LN_TT = 256


def _sb(nc, name, shape, dtype):
    return nc.sbuf_tensor(name, shape, dtype, allow_name_mangling=True)


def _ps(nc, name, shape, dtype):
    return nc.psum_tensor(name, shape, dtype, allow_name_mangling=True)


class Eng:
    def __init__(self, name, stream, sem, step, serialize=False, dma=False):
        self.name = name
        self.stream = stream
        self.sem = sem
        self.cnt = 0
        self.step = step
        self.serialize = serialize
        self.dma = dma
        self.hw = {}


class B:
    STREAMS = ("sync", "gpsimd", "tensor", "scalar", "vector")

    def __init__(self, nc: bass.Bass):
        self.nc = nc
        self.engs = {}
        self.lists = {s: [] for s in self.STREAMS}

    def reg(self, name, stream, dma=False, serialize=False, step=None):
        sem = self.nc.alloc_semaphore(f"sem_{name}")
        e = Eng(name, stream, sem, step if step is not None else (16 if dma else 1),
                serialize, dma)
        self.engs[name] = e
        setattr(self, name, e)
        return e

    AUTO = ("act", "dve", "pool")

    def op(self, e: Eng, fn, dep=None):
        if e.serialize and e.cnt > e.hw.get(e.name, 0):
            sem, v = e.sem, e.cnt
            self.lists[e.stream].append(
                [lambda h, s=sem, vv=v: h.wait_ge(s, vv), []])
            e.hw[e.name] = v
        if e.name in self.AUTO:
            if dep is None or dep:
                if e.hw.get(e.name, 0) < e.cnt:
                    sem, v = e.sem, e.cnt
                    self.lists[e.stream].append(
                        [lambda h, s=sem, vv=v: h.wait_ge(s, vv), []])
                    e.hw[e.name] = v
            entry = [fn, [(e.sem, 1)]]
            e.cnt += 1
        else:
            entry = [fn, []]
        self.lists[e.stream].append(entry)
        self._last = {**getattr(self, "_last", {}), e.name: entry}
        return entry

    def inc(self, e: Eng, n=None):
        if e.name in self.AUTO:
            return e.cnt
        n = n if n is not None else e.step
        self._last[e.name][1].append((e.sem, n))
        e.cnt += n
        return e.cnt

    def wait(self, cons: Eng, prod: Eng, upto=None):
        v = upto if upto is not None else prod.cnt
        if v <= 0 or cons.hw.get(prod.name, 0) >= v:
            return
        sem = prod.sem
        self.lists[cons.stream].append([lambda h, s=sem, vv=v: h.wait_ge(s, vv), []])
        cons.hw[prod.name] = v

    def flush(self):
        lists = {s: l for s, l in self.lists.items() if l}
        self.lists = {s: [] for s in self.STREAMS}
        if not lists:
            return
        with self.nc.Block() as blk:
            def mk(lst):
                def run(h):
                    for fn, incs in lst:
                        ins = fn(h)
                        for sem, n in incs:
                            ins.then_inc(sem, n)
                return run
            for s, lst in lists.items():
                getattr(blk, s)(mk(lst))
        for c in self.engs.values():
            for p in self.engs.values():
                if not p.dma:
                    c.hw[p.name] = p.cnt


def make_b(nc):
    b = B(nc)
    ser = os.environ.get("Q_SERIALIZE", "0") == "1"
    b.reg("qs", "sync", dma=True, serialize=ser)
    b.reg("qss", "sync", dma=True, serialize=ser)
    b.reg("qw", "sync", dma=True, serialize=ser)
    b.reg("qw2", "sync", dma=True, serialize=ser)
    b.reg("qg", "gpsimd", dma=True, serialize=ser)
    b.reg("qg2", "gpsimd", dma=True, serialize=ser)
    b.reg("pe", "tensor")
    b.reg("act", "scalar")
    b.reg("dve", "vector")
    b.reg("pool", "gpsimd")
    return b


class LnScratch:
    def __init__(self, nc, stack, pst=None, pbc=None, tag=""):
        if pst is None:
            pst = stack.enter_context(_ps(nc, f"ln_pst{tag}", [1, 2, LN_TT], F32))[:, :, :]
        if pbc is None:
            pbc = stack.enter_context(_ps(nc, f"ln_pbc{tag}", [128, 2, LN_TT], F32))[:, :, :]
        self.sq = stack.enter_context(_sb(nc, f"ln_sq{tag}", [128, 2, LN_TT], F32))
        self.stats = stack.enter_context(_sb(nc, f"ln_stats{tag}", [1, 4, LN_TT], F32))
        self.ones_col = stack.enter_context(_sb(nc, f"ones_col{tag}", [128, 1], F32))
        self.ones_row = stack.enter_context(_sb(nc, f"ones_row{tag}", [1, 128], F32))
        self.pst = pst
        self.pbc = pbc

    def init(self, b: B, dve: Eng):
        b.op(dve, lambda h, s=self: h.memset(s.ones_col[:, :], 1.0))
        b.op(dve, lambda h, s=self: h.memset(s.ones_row[:, :], 1.0))
        b.inc(dve)


def emit_ln_tile(b: B, ls: LnScratch, xt, xln_out):
    T = xt.shape[2]
    pe, act, dve = b.pe, b.act, b.dve
    st = ls.stats
    b.wait(pe, dve)
    b.wait(pe, act)
    b.wait(act, dve)
    for c in range(NCH):
        b.op(pe, lambda h, c=c, T=T, xt=xt: h.matmul(
            ls.pst[0:1, 0, 0:T], ls.ones_col[:, 0:1], xt[:, c, :],
            start=(c == 0), stop=(c == NCH - 1)))
    b.inc(pe)
    for c in range(NCH):
        b.op(act, lambda h, c=c, T=T, xt=xt: h.activation(
            ls.sq[:, c % 2, 0:T], xt[:, c, :], AF.Square))
        a_mark = b.inc(act)
        b.wait(pe, act, a_mark)
        b.op(pe, lambda h, c=c, T=T: h.matmul(
            ls.pst[0:1, 1, 0:T], ls.ones_col[:, 0:1], ls.sq[:, c % 2, 0:T],
            start=(c == 0), stop=(c == NCH - 1)))
        pe_mark = b.inc(pe)
        b.wait(act, pe, pe_mark)
    b.wait(dve, pe)
    b.op(dve, lambda h, T=T: h.tensor_scalar_mul(st[:, 0, 0:T], ls.pst[0:1, 0, 0:T], 1.0 / DM))
    b.op(dve, lambda h, T=T: h.tensor_scalar_mul(st[:, 1, 0:T], ls.pst[0:1, 1, 0:T], 1.0 / DM))
    b.op(dve, lambda h, T=T: h.tensor_tensor(st[:, 2, 0:T], st[:, 0, 0:T], st[:, 0, 0:T], ALU.mult))
    b.op(dve, lambda h, T=T: h.tensor_tensor(st[:, 2, 0:T], st[:, 1, 0:T], st[:, 2, 0:T], ALU.subtract))
    b.op(dve, lambda h, T=T: h.tensor_scalar_add(st[:, 2, 0:T], st[:, 2, 0:T], 1e-5))
    d_mark = b.inc(dve)
    b.wait(act, dve, d_mark)
    b.op(act, lambda h, T=T: h.activation(st[:, 2, 0:T], st[:, 2, 0:T], AF.Ln))
    b.op(act, lambda h, T=T: h.activation(st[:, 3, 0:T], st[:, 2, 0:T], AF.Exp, scale=-0.5))
    a_mark = b.inc(act)
    b.wait(dve, act, a_mark)
    b.op(dve, lambda h, T=T: h.tensor_tensor(st[:, 2, 0:T], st[:, 0, 0:T], st[:, 3, 0:T], ALU.mult))
    d_mark = b.inc(dve)
    b.wait(pe, dve, d_mark)
    b.op(pe, lambda h, T=T: h.matmul(ls.pbc[:, 0, 0:T], ls.ones_row[:, :], st[:, 3, 0:T],
                                     start=True, stop=True))
    b.op(pe, lambda h, T=T: h.matmul(ls.pbc[:, 1, 0:T], ls.ones_row[:, :], st[:, 2, 0:T],
                                     start=True, stop=True))
    pe_mark = b.inc(pe)
    b.wait(dve, pe, pe_mark)
    for c in range(NCH):
        b.op(dve, lambda h, c=c, T=T, xt=xt: h.scalar_tensor_tensor(
            ls.sq[:, c % 2, 0:T], xt[:, c, :], 1.0, ls.pbc[:, 0, 0:T],
            ALU.mult, ALU.mult))
        b.op(dve, lambda h, c=c, T=T, xln_out=xln_out: h.tensor_tensor(
            xln_out[:, c, :], ls.sq[:, c % 2, 0:T], ls.pbc[:, 1, 0:T], ALU.subtract))
    b.inc(dve)


TSTEPS = 16
LLOC = 256


def emit_ffn0(nc, b, W, xin_f, xout_f, first=True):
    wts = W
    dbg = False
    with ExitStack() as stack:
        xln = stack.enter_context(_sb(nc, "xln", [128, NCH, TOK], BF16))
        wu = stack.enter_context(_sb(nc, "wu", [128, 6, NCH, DM], BF16))
        bias_sb = stack.enter_context(_sb(nc, "bias_sb", [128, 5, NCH], F32))
        c_f32 = stack.enter_context(_sb(nc, "c_f32", [128, NCH, LLOC], F32))
        c_bf = stack.enter_context(_sb(nc, "c_bf", [128, 2, NCH, LLOC], BF16))
        gates = stack.enter_context(_sb(nc, "gates", [128, 3, LLOC], F32))
        hbuf = stack.enter_context(_sb(nc, "hbuf", [128, 2, LLOC], F32))
        xt0 = stack.enter_context(_sb(nc, "xt0", [128, NCH, LLOC], F32))
        xt1 = stack.enter_context(_sb(nc, "xt1", [128, NCH, LLOC], F32))
        ls = LnScratch(nc, stack)
        pg0 = stack.enter_context(_ps(nc, "pg0", [128, 3, 512], F32))[:, :, 0:LLOC]
        pg1 = stack.enter_context(_ps(nc, "pg1", [128, 3, 512], F32))[:, :, 0:LLOC]

        qs, qss, qw, qw2 = b.qs, b.qss, b.qw, b.qw2
        qg, qg2, pe, act, dve, pool = b.qg, b.qg2, b.pe, b.act, b.dve, b.pool

        for vi, v in enumerate(["bf", "bi", "bc", "bo", "in_c"]):
            b.op(qs, lambda h, vi=vi, v=v: h.dma_start(
                out=bias_sb[:, vi, :], in_=wts[v]))
            b.inc(qs)
        for gi, w in enumerate(["wfT", "wiT", "wcT", "ufT", "uiT", "ucT"]):
            b.op(qg, lambda h, gi=gi, w=w: h.dma_start(
                out=wu[:, gi, :, :].rearrange("p k (o q) -> p k o q", q=128),
                in_=wts[w].rearrange("(k p) (o q) -> p k o q", p=128, q=128)))
            b.inc(qg)
        ls.init(b, dve)
        b.flush()

        NT = TOK // LN_TT
        for i in range(NT):
            buf = xt0
            b.wait(qs, dve)
            b.op(qs, lambda h, i=i, buf=buf: h.dma_start(
                out=buf[:, :, :], in_=xin_f(i * LN_TT, LN_TT)))
            ld = b.inc(qs)
            b.wait(pe, qs, ld)
            b.wait(act, qs, ld)
            emit_ln_tile(b, ls, buf[:, :, :], xln[:, :, i * LN_TT:(i + 1) * LN_TT])
        b.flush()

        b.wait(pe, qg)
        b.wait(act, qs)
        b.wait(dve, qs)
        for c in range(NCH):
            b.op(dve, lambda h, c=c: h.memset(c_f32[:, c, :], 1.0))
            b.op(dve, lambda h, c=c: h.tensor_scalar_mul(
                c_f32[:, c, :], c_f32[:, c, :], bias_sb[:, 4, c:c + 1]))
            b.op(dve, lambda h, c=c: h.tensor_copy(c_bf[:, 0, c, :], c_f32[:, c, :]))
        b.inc(dve)

        psum_free = {}
        dve_marks = {-1: b.dve.cnt}
        for t_ in range(TSTEPS):
            cur, nxt = t_ % 2, 1 - t_ % 2
            for o in range(NCH):
                pidx = (t_ * NCH + o) % 2
                ps = pg0 if pidx == 0 else pg1
                if pidx in psum_free:
                    b.wait(pe, dve, psum_free[pidx])
                b.wait(pe, dve, dve_marks[t_ - 1])
                for gi in range(3):
                    for k in range(NCH):
                        b.op(pe, lambda h, k=k, gi=gi, o=o, t_=t_, ps=ps: h.matmul(
                            ps[:, gi, :],
                            wu[:, gi, k, o * 128:(o + 1) * 128],
                            xln[:, k, t_ * LLOC:(t_ + 1) * LLOC],
                            start=(k == 0), stop=False))
                    for k in range(NCH):
                        b.op(pe, lambda h, k=k, gi=gi, o=o, cur=cur, ps=ps: h.matmul(
                            ps[:, gi, :],
                            wu[:, 3 + gi, k, o * 128:(o + 1) * 128],
                            c_bf[:, cur, k, :],
                            start=False, stop=(k == NCH - 1)))
                pe_mark = b.inc(pe)
                b.wait(act, pe, pe_mark)
                b.wait(act, dve)
                b.op(act, lambda h, o=o, ps=ps: h.activation(
                    gates[:, 0, :], ps[:, 0, :], AF.Sigmoid, bias=bias_sb[:, 0, o:o + 1]))
                b.op(act, lambda h, o=o, ps=ps: h.activation(
                    gates[:, 1, :], ps[:, 1, :], AF.Sigmoid, bias=bias_sb[:, 1, o:o + 1]))
                b.op(act, lambda h, o=o, ps=ps: h.activation(
                    gates[:, 2, :], ps[:, 2, :], AF.Tanh, bias=bias_sb[:, 2, o:o + 1]))
                act_mark = b.inc(act)
                b.wait(dve, act, act_mark)
                b.op(dve, lambda h, o=o: h.tensor_tensor(
                    c_f32[:, o, :], c_f32[:, o, :], gates[:, 0, :], ALU.mult))
                b.op(dve, lambda h: h.tensor_tensor(
                    gates[:, 1, :], gates[:, 1, :], gates[:, 2, :], ALU.mult))
                b.op(dve, lambda h, o=o: h.tensor_tensor(
                    c_f32[:, o, :], c_f32[:, o, :], gates[:, 1, :], ALU.add))
                b.op(dve, lambda h, o=o, nxt=nxt: h.tensor_copy(
                    c_bf[:, nxt, o, :], c_f32[:, o, :]))
                psum_free[pidx] = b.inc(dve)
            dve_marks[t_] = b.dve.cnt
        b.flush()

        if dbg:
            b.op(qs, lambda h: h.dma_start(out=xln_dbg, in_=xln[:, :, :]))
            b.inc(qs)
            b.op(qs, lambda h: h.dma_start(out=c_dbg, in_=c_f32[:, :, :]))
            b.inc(qs)
            b.flush()

        cfin = TSTEPS % 2
        for gi, w in enumerate(["woT", "uoT"]):
            b.op(qg, lambda h, gi=gi, w=w: h.dma_start(
                out=wu[:, gi, :, :].rearrange("p k (o q) -> p k o q", q=128),
                in_=wts[w].rearrange("(k p) (o q) -> p k o q", p=128, q=128)))
            b.inc(qg)
        b.wait(pe, qg)
        psum_free = {}
        store_mark = 0
        for t_ in range(TSTEPS):
            buf = xt0
            obuf = xt1
            if store_mark:
                b.wait(dve, qss, store_mark)
            b.wait(qs, dve)
            b.op(qs, lambda h, t_=t_, buf=buf: h.dma_start(
                out=buf[:, :, :], in_=xin_f(t_ * LLOC, LLOC)))
            ld = b.inc(qs)
            for o in range(NCH):
                pidx = (t_ * NCH + o) % 2
                ps = pg0 if pidx == 0 else pg1
                if pidx in psum_free:
                    b.wait(pe, dve, psum_free[pidx])
                for k in range(NCH):
                    b.op(pe, lambda h, k=k, o=o, t_=t_, ps=ps: h.matmul(
                        ps[:, 0, :], wu[:, 0, k, o * 128:(o + 1) * 128],
                        xln[:, k, t_ * LLOC:(t_ + 1) * LLOC],
                        start=(k == 0), stop=False))
                for k in range(NCH):
                    b.op(pe, lambda h, k=k, o=o, ps=ps: h.matmul(
                        ps[:, 0, :], wu[:, 1, k, o * 128:(o + 1) * 128],
                        c_bf[:, cfin, k, :],
                        start=False, stop=(k == NCH - 1)))
                pe_mark = b.inc(pe)
                b.wait(act, pe, pe_mark)
                b.wait(act, dve)
                b.op(act, lambda h, o=o, ps=ps: h.activation(
                    hbuf[:, o % 2, :], ps[:, 0, :], AF.Tanh, bias=bias_sb[:, 3, o:o + 1]))
                act_mark = b.inc(act)
                b.wait(dve, act, act_mark)
                b.wait(dve, qs, ld)
                b.op(dve, lambda h, o=o, buf=buf, obuf=obuf: h.scalar_tensor_tensor(
                    obuf[:, o, :], hbuf[:, o % 2, :], 0.5, buf[:, o, :],
                    ALU.mult, ALU.add))
                psum_free[pidx] = b.inc(dve)
            b.wait(qss, dve)
            b.op(qss, lambda h, t_=t_, obuf=obuf: h.dma_start(
                out=xout_f(t_ * LLOC, LLOC), in_=obuf[:, :, :]))
            store_mark = b.inc(qss)
        b.op(pool, lambda h, v=b.qss.cnt: h.wait_ge(b.qss.sem, v))
        b.inc(pool)
        b.flush()


NSEQ = 2
SEQ = 2048
NHEAD = 16
HD = 64


def emit_ffn1(nc, b, W, xin_f, xout_f, first=True):
    PH = 3
    wqkv, bqkv_row, bqkv_col = W["wqkv"], W["bqkv_row"], W["bqkv_col"]
    wgate, bgate, wexp, bexp = W["wgate"], W["bgate"], W["wexp"], W["bexp"]
    wexp_bf = W["wexp_bf"]
    with ExitStack() as stack:
        xln = stack.enter_context(_sb(nc, "xln", [128, NCH, TOK], BF16))
        attn_sb = stack.enter_context(_sb(nc, "attn_sb", [128, NCH, TOK], BF16))
        cfm = stack.enter_context(_sb(nc, "cfm", [8, TOK], BF16))
        ident = stack.enter_context(_sb(nc, "ident", [128, 128], F32))
        bq_col = stack.enter_context(_sb(nc, "bq_col", [128, 24], F32))
        bg_col = stack.enter_context(_sb(nc, "bg_col", [8, 1], F32))
        be_sb = stack.enter_context(_sb(nc, "be_sb", [8, DM], BF16))
        wg_sb = stack.enter_context(_sb(nc, "wg_sb", [128, NCH, 8], BF16))
        orow_bf = stack.enter_context(_sb(nc, "orow_bf", [1, 128], BF16))
        pa = stack.enter_context(_ps(nc, "pa", [128, 2, 512], F32))[:, :, :]
        pb = stack.enter_context(_ps(nc, "pb", [128, 4, 512], F32))[:, :, :]

        qs, qss, qw, qw2 = b.qs, b.qss, b.qw, b.qw2
        qg, qg2, pe, act, dve, pool = b.qg, b.qg2, b.pe, b.act, b.dve, b.pool

        b.op(qs, lambda h: h.dma_start(out=bq_col[:, :], in_=bqkv_col))
        b.inc(qs)
        b.op(qs, lambda h: h.dma_start(out=bg_col[:, :], in_=bgate))
        b.inc(qs)
        b.op(qg, lambda h: h.dma_start(out=be_sb[:, :], in_=bexp))
        b.inc(qg)
        b.op(qg, lambda h: h.dma_start(
            out=wg_sb[:, :, :],
            in_=wgate.rearrange("(c p) e -> p c e", p=128)))
        b.inc(qg)
        if first:
            for e in range(8):
                b.op(qg, lambda h, e=e: h.dma_start(out=wexp_bf[e], in_=wexp[e]))
                b.inc(qg)
        b.op(pool, lambda h: h.memset(ident[:, :], 0.0))
        b.op(pool, lambda h: h.affine_select(
            out=ident[:, :], in_=ident[:, :], compare_op=ALU.not_equal,
            fill=1.0, base=0, pattern=[[-1, 128]], channel_multiplier=1))
        b.op(dve, lambda h: h.memset(orow_bf[:, :], 1.0))
        b.inc(dve)

        with ExitStack() as _phase_guard:
            ln_stack = ExitStack()
            xt0 = ln_stack.enter_context(_sb(nc, "xt0", [128, NCH, LN_TT], F32))
            ls = LnScratch(nc, ln_stack)
            ls.init(b, dve)
            for i in range(TOK // LN_TT):
                b.wait(qs, dve)
                b.op(qs, lambda h, i=i: h.dma_start(
                    out=xt0[:, :, :], in_=xin_f(i * LN_TT, LN_TT)))
                ld = b.inc(qs)
                b.wait(pe, qs, ld)
                b.wait(act, qs, ld)
                emit_ln_tile(b, ls, xt0[:, :, :], xln[:, :, i * LN_TT:(i + 1) * LN_TT])
            b.flush()
            ln_stack.close()


            with (
                _sb(nc, "glog", [8, TOK], F32) as glog,
                _sb(nc, "gt", [128, 10, 8], F32) as gt,
                _sb(nc, "gcols", [128, 4], F32) as gcols,
                _sb(nc, "ct", [128, 8], F32) as ct,
            ):
                b.wait(pe, qg)
                b.wait(act, qs)
                for i in range(TOK // 512):
                    b.wait(pe, act)
                    for k in range(NCH):
                        b.op(pe, lambda h, i=i, k=k: h.matmul(
                            pa[0:8, 0, :], wg_sb[:, k, :],
                            xln[:, k, i * 512:(i + 1) * 512],
                            start=(k == 0), stop=(k == NCH - 1)))
                    m = b.inc(pe)
                    b.wait(act, pe, m)
                    b.op(act, lambda h, i=i: h.activation(
                        glog[:, i * 512:(i + 1) * 512], pa[0:8, 0, :],
                        AF.Identity, bias=bg_col[:, :]))
                b.flush()
                for i in range(TOK // 128):
                    sl = slice(i * 128, (i + 1) * 128)
                    b.wait(pe, dve)
                    b.op(pe, lambda h, sl=sl: h.transpose(
                        pa[:, 1, 0:8], glog[:, sl], ident[0:8, 0:8]))
                    m = b.inc(pe)
                    b.wait(dve, pe, m)
                    b.op(dve, lambda h: h.tensor_copy(gt[:, 0, :], pa[:, 1, 0:8]))
                    b.op(dve, lambda h: h.reduce_max(gcols[:, 0:1], gt[:, 0, :], axis=AX.X))
                    b.op(dve, lambda h: h.tensor_scalar_mul(gcols[:, 1:2], gcols[:, 0:1], -1.0))
                    d = b.inc(dve)
                    b.wait(act, dve, d)
                    b.op(act, lambda h: h.activation(gt[:, 1, :], gt[:, 0, :], AF.Exp,
                                                     bias=gcols[:, 1:2]))
                    a = b.inc(act)
                    b.wait(dve, act, a)
                    b.op(dve, lambda h: h.reduce_sum(gcols[:, 0:1], gt[:, 1, :], axis=AX.X))
                    b.op(dve, lambda h: h.reciprocal(gcols[:, 1:2], gcols[:, 0:1]))
                    b.op(dve, lambda h: h.tensor_scalar_mul(gt[:, 2, :], gt[:, 1, :], gcols[:, 1:2]))
                    b.op(dve, lambda h: h.reduce_max(gcols[:, 2:3], gt[:, 2, :], axis=AX.X))
                    b.op(dve, lambda h: h.tensor_single_scalar(gt[:, 3, :], gt[:, 2, :], gcols[:, 2:3], ALU.is_equal))
                    b.op(dve, lambda h: h.scalar_tensor_tensor(
                        gt[:, 4, :], gt[:, 3, :], -2.0, gt[:, 2, :], ALU.mult, ALU.add))
                    b.op(dve, lambda h: h.reduce_max(gcols[:, 3:4], gt[:, 4, :], axis=AX.X))
                    b.op(dve, lambda h: h.tensor_single_scalar(gt[:, 5, :], gt[:, 2, :], gcols[:, 3:4], ALU.is_ge))
                    b.op(dve, lambda h: h.tensor_tensor(ct[:, :], gt[:, 2, :], gt[:, 5, :], ALU.mult))
                    d = b.inc(dve)
                    b.wait(pe, dve, d)
                    b.op(pe, lambda h: h.transpose(pa[0:8, 1, 128:256], ct[:, :], ident[:, :]))
                    m = b.inc(pe)
                    b.wait(dve, pe, m)
                    b.op(dve, lambda h, sl=sl: h.tensor_copy(cfm[:, sl], pa[0:8, 1, 128:256]))
                    b.inc(dve)
                b.flush()


            with (
                _sb(nc, "qk_sb", [128, 4, SEQ], BF16) as qk_sb,
                _sb(nc, "v_sb", [128, 16, 4, 65], BF16) as v_sb,
                _sb(nc, "wbuf", [128, NCH, 1024], BF16) as wbuf,
                _sb(nc, "p_sb", [128, 2, 512], BF16) as p_sb,
                _sb(nc, "av_sb", [64, 512], BF16) as av_sb,
                _sb(nc, "r_sb", [128, 512], F32) as r_sb,
                _sb(nc, "ones128", [128, 128], F32) as ones128,
                _sb(nc, "bq_row", [1, 3 * DM], BF16) as bq_row,
            ):
                b.op(qg, lambda h: h.dma_start(out=bq_row[:, :], in_=bqkv_row))
                b.inc(qg)
                b.op(dve, lambda h: h.memset(ones128[:, :], 1.0))
                b.op(dve, lambda h: h.memset(
                    v_sb[:, :, :, 64:65].rearrange("p a c o -> p (a c o)"), 1.0))
                b.inc(dve)
                b.wait(act, qs)
                exp_marks = {}
                for hg in range(4):
                    b.wait(qg, pe)
                    for j, col in enumerate([hg * 256, DM + hg * 256, 2 * DM + hg * 256]):
                        b.op(qg, lambda h, j=j, col=col: h.dma_start(
                            out=wbuf[:, :, j * 256:(j + 1) * 256],
                            in_=wqkv.rearrange("(k p) f -> p k f", p=128)
                                [:, :, col:col + 256]))
                        b.inc(qg)
                    wld = b.qg.cnt
                    b.wait(pe, qg, wld)
                    for s_ in range(NSEQ):
                        seq0 = s_ * SEQ
                        b.wait(pe, dve)
                        for fc in range(4):
                            wslice = [0, 128, 256, 384][fc]
                            bcol = [2 * hg, 2 * hg + 1, 8 + 2 * hg, 8 + 2 * hg + 1][fc]
                            for i in range(SEQ // 512):
                                b.wait(pe, act)
                                for k in range(NCH):
                                    b.op(pe, lambda h, i=i, k=k, ws=wslice, seq0=seq0: h.matmul(
                                        pa[:, 0, :], wbuf[:, k, ws:ws + 128],
                                        xln[:, k, seq0 + i * 512:seq0 + (i + 1) * 512],
                                        start=(k == 0), stop=(k == NCH - 1)))
                                m = b.inc(pe)
                                b.wait(act, pe, m)
                                b.op(act, lambda h, fc=fc, i=i, bcol=bcol: h.activation(
                                    qk_sb[:, fc, i * 512:(i + 1) * 512], pa[:, 0, :],
                                    AF.Identity, bias=bq_col[:, bcol:bcol + 1]))
                        for tt in range(16):
                            tok0 = seq0 + tt * 128
                            b.wait(pe, act)
                            for k in range(NCH):
                                b.op(pe, lambda h, k=k, tok0=tok0: h.matmul(
                                    pa[:, 1, 0:256], xln[:, k, tok0:tok0 + 128],
                                    wbuf[:, k, 512:768],
                                    start=(k == 0), stop=False))
                            b.op(pe, lambda h, hg=hg: h.matmul(
                                pa[:, 1, 0:256], orow_bf[:, :],
                                bq_row[:, 2 * DM + hg * 256:2 * DM + (hg + 1) * 256],
                                start=False, stop=True))
                            m = b.inc(pe)
                            b.wait(act, pe, m)
                            b.op(act, lambda h, tt=tt: h.activation(
                                v_sb[:, tt, :, 0:64],
                                pa[:, 1, 0:256].rearrange("p (h f) -> p h f", h=4),
                                AF.Identity))
                        b.wait(pe, act)
                        tail = getattr(b, "_ffn1_tail",
                                       {"a": 0, "dr": 0, "ds": 0, "pending": None})
                        for hi in range(4):
                            hglob = hg * 4 + hi
                            fc = hi // 2
                            rows = (hi % 2) * 64
                            for qt in range(4):
                                q0 = qt * 512
                                for kt in range(16):
                                    k0 = kt * 128
                                    pp = kt % 2
                                    if pp in exp_marks:
                                        b.wait(pe, act, exp_marks[pp])
                                    b.op(pe, lambda h, fc=fc, rows=rows, k0=k0, q0=q0, pp=pp: h.matmul(
                                        pa[:, pp, :], qk_sb[rows:rows + 64, 2 + fc, k0:k0 + 128],
                                        qk_sb[rows:rows + 64, fc, q0:q0 + 512],
                                        start=True, stop=True))
                                    ms = b.inc(pe)
                                    if kt > 0:
                                        ppv = 1 - pp
                                        b.wait(pe, act, exp_marks[ppv])
                                        if kt == 1:
                                            b.wait(pe, act, tail["a"])
                                            b.wait(pe, dve, tail["dr"])
                                        b.op(pe, lambda h, kt=kt, hi=hi, ppv=ppv: h.matmul(
                                            pb[0:65, 0, :], v_sb[:, kt - 1, hi, :],
                                            p_sb[:, ppv, :],
                                            start=(kt == 1), stop=False))
                                        b.inc(pe)
                                    if kt == 6 and tail["pending"] is not None:
                                        tail["pending"]()
                                        tail["pending"] = None
                                    b.wait(act, pe, ms)
                                    b.op(act, lambda h, pp=pp: h.activation(
                                        p_sb[:, pp, :], pa[:, pp, :], AF.Exp, scale=0.125))
                                    exp_marks[pp] = b.inc(act)
                                b.wait(pe, act, exp_marks[1])
                                b.op(pe, lambda h, hi=hi: h.matmul(
                                    pb[0:65, 0, :], v_sb[:, 15, hi, :],
                                    p_sb[:, 1, :],
                                    start=False, stop=True))
                                m = b.inc(pe)
                                b.wait(dve, pe, m)
                                b.op(dve, lambda h: h.reciprocal(r_sb[64:65, :], pb[64:65, 0, :]))
                                tail["dr"] = d = b.inc(dve)
                                b.wait(act, dve, d)
                                b.op(act, lambda h: h.activation(av_sb[:, :], pb[0:64, 0, :], AF.Identity))
                                tail["a"] = a = b.inc(act)

                                def _emit_tail(rows=rows, hglob=hglob, q0=q0,
                                               seq0=seq0, dr=d, a=a, ds_prev=tail["ds"]):
                                    b.wait(pe, dve, dr)
                                    b.wait(pe, dve, ds_prev)
                                    b.op(pe, lambda h: h.matmul(
                                        pb[0:64, 1, :], ones128[64:65, 0:64],
                                        r_sb[64:65, :], start=True, stop=True))
                                    m2 = b.inc(pe)
                                    b.wait(dve, pe, m2)
                                    b.wait(dve, act, a)
                                    b.op(dve, lambda h: h.scalar_tensor_tensor(
                                        attn_sb[rows:rows + 64, hglob // 2, seq0 + q0:seq0 + q0 + 512],
                                        pb[0:64, 1, :], 0.5, av_sb[:, :], ALU.mult, ALU.mult))
                                    tail["ds"] = b.inc(dve)
                                tail["pending"] = _emit_tail
                        b._ffn1_tail = tail
                if b._ffn1_tail["pending"] is not None:
                    b._ffn1_tail["pending"]()
                    b._ffn1_tail["pending"] = None
                b.flush()


            with (
                _sb(nc, "cb_sb", [128, 2, 512], BF16) as cb_sb,
                _sb(nc, "xe", [128, 2, NCH, 512], BF16) as xe,
                _sb(nc, "we", [128, 2, NCH, 256], BF16) as we,
                _sb(nc, "xa", [128, NCH, 512], F32) as xa,
                _sb(nc, "ob", [128, NCH, 512], F32) as ob,
                _sb(nc, "cfm_row", [1, 8, 512], BF16) as cfm_row,
            ):
                store_mark = 0
                b.wait(pe, qg)
                for tt in range(TOK // 512):
                    t0 = tt * 512
                    b.wait(qs, pe)
                    b.op(qs, lambda h, t0=t0: h.dma_start(
                        out=cfm_row.ap().rearrange("o e t -> o (e t)"),
                        in_=cfm.ap()[:, t0:t0 + 512]))
                    crm = b.inc(qs)
                    b.wait(pe, qs, crm)
                    b.wait(qs, dve)
                    b.op(qs, lambda h, t0=t0: h.dma_start(
                        out=xa[:, :, :], in_=xin_f(t0, 512)))
                    xld = b.inc(qs)
                    b.wait(dve, qs, xld)
                    for c in range(NCH):
                        b.op(dve, lambda h, c=c, t0=t0: h.tensor_tensor(
                            xa[:, c, :], xa[:, c, :], attn_sb[:, c, t0:t0 + 512], ALU.add), dep=(c == 0))
                    for fh in range(2):
                        f0 = fh * 4
                        for e in range(8):
                            eb = e % 2
                            b.wait(pe, act)
                            b.op(pe, lambda h, e=e, eb=eb: h.matmul(
                                pa[:, eb, :], orow_bf[:, :], cfm_row[0:1, e, :],
                                start=True, stop=True))
                            m = b.inc(pe)
                            b.wait(act, pe, m)
                            b.wait(act, dve)
                            b.op(act, lambda h, eb=eb: h.activation(
                                cb_sb[:, eb, :], pa[:, eb, :], AF.Identity))
                            cbm = b.inc(act)
                            for half, q in ((0, qw), (1, qw2)):
                                b.wait(q, pe)
                                b.op(q, lambda h, e=e, f0=f0, half=half: h.dma_start(
                                    out=we[:, half, :, :],
                                    in_=wexp_bf[e].rearrange("(k p) f -> p k f", p=128)
                                        [:, :, (f0 + 2 * half) * 128:(f0 + 2 * half) * 128 + 256]))
                                b.inc(q)
                            b.wait(dve, act, cbm)
                            b.wait(dve, pe)
                            for k in range(NCH):
                                b.op(dve, lambda h, k=k, t0=t0, eb=eb: h.tensor_tensor(
                                    xe[:, eb, k, :], xln[:, k, t0:t0 + 512],
                                    cb_sb[:, eb, :], ALU.mult), dep=(k == 0))
                            xe_mark = b.inc(dve)
                            b.wait(pe, dve, xe_mark)
                            for f in range(4):
                                if f == 0:
                                    b.wait(pe, qw)
                                if f == 2:
                                    b.wait(pe, qw2)
                                for k in range(NCH):
                                    b.op(pe, lambda h, f=f, k=k, eb=eb, e=e: h.matmul(
                                        pb[:, f, :], we[:, f // 2, k, (f % 2) * 128:(f % 2) * 128 + 128],
                                        xe[:, eb, k, :],
                                        start=(e == 0 and k == 0), stop=False))
                        for f in range(4):
                            b.op(pe, lambda h, f=f, f0=f0, t0=t0: h.matmul(
                                pb[:, f, :], be_sb[:, (f0 + f) * 128:(f0 + f + 1) * 128],
                                cfm[:, t0:t0 + 512], start=False, stop=True))
                        m = b.inc(pe)
                        b.wait(dve, pe, m)
                        b.wait(dve, qss, store_mark)
                        for f in range(4):
                            b.op(dve, lambda h, f=f, f0=f0: h.scalar_tensor_tensor(
                                ob[:, f0 + f, :], pb[:, f, :], 0.5, xa[:, f0 + f, :],
                                ALU.mult, ALU.add), dep=(f == 0))
                        b.inc(dve)
                    b.wait(qss, dve)
                    b.op(qss, lambda h, t0=t0: h.dma_start(
                        out=xout_f(t0, 512), in_=ob[:, :, :]))
                    store_mark = b.inc(qss)
                b.op(pool, lambda h, v=b.qss.cnt: h.wait_ge(b.qss.sem, v))
                b.inc(pool)
                b.flush()


def ffn1_weight_map(I):
    bq = I["L1_ffn_b_qkv"]
    return dict(
        wqkv=_c(I["L1_ffn_W_qkv"]),
        bqkv_row=_c(bq.reshape(1, 3 * DM)),
        bqkv_col=_c(bq.reshape(24, 128).T),
        wgate=_c(I["L1_ffn_W_gate"]),
        bgate=_c(I["L1_ffn_b_gate"].reshape(8, 1)),
        wexp=_c(I["L1_ffn_W_experts"]),
        bexp=_c(I["L1_ffn_b_experts"]))


def run_ffn1(x_fm, I):
    nc = build_ffn1()
    wmap = ffn1_weight_map(I)
    in_maps = []
    for j in range(NCORES):
        xs = _c(x_fm[:, 2 * j:2 * j + 2, :].reshape(DM, TOK))
        in_maps.append(dict(x_in=xs, **wmap))
    res = run_bass_kernel_spmd(nc, in_maps, core_ids=list(range(NCORES)))
    out = np.empty((DM, 16, 2048), np.float32)
    for j in range(NCORES):
        out[:, 2 * j:2 * j + 2, :] = res.results[j]["x_out"].reshape(DM, 2, SEQ)
    return out


INTER = 4096
PADC = 1024
SCOL = PADC + SEQ


def emit_attn1(nc, b, W, xin_f, xout_f, first=True):
    A_in, BT, CT = W["A"], W["BT"], W["CT"]
    Dcol, ldt_col, ln1_col, ln2_col = W["Dcol"], W["ldt_col"], W["ln1_col"], W["ln2_col"]
    wgT, wuT, wdT = W["wgT"], W["wuT"], W["wdT"]
    wg_bf, wu_bf, wd_bf = W["wg_bf"], W["wu_bf"], W["wd_bf"]
    with ExitStack() as stack:
        xln = stack.enter_context(_sb(nc, "xln", [128, NCH, TOK], BF16))
        m1 = stack.enter_context(_sb(nc, "m1", [128, NCH, TOK], BF16))
        cols = stack.enter_context(_sb(nc, "cols", [128, 6, NCH], F32))
        i64bf = stack.enter_context(_sb(nc, "i64bf", [64, 64], BF16))
        ones_col = stack.enter_context(_sb(nc, "ones_c", [128, 1], F32))
        ones128 = stack.enter_context(_sb(nc, "ones128", [1, 128], F32))

        qs, qss, qw, qw2 = b.qs, b.qss, b.qw, b.qw2
        qg, qg2, pe, act, dve, pool = b.qg, b.qg2, b.pe, b.act, b.dve, b.pool

        for nm, dst in [("Dcol", 0), ("ldt_col", 1), ("ln1_col", 2), ("ln2_col", 3)]:
            src = {"Dcol": Dcol, "ldt_col": ldt_col, "ln1_col": ln1_col, "ln2_col": ln2_col}[nm]
            b.op(qs, lambda h, src=src, dst=dst: h.dma_start(out=cols[:, dst, :], in_=src))
            b.inc(qs)
        if first:
            for wsrc, wdst in [(wgT, wg_bf), (wuT, wu_bf), (wdT, wd_bf)]:
                b.op(qg, lambda h, a=wsrc, bb=wdst: h.dma_start(out=bb, in_=a))
                b.inc(qg)
        b.op(dve, lambda h: h.memset(ones_col[:, :], 1.0))
        b.op(dve, lambda h: h.memset(ones128[:, :], 1.0))
        b.op(pool, lambda h: h.memset(i64bf[:, :], 0.0))
        b.op(pool, lambda h: h.affine_select(
            out=i64bf[:, :], in_=i64bf[:, :], compare_op=ALU.not_equal,
            fill=1.0, base=0, pattern=[[-1, 64]], channel_multiplier=1))
        b.inc(dve)
        b.inc(pool)

        with ExitStack() as ph12:
            sbf = ph12.enter_context(_sb(nc, "sbuf_scan", [64, 2, 2, SCOL], BF16))
            bbt = ph12.enter_context(_sb(nc, "bbt", [128, NCH, 64], BF16))
            ct_sb = ph12.enter_context(_sb(nc, "ct_sb", [64, NCH, 128], BF16))
            qbf = ph12.enter_context(_sb(nc, "qbf", [64, 12, 64], BF16))
            stats = ph12.enter_context(_sb(nc, "stats", [1, 2, 512], F32))
            sq = ph12.enter_context(_sb(nc, "sqs", [128, 512], F32))
            b.op(qg, lambda h: h.dma_start(out=bbt[:, :, :], in_=BT.rearrange("(c p) n -> p c n", p=128)))
            b.inc(qg)
            b.op(qg, lambda h: h.dma_start(out=ct_sb[:, :, :], in_=CT.rearrange("n (c f) -> n c f", f=128)))
            b.inc(qg)
            b.op(dve, lambda h: h.memset(
                sbf.ap().rearrange("p a s c -> p (a s c)"), 0.0))
            b.inc(dve)
            ls = LnScratch(nc, ph12)
            ls.init(b, dve)
            xt0 = ph12.enter_context(_sb(nc, "xt0", [128, NCH, LN_TT], F32))
            for i in range(TOK // LN_TT):
                b.wait(qs, dve)
                b.op(qs, lambda h, i=i: h.dma_start(
                    out=xt0[:, :, :], in_=xin_f(i * LN_TT, LN_TT)))
                ld = b.inc(qs)
                b.wait(pe, qs, ld)
                b.wait(act, qs, ld)
                emit_ln_tile(b, ls, xt0[:, :, :], xln[:, :, i * LN_TT:(i + 1) * LN_TT])
            b.flush()

            pst, pbc = ls.pst, ls.pbc
            for i in range(TOK // 512):
                t0 = i * 512
                b.wait(pe, dve)
                b.wait(act, dve)
                for c in range(NCH):
                    b.op(act, lambda h, c=c, t0=t0: h.activation(
                        sq[:, :], xln[:, c, t0:t0 + 512], AF.Square))
                    a = b.inc(act)
                    b.wait(pe, act, a)
                    b.op(pe, lambda h, c=c: h.matmul(
                        pst[0:1, 0, 0:256].tensor.ap()[0:1, 0:512] if False else pst[0:1, 0:2, :].rearrange("p a t -> p (a t)")[0:1, 0:512],
                        ls.ones_col[:, 0:1], sq[:, :],
                        start=(c == 0), stop=(c == NCH - 1)))
                    m = b.inc(pe)
                    b.wait(act, pe, m)
                b.wait(dve, pe)
                b.op(dve, lambda h: h.tensor_scalar(
                    stats[:, 0, :], pst[0:1, 0:2, :].rearrange("p a t -> p (a t)")[0:1, 0:512],
                    1.0 / DM, 1e-6, ALU.mult, ALU.add))
                d = b.inc(dve)
                b.wait(act, dve, d)
                b.op(act, lambda h: h.activation(stats[:, 0, :], stats[:, 0, :], AF.Ln))
                b.op(act, lambda h: h.activation(stats[:, 0, :], stats[:, 0, :], AF.Exp, scale=-0.5))
                a = b.inc(act)
                b.wait(pe, act, a)
                b.op(pe, lambda h: h.matmul(
                    pbc[:, 0:2, :].rearrange("p a t -> p (a t)")[:, 0:512],
                    ls.ones_row[:, :], stats[:, 0, :], start=True, stop=True))
                m = b.inc(pe)
                b.wait(dve, pe, m)
                for c in range(NCH):
                    b.op(dve, lambda h, c=c, t0=t0: h.scalar_tensor_tensor(
                        m1[:, c, t0:t0 + 512], xln[:, c, t0:t0 + 512], 1.0,
                        pbc[:, 0:2, :].rearrange("p a t -> p (a t)")[:, 0:512],
                        ALU.mult, ALU.mult), dep=(c == 0))
                b.inc(dve)
            b.flush()

            with (
                _sb(nc, "A_sb", [64, 64], F32) as A_sb,
                _sb(nc, "E_sb", [64, 64], F32) as E_sb,
                _sb(nc, "pt_sb", [64, 2, 64], F32) as pt_sb,
                _sb(nc, "st_sb", [64, 64], F32) as st_sb,
                _sb(nc, "q_sb", [64, 2, 64], F32) as q_sb,
                _sb(nc, "i64f", [64, 64], F32) as i64f,
                _sb(nc, "hcol", [64, 1], F32) as hcol,
                _ps(nc, "pex", [128, 512], F32) as pex_t,
            ):
                pex = pex_t[:, :]
                b.op(qs, lambda h: h.dma_start(out=A_sb[:, :], in_=A_in))
                ald = b.inc(qs)
                b.op(pool, lambda h: h.memset(i64f[:, :], 0.0))
                b.op(pool, lambda h: h.affine_select(
                    out=i64f[:, :], in_=i64f[:, :], compare_op=ALU.not_equal,
                    fill=1.0, base=0, pattern=[[-1, 64]], channel_multiplier=1))
                p = b.inc(pool)
                b.wait(pe, pool, p)
                b.wait(dve, pool, p)
                b.wait(act, qs)
                b.op(act, lambda h: h.activation(cols[:, 1, :], cols[:, 1, :], AF.Exp))
                a = b.inc(act)
                b.wait(dve, act, a)
                b.op(dve, lambda h: h.tensor_tensor(cols[:, 4, :], cols[:, 1, :], cols[:, 2, :], ALU.mult))
                b.op(dve, lambda h: h.tensor_tensor(cols[:, 0, :], cols[:, 0, :], cols[:, 2, :], ALU.mult))
                d = b.inc(dve)
                b.wait(dve, qg)
                for c in range(NCH):
                    b.op(dve, lambda h, c=c: h.tensor_scalar_mul(
                        bbt[:, c, :], bbt[:, c, :], cols[:, 4, c:c + 1]))
                b.inc(dve)
                b.wait(pe, act, a)
                b.op(pe, lambda h: h.matmul(pex[0:1, 0:8], ones_col[:, 0:1], cols[:, 1, :],
                                            start=True, stop=True))
                m = b.inc(pe)
                b.wait(dve, pe, m)
                b.op(dve, lambda h: h.reduce_sum(stats[:, 0, 0:1], pex[0:1, 0:8], axis=AX.X))
                b.op(dve, lambda h: h.tensor_scalar_mul(stats[:, 0, 0:1], stats[:, 0, 0:1], 1.0 / DM))
                d = b.inc(dve)
                b.wait(pe, dve, d)
                b.op(pe, lambda h: h.matmul(pex[0:64, 8:9], ones128[0:1, 0:64], stats[:, 0, 0:1],
                                            start=True, stop=True))
                m = b.inc(pe)
                b.wait(dve, pe, m)
                b.op(dve, lambda h: h.tensor_copy(hcol[:, :], pex[0:64, 8:9]))
                b.wait(dve, qs, ald)
                b.op(dve, lambda h: h.tensor_scalar_mul(E_sb[:, :], A_sb[:, :], hcol[:, 0:1]))
                b.op(dve, lambda h: h.tensor_copy(pt_sb[:, 0, :], i64f[:, :]))
                b.op(dve, lambda h: h.tensor_copy(st_sb[:, :], i64f[:, :]))
                d = b.inc(dve)
                for k in range(1, 20):
                    cur, nxt = (k - 1) % 2, k % 2
                    b.wait(pe, dve)
                    b.wait(pe, act)
                    b.op(pe, lambda h, cur=cur: h.matmul(
                        pex[0:64, 64:128], E_sb[:, :], pt_sb[:, cur, :], start=True, stop=True))
                    m = b.inc(pe)
                    b.wait(act, pe, m)
                    b.op(act, lambda h, k=k, nxt=nxt: h.activation(
                        pt_sb[:, nxt, :], pex[0:64, 64:128], AF.Identity, scale=1.0 / k))
                    a = b.inc(act)
                    b.wait(dve, act, a)
                    b.op(dve, lambda h, nxt=nxt: h.tensor_tensor(
                        st_sb[:, :], st_sb[:, :], pt_sb[:, nxt, :], ALU.add))
                    b.inc(dve)
                b.op(dve, lambda h: h.tensor_copy(q_sb[:, 0, :], st_sb[:, :]))
                b.op(dve, lambda h: h.tensor_copy(qbf[:, 0, :], st_sb[:, :]))
                d = b.inc(dve)
                for j in range(1, 11):
                    cur, nxt = (j - 1) % 2, j % 2
                    b.wait(pe, dve)
                    b.op(pe, lambda h, cur=cur: h.transpose(
                        pex[0:64, 128:192], q_sb[:, cur, :], i64f[:, :]))
                    m = b.inc(pe)
                    b.wait(act, pe, m)
                    b.op(act, lambda h: h.activation(st_sb[:, :], pex[0:64, 128:192], AF.Identity))
                    a = b.inc(act)
                    b.wait(pe, act, a)
                    b.op(pe, lambda h, cur=cur: h.matmul(
                        pex[0:64, 64:128], st_sb[:, :], q_sb[:, cur, :], start=True, stop=True))
                    m = b.inc(pe)
                    b.wait(dve, pe, m)
                    b.op(dve, lambda h, nxt=nxt: h.tensor_copy(q_sb[:, nxt, :], pex[0:64, 64:128]))
                    b.op(dve, lambda h, j=j, nxt=nxt: h.tensor_copy(qbf[:, j, :], q_sb[:, nxt, :]))
                    d = b.inc(dve)
                b.wait(pe, dve, d)
                for s_ in range(NSEQ):
                    for i in range(SEQ // 512):
                        t0 = s_ * SEQ + i * 512
                        b.wait(pe, act)
                        for c in range(NCH):
                            b.op(pe, lambda h, c=c, t0=t0: h.matmul(
                                pex[0:64, 0:512], bbt[:, c, :], m1[:, c, t0:t0 + 512],
                                start=(c == 0), stop=(c == NCH - 1)))
                        m = b.inc(pe)
                        b.wait(act, pe, m)
                        b.op(act, lambda h, s_=s_, i=i: h.activation(
                            sbf[:, 0, s_, PADC + i * 512:PADC + (i + 1) * 512],
                            pex[0:64, 0:512], AF.Identity))
                        b.inc(act)
                b.wait(act, dve)
                for j in range(11):
                    src, dst = j % 2, (j + 1) % 2
                    shift = 1 << j
                    prev_mark = b.act.cnt
                    b.wait(pe, act, prev_mark)
                    for s_ in range(NSEQ):
                        for i in range(SEQ // 512):
                            c0 = PADC + i * 512
                            b.wait(pe, act)
                            b.op(pe, lambda h, j=j, src=src, s_=s_, c0=c0, shift=shift: h.matmul(
                                pex[0:64, 0:512], qbf[:, j, :],
                                sbf[:, src, s_, c0 - shift:c0 - shift + 512],
                                start=True, stop=False))
                            b.op(pe, lambda h, src=src, s_=s_, c0=c0: h.matmul(
                                pex[0:64, 0:512], i64bf[:, :],
                                sbf[:, src, s_, c0:c0 + 512],
                                start=False, stop=True))
                            m = b.inc(pe)
                            b.wait(act, pe, m)
                            b.op(act, lambda h, dst=dst, s_=s_, c0=c0: h.activation(
                                sbf[:, dst, s_, c0:c0 + 512], pex[0:64, 0:512], AF.Identity))
                            b.inc(act)
                fin = 11 % 2
                b.wait(pe, act)
                for s_ in range(NSEQ):
                    for i in range(SEQ // 512):
                        t0 = s_ * SEQ + i * 512
                        c0 = PADC + i * 512
                        for fc in range(NCH):
                            b.wait(pe, dve)
                            b.op(pe, lambda h, fc=fc, fin=fin, s_=s_, c0=c0: h.matmul(
                                pex[:, 0:512], ct_sb[:, fc, :],
                                sbf[:, fin, s_, c0:c0 + 512],
                                start=True, stop=True))
                            m = b.inc(pe)
                            b.wait(dve, pe, m)
                            b.op(dve, lambda h, fc=fc, t0=t0: h.scalar_tensor_tensor(
                                sq[:, :], m1[:, fc, t0:t0 + 512], cols[:, 0, fc:fc + 1],
                                xln[:, fc, t0:t0 + 512], ALU.mult, ALU.add))
                            b.op(dve, lambda h, fc=fc, t0=t0: h.tensor_tensor(
                                m1[:, fc, t0:t0 + 512], sq[:, :], pex[:, 0:512], ALU.add))
                            b.inc(dve)
                b.flush()

            for i in range(TOK // 512):
                t0 = i * 512
                b.wait(pe, dve)
                b.wait(act, dve)
                for c in range(NCH):
                    b.op(act, lambda h, c=c, t0=t0: h.activation(
                        sq[:, :], m1[:, c, t0:t0 + 512], AF.Square))
                    a = b.inc(act)
                    b.wait(pe, act, a)
                    b.op(pe, lambda h, c=c: h.matmul(
                        pst[0:1, 0:2, :].rearrange("p a t -> p (a t)")[0:1, 0:512],
                        ls.ones_col[:, 0:1], sq[:, :],
                        start=(c == 0), stop=(c == NCH - 1)))
                    m = b.inc(pe)
                    b.wait(act, pe, m)
                b.wait(dve, pe)
                b.op(dve, lambda h: h.tensor_scalar(
                    stats[:, 0, :], pst[0:1, 0:2, :].rearrange("p a t -> p (a t)")[0:1, 0:512],
                    1.0 / DM, 1e-6, ALU.mult, ALU.add))
                d = b.inc(dve)
                b.wait(act, dve, d)
                b.op(act, lambda h: h.activation(stats[:, 0, :], stats[:, 0, :], AF.Ln))
                b.op(act, lambda h: h.activation(stats[:, 0, :], stats[:, 0, :], AF.Exp, scale=-0.5))
                a = b.inc(act)
                b.wait(pe, act, a)
                b.op(pe, lambda h: h.matmul(
                    pbc[:, 0:2, :].rearrange("p a t -> p (a t)")[:, 0:512],
                    ls.ones_row[:, :], stats[:, 0, :], start=True, stop=True))
                m = b.inc(pe)
                b.wait(dve, pe, m)
                for c in range(NCH):
                    b.op(dve, lambda h, c=c, t0=t0: h.scalar_tensor_tensor(
                        sq[:, :], m1[:, c, t0:t0 + 512], 1.0,
                        pbc[:, 0:2, :].rearrange("p a t -> p (a t)")[:, 0:512],
                        ALU.mult, ALU.mult), dep=True)
                    b.op(dve, lambda h, c=c, t0=t0: h.tensor_scalar_mul(
                        xln[:, c, t0:t0 + 512], sq[:, :], cols[:, 3, c:c + 1]))
                b.inc(dve)
            b.flush()

        with (
            _sb(nc, "sgu", [128, 32, 512], BF16) as sgu,
            _sb(nc, "wb3", [128, 2, NCH, 256], BF16) as wb3,
            _sb(nc, "wd3", [128, 2, 512], BF16) as wd3,
            _sb(nc, "xm", [128, NCH, 512], F32) as xm,
            _sb(nc, "ob", [128, NCH, 512], F32) as ob,
            _sb(nc, "sg_tmp", [128, 512], F32) as sg_tmp,
            _ps(nc, "pgu", [128, 2, 512], F32) as pgu_t,
            _ps(nc, "pdn", [128, 4, 512], F32) as pdn_t,
        ):
            pgu, pdn = pgu_t[:, :, :], pdn_t[:, :, :]
            store_mark = 0
            for tt in range(TOK // 512):
                t0 = tt * 512
                b.wait(qs, dve)
                b.op(qs, lambda h, t0=t0: h.dma_start(
                    out=xm[:, :, :], in_=xin_f(t0, 512)))
                xld = b.inc(qs)
                b.wait(dve, qs, xld)
                for c in range(NCH):
                    b.op(dve, lambda h, c=c, t0=t0: h.tensor_tensor(
                        xm[:, c, :], xm[:, c, :], m1[:, c, t0:t0 + 512], ALU.add), dep=(c == 0))
                xm_mark = b.inc(dve)
                for ii in range(32):
                    ib = ii % 2
                    b.wait(qw if ib == 0 else qw2, pe)
                    q_ = qw if ib == 0 else qw2
                    b.op(q_, lambda h, ii=ii, ib=ib: h.dma_start(
                        out=wb3[:, ib, :, 0:128],
                        in_=wg_bf.rearrange("(k p) f -> p k f", p=128)[:, :, ii * 128:(ii + 1) * 128]))
                    b.inc(q_)
                    b.op(q_, lambda h, ii=ii, ib=ib: h.dma_start(
                        out=wb3[:, ib, :, 128:256],
                        in_=wu_bf.rearrange("(k p) f -> p k f", p=128)[:, :, ii * 128:(ii + 1) * 128]))
                    wld = b.inc(q_)
                    b.wait(pe, q_, wld)
                    b.wait(pe, act)
                    b.wait(pe, dve)
                    for k in range(NCH):
                        b.op(pe, lambda h, k=k, ib=ib, t0=t0: h.matmul(
                            pgu[:, 0, :], wb3[:, ib, k, 0:128], xln[:, k, t0:t0 + 512],
                            start=(k == 0), stop=(k == NCH - 1)))
                    for k in range(NCH):
                        b.op(pe, lambda h, k=k, ib=ib, t0=t0: h.matmul(
                            pgu[:, 1, :], wb3[:, ib, k, 128:256], xln[:, k, t0:t0 + 512],
                            start=(k == 0), stop=(k == NCH - 1)))
                    m = b.inc(pe)
                    b.wait(act, pe, m)
                    b.wait(act, dve)
                    b.op(act, lambda h: h.activation(sg_tmp[:, :], pgu[:, 0, :], AF.Silu))
                    a = b.inc(act)
                    b.wait(dve, act, a)
                    b.op(dve, lambda h, ii=ii: h.tensor_tensor(
                        sgu[:, ii, :], sg_tmp[:, :], pgu[:, 1, :], ALU.mult))
                    b.inc(dve)
                sgu_mark = b.dve.cnt
                for fh in range(2):
                    f0 = fh * 4
                    for ii in range(32):
                        ib = ii % 2
                        q_ = qw if ib == 0 else qw2
                        b.wait(q_, pe)
                        b.op(q_, lambda h, ii=ii, ib=ib, f0=f0: h.dma_start(
                            out=wd3[:, ib, :],
                            in_=wd_bf.rearrange("(i p) f -> p i f", p=128)[:, ii, f0 * 128:f0 * 128 + 512]))
                        wld = b.inc(q_)
                        b.wait(pe, q_, wld)
                        if ii == 0:
                            b.wait(pe, dve)
                        for f in range(4):
                            b.op(pe, lambda h, f=f, ii=ii, ib=ib: h.matmul(
                                pdn[:, f, :], wd3[:, ib, f * 128:(f + 1) * 128], sgu[:, ii, :],
                                start=(ii == 0), stop=(ii == 31)))
                        b.inc(pe)
                    m = b.pe.cnt
                    b.wait(dve, pe, m)
                    b.wait(dve, qss, store_mark)
                    for f in range(4):
                        b.op(dve, lambda h, f=f, f0=f0: h.scalar_tensor_tensor(
                            ob[:, f0 + f, :], pdn[:, f, :], 1.0, xm[:, f0 + f, :],
                            ALU.mult, ALU.add), dep=(f == 0))
                    b.inc(dve)
                b.wait(qss, dve)
                b.op(qss, lambda h, t0=t0: h.dma_start(
                    out=xout_f(t0, 512), in_=ob[:, :, :]))
                store_mark = b.inc(qss)
            b.op(pool, lambda h, v=b.qss.cnt: h.wait_ge(b.qss.sem, v))
            b.inc(pool)
            b.flush()


def attn1_weight_map(I):
    return dict(
        A=_c(I["L1_attn_A"]),
        BT=_c(I["L1_attn_B"].T),
        CT=_c(I["L1_attn_C"].T),
        Dcol=_c(I["L1_attn_D"].reshape(NCH, 128).T),
        ldt_col=_c(I["L1_attn_log_dt"].reshape(NCH, 128).T),
        ln1_col=_c(I["L1_attn_ln1_w"].reshape(NCH, 128).T),
        ln2_col=_c(I["L1_attn_ln2_w"].reshape(NCH, 128).T),
        wgT=_c(I["L1_attn_w_gate"].T),
        wuT=_c(I["L1_attn_w_up"].T),
        wdT=_c(I["L1_attn_w_down"].T))


def run_attn1(x_fm, I):
    nc = build_attn1()
    wmap = attn1_weight_map(I)
    in_maps = []
    for j in range(NCORES):
        xs = _c(x_fm[:, 2 * j:2 * j + 2, :].reshape(DM, TOK))
        in_maps.append(dict(x_in=xs, **wmap))
    res = run_bass_kernel_spmd(nc, in_maps, core_ids=list(range(NCORES)))
    out = np.empty((DM, 16, 2048), np.float32)
    for j in range(NCORES):
        out[:, 2 * j:2 * j + 2, :] = res.results[j]["x_out"].reshape(DM, 2, SEQ)
    return out


DIN = 2048
NH = 32
HDIM = 64
DST = 128
CK = 64
CONVD = 2304
NCONV = CONVD // 128


def emit_attn0(nc, b, W, xin_f, xout_f, first=True):
    DBG = False
    win, convd, convb = W["win"], W["convd"], W["convb"]
    dtb, alog, dcol, nwcol = W["dtb"], W["alog"], W["dcol"], W["nwcol"]
    wout, e32, wout_bf = W["wout"], W["e32"], W["wout_bf"]
    with ExitStack() as stack:
        e32_sb = stack.enter_context(_sb(nc, "e32_sb", [NH, DIN], BF16))
        tri = stack.enter_context(_sb(nc, "tri", [128, 64], F32))
        ones_col = stack.enter_context(_sb(nc, "ones_c", [128, 1], F32))
        ones_row = stack.enter_context(_sb(nc, "ones_r", [1, 128], F32))
        orow_bf = stack.enter_context(_sb(nc, "orow_bf", [1, 128], BF16))
        nrow_bf = stack.enter_context(_sb(nc, "nrow_bf", [1, 128], BF16))
        identF = stack.enter_context(_sb(nc, "identF", [128, 128], F32))
        identB = stack.enter_context(_sb(nc, "identB", [128, 128], BF16))
        cols = stack.enter_context(_sb(nc, "colsA", [128, 3, 18], F32))
        hcols = stack.enter_context(_sb(nc, "hcolsA", [NH, 2], F32))
        dt_bf = stack.enter_context(_sb(nc, "dt_bf", [NH, SEQ], BF16))
        cs_hi = stack.enter_context(_sb(nc, "cs_hi", [NH, SEQ], BF16))
        cs_lo = stack.enter_context(_sb(nc, "cs_lo", [NH, SEQ], BF16))
        dtdd = stack.enter_context(_sb(nc, "dtdd", [NH, SEQ], BF16))
        gam = stack.enter_context(_sb(nc, "gam", [NH, 32], BF16))
        grow = stack.enter_context(_sb(nc, "grow", [1, DIN], BF16))
        edrow = stack.enter_context(_sb(nc, "edrow", [1, DIN], F32))
        edrow_bf = stack.enter_context(_sb(nc, "edrow_bf", [1, DIN], BF16))
        rowbuf = stack.enter_context(_sb(nc, "rowbuf", [1, 2, 2, DIN], BF16))
        Ssb = stack.enter_context(_sb(nc, "Ssb", [128, 2, DIN], BF16))
        Stmp = stack.enter_context(_sb(nc, "Stmp", [128, DIN], BF16))
        Msb = stack.enter_context(_sb(nc, "Msb", [64, DIN], BF16))
        L2 = stack.enter_context(_sb(nc, "L2", [64, DIN], BF16))
        cbt = stack.enter_context(_sb(nc, "cbt", [64, 64], BF16))
        cdec = stack.enter_context(_sb(nc, "cdec", [128, DIN], BF16))
        xdt = stack.enter_context(_sb(nc, "xdt", [64, DIN], BF16))
        xdd = stack.enter_context(_sb(nc, "xdd", [64, DIN], BF16))
        xbc = stack.enter_context(_sb(nc, "xbc", [128, NCONV, 515], BF16))
        xbcs = stack.enter_context(_sb(nc, "xbcs", [128, 16, 512], BF16))
        bc_fm = stack.enter_context(_sb(nc, "bc_fm", [128, 2, 512], BF16))
        btok = stack.enter_context(_sb(nc, "btok", [64, 128], BF16))
        x_tok = stack.enter_context(_sb(nc, "x_tok", [64, DIN], BF16))
        zs = stack.enter_context(_sb(nc, "zs", [128, 2, 512], BF16))
        y_fm = stack.enter_context(_sb(nc, "y_fm", [128, 16, 512], BF16))
        sqa = stack.enter_context(_sb(nc, "sqa", [128, 512], F32))
        sqb = stack.enter_context(_sb(nc, "sqb", [128, 512], F32))
        st2 = stack.enter_context(_sb(nc, "st2", [1, 2, 512], F32))
        ob = stack.enter_context(_sb(nc, "obA", [128, 2, 512], F32))
        wpb = stack.enter_context(_sb(nc, "wpb", [128, 2, NCH, 512], BF16))
        cvd = stack.enter_context(_sb(nc, "cvd", [128, 2, 4, 128], BF16))
        xt0 = stack.enter_context(_sb(nc, "xt0A", [128, NCH, 512], F32))
        xlnt = stack.enter_context(_sb(nc, "xlnt", [128, NCH, 512], BF16))
        csf = stack.enter_context(_sb(nc, "csf", [NH, 64], F32))
        pA_t = stack.enter_context(_ps(nc, "pA", [128, 4, 512], F32))
        pS_t = stack.enter_context(_ps(nc, "pS", [128, 4, 512], F32))
        pA = pA_t[:, :, :]
        pS = pS_t[:, :, :]
        pAfull = pA_t.ap().rearrange("p a t -> p (a t)")
        pSfull = pS_t.ap().rearrange("p a t -> p (a t)")

        qs, qss, qw, qw2 = b.qs, b.qss, b.qw, b.qw2
        qg, qg2, pe, act, dve, pool = b.qg, b.qg2, b.pe, b.act, b.dve, b.pool
        wsrc = win.rearrange("(k p) f -> p k f", p=128)

        ls = LnScratch(nc, stack,
                       pS_t[0:1, 3, :].rearrange("p (a t) -> p a t", a=2),
                       pS_t[:, 2, :].rearrange("p (a t) -> p a t", a=2))

        b.op(qs, lambda h: h.dma_start(out=cols[:, 0, :], in_=convb))
        b.inc(qs)
        b.op(qs, lambda h: h.dma_start(out=cols[:, 1, 0:16], in_=dcol))
        b.inc(qs)
        b.op(qs, lambda h: h.dma_start(out=cols[:, 2, 0:16], in_=nwcol))
        b.inc(qs)
        b.op(qs, lambda h: h.dma_start(out=hcols[:, 0:1], in_=dtb))
        b.inc(qs)
        b.op(qs, lambda h: h.dma_start(out=hcols[:, 1:2], in_=alog))
        b.inc(qs)
        b.op(qg, lambda h: h.dma_start(out=e32_sb[:, :], in_=e32))
        b.inc(qg)
        if first:
            b.op(qg, lambda h: h.dma_start(out=wout_bf, in_=wout))
            b.inc(qg)
        b.op(dve, lambda h: h.memset(ones_col[:, :], 1.0))
        b.op(dve, lambda h: h.memset(ones_row[:, :], 1.0))
        b.op(dve, lambda h: h.memset(orow_bf[:, :], 1.0))
        b.op(dve, lambda h: h.memset(nrow_bf[:, :], -1.0))
        b.op(dve, lambda h: h.memset(Ssb[:, 0, :], 0.0))
        b.inc(dve)
        for idt in ("identF", "identB"):
            t_ = {"identF": identF, "identB": identB}[idt]
            b.op(pool, lambda h, t_=t_: h.memset(t_[:, :], 0.0))
            b.op(pool, lambda h, t_=t_: h.affine_select(
                out=t_[:, :], in_=t_[:, :], compare_op=ALU.not_equal,
                fill=1.0, base=0, pattern=[[-1, 128]], channel_multiplier=1))
        b.op(pool, lambda h: h.memset(tri[:, :], 0.0))
        b.op(pool, lambda h: h.affine_select(
            out=tri[0:64, :], in_=tri[0:64, :], compare_op=ALU.is_gt,
            fill=1.0, base=0, pattern=[[-1, 64]], channel_multiplier=1))
        b.op(pool, lambda h: h.affine_select(
            out=tri[64:128, :], in_=tri[64:128, :], compare_op=ALU.is_gt,
            fill=1.0, base=0, pattern=[[-1, 64]], channel_multiplier=1))
        pm = b.inc(pool)
        b.wait(pe, pool, pm)
        b.wait(dve, pool, pm)
        ls.init(b, dve)
        b.wait(act, qs)
        b.op(act, lambda h: h.activation(hcols[:, 1:2], hcols[:, 1:2], AF.Exp))
        a = b.inc(act)
        b.wait(dve, act, a)
        b.op(dve, lambda h: h.tensor_scalar_mul(hcols[:, 1:2], hcols[:, 1:2], -1.0))
        b.inc(dve)
        b.flush()

        wdt_piece = 8

        for s_ in range(NSEQ):
            b.wait(qg, pe)
            b.op(qg, lambda h: h.dma_start(
                out=wpb[:, 0, :, 0:NH], in_=wsrc[:, :, 4352:4384]))
            wld = b.inc(qg)
            for i in range(SEQ // 512):
                t0 = i * 512
                g0 = s_ * SEQ + t0
                b.wait(qs, dve)
                b.wait(qs, pe)
                b.op(qs, lambda h, g0=g0: h.dma_start(
                    out=xt0[:, :, :], in_=xin_f(g0, 512)))
                ld = b.inc(qs)
                b.wait(pe, qs, ld)
                b.wait(act, qs, ld)
                emit_ln_tile(b, ls, xt0[:, :, 0:256], xlnt[:, :, 0:256])
                emit_ln_tile(b, ls, xt0[:, :, 256:512], xlnt[:, :, 256:512])
                b.wait(pe, qg, wld)
                b.wait(pe, dve)
                for k in range(NCH):
                    b.op(pe, lambda h, k=k: h.matmul(
                        pA[0:NH, 0, :], wpb[:, 0, k, 0:NH], xlnt[:, k, :],
                        start=(k == 0), stop=(k == NCH - 1)))
                m = b.inc(pe)
                b.wait(act, pe, m)
                b.op(act, lambda h: h.activation(
                    sqa[0:NH, :], pA[0:NH, 0, :], AF.Identity, bias=hcols[:, 0:1]))
                a = b.inc(act)
                b.op(act, lambda h: h.activation(sqb[0:NH, :], sqa[0:NH, :], AF.Abs))
                a = b.inc(act)
                b.op(act, lambda h: h.activation(sqb[0:NH, :], sqb[0:NH, :], AF.Exp, scale=-1.0))
                a = b.inc(act)
                b.wait(dve, act, a)
                b.op(dve, lambda h: h.tensor_scalar_add(sqb[0:NH, :], sqb[0:NH, :], 1.0))
                d = b.inc(dve)
                b.wait(act, dve, d)
                b.op(act, lambda h: h.activation(sqb[0:NH, :], sqb[0:NH, :], AF.Ln))
                a = b.inc(act)
                b.wait(dve, act, a)
                b.op(dve, lambda h: h.tensor_single_scalar(
                    sqa[0:NH, :], sqa[0:NH, :], 0.0, ALU.max))
                b.op(dve, lambda h: h.tensor_tensor(
                    sqa[0:NH, :], sqa[0:NH, :], sqb[0:NH, :], ALU.add))
                b.op(dve, lambda h, t0=t0: h.tensor_copy(dt_bf[:, t0:t0 + 512], sqa[0:NH, :]))
                b.op(dve, lambda h: h.tensor_scalar_mul(sqb[0:NH, :], sqa[0:NH, :], hcols[:, 1:2]))
                d = b.inc(dve)
                for sub in range(4):
                    l0 = sub * 128
                    b.wait(pe, dve, d)
                    b.op(pe, lambda h, l0=l0: h.transpose(
                        pA[0:128, 2, 0:NH], sqb[0:NH, l0:l0 + 128], identF[0:NH, 0:NH]))
                    m = b.inc(pe)
                    b.wait(act, pe, m)
                    b.op(act, lambda h: h.activation(sqa[:, 0:NH], pA[0:NH, 2, 0:128], AF.Identity, apply_transpose=True) if False else h.activation(sqb[:, 448:480] if False else sqa[:, 0:32], pA[:, 2, 0:NH] if False else pA[0:128, 2, 0:NH], AF.Identity))
                    a = b.inc(act)
                    b.wait(pe, act, a)
                    b.op(pe, lambda h: h.matmul(
                        pA[0:64, 3, 0:NH], tri[0:64, :], sqa[0:64, 0:NH],
                        start=True, stop=True))
                    b.op(pe, lambda h: h.matmul(
                        pA[64:128, 3, 0:NH], tri[64:128, :], sqa[64:128, 0:NH],
                        start=True, stop=True))
                    m = b.inc(pe)
                    b.wait(act, pe, m)
                    b.op(act, lambda h: h.activation(sqb[:, 0:NH], pA[:, 3, 0:NH], AF.Identity))
                    a = b.inc(act)
                    b.wait(pe, act, a)
                    b.op(pe, lambda h: h.transpose(
                        pA[0:NH, 2, 128:256], sqb[:, 0:NH], identF[:, :]))
                    m = b.inc(pe)
                    b.wait(dve, pe, m)
                    b.op(dve, lambda h, t0=t0, l0=l0: h.tensor_copy(
                        cs_hi[:, t0 + l0:t0 + l0 + 128], pA[0:NH, 2, 128:256]))
                    b.op(dve, lambda h, t0=t0, l0=l0: h.tensor_tensor(
                        cs_lo[:, t0 + l0:t0 + l0 + 128], pA[0:NH, 2, 128:256],
                        cs_hi[:, t0 + l0:t0 + l0 + 128], ALU.subtract))
                    d = b.inc(dve)
            b.op(dve, lambda h: h.tensor_tensor(
                sqa[0:NH, 0:32], bass.AP(cs_hi, 63, [[SEQ, NH], [64, 32]]),
                bass.AP(cs_lo, 63, [[SEQ, NH], [64, 32]]), ALU.add))
            d = b.inc(dve)
            b.wait(act, dve, d)
            b.op(act, lambda h: h.activation(gam[:, :], sqa[0:NH, 0:32], AF.Exp))
            ga = b.inc(act)
            for c in range(32):
                cl0 = c * 64
                b.op(dve, lambda h, cl0=cl0: h.tensor_tensor(
                    csf[:, :], cs_hi[:, cl0:cl0 + 64], cs_lo[:, cl0:cl0 + 64], ALU.add))
                d = b.inc(dve)
                b.wait(act, dve, d)
                b.op(act, lambda h: h.activation(
                    csf[:, :], csf[:, :], AF.Exp, scale=-1.0, bias=csf[:, 63:64]))
                a = b.inc(act)
                b.wait(dve, act, a)
                b.op(dve, lambda h, cl0=cl0: h.tensor_tensor(
                    dtdd[:, cl0:cl0 + 64], dt_bf[:, cl0:cl0 + 64], csf[:, :], ALU.mult))
                b.inc(dve)
            if DBG and s_ == 0:
                b.wait(qg, dve)
                b.op(qg, lambda h: h.dma_start(out=dbg_dt, in_=dt_bf[:, :]))
                b.inc(qg)
                b.op(qg, lambda h: h.dma_start(out=dbg_cs[0], in_=cs_hi[:, :]))
                b.inc(qg)
                b.op(qg, lambda h: h.dma_start(out=dbg_cs[1], in_=cs_lo[:, :]))
                b.inc(qg)
            b.flush()

            for i in range(SEQ // 512):
                t0g = s_ * SEQ + i * 512
                b.wait(qs, dve)
                b.wait(qs, pe)
                b.op(qs, lambda h, t0g=t0g: h.dma_start(
                    out=xt0[:, :, :], in_=xin_f(t0g, 512)))
                ld = b.inc(qs)
                b.wait(pe, qs, ld)
                b.wait(act, qs, ld)
                emit_ln_tile(b, ls, xt0[:, :, 0:256], xlnt[:, :, 0:256])
                emit_ln_tile(b, ls, xt0[:, :, 256:512], xlnt[:, :, 256:512])
                b.wait(pe, dve)
                b.wait(pe, act)
                ipld = {}

                def ip_load(fo, upto=None):
                    fp = fo % 2
                    q_ = qg if fp == 0 else qg2
                    width = 512 if fo < 8 else 256
                    b.wait(q_, pe, upto)
                    b.op(q_, lambda h, fo=fo, fp=fp, width=width: h.dma_start(
                        out=wpb[:, fp, :, 0:width],
                        in_=wsrc[:, :, fo * 512:fo * 512 + width]))
                    ipld[fo] = (b.inc(q_), q_)
                ip_load(4)
                evm = {}
                pemk = {}
                for fo in range(4, 9):
                    if fo < 8:
                        ip_load(fo + 1, pemk.get(fo - 1))
                    fp = fo % 2
                    width = 512 if fo < 8 else 256
                    wld, q_ = ipld[fo]
                    b.wait(pe, q_, wld)
                    nfc = width // 128
                    for fc in range(nfc):
                        pb_ = (fo * 4 + fc) % 2
                        if pb_ in evm:
                            b.wait(pe, act, evm[pb_])
                        for k in range(NCH):
                            b.op(pe, lambda h, k=k, fc=fc, fp=fp, pb_=pb_: h.matmul(
                                pA[:, pb_, :], wpb[:, fp, k, fc * 128:(fc + 1) * 128],
                                xlnt[:, k, :], start=(k == 0), stop=(k == NCH - 1)))
                        m = b.inc(pe)
                        b.wait(act, pe, m)
                        cglob = (fo - 4) * 4 + fc
                        b.op(act, lambda h, cglob=cglob, pb_=pb_: h.activation(
                            xbc[:, cglob, 3:515], pA[:, pb_, :], AF.Identity))
                        evm[pb_] = b.inc(act)
                    pemk[fo] = b.pe.cnt
                if i == 0:
                    b.wait(dve, act)
                    b.op(dve, lambda h: h.memset(xbc[:, :, 0:3], 0.0))
                    b.inc(dve)
                b.wait(act, dve)
                b.wait(pe, dve)
                b.wait(pe, act)
                cvld = {}

                def cv_load(c, upto=None):
                    cp_ = c % 2
                    q_ = qg if cp_ == 0 else qg2
                    b.wait(q_, pe, upto)
                    b.op(q_, lambda h, c=c, cp_=cp_: h.dma_start(
                        out=cvd[:, cp_, :, :],
                        in_=convd[:, c].rearrange("j pp q -> pp j q")))
                    cvld[c] = (b.inc(q_), q_)
                cv_load(0)
                cpemk = {}
                for c in range(NCONV):
                    if c + 1 < NCONV:
                        cv_load(c + 1, cpemk.get(c - 1))
                    cp_ = c % 2
                    bank = 2 + (c % 2)
                    wld, q_ = cvld[c]
                    b.wait(pe, q_, wld)
                    if ("cv", bank) in evm:
                        b.wait(pe, act, evm[("cv", bank)])
                    for j in range(4):
                        b.op(pe, lambda h, c=c, j=j, cp_=cp_, bank=bank: h.matmul(
                            pA[:, bank, :],
                            cvd[:, cp_, j, :], xbc[:, c, j:j + 512],
                            start=(j == 0), stop=(j == 3)))
                    m = b.inc(pe)
                    cpemk[c] = m
                    b.wait(act, pe, m)
                    if c < 16:
                        b.op(act, lambda h, c=c, bank=bank: h.activation(
                            xbcs[:, c, :], pA[:, bank, :], AF.Silu, bias=cols[:, 0, c:c + 1]))
                    else:
                        b.op(act, lambda h, c=c, bank=bank: h.activation(
                            bc_fm[:, c - 16, :], pA[:, bank, :], AF.Silu, bias=cols[:, 0, c:c + 1]))
                    evm[("cv", bank)] = b.inc(act)
                b.wait(dve, act)
                b.op(dve, lambda h: h.tensor_copy(
                    xbc[:, :, 0:3], xbc[:, :, 512:515]))
                b.inc(dve)
                if DBG and s_ == 0 and i == 0:
                    b.wait(qg, act)
                    b.op(qg, lambda h: h.dma_start(out=dbg_xbcs, in_=xbcs[:, :, :]))
                    b.inc(qg)
                    b.op(qg, lambda h: h.dma_start(out=dbg_bc, in_=bc_fm[:, :, :]))
                    b.inc(qg)
                for cp in range(4):
                    l0 = cp * 128
                    for j in range(2):
                        cseq = (i * 8 + cp * 2 + j)
                        q_ = qw if j == 0 else qw2
                        b.wait(q_, pe)
                        b.op(q_, lambda h, j=j, cseq=cseq: h.dma_start(
                            out=rowbuf[0:1, j, 0, :],
                            in_=cs_hi[:, cseq * 64:(cseq + 1) * 64]))
                        b.inc(q_)
                        b.op(q_, lambda h, j=j, cseq=cseq: h.dma_start(
                            out=rowbuf[0:1, j, 1, :],
                            in_=cs_lo[:, cseq * 64:(cseq + 1) * 64]))
                        rw = b.inc(q_)
                        b.wait(pe, q_, rw)
                    for j in range(2):
                        cseq = i * 8 + cp * 2 + j
                        jl0 = l0 + j * 64
                        base = 0
                        b.wait(pe, dve)
                        b.wait(pe, act)
                        for xc in range(16):
                            b.op(pe, lambda h, xc=xc, jl0=jl0: h.transpose(
                                pA_t.ap().bitcast(BF16)[0:64, 3, 0:128],
                                xbcs[:, xc, jl0:jl0 + 64], identB[:, :]))
                            m = b.inc(pe)
                            b.wait(act, pe, m)
                            b.op(act, lambda h, xc=xc: h.activation(
                                x_tok[:, xc * 128:(xc + 1) * 128],
                                pA_t.ap().bitcast(BF16)[0:64, 3, 0:128], AF.Identity))
                            a = b.inc(act)
                            b.wait(pe, act, a)
                        b.op(pe, lambda h, jl0=jl0: h.transpose(
                            pA_t.ap().bitcast(BF16)[0:64, 3, 0:128],
                            bc_fm[:, 0, jl0:jl0 + 64], identB[:, :]))
                        m = b.inc(pe)
                        b.wait(act, pe, m)
                        b.op(act, lambda h: h.activation(
                            btok[:, :], pA_t.ap().bitcast(BF16)[0:64, 3, 0:128], AF.Identity))
                        a = b.inc(act)
                        b.wait(pe, act, a)
                        b.wait(pe, dve)
                        b.op(pe, lambda h, jl0=jl0: h.matmul(
                            pS[0:64, 0, 0:64],
                            bc_fm[:, 0, jl0:jl0 + 64], bc_fm[:, 1, jl0:jl0 + 64],
                            start=True, stop=True))
                        m = b.inc(pe)
                        b.wait(dve, pe, m)
                        b.op(dve, lambda h: h.tensor_tensor(
                            cbt[:, :], pS[0:64, 0, 0:64], tri[0:64, :], ALU.mult))
                        dcb = b.inc(dve)
                        b.wait(pe, dve, dcb)
                        for q4 in range(4):
                            sl = slice(q4 * 512, (q4 + 1) * 512)
                            b.op(pe, lambda h, cseq=cseq, q4=q4, sl=sl: h.matmul(
                                pAfull[0:64, sl], cs_hi[:, cseq * 64:(cseq + 1) * 64],
                                e32_sb[:, sl], start=True, stop=False))
                            b.op(pe, lambda h, cseq=cseq, sl=sl: h.matmul(
                                pAfull[0:64, sl], cs_lo[:, cseq * 64:(cseq + 1) * 64],
                                e32_sb[:, sl], start=False, stop=False))
                            b.op(pe, lambda h, j=j, sl=sl: h.matmul(
                                pAfull[0:64, sl], nrow_bf[0:1, 0:64],
                                rowbuf[0:1, j, 0, sl], start=False, stop=False))
                            b.op(pe, lambda h, j=j, sl=sl: h.matmul(
                                pAfull[0:64, sl], nrow_bf[0:1, 0:64],
                                rowbuf[0:1, j, 1, sl], start=False, stop=True))
                        m = b.inc(pe)
                        b.wait(dve, pe, m)
                        b.op(dve, lambda h: h.tensor_single_scalar(
                            pAfull[0:64, :], pAfull[0:64, :], -30.0, ALU.max))
                        d = b.inc(dve)
                        b.wait(act, dve, d)
                        b.op(act, lambda h: h.activation(L2[:, :], pAfull[0:64, :], AF.Exp, scale=-1.0))
                        a = b.inc(act)
                        b.wait(dve, act, a)
                        b.op(dve, lambda h: h.tensor_tensor(
                            Msb[:, :], L2[:, :], bass.AP(cbt, 0, [[64, 64], [0, NH], [1, 64]]), ALU.mult))
                        dm = b.inc(dve)
                        b.wait(pe, dve, dm)
                        for q4 in range(4):
                            sl = slice(q4 * 512, (q4 + 1) * 512)
                            b.op(pe, lambda h, cseq=cseq, sl=sl: h.matmul(
                                pAfull[0:64, sl], dt_bf[:, cseq * 64:(cseq + 1) * 64],
                                e32_sb[:, sl], start=True, stop=True))
                        m = b.inc(pe)
                        b.wait(dve, pe, m)
                        b.op(dve, lambda h: h.scalar_tensor_tensor(
                            xdt[:, :], pAfull[0:64, :], 1.0,
                            x_tok[:, :], ALU.mult, ALU.mult))
                        dxdt = b.inc(dve)
                        b.wait(pe, dve, dxdt)
                        for q4 in range(4):
                            sl = slice(q4 * 512, (q4 + 1) * 512)
                            b.op(pe, lambda h, cseq=cseq, sl=sl: h.matmul(
                                pAfull[0:64, sl], dtdd[:, cseq * 64:(cseq + 1) * 64],
                                e32_sb[:, sl], start=True, stop=True))
                        m = b.inc(pe)
                        b.wait(dve, pe, m)
                        b.op(dve, lambda h: h.scalar_tensor_tensor(
                            xdd[:, :], pAfull[0:64, :], 1.0,
                            x_tok[:, :], ALU.mult, ALU.mult))
                        dxdd = b.inc(dve)
                        b.op(dve, lambda h, j=j: h.tensor_tensor(
                            edrow[0:1, :], rowbuf[0:1, j, 0, :], rowbuf[0:1, j, 1, :], ALU.add))
                        d = b.inc(dve)
                        b.wait(act, dve, d)
                        b.op(act, lambda h: h.activation(edrow_bf[0:1, :], edrow[0:1, :], AF.Exp))
                        a = b.inc(act)
                        b.wait(pe, act, a)
                        b.wait(pe, dve, dxdd)
                        for q4 in range(4):
                            sl = slice(q4 * 512, (q4 + 1) * 512)
                            b.op(pe, lambda h, sl=sl: h.matmul(
                                pAfull[:, sl], orow_bf[:, :], edrow_bf[0:1, sl],
                                start=True, stop=True))
                        m = b.inc(pe)
                        b.wait(dve, pe, m)
                        b.op(dve, lambda h, jl0=jl0: h.tensor_tensor(
                            cdec[:, :], pAfull[:, :],
                            bass.AP(bc_fm, 512 + jl0, [[1024, 128], [0, NH], [1, 64]]),
                            ALU.mult))
                        dcd = b.inc(dve)
                        cur = (s_ * 32 + cseq) % 2
                        nxt = 1 - cur
                        b.wait(pe, act, ga)
                        b.wait(pe, dve)
                        for q4 in range(4):
                            sl = slice(q4 * 512, (q4 + 1) * 512)
                            b.op(pe, lambda h, cseq=cseq, sl=sl: h.matmul(
                                pSfull[0:1, sl], gam[:, cseq:cseq + 1], e32_sb[:, sl],
                                start=True, stop=True))
                        m = b.inc(pe)
                        b.wait(dve, pe, m)
                        b.op(dve, lambda h: h.tensor_copy(grow[0:1, :], pSfull[0:1, 0:2048]))
                        d = b.inc(dve)
                        b.wait(pe, dve, d)
                        for q4 in range(4):
                            sl = slice(q4 * 512, (q4 + 1) * 512)
                            b.op(pe, lambda h, sl=sl: h.matmul(
                                pSfull[:, sl], orow_bf[:, :], grow[0:1, sl],
                                start=True, stop=True))
                        m = b.inc(pe)
                        b.wait(dve, pe, m)
                        b.op(dve, lambda h, cur=cur: h.tensor_tensor(
                            Stmp[:, :], pSfull[:, :], Ssb[:, cur, :], ALU.mult))
                        dstmp = b.inc(dve)
                        b.wait(pe, dve, dstmp)
                        for hh in range(NH):
                            b.op(pe, lambda h, hh=hh: h.matmul(
                                pSfull[:, hh * 64:(hh + 1) * 64],
                                btok[:, :],
                                xdd[:, hh * 64:(hh + 1) * 64],
                                start=True, stop=True))
                        m = b.inc(pe)
                        b.wait(dve, pe, m)
                        b.op(dve, lambda h, nxt=nxt: h.tensor_tensor(
                            Ssb[:, nxt, :], Stmp[:, :], pSfull[:, :], ALU.add))
                        dsn = b.inc(dve)
                        b.wait(pe, dve, dsn)
                        for hh in range(NH):
                            rb = (hh % 2) * 64
                            colr = (hh // 2) * 128 + j * 64
                            b.op(pe, lambda h, hh=hh, rb=rb, colr=colr: h.matmul(
                                pAfull[rb:rb + 64, colr:colr + 64],
                                xdt[:, hh * 64:(hh + 1) * 64],
                                Msb[:, hh * 64:(hh + 1) * 64],
                                start=True, stop=False))
                            b.op(pe, lambda h, hh=hh, cur=cur, rb=rb, colr=colr: h.matmul(
                                pAfull[rb:rb + 64, colr:colr + 64],
                                Ssb[:, cur, hh * 64:(hh + 1) * 64],
                                cdec[:, hh * 64:(hh + 1) * 64],
                                start=False, stop=True))
                        m = b.inc(pe)
                        b.wait(act, pe, m)
                        b.op(act, lambda h, l0=l0, j=j: h.activation(
                            y_fm[:, :, l0 + j * 64:l0 + j * 64 + 64],
                            pAfull[:, :].rearrange("p (c l) -> p c l", l=128)[:, :, j * 64:j * 64 + 64],
                            AF.Identity))
                        ya = b.inc(act)
                        b.wait(pe, act, ya)
                if DBG and s_ == 0 and i == 0:
                    b.wait(qg, act)
                    b.op(qg, lambda h: h.dma_start(out=dbg_y, in_=y_fm[:, :, :]))
                    b.inc(qg)
                b.wait(pe, act)
                b.wait(pe, dve)
                zld = {}

                def z_load(fo, upto=None):
                    fp = fo % 2
                    q_ = qg if fp == 0 else qg2
                    b.wait(q_, pe, upto)
                    b.op(q_, lambda h, fo=fo, fp=fp: h.dma_start(
                        out=wpb[:, fp, :, :], in_=wsrc[:, :, fo * 512:(fo + 1) * 512]))
                    zld[fo] = (b.inc(q_), q_)
                z_load(0)
                b.wait(act, dve)
                zpemk = {}
                zevm = {}
                zdvm = {}
                for fo in range(4):
                    if fo < 3:
                        z_load(fo + 1, zpemk.get(fo - 1))
                    fp = fo % 2
                    wld, q_ = zld[fo]
                    b.wait(pe, q_, wld)
                    for fc in range(4):
                        zc = fo * 4 + fc
                        pb_ = zc % 2
                        if pb_ in zevm:
                            b.wait(pe, act, zevm[pb_])
                        for k in range(NCH):
                            b.op(pe, lambda h, k=k, fc=fc, fp=fp, pb_=pb_: h.matmul(
                                pA[:, pb_, :], wpb[:, fp, k, fc * 128:(fc + 1) * 128],
                                xlnt[:, k, :], start=(k == 0), stop=(k == NCH - 1)))
                        m = b.inc(pe)
                        b.wait(act, pe, m)
                        if pb_ in zdvm:
                            b.wait(act, dve, zdvm[pb_])
                        b.op(act, lambda h, pb_=pb_, zc=zc: h.activation(
                            zs[:, pb_, :], pA[:, pb_, :], AF.Silu))
                        zevm[pb_] = a = b.inc(act)
                        b.wait(dve, act, a)
                        b.op(dve, lambda h, zc=zc: h.scalar_tensor_tensor(
                            sqa[:, :], xbcs[:, zc, :], cols[:, 1, zc:zc + 1],
                            y_fm[:, zc, :], ALU.mult, ALU.add))
                        b.op(dve, lambda h, zc=zc, pb_=pb_: h.tensor_tensor(
                            y_fm[:, zc, :], sqa[:, :], zs[:, pb_, :], ALU.mult))
                        zdvm[pb_] = b.inc(dve)
                    zpemk[fo] = b.pe.cnt
                b.wait(pe, dve)
                b.wait(act, dve)
                for zc in range(16):
                    b.op(act, lambda h, zc=zc: h.activation(
                        sqa[:, :], y_fm[:, zc, :], AF.Square))
                    a = b.inc(act)
                    b.wait(pe, act, a)
                    b.op(pe, lambda h, zc=zc: h.matmul(
                        pS[0:1, 0, :], ones_col[:, 0:1], sqa[:, :],
                        start=(zc == 0), stop=(zc == 15)))
                    m = b.inc(pe)
                    b.wait(act, pe, m)
                b.wait(dve, pe)
                b.op(dve, lambda h: h.tensor_scalar(
                    st2[:, 0, :], pS[0:1, 0, :], 1.0 / DIN, 1e-5, ALU.mult, ALU.add))
                d = b.inc(dve)
                b.wait(act, dve, d)
                b.op(act, lambda h: h.activation(st2[:, 0, :], st2[:, 0, :], AF.Ln))
                b.op(act, lambda h: h.activation(st2[:, 0, :], st2[:, 0, :], AF.Exp, scale=-0.5))
                a = b.inc(act)
                b.wait(pe, act, a)
                b.op(pe, lambda h: h.matmul(pS[:, 1, :], ones_row[:, :], st2[:, 0, :],
                                            start=True, stop=True))
                m = b.inc(pe)
                b.wait(dve, pe, m)
                for zc in range(16):
                    b.op(dve, lambda h, zc=zc: h.scalar_tensor_tensor(
                        sqa[:, :], y_fm[:, zc, :], 1.0, pS[:, 1, :], ALU.mult, ALU.mult), dep=True)
                    b.op(dve, lambda h, zc=zc: h.tensor_scalar_mul(
                        y_fm[:, zc, :], sqa[:, :], cols[:, 2, zc:zc + 1]))
                b.inc(dve)
                if DBG and s_ == 0 and i == 0:
                    b.wait(qg, dve)
                    b.op(qg, lambda h: h.dma_start(out=dbg_yg, in_=y_fm[:, :, :]))
                    b.inc(qg)
                b.wait(dve, qs)
                store_prev = b.qss.cnt
                old = {}

                def op_load(fc, upto=None):
                    fp = fc % 2
                    q_ = qw if fp == 0 else qw2
                    b.wait(q_, pe, upto)
                    b.op(q_, lambda h, fc=fc, fp=fp: h.dma_start(
                        out=wpb[:, fp, :, :].rearrange("p k f -> p (k f)")[:, 0:2048]
                            .rearrange("p (k f) -> p k f", f=128),
                        in_=wout_bf.rearrange("(k p) f -> p k f", p=128)
                            [:, :, fc * 128:(fc + 1) * 128]))
                    old[fc] = (b.inc(q_), q_)
                op_load(0)
                opemk = {}
                odvm = {}
                for fc in range(NCH):
                    if fc + 1 < NCH:
                        op_load(fc + 1, opemk.get(fc - 1))
                    fp = fc % 2
                    wld, q_ = old[fc]
                    b.wait(pe, q_, wld)
                    pb_ = fc % 2
                    if pb_ in odvm:
                        b.wait(pe, dve, odvm[pb_])
                    else:
                        b.wait(pe, dve)
                    for k in range(16):
                        b.op(pe, lambda h, k=k, fc=fc, fp=fp, pb_=pb_: h.matmul(
                            pA[:, pb_, :],
                            wpb[:, fp, :, :].rearrange("p k f -> p (k f)")
                                [:, 0:2048].rearrange("p (k f) -> p k f", f=128)[:, k, :],
                            y_fm[:, k, :], start=(k == 0), stop=(k == 15)))
                    m = b.inc(pe)
                    opemk[fc] = m
                    b.wait(dve, pe, m)
                    b.wait(dve, qss)
                    b.op(dve, lambda h, fc=fc, pb_=pb_: h.scalar_tensor_tensor(
                        ob[:, fc % 2, :], pA[:, pb_, :], 1.0, xt0[:, fc, :],
                        ALU.mult, ALU.add), dep=(fc < 2))
                    odvm[pb_] = b.inc(dve)
                    if fc % 2 == 1:
                        b.wait(qss, dve)
                        b.op(qss, lambda h, t0g=t0g, fc=fc: h.dma_start(
                            out=xout_f(t0g, 512)[:, fc - 1:fc + 1, :], in_=ob[:, :, :]))
                        b.inc(qss)
            b.wait(dve, pe)
            b.op(dve, lambda h, nxt=(s_ * 32 + 32) % 2: h.memset(Ssb[:, nxt, :], 0.0))
            b.inc(dve)
        b.op(pool, lambda h, v=b.qss.cnt: h.wait_ge(b.qss.sem, v))
        b.inc(pool)
        b.flush()


def attn0_weight_map(I):
    cw = I["L0_attn_conv_w"][:, 0, :]
    diag = np.zeros((4, NCONV, 128, 128), np.float32)
    for j in range(4):
        for c in range(NCONV):
            np.fill_diagonal(diag[j, c], cw[c * 128:(c + 1) * 128, j])
    e = np.kron(np.eye(NH, dtype=np.float32), np.ones((1, HDIM), np.float32))
    return dict(
        win=_c(I["L0_attn_in_proj_w"]),
        convd=diag,
        convb=_c(I["L0_attn_conv_b"].reshape(NCONV, 128).T),
        dtb=_c(I["L0_attn_dt_bias"].reshape(NH, 1)),
        alog=_c(I["L0_attn_A_log"].reshape(NH, 1)),
        dcol=_c(I["L0_attn_D"].repeat(HDIM).reshape(16, 128).T),
        nwcol=_c(I["L0_attn_norm_w"].reshape(16, 128).T),
        wout=_c(I["L0_attn_out_proj_w"]),
        e32=_c(e))


def run_attn0(x_fm, I):
    nc = build_attn0()
    wmap = attn0_weight_map(I)
    in_maps = []
    for j in range(NCORES):
        xs = _c(x_fm[:, 2 * j:2 * j + 2, :].reshape(DM, TOK))
        in_maps.append(dict(x_in=xs, **wmap))
    res = run_bass_kernel_spmd(nc, in_maps, core_ids=list(range(NCORES)))
    out = np.empty((DM, 16, 2048), np.float32)
    for j in range(NCORES):
        out[:, 2 * j:2 * j + 2, :] = res.results[j]["x_out"].reshape(DM, 2, SEQ)
    return out


def _c(a):
    return np.ascontiguousarray(a)


def ffn0_weight_map(I):
    w = dict(
        wfT=_c(I["L0_ffn_wf"].T), wiT=_c(I["L0_ffn_wi"].T), wcT=_c(I["L0_ffn_wc"].T),
        ufT=_c(I["L0_ffn_uf"].T), uiT=_c(I["L0_ffn_ui"].T), ucT=_c(I["L0_ffn_uc"].T),
        woT=_c(I["L0_ffn_wo"].T), uoT=_c(I["L0_ffn_uo"].T))
    for k, src in [("bf", "L0_ffn_bf"), ("bi", "L0_ffn_bi"), ("bc", "L0_ffn_bc"),
                   ("bo", "L0_ffn_bo"), ("in_c", "L0_ffn_in_c")]:
        w[k] = _c(I[src].reshape(NCH, 128).T)
    return w


def run_ffn0(x_fm, I):
    nc = build_ffn0()
    wmap = ffn0_weight_map(I)
    in_maps = []
    for j in range(NCORES):
        xs = _c(x_fm[:, :, j * LLOC:(j + 1) * LLOC].reshape(DM, TOK))
        in_maps.append(dict(x_in=xs, **wmap))
    res = run_bass_kernel_spmd(nc, in_maps, core_ids=list(range(NCORES)))
    out = np.empty((DM, 16, 2048), np.float32)
    for j in range(NCORES):
        out[:, :, j * LLOC:(j + 1) * LLOC] = \
            res.results[j]["x_out"].reshape(DM, 16, LLOC)
    return out


def decl_ffn0_w(nc, W):
    for w in ["wfT", "wiT", "wcT", "ufT", "uiT", "ucT", "woT", "uoT"]:
        W[w] = nc.dram_tensor(w, [DM, DM], F32, kind="ExternalInput").ap()
    for v in ["bf", "bi", "bc", "bo", "in_c"]:
        W[v] = nc.dram_tensor(v, [128, NCH], F32, kind="ExternalInput").ap()


def decl_ffn1_w(nc, W):
    W["wqkv"] = nc.dram_tensor("wqkv", [DM, 3 * DM], F32, kind="ExternalInput").ap()
    W["bqkv_row"] = nc.dram_tensor("bqkv_row", [1, 3 * DM], F32, kind="ExternalInput").ap()
    W["bqkv_col"] = nc.dram_tensor("bqkv_col", [128, 24], F32, kind="ExternalInput").ap()
    W["wgate"] = nc.dram_tensor("wgate", [DM, 8], F32, kind="ExternalInput").ap()
    W["bgate"] = nc.dram_tensor("bgate", [8, 1], F32, kind="ExternalInput").ap()
    W["wexp"] = nc.dram_tensor("wexp", [8, DM, DM], F32, kind="ExternalInput").ap()
    W["bexp"] = nc.dram_tensor("bexp", [8, DM], F32, kind="ExternalInput").ap()
    W["wexp_bf"] = nc.dram_tensor("wexp_bf", [8, DM, DM], BF16).ap()


def decl_attn1_w(nc, W):
    W["A"] = nc.dram_tensor("A", [64, 64], F32, kind="ExternalInput").ap()
    W["BT"] = nc.dram_tensor("BT", [DM, 64], F32, kind="ExternalInput").ap()
    W["CT"] = nc.dram_tensor("CT", [64, DM], F32, kind="ExternalInput").ap()
    for v in ["Dcol", "ldt_col", "ln1_col", "ln2_col"]:
        W[v] = nc.dram_tensor(v, [128, NCH], F32, kind="ExternalInput").ap()
    W["wgT"] = nc.dram_tensor("wgT", [DM, INTER], F32, kind="ExternalInput").ap()
    W["wuT"] = nc.dram_tensor("wuT", [DM, INTER], F32, kind="ExternalInput").ap()
    W["wdT"] = nc.dram_tensor("wdT", [INTER, DM], F32, kind="ExternalInput").ap()
    W["wg_bf"] = nc.dram_tensor("wg_bf", [DM, INTER], BF16).ap()
    W["wu_bf"] = nc.dram_tensor("wu_bf", [DM, INTER], BF16).ap()
    W["wd_bf"] = nc.dram_tensor("wd_bf", [INTER, DM], BF16).ap()


def decl_attn0_w(nc, W):
    W["win"] = nc.dram_tensor("win", [DM, 4384], F32, kind="ExternalInput").ap()
    W["convd"] = nc.dram_tensor("convd", [4, NCONV, 128, 128], F32, kind="ExternalInput").ap()
    W["convb"] = nc.dram_tensor("convb", [128, NCONV], F32, kind="ExternalInput").ap()
    W["dtb"] = nc.dram_tensor("dtb", [NH, 1], F32, kind="ExternalInput").ap()
    W["alog"] = nc.dram_tensor("alog", [NH, 1], F32, kind="ExternalInput").ap()
    W["dcol"] = nc.dram_tensor("dcol", [128, 16], F32, kind="ExternalInput").ap()
    W["nwcol"] = nc.dram_tensor("nwcol", [128, 16], F32, kind="ExternalInput").ap()
    W["wout"] = nc.dram_tensor("wout", [DIN, DM], F32, kind="ExternalInput").ap()
    W["e32"] = nc.dram_tensor("e32", [NH, DIN], F32, kind="ExternalInput").ap()
    W["wout_bf"] = nc.dram_tensor("wout_bf", [DIN, DM], BF16).ap()


def _plain_fns(xin, xout):
    xs = xin.rearrange("(c p) t -> p c t", p=128)
    xd = xout.rearrange("(c p) t -> p c t", p=128)
    return (lambda t0, w: xs[:, :, t0:t0 + w],
            lambda t0, w: xd[:, :, t0:t0 + w])


def _wrap(emit, decl):
    def build():
        nc = bass.Bass("TRN2", target_bir_lowering=False)
        xin = nc.dram_tensor("x_in", [DM, TOK], F32, kind="ExternalInput").ap()
        xout = nc.dram_tensor("x_out", [DM, TOK], F32, kind="ExternalOutput").ap()
        W = {}
        decl(nc, W)
        b = make_b(nc)
        fin, fout = _plain_fns(xin, xout)
        emit(nc, b, W, fin, fout)
        return nc
    return build


build_ffn0 = _wrap(emit_ffn0, decl_ffn0_w)
build_ffn1 = _wrap(emit_ffn1, decl_ffn1_w)
build_attn1 = _wrap(emit_attn1, decl_attn1_w)
build_attn0 = _wrap(emit_attn0, decl_attn0_w)


def build_fused():
    nc = bass.Bass("TRN2", target_bir_lowering=False)
    xin = nc.dram_tensor("x_in", [DM, TOK], F32, kind="ExternalInput").ap()
    xout = nc.dram_tensor("x_out", [DM, TOK], F32, kind="ExternalOutput").ap()
    W = {}
    decl_ffn0_w(nc, W)
    decl_attn0_w(nc, W)
    decl_ffn1_w(nc, W)
    decl_attn1_w(nc, W)
    ab = [nc.dram_tensor(f"a2a{i}{d}", [NCORES, DM * 512], F32)
          for i in range(3) for d in ("i", "o")]
    a1i, a1o, a2i, a2o, a3i, a3o = ab
    xbs = [nc.dram_tensor(f"xb{i}", [DM, TOK], F32).ap() for i in range(1, 6)]
    xb1, xb2, xb3, xb4, xb5 = xbs
    b = make_b(nc)
    cc = b.reg("cc", "gpsimd", dma=True, step=1)

    def lshard_in(t0, w):
        xs = xin.rearrange("(c p) t -> p c t", p=128)
        return xs[:, :, t0:t0 + w]

    def store_l(buf):
        def f(t0, w):
            assert w == 256 and t0 % 256 == 0
            t_ = t0 // 256
            return buf.ap().rearrange("j (c p two l) -> j p c two l", p=128,
                                      two=2, l=256)[t_ // 2][:, :, t_ % 2, :]
        return f

    load_l = store_l

    def plain(buf):
        xs = buf.rearrange("(c p) t -> p c t", p=128)
        return lambda t0, w: xs[:, :, t0:t0 + w]

    def a2a(bin_, bout):
        b.wait(cc, b.qss)
        b.wait(cc, b.qg)
        b.op(cc, lambda h: h.collective_compute(
            "AllToAll", ALU.bypass, replica_groups=[list(range(NCORES))],
            ins=[bin_.ap().opt()], outs=[bout.ap().opt()]))
        b.inc(cc)
        b.flush()

    def reorder_out(bufa2a, bufplain):
        b.wait(b.qg, cc)
        pl = bufplain.rearrange("(c p) t -> p c t", p=128)
        for j in range(NCORES):
            for two in range(2):
                dst = pl[:, :, two * SEQ + j * 256:two * SEQ + (j + 1) * 256]
                src = bufa2a.ap().rearrange("j (c p two l) -> j p c two l",
                                            p=128, two=2, l=256)[j][:, :, two, :]
                b.op(b.qg, lambda h, dst=dst, src=src: h.dma_start(out=dst, in_=src))
                b.inc(b.qg)
        b.flush()

    def reorder_in(bufplain, bufa2a):
        b.wait(b.qg, b.qss)
        pl = bufplain.rearrange("(c p) t -> p c t", p=128)
        for j in range(NCORES):
            for two in range(2):
                src = pl[:, :, two * SEQ + j * 256:two * SEQ + (j + 1) * 256]
                dst = bufa2a.ap().rearrange("j (c p two l) -> j p c two l",
                                            p=128, two=2, l=256)[j][:, :, two, :]
                b.op(b.qg, lambda h, dst=dst, src=src: h.dma_start(out=dst, in_=src))
                b.inc(b.qg)
        b.flush()

    FUSE_N = int(os.environ.get("FUSE_N", "6"))

    def finish(buf):
        b.wait(b.qg, b.qss)
        b.op(b.qg, lambda h: h.dma_start(out=xout, in_=buf))
        b.inc(b.qg)
        b.op(b.pool, lambda h, v=b.qg.cnt: h.wait_ge(b.qg.sem, v))
        b.inc(b.pool)
        b.flush()
        return nc

    emit_ffn0(nc, b, W, lshard_in, store_l(a1i), first=True)
    a2a(a1i, a1o)
    reorder_out(a1o, xb1)
    if FUSE_N <= 1:
        return finish(xb1)
    b.wait(b.qs, b.qg)
    emit_attn0(nc, b, W, plain(xb1), plain(xb2), first=True)
    if FUSE_N <= 2:
        return finish(xb2)
    reorder_in(xb2, a2i)
    a2a(a2i, a2o)
    b.wait(b.qs, cc)
    emit_ffn0(nc, b, W, load_l(a2o), store_l(a3i), first=False)
    a2a(a3i, a3o)
    reorder_out(a3o, xb3)
    if FUSE_N <= 3:
        return finish(xb3)
    b.wait(b.qs, b.qg)
    emit_ffn1(nc, b, W, plain(xb3), plain(xb4), first=True)
    if FUSE_N <= 4:
        return finish(xb4)
    b.wait(b.qs, b.qss)
    emit_attn1(nc, b, W, plain(xb4), plain(xb5), first=True)
    if FUSE_N <= 5:
        return finish(xb5)
    b.wait(b.qs, b.qss)
    emit_ffn1(nc, b, W, plain(xb5),
              _plain_fns(xin, xout)[1], first=False)
    return nc


def fused_in_maps(I):
    wmap = {}
    wmap.update(ffn0_weight_map(I))
    wmap.update(attn0_weight_map(I))
    wmap.update(ffn1_weight_map(I))
    wmap.update(attn1_weight_map(I))
    x_fm = _c(I["L0_attn_u"].astype(np.float32).transpose(2, 0, 1))
    in_maps = []
    for j in range(NCORES):
        xs = _c(x_fm[:, :, j * 256:(j + 1) * 256].reshape(DM, TOK))
        in_maps.append(dict(x_in=xs, **wmap))
    return in_maps


def kernel(**inputs):
    I = inputs
    nc = build_fused()
    res = run_bass_kernel_spmd(nc, fused_in_maps(I), core_ids=list(range(NCORES)))
    out = np.empty((16, 2048, DM), np.float32)
    for j in range(NCORES):
        xb = res.results[j]["x_out"].reshape(DM, 2, SEQ)
        out[2 * j:2 * j + 2] = xb.transpose(1, 2, 0)
    return _c(out)


def kernel_unfused(**inputs):
    I = inputs
    x_fm = _c(I["L0_attn_u"].astype(np.float32).transpose(2, 0, 1))
    x_fm = run_ffn0(x_fm, I)
    x_fm = run_attn0(x_fm, I)
    x_fm = run_ffn0(x_fm, I)
    x_fm = run_ffn1(x_fm, I)
    x_fm = run_attn1(x_fm, I)
    x_fm = run_ffn1(x_fm, I)
    return _c(x_fm.transpose(1, 2, 0))
